# version 07
# baseline (speedup 1.0000x reference)
import time
import contextlib
import numpy as np
import ml_dtypes
import concourse.bass as bass
import concourse.mybir as mybir
from concourse.bass_utils import run_bass_kernel_spmd

F32 = mybir.dt.float32
BF16 = mybir.dt.bfloat16
I32 = mybir.dt.int32
AF = mybir.ActivationFunctionType
ALU = mybir.AluOpType
AX = mybir.AxisListType.X
ds = bass.ds

NT, DM, FI, LAT = 1024, 4096, 8192, 2048
DTL, ITL, LTL = DM // 128, FI // 128, LAT // 128
NE, NZ, NH, NG, HP, SN = 8, 8, 128, 8, 64, 128
EPS = 1e-5
NCV = 80
PAIRS = [(i, j) for i in range(NE) for j in range(i + 1, NE)]

PHASES = 9
SIM_SKIP_POLL = False
DBG_SSD = False


class _C:
    pass


def _pt(w):
    K, M = w.shape
    nk, nm = K // 128, M // 128
    return np.ascontiguousarray(
        np.asarray(w, np.float32).reshape(nk, 128, nm, 128).transpose(2, 1, 0, 3)
        .reshape(nm, 128, nk * 128).astype(ml_dtypes.bfloat16))



class _S:
    pass


def mk_sems(nc, st, prefix, names):
    S = _S()
    for nm in names:
        if st is None:
            setattr(S, nm, nc.alloc_semaphore(name=prefix + "_" + nm))
        else:
            setattr(S, nm, st.enter_context(nc.semaphore(prefix + "_" + nm)))
    return S



class WR:
    def __init__(self, eng, es, name, init, step):
        self.eng, self.step = eng, step
        self.reg = es.enter_context(eng.register(name))
        eng.reg_mov(self.reg, init)
    def wait(self, sem):
        self.eng.wait_ge(sem, self.reg)
        self.eng.reg_add(self.reg, self.reg, self.step)

    def link(self, ins, sem):
        ins.then_inc(sem, 1)
        self.eng.wait_ge(sem, self.reg)
        self.eng.reg_add(self.reg, self.reg, self.step)

    def bump(self):
        self.eng.reg_add(self.reg, self.reg, self.step)


_LN_ADV = {"q_x": 128, "q_sp": 160, "q_a": 8, "q_b": 8, "q_c": 8, "q_d": 10,
           "q_vs": 24, "q_ss": 8}


def emit_ln(nc, C, src, name):
  if not hasattr(C, "ln_sems"):
      C.ln_sems = mk_sems(nc, C.st, "lnS", list(_LN_ADV))
      C.ln_calls = 0
  S = C.ln_sems
  LB = {q: C.ln_calls * _LN_ADV[q] for q in _LN_ADV}
  C.ln_calls += 1
  with contextlib.ExitStack() as _es:
    C.xa = _es.enter_context(nc.sbuf_tensor([128, 2 * DM], F32))
    C.lnw = _es.enter_context(nc.sbuf_tensor([128, 2 * DM], BF16))
    C.sq = _es.enter_context(nc.sbuf_tensor([128, DM], F32))
    with nc.Block(name + "_c") as blk:
        @blk.sync
        def _(sy):
            sy.sem_inc(S.q_d, 2)
            sy.sem_inc(S.q_sp, 32)

    with nc.Block(name) as blk:
        @blk.sync
        def _(sy):
            for k in range(8):
                b = k % 2
                sy.wait_ge(S.q_d, LB["q_d"] + 1 + k)
                sy.dma_start(out=C.xa[:, b * DM:(b + 1) * DM],
                             in_=src[k * 128:(k + 1) * 128, :]).then_inc(S.q_x, 16)
                sy.wait_ge(S.q_d, LB["q_d"] + 3 + k)
                sy.dma_start(out=C.sp_lnt[k * 128:(k + 1) * 128, :],
                             in_=C.lnw[:, b * DM:(b + 1) * DM]).then_inc(S.q_sp, 16)
            sy.wait_ge(S.q_sp, LB["q_sp"] + 32 + 16 * 8)

        @blk.vector
        def _(v):
          with contextlib.ExitStack() as es:
            ve = WR(v, es, name + "_ve", 1 + LB["q_vs"], 1)
            v_x = WR(v, es, name + "_v_x", 16 + LB["q_x"], 16)
            v_a = WR(v, es, name + "_v_a", 1 + LB["q_a"], 1)
            v_c = WR(v, es, name + "_v_c", 1 + LB["q_c"], 1)
            v_sp = WR(v, es, name + "_v_sp", 16 + LB["q_sp"], 16)
            with v.Fori(0, 4) as t:
                for b in range(2):
                    xs = C.xa[:, b * DM:(b + 1) * DM]
                    r = C.red[:, b * 8:(b + 1) * 8]
                    v_x.wait(S.q_x)
                    ve.link(v.reduce_sum(r[:, 0:1], xs, axis=AX), S.q_vs)
                    v_a.wait(S.q_a)
                    ve.link(v.tensor_scalar_mul(r[:, 2:3], r[:, 0:1], 1.0 / DM), S.q_vs)
                    ve.link(v.tensor_mul(r[:, 3:4], r[:, 2:3], r[:, 2:3]), S.q_vs)
                    v.scalar_tensor_tensor(r[:, 4:5], r[:, 1:2], 1.0 / DM, r[:, 3:4],
                                           ALU.mult, ALU.subtract).then_inc(S.q_b, 1)
                    v_c.wait(S.q_c)
                    v_sp.wait(S.q_sp)
                    v.tensor_scalar(C.lnw[:, b * DM:(b + 1) * DM], xs, r[:, 2:3], r[:, 5:6],
                                    ALU.subtract, ALU.mult).then_inc(S.q_d, 1)

        @blk.scalar
        def _(sc):
          with contextlib.ExitStack() as es:
            se = WR(sc, es, name + "_se", 1 + LB["q_ss"], 1)
            s_x = WR(sc, es, name + "_s_x", 16 + LB["q_x"], 16)
            s_b = WR(sc, es, name + "_s_b", 1 + LB["q_b"], 1)
            with sc.Fori(0, 4) as t:
                for b in range(2):
                    xs = C.xa[:, b * DM:(b + 1) * DM]
                    r = C.red[:, b * 8:(b + 1) * 8]
                    s_x.wait(S.q_x)
                    sc.activation(C.sq[:, :], xs, AF.Square, accum_out=r[:, 1:2]).then_inc(S.q_a, 1)
                    s_b.wait(S.q_b)
                    se.link(sc.activation(r[:, 6:7], r[:, 4:5], AF.Ln, bias=C.epsc[:, 0:1]), S.q_ss)
                    sc.activation(r[:, 5:6], r[:, 6:7], AF.Exp, scale=-0.5).then_inc(S.q_c, 1)


_FFN_ADV = {"q_lnf": 1024, "q_upw": 2048, "q_upm": 130, "q_upe": 130, "q_ups": 2080,
            "q_f1w": 512, "q_f1m": 34, "q_f1e": 34, "q_f1s": 544,
            "q_exw": 512, "q_exm": 34, "q_exe": 34, "q_exs": 544,
            "q_dnw": 1024, "q_dnw2": 1024, "q_dnm": 66, "q_dne": 66, "q_dns": 1056,
            "q_lg": 2, "q_lge": 2, "q_lt": 2, "q_rwc": 2, "q_rwr": 6, "q_rwb": 2,
            "q_hl": 32, "q_ll": 32, "q_ml": 32, "q_ep": 4224, "q_epo": 160,
            "q_a": 8, "q_b": 8, "q_c": 8, "q_d": 10, "q_gb": 32,
            "q_vs": 360, "q_vs2": 48, "q_ss2": 8}


def emit_ffn(nc, C, xin, xout, name, final_wait=False):
  if not hasattr(C, "ffn_sems"):
      C.ffn_sems = mk_sems(nc, C.st, "ffnS", list(_FFN_ADV))
      C.ffn_calls = 0
  s = C.ffn_sems
  FB = {q: C.ffn_calls * _FFN_ADV[q] for q in _FFN_ADV}
  C.ffn_calls += 1
  with contextlib.ExitStack() as _es:
    C.lnfm = _es.enter_context(nc.sbuf_tensor([128, DTL * 512], BF16))
    C.wbuf = _es.enter_context(nc.sbuf_tensor([128, 2 * 4096], BF16))
    C.wbig = _es.enter_context(nc.sbuf_tensor([128, 2 * 8192], BF16))
    C.w2buf = _es.enter_context(nc.sbuf_tensor([128, 2 * 2048], BF16))
    C.hbuf = _es.enter_context(nc.sbuf_tensor([128, ITL * 512], BF16))
    C.xlat = _es.enter_context(nc.sbuf_tensor([128, LTL * 512], BF16))
    C.moe = _es.enter_context(nc.sbuf_tensor([128, LTL * 512], BF16))
    C.upst = _es.enter_context(nc.sbuf_tensor([128, 2 * 512], BF16))
    C.f1st = _es.enter_context(nc.sbuf_tensor([128, 2 * 512], BF16))
    C.exst = _es.enter_context(nc.sbuf_tensor([128, 2 * 512], BF16))
    C.dnst = _es.enter_context(nc.sbuf_tensor([128, 2 * 512], BF16))
    C.scr = _es.enter_context(nc.sbuf_tensor([128, 512], F32))
    C.lgf = _es.enter_context(nc.sbuf_tensor([128, 512], F32))
    C.ltm = _es.enter_context(nc.sbuf_tensor([128, 32], F32))
    C.prs = _es.enter_context(nc.sbuf_tensor([128, 28], F32))
    C.rwc = _es.enter_context(nc.sbuf_tensor([128, 4], F32))
    C.rwrow = _es.enter_context(nc.sbuf_tensor([1, 512], F32))
    C.rwbc = _es.enter_context(nc.sbuf_tensor([128, 512], F32))
    C.ps = _es.enter_context(nc.psum_tensor([128, 2 * 512], F32))
    C.psg = _es.enter_context(nc.psum_tensor([128, 512], F32))
    C.pst = _es.enter_context(nc.psum_tensor([128, 512], F32))
    C.pst2 = _es.enter_context(nc.psum_tensor([128, 512], F32))
    with nc.Block(name + "_c") as blk:
        @blk.sync
        def _(sy):
            for q in (s.q_upm, s.q_f1m, s.q_exm, s.q_dnm):
                sy.sem_inc(q, 2)
            for q in (s.q_upe, s.q_f1e, s.q_exe, s.q_dne):
                sy.sem_inc(q, 2)
            for q in (s.q_ups, s.q_f1s, s.q_exs, s.q_dns):
                sy.sem_inc(q, 32)
        @blk.gpsimd
        def _(g):
            g.memset(C.lgf[:, :], 0.0)

    with nc.Block(name) as blk:
        @blk.sync
        def _(sy):
            for t in range(2):
                sy.wait_ge(s.q_dnm, FB["q_dnm"] + t * 32 + 2)
                for dt in range(DTL):
                    sy.dma_start_transpose(
                        C.lnfm[:, dt * 512:(dt + 1) * 512],
                        C.sp_lnt[t * 512:(t + 1) * 512, dt * 128:(dt + 1) * 128]).then_inc(s.q_lnf, 16)
                for k in range(64):
                    b = k % 2
                    sy.wait_ge(s.q_upe, FB["q_upe"] + t * 64 + k + 3)
                    sy.dma_start(out=C.sp_h[:, k * 512:(k + 1) * 512],
                                 in_=C.upst[:, b * 512:(b + 1) * 512]).then_inc(s.q_ups, 16)
                sy.wait_ge(s.q_ups, FB["q_ups"] + 32 + (t * 64 + 64) * 16)
                sy.dma_start(out=C.hbuf[:, :], in_=C.sp_h[:, :]).then_inc(s.q_hl, 16)
                for k in range(16):
                    b = k % 2
                    sy.wait_ge(s.q_f1e, FB["q_f1e"] + t * 16 + k + 3)
                    sy.dma_start(out=C.sp_lat[:, k * 512:(k + 1) * 512],
                                 in_=C.f1st[:, b * 512:(b + 1) * 512]).then_inc(s.q_f1s, 16)
                sy.wait_ge(s.q_f1s, FB["q_f1s"] + 32 + (t * 16 + 16) * 16)
                sy.dma_start(out=C.xlat[:, :], in_=C.sp_lat[:, :]).then_inc(s.q_ll, 16)
                for k in range(16):
                    b = k % 2
                    sy.wait_ge(s.q_exe, FB["q_exe"] + t * 16 + k + 3)
                    sy.dma_start(out=C.sp_moe[:, k * 512:(k + 1) * 512],
                                 in_=C.exst[:, b * 512:(b + 1) * 512]).then_inc(s.q_exs, 16)
                sy.wait_ge(s.q_exs, FB["q_exs"] + 32 + (t * 16 + 16) * 16)
                sy.dma_start(out=C.moe[:, :], in_=C.sp_moe[:, :]).then_inc(s.q_ml, 16)
                for k in range(32):
                    b = k % 2
                    sy.wait_ge(s.q_dne, FB["q_dne"] + t * 32 + k + 3)
                    sy.dma_start(out=C.sp_os[k * 128:(k + 1) * 128, t * 512:(t + 1) * 512],
                                 in_=C.dnst[:, b * 512:(b + 1) * 512]).then_inc(s.q_dns, 16)
            sy.wait_ge(s.q_dns, FB["q_dns"] + 32 + 16 * 64)

        @blk.gpsimd
        def _(g):
            for t in range(2):
                g.wait_ge(s.q_exm, FB["q_exm"] + 2 + 16 * t)
                for kk in range(64):
                    b = kk % 2
                    g.wait_ge(s.q_upm, FB["q_upm"] + 1 + 64 * t + kk)
                    g.dma_start(out=C.wbuf[:, b * 4096:(b + 1) * 4096],
                                in_=C.wup[kk, :, :]).then_inc(s.q_upw, 16)
                g.wait_ge(s.q_upm, FB["q_upm"] + 66 + 64 * t)
                for kk in range(16):
                    b = kk % 2
                    g.wait_ge(s.q_f1m, FB["q_f1m"] + 1 + 16 * t + kk)
                    g.dma_start(out=C.wbuf[:, b * 4096:(b + 1) * 4096],
                                in_=C.wf1[kk, :, :]).then_inc(s.q_f1w, 16)
                g.wait_ge(s.q_f1m, FB["q_f1m"] + 18 + 16 * t)
                for kk in range(16):
                    b = kk % 2
                    g.wait_ge(s.q_exm, FB["q_exm"] + 1 + 16 * t + kk)
                    g.dma_start(out=C.wbuf[:, b * 4096:b * 4096 + 2048],
                                in_=C.wex[kk, :, :]).then_inc(s.q_exw, 16)
                for kk in range(32):
                    b = kk % 2
                    g.wait_ge(s.q_dnm, FB["q_dnm"] + 1 + 32 * t + kk)
                    g.dma_start(out=C.wbig[:, b * 8192:(b + 1) * 8192],
                                in_=C.wdn[kk, :, :]).then_inc(s.q_dnw, 16)
                    g.dma_start(out=C.w2buf[:, b * 2048:(b + 1) * 2048],
                                in_=C.wf2[kk, :, :]).then_inc(s.q_dnw2, 16)

        @blk.tensor
        def _(tn):
          with contextlib.ExitStack() as es:
            t_lnf = WR(tn, es, name + "_t_lnf", 512 + FB["q_lnf"], 512)
            t_upw = WR(tn, es, name + "_t_upw", 16 + FB["q_upw"], 16)
            t_upe = WR(tn, es, name + "_t_upe", 1 + FB["q_upe"], 1)
            t_lge = WR(tn, es, name + "_t_lge", 1 + FB["q_lge"], 1)
            t_f1w = WR(tn, es, name + "_t_f1w", 16 + FB["q_f1w"], 16)
            t_f1e = WR(tn, es, name + "_t_f1e", 1 + FB["q_f1e"], 1)
            t_rwc = WR(tn, es, name + "_t_rwc", 1 + FB["q_rwc"], 1)
            t_rwr = WR(tn, es, name + "_t_rwr", 2 + FB["q_rwr"], 3)
            t_exw = WR(tn, es, name + "_t_exw", 16 + FB["q_exw"], 16)
            t_exe = WR(tn, es, name + "_t_exe", 1 + FB["q_exe"], 1)
            t_ll = WR(tn, es, name + "_t_ll", 16 + FB["q_ll"], 16)
            t_dnw = WR(tn, es, name + "_t_dnw", 16 + FB["q_dnw"], 16)
            t_dnw2 = WR(tn, es, name + "_t_dnw2", 16 + FB["q_dnw2"], 16)
            t_dne = WR(tn, es, name + "_t_dne", 1 + FB["q_dne"], 1)
            t_hl = WR(tn, es, name + "_t_hl", 16 + FB["q_hl"], 16)
            t_ml = WR(tn, es, name + "_t_ml", 16 + FB["q_ml"], 16)
            t_g1 = WR(tn, es, name + "_t_g1", 66 + FB["q_upe"], 64)
            t_g2 = WR(tn, es, name + "_t_g2", 18 + FB["q_f1e"], 16)
            t_g3 = WR(tn, es, name + "_t_g3", 18 + FB["q_exe"], 16)
            with tn.Fori(0, 2) as t:
                t_lnf.wait(s.q_lnf)
                for dt in range(DTL):
                    ins = tn.matmul(C.psg[0:8, 0:512], C.wg[:, dt * 8:(dt + 1) * 8],
                                    C.lnfm[:, dt * 512:(dt + 1) * 512],
                                    start=(dt == 0), stop=(dt == DTL - 1))
                ins.then_inc(s.q_lg, 1)
                with tn.Fori(0, 32) as j:
                    for b in range(2):
                        t_upw.wait(s.q_upw)
                        t_upe.wait(s.q_upe)
                        for dt in range(DTL):
                            ins = tn.matmul(C.ps[:, b * 512:(b + 1) * 512],
                                            C.wbuf[:, b * 4096 + dt * 128:b * 4096 + (dt + 1) * 128],
                                            C.lnfm[:, dt * 512:(dt + 1) * 512],
                                            start=(dt == 0), stop=(dt == DTL - 1))
                        ins.then_inc(s.q_upm, 1)
                t_lge.wait(s.q_lge)
                for sb in range(4):
                    ins = tn.transpose(C.pst[:, sb * 128:(sb + 1) * 128],
                                       C.lgf[:, sb * 128:(sb + 1) * 128], C.id32[:, :])
                ins.then_inc(s.q_lt, 1)
                t_g1.wait(s.q_upe)
                with tn.Fori(0, 8) as j:
                    for b in range(2):
                        t_f1w.wait(s.q_f1w)
                        t_f1e.wait(s.q_f1e)
                        for dt in range(DTL):
                            ins = tn.matmul(C.ps[:, b * 512:(b + 1) * 512],
                                            C.wbuf[:, b * 4096 + dt * 128:b * 4096 + (dt + 1) * 128],
                                            C.lnfm[:, dt * 512:(dt + 1) * 512],
                                            start=(dt == 0), stop=(dt == DTL - 1))
                        ins.then_inc(s.q_f1m, 1)
                t_rwc.wait(s.q_rwc)
                for sb in range(4):
                    ins = tn.matmul(C.pst2[0:1, sb * 128:(sb + 1) * 128], C.rwc[:, sb:sb + 1],
                                    C.id32[:, 0:128], start=True, stop=True)
                ins.then_inc(s.q_rwr, 1)
                t_rwr.wait(s.q_rwr)
                for sb in range(4):
                    ins = tn.matmul(C.psg[0:128, sb * 128:(sb + 1) * 128], C.ones32[0:1, 0:128],
                                    C.rwrow[0:1, sb * 128:(sb + 1) * 128], start=True, stop=True)
                ins.then_inc(s.q_rwr, 1)
                t_g2.wait(s.q_f1e)
                t_ll.wait(s.q_ll)
                with tn.Fori(0, 8) as j:
                    for b in range(2):
                        t_exw.wait(s.q_exw)
                        t_exe.wait(s.q_exe)
                        for lt in range(LTL):
                            ins = tn.matmul(C.ps[:, b * 512:(b + 1) * 512],
                                            C.wbuf[:, b * 4096 + lt * 128:b * 4096 + (lt + 1) * 128],
                                            C.xlat[:, lt * 512:(lt + 1) * 512],
                                            start=(lt == 0), stop=(lt == LTL - 1))
                        ins.then_inc(s.q_exm, 1)
                t_g3.wait(s.q_exe)
                t_hl.wait(s.q_hl)
                t_ml.wait(s.q_ml)
                with tn.Fori(0, 16) as j:
                    for b in range(2):
                        t_dnw.wait(s.q_dnw)
                        t_dnw2.wait(s.q_dnw2)
                        t_dne.wait(s.q_dne)
                        for it in range(ITL):
                            tn.matmul(C.ps[:, b * 512:(b + 1) * 512],
                                      C.wbig[:, b * 8192 + it * 128:b * 8192 + (it + 1) * 128],
                                      C.hbuf[:, it * 512:(it + 1) * 512],
                                      start=(it == 0), stop=False)
                        for lt in range(LTL):
                            ins = tn.matmul(C.ps[:, b * 512:(b + 1) * 512],
                                            C.w2buf[:, b * 2048 + lt * 128:b * 2048 + (lt + 1) * 128],
                                            C.moe[:, lt * 512:(lt + 1) * 512],
                                            start=False, stop=(lt == LTL - 1))
                        ins.then_inc(s.q_dnm, 1)

        @blk.vector
        def _(v):
            nvs = 0
            def lk(ins):
                nonlocal nvs
                ins.then_inc(s.q_vs, 1)
                nvs += 1
                v.wait_ge(s.q_vs, FB["q_vs"] + nvs)
            for t in range(2):
                v.wait_ge(s.q_lg, FB["q_lg"] + t + 1)
                v.tensor_copy(C.lgf[0:8, :], C.psg[0:8, 0:512]).then_inc(s.q_lge, 1)
                for k in range(64):
                    b = k % 2
                    kk = t * 64 + k
                    v.wait_ge(s.q_upm, FB["q_upm"] + 3 + kk)
                    v.wait_ge(s.q_ups, FB["q_ups"] + 16 * (kk + 1))
                    lk(v.tensor_relu(C.scr[:, 0:512], C.ps[:, b * 512:(b + 1) * 512]))
                    v.tensor_mul(C.upst[:, b * 512:(b + 1) * 512],
                                 C.scr[:, 0:512], C.scr[:, 0:512]).then_inc(s.q_upe, 1)
                v.wait_ge(s.q_lt, FB["q_lt"] + t + 1)
                for sb in range(4):
                    lk(v.tensor_copy(C.ltm[:, sb * 8:(sb + 1) * 8],
                                     C.pst[:, sb * 128:sb * 128 + 8]))
                for sb in range(4):
                    for ci, (i, j2) in enumerate(PAIRS):
                        lk(v.tensor_add(C.prs[:, ci:ci + 1], C.ltm[:, sb * 8 + i:sb * 8 + i + 1],
                                        C.ltm[:, sb * 8 + j2:sb * 8 + j2 + 1]))
                    ins = v.reduce_max(C.rwc[:, sb:sb + 1], C.prs[:, 0:28], axis=AX)
                ins.then_inc(s.q_rwc, 1)
                for k in range(16):
                    b = k % 2
                    kk = t * 16 + k
                    v.wait_ge(s.q_f1m, FB["q_f1m"] + 3 + kk)
                    v.wait_ge(s.q_f1s, FB["q_f1s"] + 16 * (kk + 1))
                    v.tensor_copy(C.f1st[:, b * 512:(b + 1) * 512],
                                  C.ps[:, b * 512:(b + 1) * 512]).then_inc(s.q_f1e, 1)
                v.wait_ge(s.q_rwr, FB["q_rwr"] + 1 + 3 * t)
                v.tensor_copy(C.rwrow[0:1, :], C.pst2[0:1, 0:512]).then_inc(s.q_rwr, 1)
                v.wait_ge(s.q_rwr, FB["q_rwr"] + 3 + 3 * t)
                v.tensor_copy(C.rwbc[:, :], C.psg[0:128, 0:512]).then_inc(s.q_rwb, 1)
                for k in range(16):
                    b = k % 2
                    kk = t * 16 + k
                    v.wait_ge(s.q_exm, FB["q_exm"] + 3 + kk)
                    v.wait_ge(s.q_exs, FB["q_exs"] + 16 * (kk + 1))
                    v.tensor_mul(C.exst[:, b * 512:(b + 1) * 512],
                                 C.ps[:, b * 512:(b + 1) * 512], C.rwbc[:, 0:512]).then_inc(s.q_exe, 1)
                for k in range(32):
                    b = k % 2
                    kk = t * 32 + k
                    v.wait_ge(s.q_dnm, FB["q_dnm"] + 3 + kk)
                    v.wait_ge(s.q_dns, FB["q_dns"] + 16 * (kk + 1))
                    v.tensor_copy(C.dnst[:, b * 512:(b + 1) * 512],
                                  C.ps[:, b * 512:(b + 1) * 512]).then_inc(s.q_dne, 1)


  with contextlib.ExitStack() as _es:
    C.otm = _es.enter_context(nc.sbuf_tensor([128, 2 * DM], BF16))
    C.xa = _es.enter_context(nc.sbuf_tensor([128, 2 * DM], F32))
    C.lnw = _es.enter_context(nc.sbuf_tensor([128, 2 * DM], F32))
    C.sq = _es.enter_context(nc.sbuf_tensor([128, DM], F32))
    C.sq2 = _es.enter_context(nc.sbuf_tensor([128, DM], F32))
    C.gbc = _es.enter_context(nc.sbuf_tensor([128, DM], F32))
    C.bbc = _es.enter_context(nc.sbuf_tensor([128, DM], F32))
    with nc.Block(name + "_eload") as blk:
        @blk.sync
        def _(sy):
            sy.dma_start(out=C.gbc[:, :], in_=C.gbc_in[:, :]).then_inc(s.q_gb, 16)
            sy.dma_start(out=C.bbc[:, :], in_=C.bbc_in[:, :]).then_inc(s.q_gb, 16)
            sy.wait_ge(s.q_gb, FB["q_gb"] + 32)
    with nc.Block(name + "_ec") as blk:
        @blk.sync
        def _(sy):
            sy.sem_inc(s.q_d, 2)
            sy.sem_inc(s.q_epo, 32)

    with nc.Block(name + "_e") as blk:
        @blk.sync
        def _(sy):
            for k in range(8):
                b = k % 2
                sy.wait_ge(s.q_d, FB["q_d"] + k + 1)
                for ot in range(DTL):
                    sy.dma_start_transpose(
                        C.otm[:, b * DM + ot * 128:b * DM + (ot + 1) * 128],
                        C.sp_os[ot * 128:(ot + 1) * 128, k * 128:(k + 1) * 128]).then_inc(s.q_ep, 16)
                sy.dma_start(out=C.xa[:, b * DM:(b + 1) * DM],
                             in_=xin[k * 128:(k + 1) * 128, :]).then_inc(s.q_ep, 16)
                sy.wait_ge(s.q_d, FB["q_d"] + k + 3)
                sy.dma_start(out=xout[k * 128:(k + 1) * 128, :],
                             in_=C.lnw[:, b * DM:(b + 1) * DM]).then_inc(s.q_epo, 16)
            sy.wait_ge(s.q_epo, FB["q_epo"] + 32 + 16 * 8)

        @blk.vector
        def _(v):
          with contextlib.ExitStack() as es:
            ve = WR(v, es, name + "_ve2", 1 + FB["q_vs2"], 1)
            v_ep = WR(v, es, name + "_v_ep", 33 * 16 + FB["q_ep"], 33 * 16)
            v_a = WR(v, es, name + "_v_a2", 1 + FB["q_a"], 1)
            v_c = WR(v, es, name + "_v_c2", 1 + FB["q_c"], 1)
            v_eo = WR(v, es, name + "_v_eo", 16 + FB["q_epo"], 16)
            with v.Fori(0, 4) as tt:
                for b in range(2):
                    o = C.otm[:, b * DM:(b + 1) * DM]
                    r = C.red[:, b * 8:(b + 1) * 8]
                    v_ep.wait(s.q_ep)
                    ve.link(v.reduce_sum(r[:, 0:1], o, axis=AX), s.q_vs2)
                    v_a.wait(s.q_a)
                    ve.link(v.tensor_scalar_mul(r[:, 2:3], r[:, 0:1], 1.0 / DM), s.q_vs2)
                    ve.link(v.tensor_mul(r[:, 3:4], r[:, 2:3], r[:, 2:3]), s.q_vs2)
                    v.scalar_tensor_tensor(r[:, 4:5], r[:, 1:2], 1.0 / DM, r[:, 3:4],
                                           ALU.mult, ALU.subtract).then_inc(s.q_b, 1)
                    v_c.wait(s.q_c)
                    ve.link(v.tensor_scalar(C.sq[:, :], o, r[:, 2:3], r[:, 5:6],
                                            ALU.subtract, ALU.mult), s.q_vs2)
                    ve.link(v.tensor_mul(C.sq[:, :], C.sq[:, :], C.gbc[:, :]), s.q_vs2)
                    ve.link(v.tensor_add(C.sq[:, :], C.sq[:, :], C.bbc[:, :]), s.q_vs2)
                    v_eo.wait(s.q_epo)
                    v.scalar_tensor_tensor(C.lnw[:, b * DM:(b + 1) * DM], C.sq[:, :], 0.5,
                                           C.xa[:, b * DM:(b + 1) * DM],
                                           ALU.mult, ALU.add).then_inc(s.q_d, 1)

        @blk.scalar
        def _(sc):
          with contextlib.ExitStack() as es:
            se = WR(sc, es, name + "_se2", 1 + FB["q_ss2"], 1)
            s_ep = WR(sc, es, name + "_s_ep", 33 * 16 + FB["q_ep"], 33 * 16)
            s_b = WR(sc, es, name + "_s_b2", 1 + FB["q_b"], 1)
            with sc.Fori(0, 4) as tt:
                for b in range(2):
                    o = C.otm[:, b * DM:(b + 1) * DM]
                    r = C.red[:, b * 8:(b + 1) * 8]
                    s_ep.wait(s.q_ep)
                    sc.activation(C.sq2[:, :], o, AF.Square, accum_out=r[:, 1:2]).then_inc(s.q_a, 1)
                    s_b.wait(s.q_b)
                    se.link(sc.activation(r[:, 6:7], r[:, 4:5], AF.Ln, bias=C.epsc[:, 0:1]), s.q_ss2)
                    sc.activation(r[:, 5:6], r[:, 6:7], AF.Exp, scale=-0.5).then_inc(s.q_c, 1)

    if final_wait:
        with nc.Block(name + "_fw") as blk:
            @blk.sync
            def _(sy):
                sy.wait_ge(s.q_epo, FB["q_epo"] + 32 + 16 * 8)


def build():
    nc = bass.Bass("TRN2", target_bir_lowering=False, detect_race_conditions=False)
    C = _C()

    x_in = nc.dram_tensor("x", [NT, DM], F32, kind="ExternalInput").ap()
    out = nc.dram_tensor("out", [NT, DM], F32, kind="ExternalOutput").ap()
    C.wup = nc.dram_tensor("wup", [ITL, 128, DM], BF16, kind="ExternalInput").ap()
    C.wdn = nc.dram_tensor("wdn", [DTL, 128, FI], BF16, kind="ExternalInput").ap()
    C.wf1 = nc.dram_tensor("wf1", [LTL, 128, DM], BF16, kind="ExternalInput").ap()
    C.wex = nc.dram_tensor("wex", [LTL, 128, LAT], BF16, kind="ExternalInput").ap()
    C.wf2 = nc.dram_tensor("wf2", [DTL, 128, LAT], BF16, kind="ExternalInput").ap()
    wg_in = nc.dram_tensor("wgt", [128, DTL * 8], F32, kind="ExternalInput").ap()
    C.wip = nc.dram_tensor("wip", [145, 128, DM], BF16, kind="ExternalInput").ap()
    C.wout = nc.dram_tensor("wout", [DTL, 128, FI], BF16, kind="ExternalInput").ap()
    cw_in = nc.dram_tensor("cw", [128, NCV * 4], F32, kind="ExternalInput").ap()
    C.cwt = nc.dram_tensor("cwt", [NCV, 128, 4], F32, kind="ExternalInput").ap()
    C.cbt = nc.dram_tensor("cbt", [NCV, 128, 1], F32, kind="ExternalInput").ap()
    onescol_in = nc.dram_tensor("onescol", [128, 1], F32, kind="ExternalInput").ap()
    cb_in = nc.dram_tensor("cb", [128, NCV], F32, kind="ExternalInput").ap()
    dtb_in = nc.dram_tensor("dtb", [128, 1], F32, kind="ExternalInput").ap()
    alog_in = nc.dram_tensor("alog", [128, 1], F32, kind="ExternalInput").ap()
    dbc_in = nc.dram_tensor("dbc", [128, 128], F32, kind="ExternalInput").ap()
    C.gbc_in = nc.dram_tensor("gbc", [128, DM], F32, kind="ExternalInput").ap()
    C.bbc_in = nc.dram_tensor("bbc", [128, DM], F32, kind="ExternalInput").ap()
    C.nwt_in = nc.dram_tensor("nwt", [128, ITL], F32, kind="ExternalInput").ap()
    id16_in = nc.dram_tensor("id16", [128, 128], BF16, kind="ExternalInput").ap()
    id32_in = nc.dram_tensor("id32", [128, 128], F32, kind="ExternalInput").ap()
    ones32_in = nc.dram_tensor("ones32", [1, 128], F32, kind="ExternalInput").ap()
    strict_in = nc.dram_tensor("strictm", [128, 128], BF16, kind="ExternalInput").ap()
    negs_in = nc.dram_tensor("negstrict", [128, 128], BF16, kind="ExternalInput").ap()
    qtril_in = nc.dram_tensor("qtril", [128, 128], BF16, kind="ExternalInput").ap()
    mask_in = nc.dram_tensor("pmask", [128, 1], F32, kind="ExternalInput").ap()
    nonce_in = nc.dram_tensor("nonce", [1, 1], I32, kind="ExternalInput").ap()

    C.sp_lnt = nc.dram_tensor("sp_lnt", [NT, DM], BF16).ap()
    C.sp_x1 = nc.dram_tensor("sp_x1", [NT, DM], F32).ap()
    C.sp_x2 = nc.dram_tensor("sp_x2", [NT, DM], F32).ap()
    C.sp_h = nc.dram_tensor("sp_h", [128, ITL * 512], BF16).ap()
    C.sp_lat = nc.dram_tensor("sp_lat", [128, LTL * 512], BF16).ap()
    C.sp_moe = nc.dram_tensor("sp_moe", [128, LTL * 512], BF16).ap()
    C.sp_os = nc.dram_tensor("sp_os", [DM, NT], BF16).ap()
    C.sp_gate = nc.dram_tensor("sp_gate", [FI, NT], BF16).ap()
    C.sp_hsf = nc.dram_tensor("sp_hsf", [FI, NT], BF16).ap()
    C.sp_dtraw = nc.dram_tensor("sp_dtraw", [128, NT], F32).ap()
    C.sp_X = nc.dram_tensor("sp_X", [NZ, 128, FI], BF16).ap()
    C.sp_bc = nc.dram_tensor("sp_bc", [128, 16 * NT], F32).ap()
    C.sp_pack = nc.dram_tensor("sp_pack", [NZ, 128, 2048], BF16).ap()
    C.sp_dtv = nc.dram_tensor("sp_dtv", [128, NT], F32).ap()
    C.sp_pc3 = nc.dram_tensor("sp_pc3", [128, NCV * 3], F32).ap()
    C.sp_hout = nc.dram_tensor("sp_hout", [128, NCV * 3], F32).ap()
    C.sp_outg = nc.dram_tensor("sp_outg", [NT, FI], BF16).ap()
    C.sp_attn = nc.dram_tensor("sp_attn", [DM, NT], BF16).ap()
    C.shx = nc.dram_tensor("shx", [2 * 128, 8448], F32, kind="Internal", addr_space="Shared").ap()
    C.shf = nc.dram_tensor("shf", [1, 8], I32, kind="Internal", addr_space="Shared").ap()

    st = contextlib.ExitStack()
    C.st = st
    with st:
        C.q_pro = st.enter_context(nc.semaphore("q_pro"))
        C.q_prog = st.enter_context(nc.semaphore("q_prog"))

        C.wg = st.enter_context(nc.sbuf_tensor([128, DTL * 8], BF16))
        C.cw = st.enter_context(nc.sbuf_tensor([128, NCV * 4], F32))
        C.cb = st.enter_context(nc.sbuf_tensor([128, NCV], F32))
        C.cbn = st.enter_context(nc.sbuf_tensor([128, NCV], F32))
        C.dtb = st.enter_context(nc.sbuf_tensor([128, 1], F32))
        C.aneg = st.enter_context(nc.sbuf_tensor([128, 1], F32))
        C.dbc = st.enter_context(nc.sbuf_tensor([128, 128], F32))
        C.id16 = st.enter_context(nc.sbuf_tensor([128, 128], BF16))
        C.id32 = st.enter_context(nc.sbuf_tensor([128, 128], F32))
        C.ones32 = st.enter_context(nc.sbuf_tensor([1, 128], F32))
        C.onescol = st.enter_context(nc.sbuf_tensor([128, 1], F32))
        C.strict = st.enter_context(nc.sbuf_tensor([128, 128], BF16))
        C.negstrict = st.enter_context(nc.sbuf_tensor([128, 128], BF16))
        C.qtril = st.enter_context(nc.sbuf_tensor([128, 128], BF16))
        C.pmask = st.enter_context(nc.sbuf_tensor([128, 1], F32))
        C.noncesb = st.enter_context(nc.sbuf_tensor([1, 1], I32))
        C.flsb = st.enter_context(nc.sbuf_tensor([1, 1], I32))

        C.red = st.enter_context(nc.sbuf_tensor([128, 16], F32))
        C.epsc = st.enter_context(nc.sbuf_tensor([128, 1], F32))

        with nc.Block("prol") as blk:
            @blk.sync
            def _(sy):
                for dst, srcap in ((C.cw, cw_in), (C.cb, cb_in), (C.dtb, dtb_in),
                                   (C.aneg, alog_in), (C.dbc, dbc_in), (C.id16, id16_in),
                                   (C.id32, id32_in), (C.ones32, ones32_in), (C.onescol, onescol_in), (C.strict, strict_in),
                                   (C.qtril, qtril_in), (C.pmask, mask_in), (C.noncesb, nonce_in),
                                   (C.negstrict, negs_in)):
                    sy.dma_start(out=dst[:, :], in_=srcap[:, :]).then_inc(C.q_pro, 16)
                sy.wait_ge(C.q_pro, 14 * 16)
            @blk.gpsimd
            def _(g):
                g.memset(C.epsc[:, :], EPS)
                g.dma_start(out=C.wg[:, :], in_=wg_in[:, :]).then_inc(C.q_prog, 16)
                g.wait_ge(C.q_prog, 16)
                g.wait_ge(C.q_pro, 14 * 16)
                C.gnv = g.value_load(C.noncesb[0:1, 0:1])
                pid = g.partition_id()
                par = pid % 2
                oth = 1 - par
                C.e_par128 = par * 128
                C.e_par2 = par * 2
                C.e_par2p1 = par * 2 + 1
                C.e_oth128 = oth * 128
                C.e_oth2 = oth * 2
                C.e_oth2p1 = oth * 2 + 1
        with nc.Block("prol2") as blk:
            @blk.scalar
            def _(sc):
                sc.activation(C.aneg[:, :], C.aneg[:, :], AF.Exp)
            @blk.vector
            def _(v):
                v.tensor_scalar_mul(C.cbn[:, :], C.cb[:, :], -1.0)
        with nc.Block("prol3") as blk:
            @blk.vector
            def _(v):
                v.tensor_scalar_mul(C.aneg[:, :], C.aneg[:, :], -1.0)

        C.outap = out
        emit_ln(nc, C, x_in, "ln1")
        emit_ffn(nc, C, x_in, C.sp_x1 if PHASES > 1 else out, "ffn1",
                 final_wait=(PHASES == 1))

        def _recycle(fn, *args, **kw):
            snap = nc.snapshot_sems()
            old = C.st
            C.st = None
            fn(*args, **kw)
            C.st = old
            nc.clear_and_free_semaphores(nc.allocated_since(snap))
            nc.all_engine_barrier()

        if PHASES >= 2:
            emit_ln(nc, C, C.sp_x1, "ln2")
            _recycle(emit_inproj, nc, C)
        if PHASES >= 3:
            _recycle(emit_xtrans, nc, C)
        if PHASES >= 4:
            _recycle(emit_prep, nc, C)
        if PHASES >= 5:
            with contextlib.ExitStack() as sscope:
                C.prevT = sscope.enter_context(nc.sbuf_tensor([128, FI], F32))
                _recycle(emit_ssdA, nc, C)
                if PHASES >= 6:
                    _recycle(emit_exch2, nc, C)
                if PHASES >= 7:
                    _recycle(emit_ssdB, nc, C)
                if PHASES == 5 or PHASES == 6:
                    _dump_prev(nc, C, out)
        if PHASES >= 8:
            _recycle(emit_outproj, nc, C)
        if PHASES >= 9:
            emit_ln(nc, C, C.sp_x2, "ln3")
            emit_ffn(nc, C, C.sp_x2, out, "ffn2", final_wait=True)
        if PHASES < 9:
            _dump_dev(nc, C, out)

    return nc


def _dump_prev(nc, C, out):
    qq = C.st.enter_context(nc.semaphore("q_dprev"))
    with nc.Block("dprev") as blk:
        @blk.gpsimd
        def _(g):
            g.dma_start(out=C.sp_outg[0:128, 0:FI], in_=C.prevT[:, :]).then_inc(qq, 16)
            g.wait_ge(qq, 16)


def _dump_dev(nc, C, out):
    qq = C.st.enter_context(nc.semaphore("q_dump"))
    with nc.Block("dump") as blk:
        @blk.gpsimd
        def _(g):
            n = 0
            if PHASES in (2, 3):
                g.dma_start(out=out[0:1024, 0:1024], in_=C.sp_hsf[0:1024, :]).then_inc(qq, 16)
                g.dma_start(out=out[0:1024, 1024:2048], in_=C.sp_gate[0:1024, :]).then_inc(qq, 16)
                g.dma_start(out=out[0:128, 2048:3072], in_=C.sp_dtraw[:, :]).then_inc(qq, 16)
                g.dma_start(out=out[0:128, 3072:4096], in_=C.sp_bc[:, 0:1024]).then_inc(qq, 16)
                n = 4
                if PHASES == 3:
                    for z in range(NZ):
                        g.dma_start(out=out[128 + z * 64:128 + (z + 1) * 64, 2048:4096],
                                    in_=C.sp_X[z, 64:128, 0:2048]).then_inc(qq, 16)
                    n += 8
            elif PHASES == 4:
                g.dma_start(out=out[0:128, 0:2048], in_=C.sp_pack[0, :, :]).then_inc(qq, 16)
                g.dma_start(out=out[0:128, 2048:3072], in_=C.sp_dtv[:, :]).then_inc(qq, 16)
                g.dma_start(out=out[128:256, 0:4096], in_=C.sp_X[0, :, 0:4096]).then_inc(qq, 16)
                g.dma_start(out=out[256:384, 0:2048], in_=C.sp_pack[5, :, :]).then_inc(qq, 16)
                n = 4
            elif PHASES in (5, 6):
                g.dma_start(out=out[0:128, 0:4096], in_=C.sp_outg[0:128, 0:4096]).then_inc(qq, 16)
                g.dma_start(out=out[128:256, 0:4096], in_=C.sp_outg[0:128, 4096:8192]).then_inc(qq, 16)
                n = 2
            elif PHASES == 7:
                if DBG_SSD:
                    g.dma_start(out=out[512:1024, 0:4096],
                                in_=C.sp_outg[512:1024, 0:4096]).then_inc(qq, 16)
                else:
                    g.dma_start(out=out[0:1024, 0:4096], in_=C.sp_outg[:, 0:4096]).then_inc(qq, 16)
                n = 1
            elif PHASES == 8:
                g.dma_start(out=out[:, :], in_=C.sp_x2[:, :]).then_inc(qq, 16)
                n = 1
            g.wait_ge(qq, n * 16)




def emit_inproj(nc, C, name="ip"):
  s = mk_sems(nc, C.st, name, ["q_lnf", "q_w", "q_m", "q_e", "q_ge", "q_gs",
                               "q_cv", "q_cwl", "q_cbn", "q_cv2", "q_cv3", "q_hs",
                               "q_hss", "q_pcs", "q_bce", "q_bcs", "q_dte", "q_dts",
                               "q_vs"])
  with contextlib.ExitStack() as _es:
    lnip = _es.enter_context(nc.sbuf_tensor([128, DTL * 1024], BF16))
    wipb = _es.enter_context(nc.sbuf_tensor([128, 2 * 4096], BF16))
    esb = _es.enter_context(nc.sbuf_tensor([128, 2 * 1024], BF16))
    pre = _es.enter_context(nc.sbuf_tensor([128, 2 * 1024], F32))
    cv = _es.enter_context(nc.sbuf_tensor([128, 1021], F32))
    e1 = _es.enter_context(nc.sbuf_tensor([128, 1021], F32))
    r1 = _es.enter_context(nc.sbuf_tensor([128, 1021], F32))
    hsw = _es.enter_context(nc.sbuf_tensor([128, 2 * 1024], BF16))
    bce = _es.enter_context(nc.sbuf_tensor([128, 2 * 1024], F32))
    cwx = _es.enter_context(nc.sbuf_tensor([128, 2 * 4], F32))
    cbx = _es.enter_context(nc.sbuf_tensor([128, 2], F32))
    cbnx = _es.enter_context(nc.sbuf_tensor([128, 2], F32))
    ps = _es.enter_context(nc.psum_tensor([128, 2048], F32))

    with nc.Block(name + "_c") as blk:
        @blk.sync
        def _(sy):
            sy.sem_inc(s.q_m, 2)
            sy.sem_inc(s.q_e, 2)
            sy.sem_inc(s.q_ge, 2)
            sy.sem_inc(s.q_gs, 32)
            sy.sem_inc(s.q_bcs, 32)
            sy.sem_inc(s.q_pcs, 64)
            sy.sem_inc(s.q_cwl, 64)
            sy.sem_inc(s.q_hss, 32)
            sy.sem_inc(s.q_cv2, 2)
            for dt in range(DTL):
                sy.dma_start_transpose(
                    lnip[:, dt * 1024:(dt + 1) * 1024],
                    C.sp_lnt[0:1024, dt * 128:(dt + 1) * 128]).then_inc(s.q_lnf, 16)
            sy.wait_ge(s.q_lnf, DTL * 16)
        @blk.gpsimd
        def _(g):
            g.memset(hsw[:, 0:3], 0.0)
            g.memset(hsw[:, 1024:1027], 0.0)

    with nc.Block(name) as blk:
        @blk.gpsimd
        def _(g):
          with contextlib.ExitStack() as es:
            w_m = WR(g, es, name + "_w_m", 1, 1)
            for kk in range(145):
                b = kk % 2
                w_m.wait(s.q_m)
                g.dma_start(out=wipb[:, b * 4096:(b + 1) * 4096],
                            in_=C.wip[kk, :, :]).then_inc(s.q_w, 16)

        @blk.tensor
        def _(tn):
          with contextlib.ExitStack() as es:
            t_w = WR(tn, es, name + "_t_w", 16, 16)
            t_e = WR(tn, es, name + "_t_e", 1, 1)

            def mm_etile(b):
                t_w.wait(s.q_w)
                t_e.wait(s.q_e)
                for tch in range(2):
                    for dt in range(DTL):
                        ins = tn.matmul(
                            ps[:, b * 1024 + tch * 512:b * 1024 + (tch + 1) * 512],
                            wipb[:, b * 4096 + dt * 128:b * 4096 + (dt + 1) * 128],
                            lnip[:, dt * 1024 + tch * 512:dt * 1024 + (tch + 1) * 512],
                            start=(dt == 0), stop=(dt == DTL - 1))
                ins.then_inc(s.q_m, 1)

            with tn.Fori(0, 72) as j:
                for b in range(2):
                    mm_etile(b)
            mm_etile(0)

        @blk.vector
        def _(v):
          with contextlib.ExitStack() as es:
            v_m = WR(v, es, name + "_v_m", 3, 1)
            v_gs = WR(v, es, name + "_v_gs", 16, 16)
            v_pcs = WR(v, es, name + "_v_pcs", 32, 32)
            v_cwl = WR(v, es, name + "_v_cwl", 96, 32)
            v_cv3 = WR(v, es, name + "_v_cv3", 1, 1)
            v_hss = WR(v, es, name + "_v_hss", 16, 16)
            v_bcs = WR(v, es, name + "_v_bcs", 32, 32)
            ve = WR(v, es, name + "_ve", 1, 1)
            with v.Fori(0, 32) as j:
                for b in range(2):
                    v_m.wait(s.q_m)
                    v_gs.wait(s.q_gs)
                    v.tensor_copy(esb[:, b * 1024:(b + 1) * 1024],
                                  ps[:, b * 1024:(b + 1) * 1024]).then_inc(s.q_e, 1)
            with v.Fori(0, 32) as j:
                for b in range(2):
                    p0 = b * 1024
                    v_m.wait(s.q_m)
                    v_pcs.wait(s.q_pcs)
                    v.tensor_copy(pre[:, p0:p0 + 1024],
                                  ps[:, p0:p0 + 1024]).then_inc(s.q_e, 1)
                    v_cwl.wait(s.q_cwl)
                    ve.link(v.tensor_scalar_mul(cbnx[:, b:b + 1], cbx[:, b:b + 1], -1.0), s.q_vs)
                    v.engine_nop().then_inc(s.q_cbn, 1)
                    ve.link(v.tensor_scalar_mul(cv[:, :], pre[:, p0:p0 + 1021],
                                                cwx[:, 4 * b:4 * b + 1]), s.q_vs)
                    for tp in range(1, 4):
                        ve.link(v.scalar_tensor_tensor(cv[:, :], pre[:, p0 + tp:p0 + tp + 1021],
                                                       cwx[:, 4 * b + tp:4 * b + tp + 1],
                                                       cv[:, :], ALU.mult, ALU.add), s.q_vs)
                    v.engine_nop().then_inc(s.q_cv2, 1)
                    v_cv3.wait(s.q_cv3)
                    ve.link(v.tensor_scalar_add(r1[:, :], e1[:, :], 1.0), s.q_vs)
                    ve.link(v.reciprocal(r1[:, :], r1[:, :]), s.q_vs)
                    v_hss.wait(s.q_hss)
                    v.scalar_tensor_tensor(hsw[:, b * 1024 + 3:b * 1024 + 1024], cv[:, :],
                                           cbx[:, b:b + 1], r1[:, :],
                                           ALU.add, ALU.mult).then_inc(s.q_hs, 1)
            with v.Fori(0, 8) as j:
                for b in range(2):
                    v_m.wait(s.q_m)
                    v_bcs.wait(s.q_bcs)
                    v.tensor_copy(bce[:, b * 1024:(b + 1) * 1024],
                                  ps[:, b * 1024:(b + 1) * 1024]).then_inc(s.q_e, 1)
            v_m.wait(s.q_m)
            v.wait_ge(s.q_bcs, 32 + 16 * 32)
            v.tensor_copy(bce[:, 0:1024], ps[:, 0:1024]).then_inc(s.q_e, 1)

        @blk.scalar
        def _(sc):
          with contextlib.ExitStack() as es:
            s_cv2 = WR(sc, es, name + "_s_cv2", 3, 1)
            s_cbn = WR(sc, es, name + "_s_cbn", 1, 1)
            for kk in range(64):
                b = kk % 2
                sc.wait_ge(s.q_e, 3 + kk)
                sc.dma_start(out=C.sp_gate[kk * 128:(kk + 1) * 128, :],
                             in_=esb[:, b * 1024:(b + 1) * 1024]).then_inc(s.q_gs, 16)
            with sc.Fori(0, 32) as j:
                for b in range(2):
                    s_cv2.wait(s.q_cv2)
                    s_cbn.wait(s.q_cbn)
                    sc.activation(e1[:, :], cv[:, :], AF.Exp, scale=-1.0,
                                  bias=cbnx[:, b:b + 1]).then_inc(s.q_cv3, 1)
            for kk in range(16):
                b = kk % 2
                sc.wait_ge(s.q_e, 131 + kk)
                sc.dma_start(out=C.sp_bc[:, kk * 1024:(kk + 1) * 1024],
                             in_=bce[:, b * 1024:(b + 1) * 1024]).then_inc(s.q_bcs, 16)
                sc.dma_start(out=C.sp_hout[:, (64 + kk) * 3:(64 + kk) * 3 + 3],
                             in_=bce[:, b * 1024 + 1021:b * 1024 + 1024]).then_inc(s.q_bcs, 16)
            sc.wait_ge(s.q_e, 147)
            sc.dma_start(out=C.sp_dtraw[:, :], in_=bce[:, 0:1024]).then_inc(s.q_dts, 16)

        @blk.sync
        def _(sy):
            for kk in range(64):
                b = kk % 2
                sy.wait_ge(s.q_cv2, kk + 1)
                sy.dma_start(out=cwx[:, 4 * b:4 * (b + 1)],
                             in_=C.cwt[kk, :, :]).then_inc(s.q_cwl, 16)
                sy.dma_start(out=cbx[:, b:b + 1],
                             in_=C.cbt[kk, :, :]).then_inc(s.q_cwl, 16)
                sy.wait_ge(s.q_e, 67 + kk)
                sy.dma_start(out=C.sp_pc3[:, kk * 3:kk * 3 + 3],
                             in_=pre[:, b * 1024:b * 1024 + 3]).then_inc(s.q_pcs, 16)
                sy.dma_start(out=C.sp_hout[:, kk * 3:kk * 3 + 3],
                             in_=pre[:, b * 1024 + 1021:b * 1024 + 1024]).then_inc(s.q_pcs, 16)
                sy.wait_ge(s.q_hs, kk + 1)
                sy.dma_start(out=C.sp_hsf[kk * 128:(kk + 1) * 128, :],
                             in_=hsw[:, b * 1024:(b + 1) * 1024]).then_inc(s.q_hss, 16)
            sy.wait_ge(s.q_dts, 16)
            sy.wait_ge(s.q_hss, 32 + 64 * 16)
            sy.wait_ge(s.q_gs, 32 + 64 * 16)
            sy.wait_ge(s.q_bcs, 32 + 16 * 32)
            sy.wait_ge(s.q_pcs, 64 + 128 * 16)
            sy.wait_ge(s.q_cwl, 64 + 128 * 16)



def emit_xtrans(nc, C, name="xt"):
  s = mk_sems(nc, C.st, name, ["q_l", "q_tr", "q_te", "q_sp"])
  with contextlib.ExitStack() as _es:
    ht = _es.enter_context(nc.sbuf_tensor([128, 2 * 1024], BF16))
    trev = _es.enter_context(nc.sbuf_tensor([128, 2 * 1024], BF16))
    ptr = _es.enter_context(nc.psum_tensor([128, 2048], BF16))
    with nc.Block(name + "_c") as blk:
        @blk.sync
        def _(sy):
            sy.sem_inc(s.q_tr, 2)
            sy.sem_inc(s.q_sp, 2)
            sy.sem_inc(s.q_te, 2)
    with nc.Block(name) as blk:
        @blk.sync
        def _(sy):
            for kk in range(64):
                b = kk % 2
                sy.wait_ge(s.q_tr, kk + 1)
                sy.dma_start(out=ht[:, b * 1024:(b + 1) * 1024],
                             in_=C.sp_hsf[kk * 128:(kk + 1) * 128, :]).then_inc(s.q_l, 16)
                sy.wait_ge(s.q_te, kk + 3)
                for z in range(NZ):
                    sy.dma_start(out=C.sp_X[z, :, kk * 128:(kk + 1) * 128],
                                 in_=trev[:, b * 1024 + z * 128:b * 1024 + (z + 1) * 128]
                                 ).then_inc(s.q_sp, 16)
            sy.wait_ge(s.q_sp, 2 + 128 * 64)
        @blk.tensor
        def _(tn):
          with contextlib.ExitStack() as es:
            t_l = WR(tn, es, name + "_t_l", 16, 16)
            t_te = WR(tn, es, name + "_t_te", 1, 1)
            with tn.Fori(0, 32) as j:
                for b in range(2):
                    t_l.wait(s.q_l)
                    t_te.wait(s.q_te)
                    for z in range(NZ):
                        ins = tn.transpose(ptr[:, b * 1024 + z * 128:b * 1024 + (z + 1) * 128],
                                           ht[:, b * 1024 + z * 128:b * 1024 + (z + 1) * 128],
                                           C.id16[:, :])
                    ins.then_inc(s.q_tr, 1)
        @blk.vector
        def _(v):
          with contextlib.ExitStack() as es:
            v_tr = WR(v, es, name + "_v_tr", 3, 1)
            v_sp = WR(v, es, name + "_v_sp", 2, 128)
            with v.Fori(0, 32) as j:
                for b in range(2):
                    v_tr.wait(s.q_tr)
                    v_sp.wait(s.q_sp)
                    v.tensor_copy(trev[:, b * 1024:(b + 1) * 1024],
                                  ptr[:, b * 1024:(b + 1) * 1024]).then_inc(s.q_te, 1)


def emit_prep(nc, C, name="pp"):
  s = mk_sems(nc, C.st, name, ["q_x", "q_fl", "q_po", "q_hr", "q_cv", "q_ss",
                               "q_pk", "q_ps", "q_dt", "q_dts", "q_p3a", "q_p3b",
                               "q_p3c", "q_tr", "q_te", "q_p3x", "q_vs", "q_ssl"])
  with contextlib.ExitStack() as _es:
    hout = _es.enter_context(nc.sbuf_tensor([128, 240], F32))
    hrecv = _es.enter_context(nc.sbuf_tensor([128, 240], F32))
    bcb = _es.enter_context(nc.sbuf_tensor([128, 16 * 1024], F32))
    ext = _es.enter_context(nc.sbuf_tensor([128, 1027], F32))
    cvb = _es.enter_context(nc.sbuf_tensor([128, 1024], F32))
    eb = _es.enter_context(nc.sbuf_tensor([128, 1024], F32))
    rb = _es.enter_context(nc.sbuf_tensor([128, 1024], F32))
    bcs = _es.enter_context(nc.sbuf_tensor([128, 16 * 1024], BF16))
    dtr = _es.enter_context(nc.sbuf_tensor([128, 1024], F32))
    dtv = _es.enter_context(nc.sbuf_tensor([128, 1024], F32))
    u1 = _es.enter_context(nc.sbuf_tensor([128, 1024], F32))
    u2 = _es.enter_context(nc.sbuf_tensor([128, 1024], F32))
    pc3 = _es.enter_context(nc.sbuf_tensor([128, 192], F32))
    c3 = _es.enter_context(nc.sbuf_tensor([128, 6], F32))
    w3 = _es.enter_context(nc.sbuf_tensor([128, 6], F32))
    pad = _es.enter_context(nc.sbuf_tensor([128, 128], BF16))
    p3s = _es.enter_context(nc.sbuf_tensor([4, 2 * 128], BF16))
    ptr = _es.enter_context(nc.psum_tensor([128, 2048], BF16))

    with nc.Block(name + "_x") as blk:
        @blk.sync
        def _(sy):
            sy.dma_start(out=hout[:, :], in_=C.sp_hout[:, :]).then_inc(s.q_x, 16)
            sy.dma_start(out=pc3[:, :], in_=C.sp_pc3[:, 0:192]).then_inc(s.q_x, 16)
            sy.dma_start(out=bcb[:, :], in_=C.sp_bc[:, :]).then_inc(s.q_x, 16)
            sy.dma_start(out=dtr[:, :], in_=C.sp_dtraw[:, :]).then_inc(s.q_x, 16)
            sy.wait_ge(s.q_x, 64)
        @blk.gpsimd
        def _(g):
            g.memset(pad[:, :], 0.0)
            g.wait_ge(s.q_x, 16)
            with (g.register(name + "_nd") as nd, g.register(name + "_rexp") as rexp):
                g.dma_start(out=C.shx[bass.ds(C.e_par128, 128), 0:240],
                            in_=hout[:, :]).then_inc(s.q_fl, 16)
                g.wait_ge(s.q_fl, 16)
                g.dma_start(out=C.shf[0:1, bass.ds(C.e_par2, 1)],
                            in_=C.noncesb[0:1, 0:1]).then_inc(s.q_fl, 16)
                g.wait_ge(s.q_fl, 32)
                g.reg_mov(nd, 1)
                g.reg_mov(rexp, 32)
                if not SIM_SKIP_POLL:
                    with g.While(nd):
                        g.dma_start(out=C.flsb[0:1, 0:1],
                                    in_=C.shf[0:1, bass.ds(C.e_oth2, 1)]).then_inc(s.q_fl, 16)
                        g.reg_add(rexp, rexp, 16)
                        g.wait_ge(s.q_fl, rexp)
                        fv = g.value_load(C.flsb[0:1, 0:1])
                        g.reg_mov(nd, fv - C.gnv)
                else:
                    g.dma_start(out=C.shx[bass.ds(C.e_oth128, 128), 0:240],
                                in_=hout[:, :]).then_inc(s.q_fl, 16)
                    g.wait_ge(s.q_fl, 48)
                g.dma_start(out=hrecv[:, :],
                            in_=C.shx[bass.ds(C.e_oth128, 128), 0:240]).then_inc(s.q_po, 16)
                g.wait_ge(s.q_po, 16)

    with nc.Block(name + "_m") as blk:
        @blk.vector
        def _(v):
          with contextlib.ExitStack() as es:
            ve = WR(v, es, name + "_ve", 1, 1)
            ve.link(v.tensor_scalar_mul(hrecv[:, :], hrecv[:, :], C.pmask[:, 0:1]), s.q_vs)
            v.engine_nop().then_inc(s.q_hr, 1)
            for ct in range(16):
                cg = 64 + ct
                ve.link(v.tensor_copy(ext[:, 0:3], hrecv[:, cg * 3:cg * 3 + 3]), s.q_vs)
                ve.link(v.tensor_copy(ext[:, 3:1027], bcb[:, ct * 1024:(ct + 1) * 1024]), s.q_vs)
                ve.link(v.tensor_scalar_mul(cvb[:, :], ext[:, 0:1024],
                                            C.cw[:, cg * 4:cg * 4 + 1]), s.q_vs)
                for tp in range(1, 4):
                    ve.link(v.scalar_tensor_tensor(cvb[:, :], ext[:, tp:tp + 1024],
                                                   C.cw[:, cg * 4 + tp:cg * 4 + tp + 1],
                                                   cvb[:, :], ALU.mult, ALU.add), s.q_vs)
                v.engine_nop().then_inc(s.q_cv, 1)
                v.wait_ge(s.q_ss, ct + 1)
                ve.link(v.tensor_scalar_add(rb[:, :], eb[:, :], 1.0), s.q_vs)
                ve.link(v.reciprocal(rb[:, :], rb[:, :]), s.q_vs)
                v.scalar_tensor_tensor(bcs[:, ct * 1024:(ct + 1) * 1024], cvb[:, :],
                                       C.cb[:, cg:cg + 1], rb[:, :],
                                       ALU.add, ALU.mult).then_inc(s.q_pk, 1)
            ve.link(v.tensor_scalar_add(u1[:, :], dtr[:, :], C.dtb[:, 0:1]), s.q_vs)
            v.engine_nop().then_inc(s.q_dt, 1)
            v.wait_ge(s.q_dt, 2)
            ve.link(v.tensor_scalar_max(dtv[:, :], u1[:, :], 0.0), s.q_vs)
            ve.link(v.tensor_add(dtv[:, :], dtv[:, :], u2[:, :]), s.q_vs)
            ve.link(v.tensor_scalar_min(dtv[:, :], dtv[:, :], 100.0), s.q_vs)
            v.engine_nop().then_inc(s.q_dt, 1)
            for ct in range(64):
                ve.link(v.tensor_copy(c3[:, 0:3], hrecv[:, ct * 3:ct * 3 + 3]), s.q_vs)
                ve.link(v.tensor_copy(c3[:, 3:6], pc3[:, ct * 3:ct * 3 + 3]), s.q_vs)
                ve.link(v.tensor_scalar_mul(w3[:, 0:3], c3[:, 0:3],
                                            C.cw[:, ct * 4:ct * 4 + 1]), s.q_vs)
                for tp in range(1, 4):
                    ve.link(v.scalar_tensor_tensor(w3[:, 0:3], c3[:, tp:tp + 3],
                                                   C.cw[:, ct * 4 + tp:ct * 4 + tp + 1],
                                                   w3[:, 0:3], ALU.mult, ALU.add), s.q_vs)
                v.engine_nop().then_inc(s.q_p3a, 1)
                v.wait_ge(s.q_p3b, ct + 1)
                ve.link(v.tensor_scalar_add(w3[:, 3:6], eb[:, 0:3], 1.0), s.q_vs)
                ve.link(v.reciprocal(w3[:, 3:6], w3[:, 3:6]), s.q_vs)
                v.wait_ge(s.q_tr, ct)
                ve.link(v.scalar_tensor_tensor(pad[:, 0:3], w3[:, 0:3], C.cb[:, ct:ct + 1],
                                               w3[:, 3:6], ALU.add, ALU.mult), s.q_vs)
                v.engine_nop().then_inc(s.q_p3c, 1)
                v.wait_ge(s.q_tr, ct + 1)
                v.wait_ge(s.q_p3x, max(ct - 1, 0) * 16)
                v.tensor_copy(p3s[0:4, (ct % 2) * 128:(ct % 2 + 1) * 128],
                              ptr[0:4, (ct % 2) * 1024:(ct % 2) * 1024 + 128]).then_inc(s.q_te, 1)

        @blk.scalar
        def _(sc):
          with contextlib.ExitStack() as es:
            se = WR(sc, es, name + "_se", 1, 1)
            for ct in range(16):
                sc.wait_ge(s.q_cv, ct + 1)
                sc.activation(eb[:, :], cvb[:, :], AF.Exp, scale=-1.0,
                              bias=C.cbn[:, 64 + ct:64 + ct + 1]).then_inc(s.q_ss, 1)
            sc.wait_ge(s.q_dt, 1)
            se.link(sc.activation(u2[:, :], u1[:, :], AF.Abs), s.q_ssl)
            se.link(sc.activation(u2[:, :], u2[:, :], AF.Exp, scale=-1.0), s.q_ssl)
            sc.activation(u2[:, :], u2[:, :], AF.Ln, bias=1.0).then_inc(s.q_dt, 1)
            for ct in range(64):
                sc.wait_ge(s.q_p3a, ct + 1)
                sc.activation(eb[:, 0:3], w3[:, 0:3], AF.Exp, scale=-1.0,
                              bias=C.cbn[:, ct:ct + 1]).then_inc(s.q_p3b, 1)

        @blk.tensor
        def _(tn):
            for ct in range(64):
                tn.wait_ge(s.q_p3c, ct + 1)
                if ct >= 2:
                    tn.wait_ge(s.q_te, ct - 1)
                tn.transpose(ptr[:, (ct % 2) * 1024:(ct % 2) * 1024 + 128],
                             pad[:, :], C.id16[:, :]).then_inc(s.q_tr, 1)

        @blk.sync
        def _(sy):
            for ct in range(16):
                sy.wait_ge(s.q_pk, ct + 1)
                base = ct * 128 if ct < 8 else 1024 + (ct - 8) * 128
                for z in range(NZ):
                    sy.dma_start(out=C.sp_pack[z, :, base:base + 128],
                                 in_=bcs[:, ct * 1024 + z * 128:ct * 1024 + (z + 1) * 128]
                                 ).then_inc(s.q_ps, 16)
            sy.wait_ge(s.q_dt, 3)
            sy.dma_start(out=C.sp_dtv[:, :], in_=dtv[:, :]).then_inc(s.q_dts, 16)
            for ct in range(64):
                sy.wait_ge(s.q_te, ct + 1)
                sy.dma_start(out=C.sp_X[0, 0:3, ct * 128:(ct + 1) * 128],
                             in_=p3s[0:3, (ct % 2) * 128:(ct % 2 + 1) * 128]).then_inc(s.q_p3x, 16)
            sy.wait_ge(s.q_p3x, 64 * 16)
            sy.wait_ge(s.q_ps, 16 * 16 * NZ)
            sy.wait_ge(s.q_dts, 16)


def emit_ssdA(nc, C, name="sa"):
  s = mk_sems(nc, C.st, name, ["q_zl", "q_zc", "q_ch", "q_xd", "q_stm", "q_stf",
                               "q_so", "q_vs"])
  M = 12
  with contextlib.ExitStack() as _es:
    X = _es.enter_context(nc.sbuf_tensor([128, 2 * FI], BF16))
    pk = _es.enter_context(nc.sbuf_tensor([128, 2 * 2048], BF16))
    dtz = _es.enter_context(nc.sbuf_tensor([128, 2 * 128], F32))
    adt = _es.enter_context(nc.sbuf_tensor([128, 128], F32))
    acum = _es.enter_context(nc.sbuf_tensor([128, 128], F32))
    dcy = _es.enter_context(nc.sbuf_tensor([128, 128], F32))
    dtT = _es.enter_context(nc.sbuf_tensor([128, 128], F32))
    dcyT = _es.enter_context(nc.sbuf_tensor([128, 128], F32))
    adtT = _es.enter_context(nc.sbuf_tensor([128, 128], F32))
    dd = _es.enter_context(nc.sbuf_tensor([128, 128], F32))
    easr = _es.enter_context(nc.sbuf_tensor([1, 128], F32))
    easbc = _es.enter_context(nc.sbuf_tensor([128, 128], F32))
    btm = _es.enter_context(nc.sbuf_tensor([128, 1024], BF16))
    xd = _es.enter_context(nc.sbuf_tensor([128, 2 * 512], BF16))
    zbuf = _es.enter_context(nc.sbuf_tensor([128, 128], F32))
    p_st = _es.enter_context(nc.psum_tensor([128, 1024], F32))
    p_pp = _es.enter_context(nc.psum_tensor([128, 1024], F32))
    p_bt = _es.enter_context(nc.psum_tensor([128, 1024], BF16))

    with nc.Block(name + "_c") as blk:
        @blk.gpsimd
        def _(g):
            g.memset(C.prevT[:, :], 0.0)
            g.memset(zbuf[:, :], 0.0)
        @blk.sync
        def _(sy):
            sy.sem_inc(s.q_zc, 2)
            sy.sem_inc(s.q_stf, 2)

    with nc.Block(name) as blk:
        @blk.sync
        def _(sy):
          with contextlib.ExitStack() as es:
            y_zc = WR(sy, es, name + "_y_zc", 1, 1)
            with sy.Fori(0, 4) as jz:
                for b in range(2):
                    y_zc.wait(s.q_zc)
                    sy.dma_start(out=X[:, b * FI:(b + 1) * FI],
                                 in_=C.sp_X[ds(jz * 2 + b, 1), :, :]).then_inc(s.q_zl, 16)
                    sy.dma_start(out=pk[:, b * 2048:(b + 1) * 2048],
                                 in_=C.sp_pack[ds(jz * 2 + b, 1), :, :]).then_inc(s.q_zl, 16)
                    sy.dma_start(out=dtz[:, b * 128:(b + 1) * 128],
                                 in_=C.sp_dtv[:, ds(jz * 256 + b * 128, 128)]).then_inc(s.q_zl, 16)
            sy.wait_ge(s.q_zc, 10)

        @blk.vector
        def _(v):
          with contextlib.ExitStack() as es:
            v_zl = WR(v, es, name + "_v_zl", 48, 48)
            v_c5 = WR(v, es, name + "_v_c5", 5, M)
            v_c9 = WR(v, es, name + "_v_c9", 9, M)
            v_c11 = WR(v, es, name + "_v_c11", 11, M)
            v_stm = WR(v, es, name + "_v_stm", 1, 1)
            ve = WR(v, es, name + "_ve", 1, 1)
            with v.Fori(0, 4) as jz:
                for b in range(2):
                    v_zl.wait(s.q_zl)
                    ve.link(v.tensor_scalar_mul(adt[:, :], dtz[:, b * 128:(b + 1) * 128],
                                                C.aneg[:, 0:1]), s.q_vs)
                    v.engine_nop().then_inc(s.q_ch, 1)
                    ve.link(v.tensor_tensor_scan(acum[:, :], adt[:, :], zbuf[:, :], 0.0,
                                                 ALU.add, ALU.add), s.q_vs)
                    v.engine_nop().then_inc(s.q_ch, 1)
                    ve.link(v.tensor_scalar(dcy[:, :], acum[:, :], acum[:, 127:128], None,
                                            ALU.subtract), s.q_vs)
                    v.engine_nop().then_inc(s.q_ch, 1)
                    v_c5.wait(s.q_ch)
                    ve.link(v.tensor_copy(dtT[:, :], p_pp[:, 0:128]), s.q_vs)
                    ve.link(v.tensor_copy(dcyT[:, :], p_pp[:, 128:256]), s.q_vs)
                    ve.link(v.tensor_copy(adtT[:, :], p_pp[:, 256:384]), s.q_vs)
                    ve.link(v.tensor_mul(dd[:, :], dtT[:, :], dcyT[:, :]), s.q_vs)
                    v.engine_nop().then_inc(s.q_ch, 1)
                    v_c9.wait(s.q_ch)
                    ve.link(v.tensor_copy(easbc[:, :], p_pp[:, 640:768]), s.q_vs)
                    v.engine_nop().then_inc(s.q_ch, 1)
                    v_c11.wait(s.q_ch)
                    ve.link(v.tensor_copy(btm[:, :], p_bt[:, 0:1024]), s.q_vs)
                    v.engine_nop().then_inc(s.q_ch, 1)
                    for grp in range(16):
                        gslot = grp % 2
                        for hh in range(8):
                            h = grp * 8 + hh
                            ins = v.tensor_scalar_mul(
                                xd[:, gslot * 512 + hh * 64:gslot * 512 + (hh + 1) * 64],
                                X[:, b * FI + h * 64:b * FI + (h + 1) * 64], dd[:, h:h + 1])
                        ins.then_inc(s.q_xd, 1)
                        v_stm.wait(s.q_stm)
                        for hh in range(8):
                            h = grp * 8 + hh
                            ins = v.scalar_tensor_tensor(
                                C.prevT[:, h * 64:(h + 1) * 64],
                                C.prevT[:, h * 64:(h + 1) * 64], easbc[:, h:h + 1],
                                p_st[:, gslot * 512 + hh * 64:gslot * 512 + (hh + 1) * 64],
                                ALU.mult, ALU.add)
                        ins.then_inc(s.q_stf, 1)
                    v.engine_nop().then_inc(s.q_zc, 1)

        @blk.scalar
        def _(sc):
          with contextlib.ExitStack() as es:
            s_c4 = WR(sc, es, name + "_s_c4", 3, M)
            s_c8 = WR(sc, es, name + "_s_c8", 7, M)
            with sc.Fori(0, 4) as jz:
                for b in range(2):
                    s_c4.wait(s.q_ch)
                    sc.activation(dcy[:, :], dcy[:, :], AF.Exp, scale=-1.0).then_inc(s.q_ch, 1)
                    s_c8.wait(s.q_ch)
                    sc.activation(easr[0:1, :], p_pp[0:1, 512:640], AF.Exp).then_inc(s.q_ch, 1)

        @blk.tensor
        def _(tn):
          with contextlib.ExitStack() as es:
            t_zl = WR(tn, es, name + "_t_zl", 48, 48)
            t_c5 = WR(tn, es, name + "_t_c5", 4, M)
            t_c7 = WR(tn, es, name + "_t_c7", 6, M)
            t_c9 = WR(tn, es, name + "_t_c9", 8, M)
            t_c11 = WR(tn, es, name + "_t_c11", 10, M)
            t_xd = WR(tn, es, name + "_t_xd", 1, 1)
            t_stf = WR(tn, es, name + "_t_stf", 1, 1)
            with tn.Fori(0, 4) as jz:
                for b in range(2):
                    t_zl.wait(s.q_zl)
                    t_c5.wait(s.q_ch)
                    tn.transpose(p_pp[:, 0:128], dtz[:, b * 128:(b + 1) * 128], C.id32[:, :])
                    tn.transpose(p_pp[:, 128:256], dcy[:, :], C.id32[:, :])
                    ins = tn.transpose(p_pp[:, 256:384], adt[:, :], C.id32[:, :])
                    ins.then_inc(s.q_ch, 1)
                    t_c7.wait(s.q_ch)
                    tn.matmul(p_pp[0:1, 512:640], C.onescol[:, 0:1], adtT[:, :],
                              start=True, stop=True).then_inc(s.q_ch, 1)
                    t_c9.wait(s.q_ch)
                    tn.matmul(p_pp[:, 640:768], C.ones32[0:1, 0:128], easr[0:1, :],
                              start=True, stop=True).then_inc(s.q_ch, 1)
                    t_c11.wait(s.q_ch)
                    for gg in range(NG):
                        ins = tn.transpose(p_bt[:, gg * 128:(gg + 1) * 128],
                                           pk[:, b * 2048 + gg * 128:b * 2048 + (gg + 1) * 128],
                                           C.id16[:, :])
                    ins.then_inc(s.q_ch, 1)
                    for grp in range(16):
                        gslot = grp % 2
                        t_xd.wait(s.q_xd)
                        t_stf.wait(s.q_stf)
                        for hh in range(8):
                            h = grp * 8 + hh
                            ins = tn.matmul(
                                p_st[:, gslot * 512 + hh * 64:gslot * 512 + (hh + 1) * 64],
                                btm[:, (h // 16) * 128:(h // 16 + 1) * 128],
                                xd[:, gslot * 512 + hh * 64:gslot * 512 + (hh + 1) * 64],
                                start=True, stop=True)
                        ins.then_inc(s.q_stm, 1)

    with nc.Block(name + "_so") as blk:
        @blk.gpsimd
        def _(g):
            g.dma_start(out=C.shx[bass.ds(C.e_par128, 128), 256:8448],
                        in_=C.prevT[:, :]).then_inc(s.q_so, 16)
            g.wait_ge(s.q_so, 16)
            g.dma_start(out=C.shf[0:1, bass.ds(C.e_par2p1, 1)],
                        in_=C.noncesb[0:1, 0:1]).then_inc(s.q_so, 16)
            g.wait_ge(s.q_so, 32)


def emit_exch2(nc, C, name="x2"):
  s = mk_sems(nc, C.st, name, ["q_fl", "q_po"])
  with contextlib.ExitStack() as _es:
    sin = _es.enter_context(nc.sbuf_tensor([128, FI], F32))
    with nc.Block(name) as blk:
        @blk.gpsimd
        def _(g):
            with (g.register(name + "_nd") as nd, g.register(name + "_rexp") as rexp):
                g.reg_mov(nd, 1)
                g.reg_mov(rexp, 16)
                if not SIM_SKIP_POLL:
                    with g.While(nd):
                        g.dma_start(out=C.flsb[0:1, 0:1],
                                    in_=C.shf[0:1, bass.ds(C.e_oth2p1, 1)]).then_inc(s.q_fl, 16)
                        g.wait_ge(s.q_fl, rexp)
                        g.reg_add(rexp, rexp, 16)
                        fv = g.value_load(C.flsb[0:1, 0:1])
                        g.reg_mov(nd, fv - C.gnv)
                else:
                    g.dma_start(out=C.shx[bass.ds(C.e_oth128, 128), 256:8448],
                                in_=C.prevT[:, :]).then_inc(s.q_fl, 16)
                    g.wait_ge(s.q_fl, 16)
                g.dma_start(out=sin[:, :],
                            in_=C.shx[bass.ds(C.e_oth128, 128), 256:8448]).then_inc(s.q_po, 16)
                g.wait_ge(s.q_po, 16)
    with nc.Block(name + "_m") as blk:
        @blk.vector
        def _(v):
            v.tensor_scalar_mul(C.prevT[:, :], sin[:, :], C.pmask[:, 0:1])


def emit_ssdB(nc, C, name="sb"):
  s = mk_sems(nc, C.st, name, ["q_zl", "q_gt", "q_zc", "q_ch", "q_pb", "q_dm",
                               "q_l1", "q_mg", "q_mh", "q_xc", "q_xd",
                               "q_pc", "q_ym", "q_t1", "q_t2", "q_ev", "q_sg",
                               "q_go1", "q_rstd", "q_og", "q_ogc", "q_gs", "q_vs",
                               "q_ssl"])
  M = 13
  with contextlib.ExitStack() as _es:
    X = _es.enter_context(nc.sbuf_tensor([128, 2 * FI], BF16))
    gt = _es.enter_context(nc.sbuf_tensor([128, 2 * FI], BF16))
    pk = _es.enter_context(nc.sbuf_tensor([128, 2 * 2048], BF16))
    dtz = _es.enter_context(nc.sbuf_tensor([128, 2 * 128], F32))
    adt = _es.enter_context(nc.sbuf_tensor([128, 128], F32))
    acum = _es.enter_context(nc.sbuf_tensor([128, 128], F32))
    dcy = _es.enter_context(nc.sbuf_tensor([128, 128], F32))
    dtT = _es.enter_context(nc.sbuf_tensor([128, 128], F32))
    dcyT = _es.enter_context(nc.sbuf_tensor([128, 128], F32))
    adtT = _es.enter_context(nc.sbuf_tensor([128, 128], F32))
    acumT = _es.enter_context(nc.sbuf_tensor([128, 128], F32))
    eacT = _es.enter_context(nc.sbuf_tensor([128, 128], F32))
    dd = _es.enter_context(nc.sbuf_tensor([128, 128], F32))
    easr = _es.enter_context(nc.sbuf_tensor([1, 128], F32))
    easbc = _es.enter_context(nc.sbuf_tensor([128, 128], F32))
    btm = _es.enter_context(nc.sbuf_tensor([128, 1024], BF16))
    zbuf = _es.enter_context(nc.sbuf_tensor([128, 128], F32))
    Pb = _es.enter_context(nc.sbuf_tensor([128, 2 * 128], BF16))
    Lm = _es.enter_context(nc.sbuf_tensor([128, 2 * 128], F32))
    Mh = _es.enter_context(nc.sbuf_tensor([128, 2 * 128], BF16))
    xch = _es.enter_context(nc.sbuf_tensor([128, 2 * 64], BF16))
    xdb = _es.enter_context(nc.sbuf_tensor([128, 2 * 64], BF16))
    pc16 = _es.enter_context(nc.sbuf_tensor([128, 2 * 64], BF16))
    t1b = _es.enter_context(nc.sbuf_tensor([128, 2 * 64], F32))
    t2b = _es.enter_context(nc.sbuf_tensor([128, 2 * 64], F32))
    ych = _es.enter_context(nc.sbuf_tensor([128, FI], F32))
    sgb = _es.enter_context(nc.sbuf_tensor([128, FI // 2], F32))
    gout = _es.enter_context(nc.sbuf_tensor([128, FI // 2], F32))
    og = _es.enter_context(nc.sbuf_tensor([128, FI // 2], BF16))
    ssq = _es.enter_context(nc.sbuf_tensor([128, 16], F32))
    p_d = _es.enter_context(nc.psum_tensor([128, 1024], F32))
    p_y = _es.enter_context(nc.psum_tensor([128, 1024], F32))
    p_m = _es.enter_context(nc.psum_tensor([128, 1024], F32))
    p_pp = _es.enter_context(nc.psum_tensor([128, 512], F32))
    p_bt = _es.enter_context(nc.psum_tensor([128, 1024], BF16))

    with nc.Block(name + "_c") as blk:
        @blk.gpsimd
        def _(g):
            g.memset(zbuf[:, :], 0.0)
        @blk.sync
        def _(sy):
            sy.sem_inc(s.q_zc, 2)
            sy.sem_inc(s.q_gs, 16)
            sy.sem_inc(s.q_ev, 2)
            sy.sem_inc(s.q_ym, 2)
            sy.sem_inc(s.q_mh, 16)

    with nc.Block(name) as blk:
        @blk.sync
        def _(sy):
            def spills(c):
                for hf in range(2):
                    sy.wait_ge(s.q_ogc, 2 * c + hf + 1)
                    sy.dma_start(out=C.sp_outg[c * 128:(c + 1) * 128,
                                               hf * (FI // 2):(hf + 1) * (FI // 2)],
                                 in_=og[:, :]).then_inc(s.q_gs, 16)
            for z in range(NZ):
                b = z % 2
                if z >= 1:
                    spills(z - 1)
                sy.wait_ge(s.q_zc, z + 1)
                sy.dma_start(out=X[:, b * FI:(b + 1) * FI],
                             in_=C.sp_X[z, :, :]).then_inc(s.q_zl, 16)
                sy.dma_start(out=pk[:, b * 2048:(b + 1) * 2048],
                             in_=C.sp_pack[z, :, :]).then_inc(s.q_zl, 16)
                sy.dma_start(out=dtz[:, b * 128:(b + 1) * 128],
                             in_=C.sp_dtv[:, z * 128:(z + 1) * 128]).then_inc(s.q_zl, 16)
                for it in range(ITL):
                    ins = sy.dma_start_transpose(
                        gt[:, b * FI + it * 128:b * FI + (it + 1) * 128],
                        C.sp_gate[it * 128:(it + 1) * 128, z * 128:(z + 1) * 128])
                    ins.then_inc(s.q_gt, 16)
            spills(NZ - 1)
            sy.wait_ge(s.q_gs, 16 + 32 * NZ)

        @blk.scalar
        def _(sc):
          with contextlib.ExitStack() as es:
            s_c4 = WR(sc, es, name + "_s_c4", 3, M)
            s_c7 = WR(sc, es, name + "_s_c7", 6, M)
            s_c9 = WR(sc, es, name + "_s_c9", 8, M)
            s_dm0 = WR(sc, es, name + "_s_dm0", 0, 128)
            s_dm1 = WR(sc, es, name + "_s_dm1", 1, 1)
            s_mh = WR(sc, es, name + "_s_mh", 17, 1)
            s_ym = WR(sc, es, name + "_s_ym", 3, 1)
            s_t2 = WR(sc, es, name + "_s_t2", 1, 1)
            s_gt = WR(sc, es, name + "_s_gt", ITL * 16, ITL * 16)
            s_go1 = WR(sc, es, name + "_s_go1", 1, 1)
            s_og = WR(sc, es, name + "_s_og", 1, 1)
            s_gs = WR(sc, es, name + "_s_gs", 16, 16)
            se = WR(sc, es, name + "_se", 1, 1)

            def mkpb(h):
                sc.activation(Pb[:, (h % 2) * 128:(h % 2) * 128 + 128], C.strict[:, :],
                              AF.Copy, scale=adtT[:, h:h + 1]).then_inc(s.q_pb, 1)

            with sc.Fori(0, 4) as jz:
                for b in range(2):
                    s_c4.wait(s.q_ch)
                    sc.activation(dcy[:, :], dcy[:, :], AF.Exp, scale=-1.0).then_inc(s.q_ch, 1)
                    s_c7.wait(s.q_ch)
                    sc.activation(eacT[:, :], acumT[:, :], AF.Exp).then_inc(s.q_ch, 1)
                    s_c9.wait(s.q_ch)
                    sc.activation(easr[0:1, :], p_m[0:1, 128:256], AF.Exp).then_inc(s.q_ch, 1)
                    s_dm0.wait(s.q_dm)
                    mkpb(0)
                    mkpb(1)
                    for h in range(NH):
                        hs = (h % 2)
                        s_dm1.wait(s.q_dm)
                        if h < NH - 2:
                            mkpb(h + 2)
                        if h >= 2:
                            s_mh.wait(s.q_mh)
                        sc.activation(Lm[:, hs * 128:(hs + 1) * 128],
                                      p_d[:, hs * 512:hs * 512 + 128],
                                      AF.Exp).then_inc(s.q_l1, 1)
                        sc.activation(pc16[:, hs * 64:(hs + 1) * 64],
                                      C.prevT[:, h * 64:(h + 1) * 64],
                                      AF.Copy).then_inc(s.q_pc, 1)
                        s_ym.wait(s.q_ym)
                        if h >= 2:
                            s_t2.wait(s.q_t2)
                        sc.activation(t1b[:, hs * 64:(hs + 1) * 64],
                                      p_y[:, hs * 512 + 64:hs * 512 + 128],
                                      AF.Copy, scale=eacT[:, h:h + 1]).then_inc(s.q_t1, 1)
                    s_mh.bump(); s_mh.bump()
                    s_t2.bump(); s_t2.bump()
                    s_gt.wait(s.q_gt)
                    for hf in range(2):
                        fo = hf * (FI // 2)
                        sc.activation(sgb[:, :], gt[:, b * FI + fo:b * FI + fo + FI // 2],
                                      AF.Exp, scale=-1.0).then_inc(s.q_sg, 1)
                        s_go1.wait(s.q_go1)
                        for gg in range(4):
                            sc.activation(sgb[:, gg * 1024:(gg + 1) * 1024],
                                          gout[:, gg * 1024:(gg + 1) * 1024], AF.Square,
                                          accum_out=ssq[:, hf * 4 + gg:hf * 4 + gg + 1])
                        r0 = 8 + hf * 4
                        se.link(sc.activation(ssq[:, r0:r0 + 4], ssq[:, hf * 4:hf * 4 + 4],
                                              AF.Copy, scale=1.0 / 1024.0), s.q_ssl)
                        se.link(sc.activation(ssq[:, r0:r0 + 4], ssq[:, r0:r0 + 4],
                                              AF.Ln, bias=C.epsc[:, 0:1]), s.q_ssl)
                        sc.activation(ssq[:, r0:r0 + 4], ssq[:, r0:r0 + 4],
                                      AF.Exp, scale=-0.5).then_inc(s.q_rstd, 1)
                        s_og.wait(s.q_og)
                        s_gs.wait(s.q_gs)
                        sc.activation(og[:, :], gout[:, :], AF.Copy).then_inc(s.q_ogc, 1)

        @blk.tensor
        def _(tn):
          with contextlib.ExitStack() as es:
            t_zl = WR(tn, es, name + "_t_zl", 48, 48)
            t_c5 = WR(tn, es, name + "_t_c5", 4, M)
            t_c8 = WR(tn, es, name + "_t_c8", 7, M)
            t_c10 = WR(tn, es, name + "_t_c10", 9, M)
            t_c12 = WR(tn, es, name + "_t_c12", 11, M)
            t_pb = WR(tn, es, name + "_t_pb", 1, 1)
            t_l1 = WR(tn, es, name + "_t_l1", 0, 1)
            t_mh = WR(tn, es, name + "_t_mh", 17, 1)
            t_mhg = WR(tn, es, name + "_t_mhg", 0, 16)
            t_mhall = WR(tn, es, name + "_t_mhall", 16, 128)
            t_pc = WR(tn, es, name + "_t_pc", 1, 1)
            t_xc = WR(tn, es, name + "_t_xc", 1, 1)
            t_xd = WR(tn, es, name + "_t_xd", 1, 1)
            t_ev = WR(tn, es, name + "_t_ev", 1, 1)

            def pdmm(h):
                hs = (h % 2)
                tn.matmul(p_d[:, hs * 512:hs * 512 + 128],
                          Pb[:, hs * 128:(hs + 1) * 128], C.qtril[:, :],
                          start=True, stop=False)
                tn.matmul(p_d[:, hs * 512:hs * 512 + 128],
                          C.id16[:, :], C.negstrict[:, :],
                          start=False, stop=True).then_inc(s.q_dm, 1)

            with tn.Fori(0, 4) as jz:
                for b in range(2):
                    t_zl.wait(s.q_zl)
                    t_c5.wait(s.q_ch)
                    tn.transpose(p_pp[:, 0:128], dtz[:, b * 128:(b + 1) * 128], C.id32[:, :])
                    tn.transpose(p_pp[:, 128:256], dcy[:, :], C.id32[:, :])
                    tn.transpose(p_pp[:, 256:384], adt[:, :], C.id32[:, :])
                    ins = tn.transpose(p_pp[:, 384:512], acum[:, :], C.id32[:, :])
                    ins.then_inc(s.q_ch, 1)
                    t_c8.wait(s.q_ch)
                    t_mhall.wait(s.q_mh)
                    tn.matmul(p_m[0:1, 128:256], C.onescol[:, 0:1], adtT[:, :],
                              start=True, stop=True).then_inc(s.q_ch, 1)
                    t_c10.wait(s.q_ch)
                    tn.matmul(p_m[:, 256:384], C.ones32[0:1, 0:128], easr[0:1, :],
                              start=True, stop=True).then_inc(s.q_ch, 1)
                    t_c12.wait(s.q_ch)
                    for gg in range(NG):
                        ins = tn.transpose(p_bt[:, gg * 128:(gg + 1) * 128],
                                           pk[:, b * 2048 + gg * 128:b * 2048 + (gg + 1) * 128],
                                           C.id16[:, :])
                    ins.then_inc(s.q_ch, 1)
                    t_pb.wait(s.q_pb)
                    t_l1.wait(s.q_l1)
                    pdmm(0)
                    for h in range(NH):
                        g2 = h // 16
                        ys = (h % 2) * 512
                        if h % 16 == 0:
                            gg2 = (g2 % 2) * 512
                            t_mhg.wait(s.q_mh)
                            tn.matmul(p_m[:, gg2:gg2 + 128],
                                      pk[:, b * 2048 + g2 * 128:b * 2048 + (g2 + 1) * 128],
                                      pk[:, b * 2048 + 1024 + g2 * 128:b * 2048 + 1024 + (g2 + 1) * 128],
                                      start=True, stop=True).then_inc(s.q_mg, 1)
                        if h < NH - 1:
                            t_pb.wait(s.q_pb)
                            t_l1.wait(s.q_l1)
                            pdmm(h + 1)
                        t_mh.wait(s.q_mh)
                        t_ev.wait(s.q_ev)
                        t_xc.wait(s.q_xc)
                        tn.matmul(p_y[:, ys:ys + 64], Mh[:, (h % 2) * 128:(h % 2 + 1) * 128],
                                  xch[:, (h % 2) * 64:(h % 2 + 1) * 64], start=True, stop=True)
                        t_pc.wait(s.q_pc)
                        tn.matmul(p_y[:, ys + 64:ys + 128],
                                  pk[:, b * 2048 + 1024 + g2 * 128:b * 2048 + 1024 + (g2 + 1) * 128],
                                  pc16[:, (h % 2) * 64:(h % 2 + 1) * 64], start=True, stop=True)
                        t_xd.wait(s.q_xd)
                        ins = tn.matmul(p_y[:, ys + 128:ys + 192],
                                        btm[:, g2 * 128:(g2 + 1) * 128],
                                        xdb[:, (h % 2) * 64:(h % 2 + 1) * 64],
                                        start=True, stop=True)
                        ins.then_inc(s.q_ym, 1)

        @blk.vector
        def _(v):
          with contextlib.ExitStack() as es:
            v_zl = WR(v, es, name + "_v_zl", 48, 48)
            v_c5 = WR(v, es, name + "_v_c5", 5, M)
            v_c10 = WR(v, es, name + "_v_c10", 10, M)
            v_c13x = WR(v, es, name + "_v_c13x", 12, M)
            v_ymg = WR(v, es, name + "_v_ymg", 1, 1)
            v_l1 = WR(v, es, name + "_v_l1", 1, 1)
            v_mg = WR(v, es, name + "_v_mg", 1, 1)
            v_ym = WR(v, es, name + "_v_ym", 3, 1)
            v_t1 = WR(v, es, name + "_v_t1", 1, 1)
            v_sg = WR(v, es, name + "_v_sg", 1, 1)
            v_rstd = WR(v, es, name + "_v_rstd", 1, 1)
            v_ogc = WR(v, es, name + "_v_ogc", 0, 1)
            ve = WR(v, es, name + "_ve", 1, 1)

            def late(v, hp, b):
                hps = (hp % 2)
                v_t1.wait(s.q_t1)
                v.scalar_tensor_tensor(
                    t2b[:, hps * 64:(hps + 1) * 64],
                    X[:, b * FI + hp * 64:b * FI + (hp + 1) * 64],
                    C.dbc[:, hp:hp + 1], t1b[:, hps * 64:(hps + 1) * 64],
                    ALU.mult, ALU.add).then_inc(s.q_t2, 1)
                v.tensor_add(ych[:, hp * 64:(hp + 1) * 64],
                             t2b[:, hps * 64:(hps + 1) * 64],
                             p_y[:, hps * 512:hps * 512 + 64])
                v.scalar_tensor_tensor(
                    C.prevT[:, hp * 64:(hp + 1) * 64],
                    C.prevT[:, hp * 64:(hp + 1) * 64], easbc[:, hp:hp + 1],
                    p_y[:, hps * 512 + 128:hps * 512 + 192],
                    ALU.mult, ALU.add).then_inc(s.q_ev, 1)

            with v.Fori(0, 4) as jz:
                for b in range(2):
                    v_zl.wait(s.q_zl)
                    ve.link(v.tensor_scalar_mul(adt[:, :], dtz[:, b * 128:(b + 1) * 128],
                                                C.aneg[:, 0:1]), s.q_vs)
                    v.engine_nop().then_inc(s.q_ch, 1)
                    ve.link(v.tensor_tensor_scan(acum[:, :], adt[:, :], zbuf[:, :], 0.0,
                                                 ALU.add, ALU.add), s.q_vs)
                    v.engine_nop().then_inc(s.q_ch, 1)
                    ve.link(v.tensor_scalar(dcy[:, :], acum[:, :], acum[:, 127:128], None,
                                            ALU.subtract), s.q_vs)
                    v.engine_nop().then_inc(s.q_ch, 1)
                    v_c5.wait(s.q_ch)
                    ve.link(v.tensor_copy(dtT[:, :], p_pp[:, 0:128]), s.q_vs)
                    ve.link(v.tensor_copy(dcyT[:, :], p_pp[:, 128:256]), s.q_vs)
                    ve.link(v.tensor_copy(adtT[:, :], p_pp[:, 256:384]), s.q_vs)
                    ve.link(v.tensor_copy(acumT[:, :], p_pp[:, 384:512]), s.q_vs)
                    ve.link(v.tensor_mul(dd[:, :], dtT[:, :], dcyT[:, :]), s.q_vs)
                    v.engine_nop().then_inc(s.q_ch, 1)
                    v_c10.wait(s.q_ch)
                    ve.link(v.tensor_copy(easbc[:, :], p_m[:, 256:384]), s.q_vs)
                    v.engine_nop().then_inc(s.q_ch, 1)
                    v_c13x.wait(s.q_ch)
                    ve.link(v.tensor_copy(btm[:, :], p_bt[:, 0:1024]), s.q_vs)
                    v.engine_nop().then_inc(s.q_ch, 1)
                    for h in range(NH):
                        hs = (h % 2)
                        v_ymg.wait(s.q_ym)
                        v.tensor_scalar_mul(xch[:, hs * 64:(hs + 1) * 64],
                                            X[:, b * FI + h * 64:b * FI + (h + 1) * 64],
                                            dtT[:, h:h + 1]).then_inc(s.q_xc, 1)
                        v.tensor_scalar_mul(xdb[:, hs * 64:(hs + 1) * 64],
                                            X[:, b * FI + h * 64:b * FI + (h + 1) * 64],
                                            dd[:, h:h + 1]).then_inc(s.q_xd, 1)
                        if h % 16 == 0:
                            v_mg.wait(s.q_mg)
                        v_l1.wait(s.q_l1)
                        v.tensor_mul(Mh[:, hs * 128:(hs + 1) * 128],
                                     p_m[:, ((h // 16) % 2) * 512:((h // 16) % 2) * 512 + 128],
                                     Lm[:, hs * 128:(hs + 1) * 128]).then_inc(s.q_mh, 1)
                        if h >= 1:
                            v_ym.wait(s.q_ym)
                            late(v, h - 1, b)
                    v_ym.wait(s.q_ym)
                    late(v, NH - 1, b)
                    for hf in range(2):
                        fo = hf * (FI // 2)
                        v_sg.wait(s.q_sg)
                        v.tensor_scalar_add(sgb[:, :], sgb[:, :], 1.0)
                        v.reciprocal(sgb[:, :], sgb[:, :])
                        v.tensor_mul(sgb[:, :], sgb[:, :],
                                     gt[:, b * FI + fo:b * FI + fo + FI // 2])
                        v_ogc.wait(s.q_ogc)
                        v.tensor_mul(gout[:, :], sgb[:, :],
                                     ych[:, fo:fo + FI // 2]).then_inc(s.q_go1, 1)
                        v_rstd.wait(s.q_rstd)
                        for gg in range(4):
                            ins = v.tensor_scalar_mul(gout[:, gg * 1024:(gg + 1) * 1024],
                                                      gout[:, gg * 1024:(gg + 1) * 1024],
                                                      ssq[:, 8 + hf * 4 + gg:9 + hf * 4 + gg])
                        ins.then_inc(s.q_og, 1)
                    v.engine_nop().then_inc(s.q_zc, 1)

    if PHASES == 7 and DBG_SSD:
        sd = mk_sems(nc, C.st, name + "dbg", ["q"])
        with nc.Block(name + "_dbg") as blk:
            @blk.gpsimd
            def _(g):
                g.dma_start(out=C.outap[0:128, 0:4096], in_=ych[:, 0:4096]).then_inc(sd.q, 16)
                g.dma_start(out=C.outap[128:256, 0:4096], in_=ych[:, 4096:8192]).then_inc(sd.q, 16)
                g.dma_start(out=C.outap[256:384, 0:4096], in_=C.prevT[:, 0:4096]).then_inc(sd.q, 16)
                g.dma_start(out=C.outap[384:512, 0:4096], in_=C.prevT[:, 4096:8192]).then_inc(sd.q, 16)
                g.wait_ge(sd.q, 64)


def emit_outproj(nc, C, name="op"):
  s = mk_sems(nc, C.st, name, ["q_ol", "q_w", "q_m", "q_e", "q_as", "q_al",
                               "q_x1", "q_x2", "q_vs", "q_nw"])
  with contextlib.ExitStack() as _es:
    ofm = _es.enter_context(nc.sbuf_tensor([128, ITL * 1024], BF16))
    wob = _es.enter_context(nc.sbuf_tensor([128, 2 * 8192], BF16))
    ast = _es.enter_context(nc.sbuf_tensor([128, 2 * 1024], BF16))
    nwt = _es.enter_context(nc.sbuf_tensor([128, ITL], F32))
    ps = _es.enter_context(nc.psum_tensor([128, 2048], F32))

    with nc.Block(name + "_c") as blk:
        @blk.sync
        def _(sy):
            sy.sem_inc(s.q_m, 2)
            sy.sem_inc(s.q_e, 2)
            sy.sem_inc(s.q_as, 32)
            sy.dma_start(out=nwt[:, :], in_=C.nwt_in[:, :]).then_inc(s.q_nw, 16)
            for it in range(ITL):
                sy.dma_start_transpose(
                    ofm[:, it * 1024:(it + 1) * 1024],
                    C.sp_outg[0:1024, it * 128:(it + 1) * 128]).then_inc(s.q_ol, 16)
            sy.wait_ge(s.q_ol, ITL * 16)
        @blk.vector
        def _(v):
            v.wait_ge(s.q_nw, 16)
            for it in range(ITL):
                v.wait_ge(s.q_ol, (it + 1) * 16)
                v.tensor_scalar_mul(ofm[:, it * 1024:(it + 1) * 1024],
                                    ofm[:, it * 1024:(it + 1) * 1024],
                                    nwt[:, it:it + 1])

    with nc.Block(name) as blk:
        @blk.gpsimd
        def _(g):
            for k in range(DTL):
                g.wait_ge(s.q_m, k + 1)
                g.dma_start(out=wob[:, (k % 2) * 8192:(k % 2 + 1) * 8192],
                            in_=C.wout[k, :, :]).then_inc(s.q_w, 16)
        @blk.tensor
        def _(tn):
          with contextlib.ExitStack() as es:
            t_w = WR(tn, es, name + "_t_w", 16, 16)
            t_e = WR(tn, es, name + "_t_e", 1, 1)
            with tn.Fori(0, 16) as j:
                for b in range(2):
                    t_w.wait(s.q_w)
                    t_e.wait(s.q_e)
                    for tch in range(2):
                        for it in range(ITL):
                            ins = tn.matmul(
                                ps[:, b * 1024 + tch * 512:b * 1024 + (tch + 1) * 512],
                                wob[:, b * 8192 + it * 128:b * 8192 + (it + 1) * 128],
                                ofm[:, it * 1024 + tch * 512:it * 1024 + (tch + 1) * 512],
                                start=(it == 0), stop=(it == ITL - 1))
                    ins.then_inc(s.q_m, 1)
        @blk.vector
        def _(v):
          with contextlib.ExitStack() as es:
            v_m = WR(v, es, name + "_v_m", 3, 1)
            v_as = WR(v, es, name + "_v_as", 16, 16)
            with v.Fori(0, 16) as j:
                for b in range(2):
                    v_m.wait(s.q_m)
                    v_as.wait(s.q_as)
                    v.tensor_copy(ast[:, b * 1024:(b + 1) * 1024],
                                  ps[:, b * 1024:(b + 1) * 1024]).then_inc(s.q_e, 1)
        @blk.sync
        def _(sy):
          with contextlib.ExitStack() as es:
            y_e = WR(sy, es, name + "_y_e", 3, 1)
            with sy.Fori(0, 16) as j:
                for b in range(2):
                    y_e.wait(s.q_e)
                    sy.dma_start(out=C.sp_attn[ds(j * 256 + b * 128, 128), :],
                                 in_=ast[:, b * 1024:(b + 1) * 1024]).then_inc(s.q_as, 16)
            sy.wait_ge(s.q_as, 32 + 32 * 16)

  s2 = mk_sems(nc, C.st, name + "b", ["q_l", "q_d", "q_o", "q_vs"])
  with contextlib.ExitStack() as _es:
    atm = _es.enter_context(nc.sbuf_tensor([128, 2 * DM], BF16))
    x1t = _es.enter_context(nc.sbuf_tensor([128, 2 * DM], F32))
    x2t = _es.enter_context(nc.sbuf_tensor([128, 2 * DM], F32))
    with nc.Block(name + "b_c") as blk:
        @blk.sync
        def _(sy):
            sy.sem_inc(s2.q_d, 2)
            sy.sem_inc(s2.q_o, 32)
    with nc.Block(name + "b") as blk:
        @blk.sync
        def _(sy):
            for k in range(8):
                b = k % 2
                sy.wait_ge(s2.q_d, k + 1)
                for ot in range(DTL):
                    sy.dma_start_transpose(
                        atm[:, b * DM + ot * 128:b * DM + (ot + 1) * 128],
                        C.sp_attn[ot * 128:(ot + 1) * 128, k * 128:(k + 1) * 128]
                        ).then_inc(s2.q_l, 16)
                sy.dma_start(out=x1t[:, b * DM:(b + 1) * DM],
                             in_=C.sp_x1[k * 128:(k + 1) * 128, :]).then_inc(s2.q_l, 16)
                sy.wait_ge(s2.q_d, k + 3)
                sy.dma_start(out=C.sp_x2[k * 128:(k + 1) * 128, :],
                             in_=x2t[:, b * DM:(b + 1) * DM]).then_inc(s2.q_o, 16)
            sy.wait_ge(s2.q_o, 32 + 8 * 16)
        @blk.vector
        def _(v):
          with contextlib.ExitStack() as es:
            v_l = WR(v, es, name + "_v_l", 33 * 16, 33 * 16)
            v_o = WR(v, es, name + "_v_o", 16, 16)
            with v.Fori(0, 4) as jz:
                for b in range(2):
                    v_l.wait(s2.q_l)
                    v_o.wait(s2.q_o)
                    v.tensor_add(x2t[:, b * DM:(b + 1) * DM], x1t[:, b * DM:(b + 1) * DM],
                                 atm[:, b * DM:(b + 1) * DM]).then_inc(s2.q_d, 1)


def _stage(inputs):
    g = {}
    g["wup"] = _pt(inputs["L0_ffn_up_w"])
    g["wdn"] = _pt(inputs["L0_ffn_down_w"])
    g["wf1"] = _pt(inputs["L0_ffn_fc1_w"])
    g["wex"] = _pt(inputs["L0_ffn_expert_w"])
    g["wf2"] = _pt(inputs["L0_ffn_fc2_w"])
    gw = np.asarray(inputs["L0_ffn_gate_w"], np.float32)
    g["wgt"] = np.ascontiguousarray(gw.reshape(DTL, 128, 8).transpose(1, 0, 2).reshape(128, DTL * 8))
    ip = np.asarray(inputs["L0_attn_in_proj_w"], np.float32)
    g["wip"] = _pt(np.ascontiguousarray(ip.T))
    op = np.asarray(inputs["L0_attn_out_proj_w"], np.float32)
    g["wout"] = _pt(np.ascontiguousarray(op.T))
    cw = np.asarray(inputs["L0_attn_conv_w"], np.float32).reshape(NCV * 128, 4)
    g["cw"] = np.ascontiguousarray(cw.reshape(NCV, 128, 4).transpose(1, 0, 2).reshape(128, NCV * 4))
    g["cwt"] = np.ascontiguousarray(cw.reshape(NCV, 128, 4))
    cb = np.asarray(inputs["L0_attn_conv_b"], np.float32)
    g["cb"] = np.ascontiguousarray(cb.reshape(NCV, 128).T)
    g["cbt"] = np.ascontiguousarray(cb.reshape(NCV, 128, 1))
    g["onescol"] = np.ones((128, 1), np.float32)
    g["dtb"] = np.asarray(inputs["L0_attn_dt_bias"], np.float32).reshape(128, 1)
    g["alog"] = np.asarray(inputs["L0_attn_A_log"], np.float32).reshape(128, 1)
    g["dbc"] = np.ascontiguousarray(np.broadcast_to(
        np.asarray(inputs["L0_attn_D"], np.float32)[None, :], (128, 128)))
    g["gbc"] = np.ascontiguousarray(np.broadcast_to(
        np.asarray(inputs["L0_ffn_ln_gamma"], np.float32)[None, :], (128, DM)))
    g["bbc"] = np.ascontiguousarray(np.broadcast_to(
        np.asarray(inputs["L0_ffn_ln_beta"], np.float32)[None, :], (128, DM)))
    g["nwt"] = np.ascontiguousarray(
        np.asarray(inputs["L0_attn_norm_w"], np.float32).reshape(ITL, 128).T)
    g["id16"] = np.eye(128, dtype=ml_dtypes.bfloat16)
    g["id32"] = np.eye(128, dtype=np.float32)
    g["ones32"] = np.ones((1, 128), np.float32)
    ar = np.arange(128)
    g["strictm"] = (ar[:, None] > ar[None, :]).astype(ml_dtypes.bfloat16)
    g["negstrict"] = (-30000.0 * (ar[:, None] > ar[None, :])).astype(ml_dtypes.bfloat16)
    g["qtril"] = (ar[:, None] <= ar[None, :]).astype(ml_dtypes.bfloat16)
    return g


def kernel(**inputs):
    xfull = np.asarray(inputs["L0_attn_x"], np.float32)
    g = _stage(inputs)
    nonce = np.array([[int(time.time_ns() & 0x7FFFFFF0) | 5]], np.int32)
    in_maps = []
    for c in range(8):
        b, h = c // 2, c % 2
        m = dict(g)
        m["x"] = np.ascontiguousarray(xfull[b, h * NT:(h + 1) * NT, :])
        m["pmask"] = np.full((128, 1), float(h), np.float32)
        m["nonce"] = nonce
        in_maps.append(m)
    nc = build()
    res = run_bass_kernel_spmd(nc, in_maps, core_ids=list(range(8)))
    outp = np.empty((4, 2048, DM), np.float32)
    for c in range(8):
        b, h = c // 2, c % 2
        outp[b, h * NT:(h + 1) * NT, :] = res.results[c]["out"]
    return outp


# version 08
# speedup vs baseline: 1.8754x; 1.8754x over previous
import time
import contextlib
import numpy as np
import ml_dtypes
import concourse.bass as bass
import concourse.mybir as mybir
from concourse.bass_utils import run_bass_kernel_spmd

F32 = mybir.dt.float32
BF16 = mybir.dt.bfloat16
I32 = mybir.dt.int32
AF = mybir.ActivationFunctionType
ALU = mybir.AluOpType
AX = mybir.AxisListType.X
ds = bass.ds

NT, DM, FI, LAT = 1024, 4096, 8192, 2048
DTL, ITL, LTL = DM // 128, FI // 128, LAT // 128
NE, NZ, NH, NG, HP, SN = 8, 8, 128, 8, 64, 128
EPS = 1e-5
NCV = 80
PAIRS = [(i, j) for i in range(NE) for j in range(i + 1, NE)]

PHASES = 9
SIM_SKIP_POLL = False
DBG_SSD = False


class _C:
    pass


def _pt(w):
    K, M = w.shape
    nk, nm = K // 128, M // 128
    return np.ascontiguousarray(
        np.asarray(w, np.float32).reshape(nk, 128, nm, 128).transpose(2, 1, 0, 3)
        .reshape(nm, 128, nk * 128).astype(ml_dtypes.bfloat16))



class _S:
    pass


def mk_sems(nc, st, prefix, names):
    S = _S()
    for nm in names:
        if st is None:
            setattr(S, nm, nc.alloc_semaphore(name=prefix + "_" + nm))
        else:
            setattr(S, nm, st.enter_context(nc.semaphore(prefix + "_" + nm)))
    return S



class WR:
    def __init__(self, eng, es, name, init, step):
        self.eng, self.step = eng, step
        self.reg = es.enter_context(eng.register(name))
        eng.reg_mov(self.reg, init)
    def wait(self, sem):
        self.eng.wait_ge(sem, self.reg)
        self.eng.reg_add(self.reg, self.reg, self.step)

    def link(self, ins, sem):
        ins.then_inc(sem, 1)
        self.eng.wait_ge(sem, self.reg)
        self.eng.reg_add(self.reg, self.reg, self.step)

    def bump(self):
        self.eng.reg_add(self.reg, self.reg, self.step)


_LN_ADV = {"q_x": 128, "q_sp": 160, "q_a": 8, "q_b": 8, "q_c": 8, "q_d": 10,
           "q_vs": 24, "q_ss": 8}


def emit_ln(nc, C, src, name):
  if not hasattr(C, "ln_sems"):
      C.ln_sems = mk_sems(nc, C.st, "lnS", list(_LN_ADV))
      C.ln_calls = 0
  S = C.ln_sems
  LB = {q: C.ln_calls * _LN_ADV[q] for q in _LN_ADV}
  C.ln_calls += 1
  with contextlib.ExitStack() as _es:
    C.xa = _es.enter_context(nc.sbuf_tensor([128, 2 * DM], F32))
    C.lnw = _es.enter_context(nc.sbuf_tensor([128, 2 * DM], BF16))
    C.sq = _es.enter_context(nc.sbuf_tensor([128, DM], F32))
    with nc.Block(name + "_c") as blk:
        @blk.sync
        def _(sy):
            sy.sem_inc(S.q_d, 2)
            sy.sem_inc(S.q_sp, 32)

    with nc.Block(name) as blk:
        @blk.sync
        def _(sy):
            for k in range(8):
                b = k % 2
                sy.wait_ge(S.q_d, LB["q_d"] + 1 + k)
                sy.dma_start(out=C.xa[:, b * DM:(b + 1) * DM],
                             in_=src[k * 128:(k + 1) * 128, :]).then_inc(S.q_x, 16)
                sy.wait_ge(S.q_d, LB["q_d"] + 3 + k)
                sy.dma_start(out=C.sp_lnt[k * 128:(k + 1) * 128, :],
                             in_=C.lnw[:, b * DM:(b + 1) * DM]).then_inc(S.q_sp, 16)
            sy.wait_ge(S.q_sp, LB["q_sp"] + 32 + 16 * 8)

        @blk.vector
        def _(v):
          with contextlib.ExitStack() as es:
            ve = WR(v, es, name + "_ve", 1 + LB["q_vs"], 1)
            v_x = WR(v, es, name + "_v_x", 16 + LB["q_x"], 16)
            v_a = WR(v, es, name + "_v_a", 1 + LB["q_a"], 1)
            v_c = WR(v, es, name + "_v_c", 1 + LB["q_c"], 1)
            v_sp = WR(v, es, name + "_v_sp", 16 + LB["q_sp"], 16)
            with v.Fori(0, 4) as t:
                for b in range(2):
                    xs = C.xa[:, b * DM:(b + 1) * DM]
                    r = C.red[:, b * 8:(b + 1) * 8]
                    v_x.wait(S.q_x)
                    ve.link(v.reduce_sum(r[:, 0:1], xs, axis=AX), S.q_vs)
                    v_a.wait(S.q_a)
                    ve.link(v.tensor_scalar_mul(r[:, 2:3], r[:, 0:1], 1.0 / DM), S.q_vs)
                    ve.link(v.tensor_mul(r[:, 3:4], r[:, 2:3], r[:, 2:3]), S.q_vs)
                    v.scalar_tensor_tensor(r[:, 4:5], r[:, 1:2], 1.0 / DM, r[:, 3:4],
                                           ALU.mult, ALU.subtract).then_inc(S.q_b, 1)
                    v_c.wait(S.q_c)
                    v_sp.wait(S.q_sp)
                    v.tensor_scalar(C.lnw[:, b * DM:(b + 1) * DM], xs, r[:, 2:3], r[:, 5:6],
                                    ALU.subtract, ALU.mult).then_inc(S.q_d, 1)

        @blk.scalar
        def _(sc):
          with contextlib.ExitStack() as es:
            se = WR(sc, es, name + "_se", 1 + LB["q_ss"], 1)
            s_x = WR(sc, es, name + "_s_x", 16 + LB["q_x"], 16)
            s_b = WR(sc, es, name + "_s_b", 1 + LB["q_b"], 1)
            with sc.Fori(0, 4) as t:
                for b in range(2):
                    xs = C.xa[:, b * DM:(b + 1) * DM]
                    r = C.red[:, b * 8:(b + 1) * 8]
                    s_x.wait(S.q_x)
                    sc.activation(C.sq[:, :], xs, AF.Square, accum_out=r[:, 1:2]).then_inc(S.q_a, 1)
                    s_b.wait(S.q_b)
                    se.link(sc.activation(r[:, 6:7], r[:, 4:5], AF.Ln, bias=C.epsc[:, 0:1]), S.q_ss)
                    sc.activation(r[:, 5:6], r[:, 6:7], AF.Exp, scale=-0.5).then_inc(S.q_c, 1)


_FFN_ADV = {"q_lnf": 1024, "q_upw": 2048, "q_upm": 130, "q_upe": 130, "q_ups": 2080,
            "q_f1w": 512, "q_f1m": 34, "q_f1e": 34, "q_f1s": 544,
            "q_exw": 512, "q_exm": 34, "q_exe": 34, "q_exs": 544,
            "q_dnw": 1024, "q_dnw2": 1024, "q_dnm": 66, "q_dne": 66, "q_dns": 1056,
            "q_lg": 2, "q_lge": 2, "q_lt": 2, "q_rwc": 2, "q_rwr": 6, "q_rwb": 2,
            "q_hl": 32, "q_ll": 32, "q_ml": 32, "q_ep": 256, "q_epo": 160,
            "q_a": 8, "q_b": 8, "q_c": 8, "q_d": 10, "q_gb": 32,
            "q_vs": 360, "q_vs2": 48, "q_ss2": 8}


def emit_ffn(nc, C, xin, xout, name, final_wait=False):
  if not hasattr(C, "ffn_sems"):
      C.ffn_sems = mk_sems(nc, C.st, "ffnS", list(_FFN_ADV))
      C.ffn_calls = 0
  s = C.ffn_sems
  FB = {q: C.ffn_calls * _FFN_ADV[q] for q in _FFN_ADV}
  C.ffn_calls += 1
  with contextlib.ExitStack() as _es:
    C.lnfm = _es.enter_context(nc.sbuf_tensor([128, DTL * 512], BF16))
    C.wbuf = _es.enter_context(nc.sbuf_tensor([128, 2 * 4096], BF16))
    C.wbig = _es.enter_context(nc.sbuf_tensor([128, 2 * 8192], BF16))
    C.w2buf = _es.enter_context(nc.sbuf_tensor([128, 2 * 2048], BF16))
    C.hbuf = _es.enter_context(nc.sbuf_tensor([128, ITL * 512], BF16))
    C.xlat = _es.enter_context(nc.sbuf_tensor([128, LTL * 512], BF16))
    C.moe = _es.enter_context(nc.sbuf_tensor([128, LTL * 512], BF16))
    C.upst = _es.enter_context(nc.sbuf_tensor([128, 2 * 512], BF16))
    C.f1st = _es.enter_context(nc.sbuf_tensor([128, 2 * 512], BF16))
    C.exst = _es.enter_context(nc.sbuf_tensor([128, 2 * 512], BF16))
    C.dnst = _es.enter_context(nc.sbuf_tensor([128, 2 * 512], BF16))
    C.scr = _es.enter_context(nc.sbuf_tensor([128, 512], F32))
    C.lgf = _es.enter_context(nc.sbuf_tensor([128, 512], F32))
    C.ltm = _es.enter_context(nc.sbuf_tensor([128, 32], F32))
    C.prs = _es.enter_context(nc.sbuf_tensor([128, 28], F32))
    C.rwc = _es.enter_context(nc.sbuf_tensor([128, 4], F32))
    C.rwrow = _es.enter_context(nc.sbuf_tensor([1, 512], F32))
    C.rwbc = _es.enter_context(nc.sbuf_tensor([128, 512], F32))
    C.ps = _es.enter_context(nc.psum_tensor([128, 2 * 512], F32))
    C.psg = _es.enter_context(nc.psum_tensor([128, 512], F32))
    C.pst = _es.enter_context(nc.psum_tensor([128, 512], F32))
    C.pst2 = _es.enter_context(nc.psum_tensor([128, 512], F32))
    with nc.Block(name + "_c") as blk:
        @blk.sync
        def _(sy):
            for q in (s.q_upm, s.q_f1m, s.q_exm, s.q_dnm):
                sy.sem_inc(q, 2)
            for q in (s.q_upe, s.q_f1e, s.q_exe, s.q_dne):
                sy.sem_inc(q, 2)
            for q in (s.q_ups, s.q_f1s, s.q_exs, s.q_dns):
                sy.sem_inc(q, 32)
        @blk.gpsimd
        def _(g):
            g.memset(C.lgf[:, :], 0.0)

    with nc.Block(name) as blk:
        @blk.sync
        def _(sy):
            for t in range(2):
                sy.wait_ge(s.q_dnm, FB["q_dnm"] + t * 32 + 2)
                for dt in range(DTL):
                    sy.dma_start_transpose(
                        C.lnfm[:, dt * 512:(dt + 1) * 512],
                        C.sp_lnt[t * 512:(t + 1) * 512, dt * 128:(dt + 1) * 128]).then_inc(s.q_lnf, 16)
                for k in range(64):
                    b = k % 2
                    sy.wait_ge(s.q_upe, FB["q_upe"] + t * 64 + k + 3)
                    sy.dma_start(out=C.sp_h[:, k * 512:(k + 1) * 512],
                                 in_=C.upst[:, b * 512:(b + 1) * 512]).then_inc(s.q_ups, 16)
                sy.wait_ge(s.q_ups, FB["q_ups"] + 32 + (t * 64 + 64) * 16)
                sy.dma_start(out=C.hbuf[:, :], in_=C.sp_h[:, :]).then_inc(s.q_hl, 16)
                for k in range(16):
                    b = k % 2
                    sy.wait_ge(s.q_f1e, FB["q_f1e"] + t * 16 + k + 3)
                    sy.dma_start(out=C.sp_lat[:, k * 512:(k + 1) * 512],
                                 in_=C.f1st[:, b * 512:(b + 1) * 512]).then_inc(s.q_f1s, 16)
                sy.wait_ge(s.q_f1s, FB["q_f1s"] + 32 + (t * 16 + 16) * 16)
                sy.dma_start(out=C.xlat[:, :], in_=C.sp_lat[:, :]).then_inc(s.q_ll, 16)
                for k in range(16):
                    b = k % 2
                    sy.wait_ge(s.q_exe, FB["q_exe"] + t * 16 + k + 3)
                    sy.dma_start(out=C.sp_moe[:, k * 512:(k + 1) * 512],
                                 in_=C.exst[:, b * 512:(b + 1) * 512]).then_inc(s.q_exs, 16)
                sy.wait_ge(s.q_exs, FB["q_exs"] + 32 + (t * 16 + 16) * 16)
                sy.dma_start(out=C.moe[:, :], in_=C.sp_moe[:, :]).then_inc(s.q_ml, 16)
                for k in range(32):
                    b = k % 2
                    sy.wait_ge(s.q_dne, FB["q_dne"] + t * 32 + k + 3)
                    sy.dma_start(out=C.sp_os[k * 128:(k + 1) * 128, t * 512:(t + 1) * 512],
                                 in_=C.dnst[:, b * 512:(b + 1) * 512]).then_inc(s.q_dns, 16)
            sy.wait_ge(s.q_dns, FB["q_dns"] + 32 + 16 * 64)

        @blk.gpsimd
        def _(g):
            for t in range(2):
                g.wait_ge(s.q_exm, FB["q_exm"] + 2 + 16 * t)
                for kk in range(64):
                    b = kk % 2
                    g.wait_ge(s.q_upm, FB["q_upm"] + 1 + 64 * t + kk)
                    g.dma_start(out=C.wbuf[:, b * 4096:(b + 1) * 4096],
                                in_=C.wup[kk, :, :]).then_inc(s.q_upw, 16)
                g.wait_ge(s.q_upm, FB["q_upm"] + 66 + 64 * t)
                for kk in range(16):
                    b = kk % 2
                    g.wait_ge(s.q_f1m, FB["q_f1m"] + 1 + 16 * t + kk)
                    g.dma_start(out=C.wbuf[:, b * 4096:(b + 1) * 4096],
                                in_=C.wf1[kk, :, :]).then_inc(s.q_f1w, 16)
                g.wait_ge(s.q_f1m, FB["q_f1m"] + 18 + 16 * t)
                for kk in range(16):
                    b = kk % 2
                    g.wait_ge(s.q_exm, FB["q_exm"] + 1 + 16 * t + kk)
                    g.dma_start(out=C.wbuf[:, b * 4096:b * 4096 + 2048],
                                in_=C.wex[kk, :, :]).then_inc(s.q_exw, 16)
                for kk in range(32):
                    b = kk % 2
                    g.wait_ge(s.q_dnm, FB["q_dnm"] + 1 + 32 * t + kk)
                    g.dma_start(out=C.wbig[:, b * 8192:(b + 1) * 8192],
                                in_=C.wdn[kk, :, :]).then_inc(s.q_dnw, 16)
                    g.dma_start(out=C.w2buf[:, b * 2048:(b + 1) * 2048],
                                in_=C.wf2[kk, :, :]).then_inc(s.q_dnw2, 16)

        @blk.tensor
        def _(tn):
          with contextlib.ExitStack() as es:
            t_lnf = WR(tn, es, name + "_t_lnf", 512 + FB["q_lnf"], 512)
            t_upw = WR(tn, es, name + "_t_upw", 16 + FB["q_upw"], 16)
            t_upe = WR(tn, es, name + "_t_upe", 1 + FB["q_upe"], 1)
            t_lge = WR(tn, es, name + "_t_lge", 1 + FB["q_lge"], 1)
            t_f1w = WR(tn, es, name + "_t_f1w", 16 + FB["q_f1w"], 16)
            t_f1e = WR(tn, es, name + "_t_f1e", 1 + FB["q_f1e"], 1)
            t_rwc = WR(tn, es, name + "_t_rwc", 1 + FB["q_rwc"], 1)
            t_rwr = WR(tn, es, name + "_t_rwr", 2 + FB["q_rwr"], 3)
            t_exw = WR(tn, es, name + "_t_exw", 16 + FB["q_exw"], 16)
            t_exe = WR(tn, es, name + "_t_exe", 1 + FB["q_exe"], 1)
            t_ll = WR(tn, es, name + "_t_ll", 16 + FB["q_ll"], 16)
            t_dnw = WR(tn, es, name + "_t_dnw", 16 + FB["q_dnw"], 16)
            t_dnw2 = WR(tn, es, name + "_t_dnw2", 16 + FB["q_dnw2"], 16)
            t_dne = WR(tn, es, name + "_t_dne", 1 + FB["q_dne"], 1)
            t_hl = WR(tn, es, name + "_t_hl", 16 + FB["q_hl"], 16)
            t_ml = WR(tn, es, name + "_t_ml", 16 + FB["q_ml"], 16)
            t_g1 = WR(tn, es, name + "_t_g1", 66 + FB["q_upe"], 64)
            t_g2 = WR(tn, es, name + "_t_g2", 18 + FB["q_f1e"], 16)
            t_g3 = WR(tn, es, name + "_t_g3", 18 + FB["q_exe"], 16)
            with tn.Fori(0, 2) as t:
                t_lnf.wait(s.q_lnf)
                for dt in range(DTL):
                    ins = tn.matmul(C.psg[0:8, 0:512], C.wg[:, dt * 8:(dt + 1) * 8],
                                    C.lnfm[:, dt * 512:(dt + 1) * 512],
                                    start=(dt == 0), stop=(dt == DTL - 1))
                ins.then_inc(s.q_lg, 1)
                with tn.Fori(0, 32) as j:
                    for b in range(2):
                        t_upw.wait(s.q_upw)
                        t_upe.wait(s.q_upe)
                        for dt in range(DTL):
                            ins = tn.matmul(C.ps[:, b * 512:(b + 1) * 512],
                                            C.wbuf[:, b * 4096 + dt * 128:b * 4096 + (dt + 1) * 128],
                                            C.lnfm[:, dt * 512:(dt + 1) * 512],
                                            start=(dt == 0), stop=(dt == DTL - 1))
                        ins.then_inc(s.q_upm, 1)
                t_lge.wait(s.q_lge)
                for sb in range(4):
                    ins = tn.transpose(C.pst[:, sb * 128:(sb + 1) * 128],
                                       C.lgf[:, sb * 128:(sb + 1) * 128], C.id32[:, :])
                ins.then_inc(s.q_lt, 1)
                t_g1.wait(s.q_upe)
                with tn.Fori(0, 8) as j:
                    for b in range(2):
                        t_f1w.wait(s.q_f1w)
                        t_f1e.wait(s.q_f1e)
                        for dt in range(DTL):
                            ins = tn.matmul(C.ps[:, b * 512:(b + 1) * 512],
                                            C.wbuf[:, b * 4096 + dt * 128:b * 4096 + (dt + 1) * 128],
                                            C.lnfm[:, dt * 512:(dt + 1) * 512],
                                            start=(dt == 0), stop=(dt == DTL - 1))
                        ins.then_inc(s.q_f1m, 1)
                t_rwc.wait(s.q_rwc)
                for sb in range(4):
                    ins = tn.matmul(C.pst2[0:1, sb * 128:(sb + 1) * 128], C.rwc[:, sb:sb + 1],
                                    C.id32[:, 0:128], start=True, stop=True)
                ins.then_inc(s.q_rwr, 1)
                t_rwr.wait(s.q_rwr)
                for sb in range(4):
                    ins = tn.matmul(C.psg[0:128, sb * 128:(sb + 1) * 128], C.ones32[0:1, 0:128],
                                    C.rwrow[0:1, sb * 128:(sb + 1) * 128], start=True, stop=True)
                ins.then_inc(s.q_rwr, 1)
                t_g2.wait(s.q_f1e)
                t_ll.wait(s.q_ll)
                with tn.Fori(0, 8) as j:
                    for b in range(2):
                        t_exw.wait(s.q_exw)
                        t_exe.wait(s.q_exe)
                        for lt in range(LTL):
                            ins = tn.matmul(C.ps[:, b * 512:(b + 1) * 512],
                                            C.wbuf[:, b * 4096 + lt * 128:b * 4096 + (lt + 1) * 128],
                                            C.xlat[:, lt * 512:(lt + 1) * 512],
                                            start=(lt == 0), stop=(lt == LTL - 1))
                        ins.then_inc(s.q_exm, 1)
                t_g3.wait(s.q_exe)
                t_hl.wait(s.q_hl)
                t_ml.wait(s.q_ml)
                with tn.Fori(0, 16) as j:
                    for b in range(2):
                        t_dnw.wait(s.q_dnw)
                        t_dnw2.wait(s.q_dnw2)
                        t_dne.wait(s.q_dne)
                        for it in range(ITL):
                            tn.matmul(C.ps[:, b * 512:(b + 1) * 512],
                                      C.wbig[:, b * 8192 + it * 128:b * 8192 + (it + 1) * 128],
                                      C.hbuf[:, it * 512:(it + 1) * 512],
                                      start=(it == 0), stop=False)
                        for lt in range(LTL):
                            ins = tn.matmul(C.ps[:, b * 512:(b + 1) * 512],
                                            C.w2buf[:, b * 2048 + lt * 128:b * 2048 + (lt + 1) * 128],
                                            C.moe[:, lt * 512:(lt + 1) * 512],
                                            start=False, stop=(lt == LTL - 1))
                        ins.then_inc(s.q_dnm, 1)

        @blk.vector
        def _(v):
            nvs = 0
            def lk(ins):
                nonlocal nvs
                ins.then_inc(s.q_vs, 1)
                nvs += 1
                v.wait_ge(s.q_vs, FB["q_vs"] + nvs)
            for t in range(2):
                v.wait_ge(s.q_lg, FB["q_lg"] + t + 1)
                v.tensor_copy(C.lgf[0:8, :], C.psg[0:8, 0:512]).then_inc(s.q_lge, 1)
                for k in range(64):
                    b = k % 2
                    kk = t * 64 + k
                    v.wait_ge(s.q_upm, FB["q_upm"] + 3 + kk)
                    v.wait_ge(s.q_ups, FB["q_ups"] + 16 * (kk + 1))
                    lk(v.tensor_relu(C.scr[:, 0:512], C.ps[:, b * 512:(b + 1) * 512]))
                    v.tensor_mul(C.upst[:, b * 512:(b + 1) * 512],
                                 C.scr[:, 0:512], C.scr[:, 0:512]).then_inc(s.q_upe, 1)
                v.wait_ge(s.q_lt, FB["q_lt"] + t + 1)
                for sb in range(4):
                    lk(v.tensor_copy(C.ltm[:, sb * 8:(sb + 1) * 8],
                                     C.pst[:, sb * 128:sb * 128 + 8]))
                for sb in range(4):
                    for ci, (i, j2) in enumerate(PAIRS):
                        lk(v.tensor_add(C.prs[:, ci:ci + 1], C.ltm[:, sb * 8 + i:sb * 8 + i + 1],
                                        C.ltm[:, sb * 8 + j2:sb * 8 + j2 + 1]))
                    ins = v.reduce_max(C.rwc[:, sb:sb + 1], C.prs[:, 0:28], axis=AX)
                ins.then_inc(s.q_rwc, 1)
                for k in range(16):
                    b = k % 2
                    kk = t * 16 + k
                    v.wait_ge(s.q_f1m, FB["q_f1m"] + 3 + kk)
                    v.wait_ge(s.q_f1s, FB["q_f1s"] + 16 * (kk + 1))
                    v.tensor_copy(C.f1st[:, b * 512:(b + 1) * 512],
                                  C.ps[:, b * 512:(b + 1) * 512]).then_inc(s.q_f1e, 1)
                v.wait_ge(s.q_rwr, FB["q_rwr"] + 1 + 3 * t)
                v.tensor_copy(C.rwrow[0:1, :], C.pst2[0:1, 0:512]).then_inc(s.q_rwr, 1)
                v.wait_ge(s.q_rwr, FB["q_rwr"] + 3 + 3 * t)
                v.tensor_copy(C.rwbc[:, :], C.psg[0:128, 0:512]).then_inc(s.q_rwb, 1)
                for k in range(16):
                    b = k % 2
                    kk = t * 16 + k
                    v.wait_ge(s.q_exm, FB["q_exm"] + 3 + kk)
                    v.wait_ge(s.q_exs, FB["q_exs"] + 16 * (kk + 1))
                    v.tensor_mul(C.exst[:, b * 512:(b + 1) * 512],
                                 C.ps[:, b * 512:(b + 1) * 512], C.rwbc[:, 0:512]).then_inc(s.q_exe, 1)
                for k in range(32):
                    b = k % 2
                    kk = t * 32 + k
                    v.wait_ge(s.q_dnm, FB["q_dnm"] + 3 + kk)
                    v.wait_ge(s.q_dns, FB["q_dns"] + 16 * (kk + 1))
                    v.tensor_copy(C.dnst[:, b * 512:(b + 1) * 512],
                                  C.ps[:, b * 512:(b + 1) * 512]).then_inc(s.q_dne, 1)


  with contextlib.ExitStack() as _es:
    C.otm = _es.enter_context(nc.sbuf_tensor([128, 2 * DM], BF16))
    C.xa = _es.enter_context(nc.sbuf_tensor([128, 2 * DM], F32))
    C.lnw = _es.enter_context(nc.sbuf_tensor([128, 2 * DM], F32))
    C.sq = _es.enter_context(nc.sbuf_tensor([128, DM], F32))
    C.sq2 = _es.enter_context(nc.sbuf_tensor([128, DM], F32))
    C.gbc = _es.enter_context(nc.sbuf_tensor([128, DM], F32))
    C.bbc = _es.enter_context(nc.sbuf_tensor([128, DM], F32))
    with nc.Block(name + "_eload") as blk:
        @blk.sync
        def _(sy):
            sy.dma_start(out=C.gbc[:, :], in_=C.gbc_in[:, :]).then_inc(s.q_gb, 16)
            sy.dma_start(out=C.bbc[:, :], in_=C.bbc_in[:, :]).then_inc(s.q_gb, 16)
            sy.wait_ge(s.q_gb, FB["q_gb"] + 32)
    with nc.Block(name + "_ec") as blk:
        @blk.sync
        def _(sy):
            sy.sem_inc(s.q_d, 2)
            sy.sem_inc(s.q_epo, 32)

    with nc.Block(name + "_e") as blk:
        @blk.sync
        def _(sy):
            for k in range(8):
                b = k % 2
                sy.wait_ge(s.q_d, FB["q_d"] + k + 1)
                sy.dma_start_transpose(
                    C.otm[:, b * DM:(b + 1) * DM],
                    C.sp_os[0:DM, k * 128:(k + 1) * 128]).then_inc(s.q_ep, 16)
                sy.dma_start(out=C.xa[:, b * DM:(b + 1) * DM],
                             in_=xin[k * 128:(k + 1) * 128, :]).then_inc(s.q_ep, 16)
                sy.wait_ge(s.q_d, FB["q_d"] + k + 3)
                sy.dma_start(out=xout[k * 128:(k + 1) * 128, :],
                             in_=C.lnw[:, b * DM:(b + 1) * DM]).then_inc(s.q_epo, 16)
            sy.wait_ge(s.q_epo, FB["q_epo"] + 32 + 16 * 8)

        @blk.vector
        def _(v):
          with contextlib.ExitStack() as es:
            ve = WR(v, es, name + "_ve2", 1 + FB["q_vs2"], 1)
            v_ep = WR(v, es, name + "_v_ep", 32 + FB["q_ep"], 32)
            v_a = WR(v, es, name + "_v_a2", 1 + FB["q_a"], 1)
            v_c = WR(v, es, name + "_v_c2", 1 + FB["q_c"], 1)
            v_eo = WR(v, es, name + "_v_eo", 16 + FB["q_epo"], 16)
            with v.Fori(0, 4) as tt:
                for b in range(2):
                    o = C.otm[:, b * DM:(b + 1) * DM]
                    r = C.red[:, b * 8:(b + 1) * 8]
                    v_ep.wait(s.q_ep)
                    ve.link(v.reduce_sum(r[:, 0:1], o, axis=AX), s.q_vs2)
                    v_a.wait(s.q_a)
                    ve.link(v.tensor_scalar_mul(r[:, 2:3], r[:, 0:1], 1.0 / DM), s.q_vs2)
                    ve.link(v.tensor_mul(r[:, 3:4], r[:, 2:3], r[:, 2:3]), s.q_vs2)
                    v.scalar_tensor_tensor(r[:, 4:5], r[:, 1:2], 1.0 / DM, r[:, 3:4],
                                           ALU.mult, ALU.subtract).then_inc(s.q_b, 1)
                    v_c.wait(s.q_c)
                    ve.link(v.tensor_scalar(C.sq[:, :], o, r[:, 2:3], r[:, 5:6],
                                            ALU.subtract, ALU.mult), s.q_vs2)
                    ve.link(v.tensor_mul(C.sq[:, :], C.sq[:, :], C.gbc[:, :]), s.q_vs2)
                    ve.link(v.tensor_add(C.sq[:, :], C.sq[:, :], C.bbc[:, :]), s.q_vs2)
                    v_eo.wait(s.q_epo)
                    v.scalar_tensor_tensor(C.lnw[:, b * DM:(b + 1) * DM], C.sq[:, :], 0.5,
                                           C.xa[:, b * DM:(b + 1) * DM],
                                           ALU.mult, ALU.add).then_inc(s.q_d, 1)

        @blk.scalar
        def _(sc):
          with contextlib.ExitStack() as es:
            se = WR(sc, es, name + "_se2", 1 + FB["q_ss2"], 1)
            s_ep = WR(sc, es, name + "_s_ep", 32 + FB["q_ep"], 32)
            s_b = WR(sc, es, name + "_s_b2", 1 + FB["q_b"], 1)
            with sc.Fori(0, 4) as tt:
                for b in range(2):
                    o = C.otm[:, b * DM:(b + 1) * DM]
                    r = C.red[:, b * 8:(b + 1) * 8]
                    s_ep.wait(s.q_ep)
                    sc.activation(C.sq2[:, :], o, AF.Square, accum_out=r[:, 1:2]).then_inc(s.q_a, 1)
                    s_b.wait(s.q_b)
                    se.link(sc.activation(r[:, 6:7], r[:, 4:5], AF.Ln, bias=C.epsc[:, 0:1]), s.q_ss2)
                    sc.activation(r[:, 5:6], r[:, 6:7], AF.Exp, scale=-0.5).then_inc(s.q_c, 1)

    if final_wait:
        with nc.Block(name + "_fw") as blk:
            @blk.sync
            def _(sy):
                sy.wait_ge(s.q_epo, FB["q_epo"] + 32 + 16 * 8)


def build():
    nc = bass.Bass("TRN2", target_bir_lowering=False, detect_race_conditions=False)
    C = _C()

    x_in = nc.dram_tensor("x", [NT, DM], F32, kind="ExternalInput").ap()
    out = nc.dram_tensor("out", [NT, DM], F32, kind="ExternalOutput").ap()
    C.wup = nc.dram_tensor("wup", [ITL, 128, DM], BF16, kind="ExternalInput").ap()
    C.wdn = nc.dram_tensor("wdn", [DTL, 128, FI], BF16, kind="ExternalInput").ap()
    C.wf1 = nc.dram_tensor("wf1", [LTL, 128, DM], BF16, kind="ExternalInput").ap()
    C.wex = nc.dram_tensor("wex", [LTL, 128, LAT], BF16, kind="ExternalInput").ap()
    C.wf2 = nc.dram_tensor("wf2", [DTL, 128, LAT], BF16, kind="ExternalInput").ap()
    wg_in = nc.dram_tensor("wgt", [128, DTL * 8], F32, kind="ExternalInput").ap()
    C.wip = nc.dram_tensor("wip", [145, 128, DM], BF16, kind="ExternalInput").ap()
    C.wout = nc.dram_tensor("wout", [DTL, 128, FI], BF16, kind="ExternalInput").ap()
    cw_in = nc.dram_tensor("cw", [128, NCV * 4], F32, kind="ExternalInput").ap()
    C.cwt = nc.dram_tensor("cwt", [NCV, 128, 4], F32, kind="ExternalInput").ap()
    C.cbt = nc.dram_tensor("cbt", [NCV, 128, 1], F32, kind="ExternalInput").ap()
    onescol_in = nc.dram_tensor("onescol", [128, 1], F32, kind="ExternalInput").ap()
    cb_in = nc.dram_tensor("cb", [128, NCV], F32, kind="ExternalInput").ap()
    dtb_in = nc.dram_tensor("dtb", [128, 1], F32, kind="ExternalInput").ap()
    alog_in = nc.dram_tensor("alog", [128, 1], F32, kind="ExternalInput").ap()
    dbc_in = nc.dram_tensor("dbc", [128, 128], F32, kind="ExternalInput").ap()
    C.gbc_in = nc.dram_tensor("gbc", [128, DM], F32, kind="ExternalInput").ap()
    C.bbc_in = nc.dram_tensor("bbc", [128, DM], F32, kind="ExternalInput").ap()
    C.nwt_in = nc.dram_tensor("nwt", [128, ITL], F32, kind="ExternalInput").ap()
    id16_in = nc.dram_tensor("id16", [128, 128], BF16, kind="ExternalInput").ap()
    id32_in = nc.dram_tensor("id32", [128, 128], F32, kind="ExternalInput").ap()
    ones32_in = nc.dram_tensor("ones32", [1, 128], F32, kind="ExternalInput").ap()
    strict_in = nc.dram_tensor("strictm", [128, 128], BF16, kind="ExternalInput").ap()
    negs_in = nc.dram_tensor("negstrict", [128, 128], BF16, kind="ExternalInput").ap()
    qtril_in = nc.dram_tensor("qtril", [128, 128], BF16, kind="ExternalInput").ap()
    mask_in = nc.dram_tensor("pmask", [128, 1], F32, kind="ExternalInput").ap()
    nonce_in = nc.dram_tensor("nonce", [1, 1], I32, kind="ExternalInput").ap()

    C.sp_lnt = nc.dram_tensor("sp_lnt", [NT, DM], BF16).ap()
    C.sp_x1 = nc.dram_tensor("sp_x1", [NT, DM], F32).ap()
    C.sp_x2 = nc.dram_tensor("sp_x2", [NT, DM], F32).ap()
    C.sp_h = nc.dram_tensor("sp_h", [128, ITL * 512], BF16).ap()
    C.sp_lat = nc.dram_tensor("sp_lat", [128, LTL * 512], BF16).ap()
    C.sp_moe = nc.dram_tensor("sp_moe", [128, LTL * 512], BF16).ap()
    C.sp_os = nc.dram_tensor("sp_os", [DM, NT], BF16).ap()
    C.sp_gate = nc.dram_tensor("sp_gate", [FI, NT], BF16).ap()
    C.sp_hsf = nc.dram_tensor("sp_hsf", [FI, NT], BF16).ap()
    C.sp_dtraw = nc.dram_tensor("sp_dtraw", [128, NT], F32).ap()
    C.sp_X = nc.dram_tensor("sp_X", [NZ, 128, FI], BF16).ap()
    C.sp_bc = nc.dram_tensor("sp_bc", [128, 16 * NT], F32).ap()
    C.sp_pack = nc.dram_tensor("sp_pack", [NZ, 128, 2048], BF16).ap()
    C.sp_dtv = nc.dram_tensor("sp_dtv", [128, NT], F32).ap()
    C.sp_pc3 = nc.dram_tensor("sp_pc3", [128, NCV * 3], F32).ap()
    C.sp_hout = nc.dram_tensor("sp_hout", [128, NCV * 3], F32).ap()
    C.sp_outg = nc.dram_tensor("sp_outg", [NT, FI], BF16).ap()
    C.sp_attn = nc.dram_tensor("sp_attn", [DM, NT], BF16).ap()
    C.shx = nc.dram_tensor("shx", [2 * 128, 8448], F32, kind="Internal", addr_space="Shared").ap()
    C.shf = nc.dram_tensor("shf", [1, 8], I32, kind="Internal", addr_space="Shared").ap()

    st = contextlib.ExitStack()
    C.st = st
    with st:
        C.q_pro = st.enter_context(nc.semaphore("q_pro"))
        C.q_prog = st.enter_context(nc.semaphore("q_prog"))

        C.wg = st.enter_context(nc.sbuf_tensor([128, DTL * 8], BF16))
        C.cw = st.enter_context(nc.sbuf_tensor([128, NCV * 4], F32))
        C.cb = st.enter_context(nc.sbuf_tensor([128, NCV], F32))
        C.cbn = st.enter_context(nc.sbuf_tensor([128, NCV], F32))
        C.dtb = st.enter_context(nc.sbuf_tensor([128, 1], F32))
        C.aneg = st.enter_context(nc.sbuf_tensor([128, 1], F32))
        C.dbc = st.enter_context(nc.sbuf_tensor([128, 128], F32))
        C.id16 = st.enter_context(nc.sbuf_tensor([128, 128], BF16))
        C.id32 = st.enter_context(nc.sbuf_tensor([128, 128], F32))
        C.ones32 = st.enter_context(nc.sbuf_tensor([1, 128], F32))
        C.onescol = st.enter_context(nc.sbuf_tensor([128, 1], F32))
        C.strict = st.enter_context(nc.sbuf_tensor([128, 128], BF16))
        C.negstrict = st.enter_context(nc.sbuf_tensor([128, 128], BF16))
        C.qtril = st.enter_context(nc.sbuf_tensor([128, 128], BF16))
        C.pmask = st.enter_context(nc.sbuf_tensor([128, 1], F32))
        C.noncesb = st.enter_context(nc.sbuf_tensor([1, 1], I32))
        C.flsb = st.enter_context(nc.sbuf_tensor([1, 1], I32))

        C.red = st.enter_context(nc.sbuf_tensor([128, 16], F32))
        C.epsc = st.enter_context(nc.sbuf_tensor([128, 1], F32))

        with nc.Block("prol") as blk:
            @blk.sync
            def _(sy):
                for dst, srcap in ((C.cw, cw_in), (C.cb, cb_in), (C.dtb, dtb_in),
                                   (C.aneg, alog_in), (C.dbc, dbc_in), (C.id16, id16_in),
                                   (C.id32, id32_in), (C.ones32, ones32_in), (C.onescol, onescol_in), (C.strict, strict_in),
                                   (C.qtril, qtril_in), (C.pmask, mask_in), (C.noncesb, nonce_in),
                                   (C.negstrict, negs_in)):
                    sy.dma_start(out=dst[:, :], in_=srcap[:, :]).then_inc(C.q_pro, 16)
                sy.wait_ge(C.q_pro, 14 * 16)
            @blk.gpsimd
            def _(g):
                g.memset(C.epsc[:, :], EPS)
                g.dma_start(out=C.wg[:, :], in_=wg_in[:, :]).then_inc(C.q_prog, 16)
                g.wait_ge(C.q_prog, 16)
                g.wait_ge(C.q_pro, 14 * 16)
                C.gnv = g.value_load(C.noncesb[0:1, 0:1])
                pid = g.partition_id()
                par = pid % 2
                oth = 1 - par
                C.e_par128 = par * 128
                C.e_par2 = par * 2
                C.e_par2p1 = par * 2 + 1
                C.e_oth128 = oth * 128
                C.e_oth2 = oth * 2
                C.e_oth2p1 = oth * 2 + 1
        with nc.Block("prol2") as blk:
            @blk.scalar
            def _(sc):
                sc.activation(C.aneg[:, :], C.aneg[:, :], AF.Exp)
            @blk.vector
            def _(v):
                v.tensor_scalar_mul(C.cbn[:, :], C.cb[:, :], -1.0)
        with nc.Block("prol3") as blk:
            @blk.vector
            def _(v):
                v.tensor_scalar_mul(C.aneg[:, :], C.aneg[:, :], -1.0)

        C.outap = out
        emit_ln(nc, C, x_in, "ln1")
        emit_ffn(nc, C, x_in, C.sp_x1 if PHASES > 1 else out, "ffn1",
                 final_wait=(PHASES == 1))

        def _recycle(fn, *args, **kw):
            snap = nc.snapshot_sems()
            old = C.st
            C.st = None
            fn(*args, **kw)
            C.st = old
            nc.clear_and_free_semaphores(nc.allocated_since(snap))
            nc.all_engine_barrier()

        if PHASES >= 2:
            emit_ln(nc, C, C.sp_x1, "ln2")
            _recycle(emit_inproj, nc, C)
        if PHASES >= 3:
            _recycle(emit_xtrans, nc, C)
        if PHASES >= 4:
            _recycle(emit_prep, nc, C)
        if PHASES >= 5:
            with contextlib.ExitStack() as sscope:
                C.prevT = sscope.enter_context(nc.sbuf_tensor([128, FI], F32))
                _recycle(emit_ssdA, nc, C)
                if PHASES >= 6:
                    _recycle(emit_exch2, nc, C)
                if PHASES >= 7:
                    _recycle(emit_ssdB, nc, C)
                if PHASES == 5 or PHASES == 6:
                    _dump_prev(nc, C, out)
        if PHASES >= 8:
            _recycle(emit_outproj, nc, C)
        if PHASES >= 9:
            emit_ln(nc, C, C.sp_x2, "ln3")
            emit_ffn(nc, C, C.sp_x2, out, "ffn2", final_wait=True)
        if PHASES < 9:
            _dump_dev(nc, C, out)

    return nc


def _dump_prev(nc, C, out):
    qq = C.st.enter_context(nc.semaphore("q_dprev"))
    with nc.Block("dprev") as blk:
        @blk.gpsimd
        def _(g):
            g.dma_start(out=C.sp_outg[0:128, 0:FI], in_=C.prevT[:, :]).then_inc(qq, 16)
            g.wait_ge(qq, 16)


def _dump_dev(nc, C, out):
    qq = C.st.enter_context(nc.semaphore("q_dump"))
    with nc.Block("dump") as blk:
        @blk.gpsimd
        def _(g):
            n = 0
            if PHASES in (2, 3):
                g.dma_start(out=out[0:1024, 0:1024], in_=C.sp_hsf[0:1024, :]).then_inc(qq, 16)
                g.dma_start(out=out[0:1024, 1024:2048], in_=C.sp_gate[0:1024, :]).then_inc(qq, 16)
                g.dma_start(out=out[0:128, 2048:3072], in_=C.sp_dtraw[:, :]).then_inc(qq, 16)
                g.dma_start(out=out[0:128, 3072:4096], in_=C.sp_bc[:, 0:1024]).then_inc(qq, 16)
                n = 4
                if PHASES == 3:
                    for z in range(NZ):
                        g.dma_start(out=out[128 + z * 64:128 + (z + 1) * 64, 2048:4096],
                                    in_=C.sp_X[z, 64:128, 0:2048]).then_inc(qq, 16)
                    n += 8
            elif PHASES == 4:
                g.dma_start(out=out[0:128, 0:2048], in_=C.sp_pack[0, :, :]).then_inc(qq, 16)
                g.dma_start(out=out[0:128, 2048:3072], in_=C.sp_dtv[:, :]).then_inc(qq, 16)
                g.dma_start(out=out[128:256, 0:4096], in_=C.sp_X[0, :, 0:4096]).then_inc(qq, 16)
                g.dma_start(out=out[256:384, 0:2048], in_=C.sp_pack[5, :, :]).then_inc(qq, 16)
                n = 4
            elif PHASES in (5, 6):
                g.dma_start(out=out[0:128, 0:4096], in_=C.sp_outg[0:128, 0:4096]).then_inc(qq, 16)
                g.dma_start(out=out[128:256, 0:4096], in_=C.sp_outg[0:128, 4096:8192]).then_inc(qq, 16)
                n = 2
            elif PHASES == 7:
                if DBG_SSD:
                    g.dma_start(out=out[512:1024, 0:4096],
                                in_=C.sp_outg[512:1024, 0:4096]).then_inc(qq, 16)
                else:
                    g.dma_start(out=out[0:1024, 0:4096], in_=C.sp_outg[:, 0:4096]).then_inc(qq, 16)
                n = 1
            elif PHASES == 8:
                g.dma_start(out=out[:, :], in_=C.sp_x2[:, :]).then_inc(qq, 16)
                n = 1
            g.wait_ge(qq, n * 16)




def emit_inproj(nc, C, name="ip"):
  s = mk_sems(nc, C.st, name, ["q_lnf", "q_w", "q_m", "q_e", "q_ge", "q_gs",
                               "q_cv", "q_cwl", "q_cbn", "q_cv2", "q_cv3", "q_hs",
                               "q_hss", "q_pcs", "q_bce", "q_bcs", "q_dte", "q_dts",
                               "q_vs"])
  with contextlib.ExitStack() as _es:
    lnip = _es.enter_context(nc.sbuf_tensor([128, DTL * 1024], BF16))
    wipb = _es.enter_context(nc.sbuf_tensor([128, 2 * 4096], BF16))
    esb = _es.enter_context(nc.sbuf_tensor([128, 2 * 1024], BF16))
    pre = _es.enter_context(nc.sbuf_tensor([128, 2 * 1024], F32))
    cv = _es.enter_context(nc.sbuf_tensor([128, 1021], F32))
    e1 = _es.enter_context(nc.sbuf_tensor([128, 1021], F32))
    r1 = _es.enter_context(nc.sbuf_tensor([128, 1021], F32))
    hsw = _es.enter_context(nc.sbuf_tensor([128, 2 * 1024], BF16))
    bce = _es.enter_context(nc.sbuf_tensor([128, 2 * 1024], F32))
    cwx = _es.enter_context(nc.sbuf_tensor([128, 2 * 4], F32))
    cbx = _es.enter_context(nc.sbuf_tensor([128, 2], F32))
    cbnx = _es.enter_context(nc.sbuf_tensor([128, 2], F32))
    ps = _es.enter_context(nc.psum_tensor([128, 2048], F32))

    with nc.Block(name + "_c") as blk:
        @blk.sync
        def _(sy):
            sy.sem_inc(s.q_m, 2)
            sy.sem_inc(s.q_e, 2)
            sy.sem_inc(s.q_ge, 2)
            sy.sem_inc(s.q_gs, 32)
            sy.sem_inc(s.q_bcs, 32)
            sy.sem_inc(s.q_pcs, 64)
            sy.sem_inc(s.q_cwl, 64)
            sy.sem_inc(s.q_hss, 32)
            sy.sem_inc(s.q_cv2, 2)
            for dt in range(DTL):
                sy.dma_start_transpose(
                    lnip[:, dt * 1024:(dt + 1) * 1024],
                    C.sp_lnt[0:1024, dt * 128:(dt + 1) * 128]).then_inc(s.q_lnf, 16)
            sy.wait_ge(s.q_lnf, DTL * 16)
        @blk.gpsimd
        def _(g):
            g.memset(hsw[:, 0:3], 0.0)
            g.memset(hsw[:, 1024:1027], 0.0)

    with nc.Block(name) as blk:
        @blk.gpsimd
        def _(g):
          with contextlib.ExitStack() as es:
            w_m = WR(g, es, name + "_w_m", 1, 1)
            for kk in range(145):
                b = kk % 2
                w_m.wait(s.q_m)
                g.dma_start(out=wipb[:, b * 4096:(b + 1) * 4096],
                            in_=C.wip[kk, :, :]).then_inc(s.q_w, 16)

        @blk.tensor
        def _(tn):
          with contextlib.ExitStack() as es:
            t_w = WR(tn, es, name + "_t_w", 16, 16)
            t_e = WR(tn, es, name + "_t_e", 1, 1)

            def mm_etile(b):
                t_w.wait(s.q_w)
                t_e.wait(s.q_e)
                for tch in range(2):
                    for dt in range(DTL):
                        ins = tn.matmul(
                            ps[:, b * 1024 + tch * 512:b * 1024 + (tch + 1) * 512],
                            wipb[:, b * 4096 + dt * 128:b * 4096 + (dt + 1) * 128],
                            lnip[:, dt * 1024 + tch * 512:dt * 1024 + (tch + 1) * 512],
                            start=(dt == 0), stop=(dt == DTL - 1))
                ins.then_inc(s.q_m, 1)

            with tn.Fori(0, 72) as j:
                for b in range(2):
                    mm_etile(b)
            mm_etile(0)

        @blk.vector
        def _(v):
          with contextlib.ExitStack() as es:
            v_m = WR(v, es, name + "_v_m", 3, 1)
            v_gs = WR(v, es, name + "_v_gs", 16, 16)
            v_pcs = WR(v, es, name + "_v_pcs", 32, 32)
            v_cwl = WR(v, es, name + "_v_cwl", 96, 32)
            v_cv3 = WR(v, es, name + "_v_cv3", 1, 1)
            v_hss = WR(v, es, name + "_v_hss", 16, 16)
            v_bcs = WR(v, es, name + "_v_bcs", 32, 32)
            ve = WR(v, es, name + "_ve", 1, 1)
            with v.Fori(0, 32) as j:
                for b in range(2):
                    v_m.wait(s.q_m)
                    v_gs.wait(s.q_gs)
                    v.tensor_copy(esb[:, b * 1024:(b + 1) * 1024],
                                  ps[:, b * 1024:(b + 1) * 1024]).then_inc(s.q_e, 1)
            with v.Fori(0, 32) as j:
                for b in range(2):
                    p0 = b * 1024
                    v_m.wait(s.q_m)
                    v_pcs.wait(s.q_pcs)
                    v.tensor_copy(pre[:, p0:p0 + 1024],
                                  ps[:, p0:p0 + 1024]).then_inc(s.q_e, 1)
                    v_cwl.wait(s.q_cwl)
                    ve.link(v.tensor_scalar_mul(cbnx[:, b:b + 1], cbx[:, b:b + 1], -1.0), s.q_vs)
                    v.engine_nop().then_inc(s.q_cbn, 1)
                    ve.link(v.tensor_scalar_mul(cv[:, :], pre[:, p0:p0 + 1021],
                                                cwx[:, 4 * b:4 * b + 1]), s.q_vs)
                    for tp in range(1, 4):
                        ve.link(v.scalar_tensor_tensor(cv[:, :], pre[:, p0 + tp:p0 + tp + 1021],
                                                       cwx[:, 4 * b + tp:4 * b + tp + 1],
                                                       cv[:, :], ALU.mult, ALU.add), s.q_vs)
                    v.engine_nop().then_inc(s.q_cv2, 1)
                    v_cv3.wait(s.q_cv3)
                    ve.link(v.tensor_scalar_add(r1[:, :], e1[:, :], 1.0), s.q_vs)
                    ve.link(v.reciprocal(r1[:, :], r1[:, :]), s.q_vs)
                    v_hss.wait(s.q_hss)
                    v.scalar_tensor_tensor(hsw[:, b * 1024 + 3:b * 1024 + 1024], cv[:, :],
                                           cbx[:, b:b + 1], r1[:, :],
                                           ALU.add, ALU.mult).then_inc(s.q_hs, 1)
            with v.Fori(0, 8) as j:
                for b in range(2):
                    v_m.wait(s.q_m)
                    v_bcs.wait(s.q_bcs)
                    v.tensor_copy(bce[:, b * 1024:(b + 1) * 1024],
                                  ps[:, b * 1024:(b + 1) * 1024]).then_inc(s.q_e, 1)
            v_m.wait(s.q_m)
            v.wait_ge(s.q_bcs, 32 + 16 * 32)
            v.tensor_copy(bce[:, 0:1024], ps[:, 0:1024]).then_inc(s.q_e, 1)

        @blk.scalar
        def _(sc):
          with contextlib.ExitStack() as es:
            s_cv2 = WR(sc, es, name + "_s_cv2", 3, 1)
            s_cbn = WR(sc, es, name + "_s_cbn", 1, 1)
            for kk in range(64):
                b = kk % 2
                sc.wait_ge(s.q_e, 3 + kk)
                sc.dma_start(out=C.sp_gate[kk * 128:(kk + 1) * 128, :],
                             in_=esb[:, b * 1024:(b + 1) * 1024]).then_inc(s.q_gs, 16)
            with sc.Fori(0, 32) as j:
                for b in range(2):
                    s_cv2.wait(s.q_cv2)
                    s_cbn.wait(s.q_cbn)
                    sc.activation(e1[:, :], cv[:, :], AF.Exp, scale=-1.0,
                                  bias=cbnx[:, b:b + 1]).then_inc(s.q_cv3, 1)
            for kk in range(16):
                b = kk % 2
                sc.wait_ge(s.q_e, 131 + kk)
                sc.dma_start(out=C.sp_bc[:, kk * 1024:(kk + 1) * 1024],
                             in_=bce[:, b * 1024:(b + 1) * 1024]).then_inc(s.q_bcs, 16)
                sc.dma_start(out=C.sp_hout[:, (64 + kk) * 3:(64 + kk) * 3 + 3],
                             in_=bce[:, b * 1024 + 1021:b * 1024 + 1024]).then_inc(s.q_bcs, 16)
            sc.wait_ge(s.q_e, 147)
            sc.dma_start(out=C.sp_dtraw[:, :], in_=bce[:, 0:1024]).then_inc(s.q_dts, 16)

        @blk.sync
        def _(sy):
            for kk in range(64):
                b = kk % 2
                sy.wait_ge(s.q_cv2, kk + 1)
                sy.dma_start(out=cwx[:, 4 * b:4 * (b + 1)],
                             in_=C.cwt[kk, :, :]).then_inc(s.q_cwl, 16)
                sy.dma_start(out=cbx[:, b:b + 1],
                             in_=C.cbt[kk, :, :]).then_inc(s.q_cwl, 16)
                sy.wait_ge(s.q_e, 67 + kk)
                sy.dma_start(out=C.sp_pc3[:, kk * 3:kk * 3 + 3],
                             in_=pre[:, b * 1024:b * 1024 + 3]).then_inc(s.q_pcs, 16)
                sy.dma_start(out=C.sp_hout[:, kk * 3:kk * 3 + 3],
                             in_=pre[:, b * 1024 + 1021:b * 1024 + 1024]).then_inc(s.q_pcs, 16)
                sy.wait_ge(s.q_hs, kk + 1)
                sy.dma_start(out=C.sp_hsf[kk * 128:(kk + 1) * 128, :],
                             in_=hsw[:, b * 1024:(b + 1) * 1024]).then_inc(s.q_hss, 16)
            sy.wait_ge(s.q_dts, 16)
            sy.wait_ge(s.q_hss, 32 + 64 * 16)
            sy.wait_ge(s.q_gs, 32 + 64 * 16)
            sy.wait_ge(s.q_bcs, 32 + 16 * 32)
            sy.wait_ge(s.q_pcs, 64 + 128 * 16)
            sy.wait_ge(s.q_cwl, 64 + 128 * 16)



def emit_xtrans(nc, C, name="xt"):
  s = mk_sems(nc, C.st, name, ["q_l", "q_tr", "q_te", "q_sp"])
  with contextlib.ExitStack() as _es:
    ht = _es.enter_context(nc.sbuf_tensor([128, 2 * 1024], BF16))
    trev = _es.enter_context(nc.sbuf_tensor([128, 2 * 1024], BF16))
    ptr = _es.enter_context(nc.psum_tensor([128, 2048], BF16))
    with nc.Block(name + "_c") as blk:
        @blk.sync
        def _(sy):
            sy.sem_inc(s.q_tr, 2)
            sy.sem_inc(s.q_sp, 2)
            sy.sem_inc(s.q_te, 2)
    with nc.Block(name) as blk:
        @blk.sync
        def _(sy):
            for kk in range(64):
                b = kk % 2
                sy.wait_ge(s.q_tr, kk + 1)
                sy.dma_start(out=ht[:, b * 1024:(b + 1) * 1024],
                             in_=C.sp_hsf[kk * 128:(kk + 1) * 128, :]).then_inc(s.q_l, 16)
                sy.wait_ge(s.q_te, kk + 3)
                for z in range(NZ):
                    sy.dma_start(out=C.sp_X[z, :, kk * 128:(kk + 1) * 128],
                                 in_=trev[:, b * 1024 + z * 128:b * 1024 + (z + 1) * 128]
                                 ).then_inc(s.q_sp, 16)
            sy.wait_ge(s.q_sp, 2 + 128 * 64)
        @blk.tensor
        def _(tn):
          with contextlib.ExitStack() as es:
            t_l = WR(tn, es, name + "_t_l", 16, 16)
            t_te = WR(tn, es, name + "_t_te", 1, 1)
            with tn.Fori(0, 32) as j:
                for b in range(2):
                    t_l.wait(s.q_l)
                    t_te.wait(s.q_te)
                    for z in range(NZ):
                        ins = tn.transpose(ptr[:, b * 1024 + z * 128:b * 1024 + (z + 1) * 128],
                                           ht[:, b * 1024 + z * 128:b * 1024 + (z + 1) * 128],
                                           C.id16[:, :])
                    ins.then_inc(s.q_tr, 1)
        @blk.vector
        def _(v):
          with contextlib.ExitStack() as es:
            v_tr = WR(v, es, name + "_v_tr", 3, 1)
            v_sp = WR(v, es, name + "_v_sp", 2, 128)
            with v.Fori(0, 32) as j:
                for b in range(2):
                    v_tr.wait(s.q_tr)
                    v_sp.wait(s.q_sp)
                    v.tensor_copy(trev[:, b * 1024:(b + 1) * 1024],
                                  ptr[:, b * 1024:(b + 1) * 1024]).then_inc(s.q_te, 1)


def emit_prep(nc, C, name="pp"):
  s = mk_sems(nc, C.st, name, ["q_x", "q_fl", "q_po", "q_hr", "q_cv", "q_ss",
                               "q_pk", "q_ps", "q_dt", "q_dts", "q_p3a", "q_p3b",
                               "q_p3c", "q_tr", "q_te", "q_p3x", "q_vs", "q_ssl"])
  with contextlib.ExitStack() as _es:
    hout = _es.enter_context(nc.sbuf_tensor([128, 240], F32))
    hrecv = _es.enter_context(nc.sbuf_tensor([128, 240], F32))
    bcb = _es.enter_context(nc.sbuf_tensor([128, 16 * 1024], F32))
    ext = _es.enter_context(nc.sbuf_tensor([128, 1027], F32))
    cvb = _es.enter_context(nc.sbuf_tensor([128, 1024], F32))
    eb = _es.enter_context(nc.sbuf_tensor([128, 1024], F32))
    rb = _es.enter_context(nc.sbuf_tensor([128, 1024], F32))
    bcs = _es.enter_context(nc.sbuf_tensor([128, 16 * 1024], BF16))
    dtr = _es.enter_context(nc.sbuf_tensor([128, 1024], F32))
    dtv = _es.enter_context(nc.sbuf_tensor([128, 1024], F32))
    u1 = _es.enter_context(nc.sbuf_tensor([128, 1024], F32))
    u2 = _es.enter_context(nc.sbuf_tensor([128, 1024], F32))
    pc3 = _es.enter_context(nc.sbuf_tensor([128, 192], F32))
    c3 = _es.enter_context(nc.sbuf_tensor([128, 6], F32))
    w3 = _es.enter_context(nc.sbuf_tensor([128, 6], F32))
    pad = _es.enter_context(nc.sbuf_tensor([128, 128], BF16))
    p3s = _es.enter_context(nc.sbuf_tensor([4, 2 * 128], BF16))
    ptr = _es.enter_context(nc.psum_tensor([128, 2048], BF16))

    with nc.Block(name + "_x") as blk:
        @blk.sync
        def _(sy):
            sy.dma_start(out=hout[:, :], in_=C.sp_hout[:, :]).then_inc(s.q_x, 16)
            sy.dma_start(out=pc3[:, :], in_=C.sp_pc3[:, 0:192]).then_inc(s.q_x, 16)
            sy.dma_start(out=bcb[:, :], in_=C.sp_bc[:, :]).then_inc(s.q_x, 16)
            sy.dma_start(out=dtr[:, :], in_=C.sp_dtraw[:, :]).then_inc(s.q_x, 16)
            sy.wait_ge(s.q_x, 64)
        @blk.gpsimd
        def _(g):
            g.memset(pad[:, :], 0.0)
            g.wait_ge(s.q_x, 16)
            with (g.register(name + "_nd") as nd, g.register(name + "_rexp") as rexp):
                g.dma_start(out=C.shx[bass.ds(C.e_par128, 128), 0:240],
                            in_=hout[:, :]).then_inc(s.q_fl, 16)
                g.wait_ge(s.q_fl, 16)
                g.dma_start(out=C.shf[0:1, bass.ds(C.e_par2, 1)],
                            in_=C.noncesb[0:1, 0:1]).then_inc(s.q_fl, 16)
                g.wait_ge(s.q_fl, 32)
                g.reg_mov(nd, 1)
                g.reg_mov(rexp, 32)
                if not SIM_SKIP_POLL:
                    with g.While(nd):
                        g.dma_start(out=C.flsb[0:1, 0:1],
                                    in_=C.shf[0:1, bass.ds(C.e_oth2, 1)]).then_inc(s.q_fl, 16)
                        g.reg_add(rexp, rexp, 16)
                        g.wait_ge(s.q_fl, rexp)
                        fv = g.value_load(C.flsb[0:1, 0:1])
                        g.reg_mov(nd, fv - C.gnv)
                else:
                    g.dma_start(out=C.shx[bass.ds(C.e_oth128, 128), 0:240],
                                in_=hout[:, :]).then_inc(s.q_fl, 16)
                    g.wait_ge(s.q_fl, 48)
                g.dma_start(out=hrecv[:, :],
                            in_=C.shx[bass.ds(C.e_oth128, 128), 0:240]).then_inc(s.q_po, 16)
                g.wait_ge(s.q_po, 16)

    with nc.Block(name + "_m") as blk:
        @blk.vector
        def _(v):
          with contextlib.ExitStack() as es:
            ve = WR(v, es, name + "_ve", 1, 1)
            ve.link(v.tensor_scalar_mul(hrecv[:, :], hrecv[:, :], C.pmask[:, 0:1]), s.q_vs)
            v.engine_nop().then_inc(s.q_hr, 1)
            for ct in range(16):
                cg = 64 + ct
                ve.link(v.tensor_copy(ext[:, 0:3], hrecv[:, cg * 3:cg * 3 + 3]), s.q_vs)
                ve.link(v.tensor_copy(ext[:, 3:1027], bcb[:, ct * 1024:(ct + 1) * 1024]), s.q_vs)
                ve.link(v.tensor_scalar_mul(cvb[:, :], ext[:, 0:1024],
                                            C.cw[:, cg * 4:cg * 4 + 1]), s.q_vs)
                for tp in range(1, 4):
                    ve.link(v.scalar_tensor_tensor(cvb[:, :], ext[:, tp:tp + 1024],
                                                   C.cw[:, cg * 4 + tp:cg * 4 + tp + 1],
                                                   cvb[:, :], ALU.mult, ALU.add), s.q_vs)
                v.engine_nop().then_inc(s.q_cv, 1)
                v.wait_ge(s.q_ss, ct + 1)
                ve.link(v.tensor_scalar_add(rb[:, :], eb[:, :], 1.0), s.q_vs)
                ve.link(v.reciprocal(rb[:, :], rb[:, :]), s.q_vs)
                v.scalar_tensor_tensor(bcs[:, ct * 1024:(ct + 1) * 1024], cvb[:, :],
                                       C.cb[:, cg:cg + 1], rb[:, :],
                                       ALU.add, ALU.mult).then_inc(s.q_pk, 1)
            ve.link(v.tensor_scalar_add(u1[:, :], dtr[:, :], C.dtb[:, 0:1]), s.q_vs)
            v.engine_nop().then_inc(s.q_dt, 1)
            v.wait_ge(s.q_dt, 2)
            ve.link(v.tensor_scalar_max(dtv[:, :], u1[:, :], 0.0), s.q_vs)
            ve.link(v.tensor_add(dtv[:, :], dtv[:, :], u2[:, :]), s.q_vs)
            ve.link(v.tensor_scalar_min(dtv[:, :], dtv[:, :], 100.0), s.q_vs)
            v.engine_nop().then_inc(s.q_dt, 1)
            for ct in range(64):
                ve.link(v.tensor_copy(c3[:, 0:3], hrecv[:, ct * 3:ct * 3 + 3]), s.q_vs)
                ve.link(v.tensor_copy(c3[:, 3:6], pc3[:, ct * 3:ct * 3 + 3]), s.q_vs)
                ve.link(v.tensor_scalar_mul(w3[:, 0:3], c3[:, 0:3],
                                            C.cw[:, ct * 4:ct * 4 + 1]), s.q_vs)
                for tp in range(1, 4):
                    ve.link(v.scalar_tensor_tensor(w3[:, 0:3], c3[:, tp:tp + 3],
                                                   C.cw[:, ct * 4 + tp:ct * 4 + tp + 1],
                                                   w3[:, 0:3], ALU.mult, ALU.add), s.q_vs)
                v.engine_nop().then_inc(s.q_p3a, 1)
                v.wait_ge(s.q_p3b, ct + 1)
                ve.link(v.tensor_scalar_add(w3[:, 3:6], eb[:, 0:3], 1.0), s.q_vs)
                ve.link(v.reciprocal(w3[:, 3:6], w3[:, 3:6]), s.q_vs)
                v.wait_ge(s.q_tr, ct)
                ve.link(v.scalar_tensor_tensor(pad[:, 0:3], w3[:, 0:3], C.cb[:, ct:ct + 1],
                                               w3[:, 3:6], ALU.add, ALU.mult), s.q_vs)
                v.engine_nop().then_inc(s.q_p3c, 1)
                v.wait_ge(s.q_tr, ct + 1)
                v.wait_ge(s.q_p3x, max(ct - 1, 0) * 16)
                v.tensor_copy(p3s[0:4, (ct % 2) * 128:(ct % 2 + 1) * 128],
                              ptr[0:4, (ct % 2) * 1024:(ct % 2) * 1024 + 128]).then_inc(s.q_te, 1)

        @blk.scalar
        def _(sc):
          with contextlib.ExitStack() as es:
            se = WR(sc, es, name + "_se", 1, 1)
            for ct in range(16):
                sc.wait_ge(s.q_cv, ct + 1)
                sc.activation(eb[:, :], cvb[:, :], AF.Exp, scale=-1.0,
                              bias=C.cbn[:, 64 + ct:64 + ct + 1]).then_inc(s.q_ss, 1)
            sc.wait_ge(s.q_dt, 1)
            se.link(sc.activation(u2[:, :], u1[:, :], AF.Abs), s.q_ssl)
            se.link(sc.activation(u2[:, :], u2[:, :], AF.Exp, scale=-1.0), s.q_ssl)
            sc.activation(u2[:, :], u2[:, :], AF.Ln, bias=1.0).then_inc(s.q_dt, 1)
            for ct in range(64):
                sc.wait_ge(s.q_p3a, ct + 1)
                sc.activation(eb[:, 0:3], w3[:, 0:3], AF.Exp, scale=-1.0,
                              bias=C.cbn[:, ct:ct + 1]).then_inc(s.q_p3b, 1)

        @blk.tensor
        def _(tn):
            for ct in range(64):
                tn.wait_ge(s.q_p3c, ct + 1)
                if ct >= 2:
                    tn.wait_ge(s.q_te, ct - 1)
                tn.transpose(ptr[:, (ct % 2) * 1024:(ct % 2) * 1024 + 128],
                             pad[:, :], C.id16[:, :]).then_inc(s.q_tr, 1)

        @blk.sync
        def _(sy):
            for ct in range(16):
                sy.wait_ge(s.q_pk, ct + 1)
                base = ct * 128 if ct < 8 else 1024 + (ct - 8) * 128
                for z in range(NZ):
                    sy.dma_start(out=C.sp_pack[z, :, base:base + 128],
                                 in_=bcs[:, ct * 1024 + z * 128:ct * 1024 + (z + 1) * 128]
                                 ).then_inc(s.q_ps, 16)
            sy.wait_ge(s.q_dt, 3)
            sy.dma_start(out=C.sp_dtv[:, :], in_=dtv[:, :]).then_inc(s.q_dts, 16)
            for ct in range(64):
                sy.wait_ge(s.q_te, ct + 1)
                sy.dma_start(out=C.sp_X[0, 0:3, ct * 128:(ct + 1) * 128],
                             in_=p3s[0:3, (ct % 2) * 128:(ct % 2 + 1) * 128]).then_inc(s.q_p3x, 16)
            sy.wait_ge(s.q_p3x, 64 * 16)
            sy.wait_ge(s.q_ps, 16 * 16 * NZ)
            sy.wait_ge(s.q_dts, 16)


def emit_ssdA(nc, C, name="sa"):
  s = mk_sems(nc, C.st, name, ["q_zl", "q_zc", "q_ch", "q_xd", "q_stm", "q_stf",
                               "q_so", "q_vs"])
  M = 12
  with contextlib.ExitStack() as _es:
    X = _es.enter_context(nc.sbuf_tensor([128, 2 * FI], BF16))
    pk = _es.enter_context(nc.sbuf_tensor([128, 2 * 2048], BF16))
    dtz = _es.enter_context(nc.sbuf_tensor([128, 2 * 128], F32))
    adt = _es.enter_context(nc.sbuf_tensor([128, 128], F32))
    acum = _es.enter_context(nc.sbuf_tensor([128, 128], F32))
    dcy = _es.enter_context(nc.sbuf_tensor([128, 128], F32))
    dtT = _es.enter_context(nc.sbuf_tensor([128, 128], F32))
    dcyT = _es.enter_context(nc.sbuf_tensor([128, 128], F32))
    adtT = _es.enter_context(nc.sbuf_tensor([128, 128], F32))
    dd = _es.enter_context(nc.sbuf_tensor([128, 128], F32))
    easr = _es.enter_context(nc.sbuf_tensor([1, 128], F32))
    easbc = _es.enter_context(nc.sbuf_tensor([128, 128], F32))
    btm = _es.enter_context(nc.sbuf_tensor([128, 1024], BF16))
    xd = _es.enter_context(nc.sbuf_tensor([128, 2 * 512], BF16))
    zbuf = _es.enter_context(nc.sbuf_tensor([128, 128], F32))
    p_st = _es.enter_context(nc.psum_tensor([128, 1024], F32))
    p_pp = _es.enter_context(nc.psum_tensor([128, 1024], F32))
    p_bt = _es.enter_context(nc.psum_tensor([128, 1024], BF16))

    with nc.Block(name + "_c") as blk:
        @blk.gpsimd
        def _(g):
            g.memset(C.prevT[:, :], 0.0)
            g.memset(zbuf[:, :], 0.0)
        @blk.sync
        def _(sy):
            sy.sem_inc(s.q_zc, 2)
            sy.sem_inc(s.q_stf, 2)

    with nc.Block(name) as blk:
        @blk.sync
        def _(sy):
          with contextlib.ExitStack() as es:
            y_zc = WR(sy, es, name + "_y_zc", 1, 1)
            with sy.Fori(0, 4) as jz:
                for b in range(2):
                    y_zc.wait(s.q_zc)
                    sy.dma_start(out=X[:, b * FI:(b + 1) * FI],
                                 in_=C.sp_X[ds(jz * 2 + b, 1), :, :]).then_inc(s.q_zl, 16)
                    sy.dma_start(out=pk[:, b * 2048:(b + 1) * 2048],
                                 in_=C.sp_pack[ds(jz * 2 + b, 1), :, :]).then_inc(s.q_zl, 16)
                    sy.dma_start(out=dtz[:, b * 128:(b + 1) * 128],
                                 in_=C.sp_dtv[:, ds(jz * 256 + b * 128, 128)]).then_inc(s.q_zl, 16)
            sy.wait_ge(s.q_zc, 10)

        @blk.vector
        def _(v):
          with contextlib.ExitStack() as es:
            v_zl = WR(v, es, name + "_v_zl", 48, 48)
            v_c5 = WR(v, es, name + "_v_c5", 5, M)
            v_c9 = WR(v, es, name + "_v_c9", 9, M)
            v_c11 = WR(v, es, name + "_v_c11", 11, M)
            v_stm = WR(v, es, name + "_v_stm", 1, 1)
            ve = WR(v, es, name + "_ve", 1, 1)
            with v.Fori(0, 4) as jz:
                for b in range(2):
                    v_zl.wait(s.q_zl)
                    ve.link(v.tensor_scalar_mul(adt[:, :], dtz[:, b * 128:(b + 1) * 128],
                                                C.aneg[:, 0:1]), s.q_vs)
                    v.engine_nop().then_inc(s.q_ch, 1)
                    ve.link(v.tensor_tensor_scan(acum[:, :], adt[:, :], zbuf[:, :], 0.0,
                                                 ALU.add, ALU.add), s.q_vs)
                    v.engine_nop().then_inc(s.q_ch, 1)
                    ve.link(v.tensor_scalar(dcy[:, :], acum[:, :], acum[:, 127:128], None,
                                            ALU.subtract), s.q_vs)
                    v.engine_nop().then_inc(s.q_ch, 1)
                    v_c5.wait(s.q_ch)
                    ve.link(v.tensor_copy(dtT[:, :], p_pp[:, 0:128]), s.q_vs)
                    ve.link(v.tensor_copy(dcyT[:, :], p_pp[:, 128:256]), s.q_vs)
                    ve.link(v.tensor_copy(adtT[:, :], p_pp[:, 256:384]), s.q_vs)
                    ve.link(v.tensor_mul(dd[:, :], dtT[:, :], dcyT[:, :]), s.q_vs)
                    v.engine_nop().then_inc(s.q_ch, 1)
                    v_c9.wait(s.q_ch)
                    ve.link(v.tensor_copy(easbc[:, :], p_pp[:, 640:768]), s.q_vs)
                    v.engine_nop().then_inc(s.q_ch, 1)
                    v_c11.wait(s.q_ch)
                    ve.link(v.tensor_copy(btm[:, :], p_bt[:, 0:1024]), s.q_vs)
                    v.engine_nop().then_inc(s.q_ch, 1)
                    for grp in range(16):
                        gslot = grp % 2
                        for hh in range(8):
                            h = grp * 8 + hh
                            ins = v.tensor_scalar_mul(
                                xd[:, gslot * 512 + hh * 64:gslot * 512 + (hh + 1) * 64],
                                X[:, b * FI + h * 64:b * FI + (h + 1) * 64], dd[:, h:h + 1])
                        ins.then_inc(s.q_xd, 1)
                        v_stm.wait(s.q_stm)
                        for hh in range(8):
                            h = grp * 8 + hh
                            ins = v.scalar_tensor_tensor(
                                C.prevT[:, h * 64:(h + 1) * 64],
                                C.prevT[:, h * 64:(h + 1) * 64], easbc[:, h:h + 1],
                                p_st[:, gslot * 512 + hh * 64:gslot * 512 + (hh + 1) * 64],
                                ALU.mult, ALU.add)
                        ins.then_inc(s.q_stf, 1)
                    v.engine_nop().then_inc(s.q_zc, 1)

        @blk.scalar
        def _(sc):
          with contextlib.ExitStack() as es:
            s_c4 = WR(sc, es, name + "_s_c4", 3, M)
            s_c8 = WR(sc, es, name + "_s_c8", 7, M)
            with sc.Fori(0, 4) as jz:
                for b in range(2):
                    s_c4.wait(s.q_ch)
                    sc.activation(dcy[:, :], dcy[:, :], AF.Exp, scale=-1.0).then_inc(s.q_ch, 1)
                    s_c8.wait(s.q_ch)
                    sc.activation(easr[0:1, :], p_pp[0:1, 512:640], AF.Exp).then_inc(s.q_ch, 1)

        @blk.tensor
        def _(tn):
          with contextlib.ExitStack() as es:
            t_zl = WR(tn, es, name + "_t_zl", 48, 48)
            t_c5 = WR(tn, es, name + "_t_c5", 4, M)
            t_c7 = WR(tn, es, name + "_t_c7", 6, M)
            t_c9 = WR(tn, es, name + "_t_c9", 8, M)
            t_c11 = WR(tn, es, name + "_t_c11", 10, M)
            t_xd = WR(tn, es, name + "_t_xd", 1, 1)
            t_stf = WR(tn, es, name + "_t_stf", 1, 1)
            with tn.Fori(0, 4) as jz:
                for b in range(2):
                    t_zl.wait(s.q_zl)
                    t_c5.wait(s.q_ch)
                    tn.transpose(p_pp[:, 0:128], dtz[:, b * 128:(b + 1) * 128], C.id32[:, :])
                    tn.transpose(p_pp[:, 128:256], dcy[:, :], C.id32[:, :])
                    ins = tn.transpose(p_pp[:, 256:384], adt[:, :], C.id32[:, :])
                    ins.then_inc(s.q_ch, 1)
                    t_c7.wait(s.q_ch)
                    tn.matmul(p_pp[0:1, 512:640], C.onescol[:, 0:1], adtT[:, :],
                              start=True, stop=True).then_inc(s.q_ch, 1)
                    t_c9.wait(s.q_ch)
                    tn.matmul(p_pp[:, 640:768], C.ones32[0:1, 0:128], easr[0:1, :],
                              start=True, stop=True).then_inc(s.q_ch, 1)
                    t_c11.wait(s.q_ch)
                    for gg in range(NG):
                        ins = tn.transpose(p_bt[:, gg * 128:(gg + 1) * 128],
                                           pk[:, b * 2048 + gg * 128:b * 2048 + (gg + 1) * 128],
                                           C.id16[:, :])
                    ins.then_inc(s.q_ch, 1)
                    for grp in range(16):
                        gslot = grp % 2
                        t_xd.wait(s.q_xd)
                        t_stf.wait(s.q_stf)
                        for hh in range(8):
                            h = grp * 8 + hh
                            ins = tn.matmul(
                                p_st[:, gslot * 512 + hh * 64:gslot * 512 + (hh + 1) * 64],
                                btm[:, (h // 16) * 128:(h // 16 + 1) * 128],
                                xd[:, gslot * 512 + hh * 64:gslot * 512 + (hh + 1) * 64],
                                start=True, stop=True)
                        ins.then_inc(s.q_stm, 1)

    with nc.Block(name + "_so") as blk:
        @blk.gpsimd
        def _(g):
            g.dma_start(out=C.shx[bass.ds(C.e_par128, 128), 256:8448],
                        in_=C.prevT[:, :]).then_inc(s.q_so, 16)
            g.wait_ge(s.q_so, 16)
            g.dma_start(out=C.shf[0:1, bass.ds(C.e_par2p1, 1)],
                        in_=C.noncesb[0:1, 0:1]).then_inc(s.q_so, 16)
            g.wait_ge(s.q_so, 32)


def emit_exch2(nc, C, name="x2"):
  s = mk_sems(nc, C.st, name, ["q_fl", "q_po"])
  with contextlib.ExitStack() as _es:
    sin = _es.enter_context(nc.sbuf_tensor([128, FI], F32))
    with nc.Block(name) as blk:
        @blk.gpsimd
        def _(g):
            with (g.register(name + "_nd") as nd, g.register(name + "_rexp") as rexp):
                g.reg_mov(nd, 1)
                g.reg_mov(rexp, 16)
                if not SIM_SKIP_POLL:
                    with g.While(nd):
                        g.dma_start(out=C.flsb[0:1, 0:1],
                                    in_=C.shf[0:1, bass.ds(C.e_oth2p1, 1)]).then_inc(s.q_fl, 16)
                        g.wait_ge(s.q_fl, rexp)
                        g.reg_add(rexp, rexp, 16)
                        fv = g.value_load(C.flsb[0:1, 0:1])
                        g.reg_mov(nd, fv - C.gnv)
                else:
                    g.dma_start(out=C.shx[bass.ds(C.e_oth128, 128), 256:8448],
                                in_=C.prevT[:, :]).then_inc(s.q_fl, 16)
                    g.wait_ge(s.q_fl, 16)
                g.dma_start(out=sin[:, :],
                            in_=C.shx[bass.ds(C.e_oth128, 128), 256:8448]).then_inc(s.q_po, 16)
                g.wait_ge(s.q_po, 16)
    with nc.Block(name + "_m") as blk:
        @blk.vector
        def _(v):
            v.tensor_scalar_mul(C.prevT[:, :], sin[:, :], C.pmask[:, 0:1])


def emit_ssdB(nc, C, name="sb"):
  s = mk_sems(nc, C.st, name, ["q_zl", "q_gt", "q_zc", "q_ch", "q_pb", "q_dm",
                               "q_l1", "q_mg", "q_mh", "q_xc", "q_xd",
                               "q_pc", "q_ym", "q_t1", "q_t2", "q_ev", "q_sg",
                               "q_go1", "q_rstd", "q_og", "q_ogc", "q_gs", "q_vs",
                               "q_ssl"])
  M = 13
  with contextlib.ExitStack() as _es:
    X = _es.enter_context(nc.sbuf_tensor([128, 2 * FI], BF16))
    gt = _es.enter_context(nc.sbuf_tensor([128, 2 * FI], BF16))
    pk = _es.enter_context(nc.sbuf_tensor([128, 2 * 2048], BF16))
    dtz = _es.enter_context(nc.sbuf_tensor([128, 2 * 128], F32))
    adt = _es.enter_context(nc.sbuf_tensor([128, 128], F32))
    acum = _es.enter_context(nc.sbuf_tensor([128, 128], F32))
    dcy = _es.enter_context(nc.sbuf_tensor([128, 128], F32))
    dtT = _es.enter_context(nc.sbuf_tensor([128, 128], F32))
    dcyT = _es.enter_context(nc.sbuf_tensor([128, 128], F32))
    adtT = _es.enter_context(nc.sbuf_tensor([128, 128], F32))
    acumT = _es.enter_context(nc.sbuf_tensor([128, 128], F32))
    eacT = _es.enter_context(nc.sbuf_tensor([128, 128], F32))
    dd = _es.enter_context(nc.sbuf_tensor([128, 128], F32))
    easr = _es.enter_context(nc.sbuf_tensor([1, 128], F32))
    easbc = _es.enter_context(nc.sbuf_tensor([128, 128], F32))
    btm = _es.enter_context(nc.sbuf_tensor([128, 1024], BF16))
    zbuf = _es.enter_context(nc.sbuf_tensor([128, 128], F32))
    Pb = _es.enter_context(nc.sbuf_tensor([128, 2 * 128], BF16))
    Lm = _es.enter_context(nc.sbuf_tensor([128, 2 * 128], F32))
    Mh = _es.enter_context(nc.sbuf_tensor([128, 2 * 128], BF16))
    xch = _es.enter_context(nc.sbuf_tensor([128, 2 * 64], BF16))
    xdb = _es.enter_context(nc.sbuf_tensor([128, 2 * 64], BF16))
    pc16 = _es.enter_context(nc.sbuf_tensor([128, 2 * 64], BF16))
    t1b = _es.enter_context(nc.sbuf_tensor([128, 2 * 64], F32))
    t2b = _es.enter_context(nc.sbuf_tensor([128, 2 * 64], F32))
    ych = _es.enter_context(nc.sbuf_tensor([128, FI], F32))
    sgb = _es.enter_context(nc.sbuf_tensor([128, FI // 2], F32))
    gout = _es.enter_context(nc.sbuf_tensor([128, FI // 2], F32))
    og = _es.enter_context(nc.sbuf_tensor([128, FI // 2], BF16))
    ssq = _es.enter_context(nc.sbuf_tensor([128, 16], F32))
    p_d = _es.enter_context(nc.psum_tensor([128, 1024], F32))
    p_y = _es.enter_context(nc.psum_tensor([128, 1024], F32))
    p_m = _es.enter_context(nc.psum_tensor([128, 1024], F32))
    p_pp = _es.enter_context(nc.psum_tensor([128, 512], F32))
    p_bt = _es.enter_context(nc.psum_tensor([128, 1024], BF16))

    with nc.Block(name + "_c") as blk:
        @blk.gpsimd
        def _(g):
            g.memset(zbuf[:, :], 0.0)
        @blk.sync
        def _(sy):
            sy.sem_inc(s.q_zc, 2)
            sy.sem_inc(s.q_gs, 16)
            sy.sem_inc(s.q_ev, 2)
            sy.sem_inc(s.q_ym, 2)
            sy.sem_inc(s.q_mh, 16)

    with nc.Block(name) as blk:
        @blk.sync
        def _(sy):
            def spills(c):
                for hf in range(2):
                    sy.wait_ge(s.q_ogc, 2 * c + hf + 1)
                    sy.dma_start(out=C.sp_outg[c * 128:(c + 1) * 128,
                                               hf * (FI // 2):(hf + 1) * (FI // 2)],
                                 in_=og[:, :]).then_inc(s.q_gs, 16)
            for z in range(NZ):
                b = z % 2
                if z >= 1:
                    spills(z - 1)
                sy.wait_ge(s.q_zc, z + 1)
                sy.dma_start(out=X[:, b * FI:(b + 1) * FI],
                             in_=C.sp_X[z, :, :]).then_inc(s.q_zl, 16)
                sy.dma_start(out=pk[:, b * 2048:(b + 1) * 2048],
                             in_=C.sp_pack[z, :, :]).then_inc(s.q_zl, 16)
                sy.dma_start(out=dtz[:, b * 128:(b + 1) * 128],
                             in_=C.sp_dtv[:, z * 128:(z + 1) * 128]).then_inc(s.q_zl, 16)
                sy.dma_start_transpose(
                    gt[:, b * FI:(b + 1) * FI],
                    C.sp_gate[0:FI, z * 128:(z + 1) * 128]).then_inc(s.q_gt, 16)
            spills(NZ - 1)
            sy.wait_ge(s.q_gs, 16 + 32 * NZ)

        @blk.scalar
        def _(sc):
          with contextlib.ExitStack() as es:
            s_c4 = WR(sc, es, name + "_s_c4", 3, M)
            s_c7 = WR(sc, es, name + "_s_c7", 6, M)
            s_c9 = WR(sc, es, name + "_s_c9", 8, M)
            s_dm0 = WR(sc, es, name + "_s_dm0", 0, 128)
            s_dm1 = WR(sc, es, name + "_s_dm1", 1, 1)
            s_mh = WR(sc, es, name + "_s_mh", 17, 1)
            s_ym = WR(sc, es, name + "_s_ym", 3, 1)
            s_t2 = WR(sc, es, name + "_s_t2", 1, 1)
            s_gt = WR(sc, es, name + "_s_gt", 16, 16)
            s_go1 = WR(sc, es, name + "_s_go1", 1, 1)
            s_og = WR(sc, es, name + "_s_og", 1, 1)
            s_gs = WR(sc, es, name + "_s_gs", 16, 16)
            se = WR(sc, es, name + "_se", 1, 1)

            def mkpb(h):
                sc.activation(Pb[:, (h % 2) * 128:(h % 2) * 128 + 128], C.strict[:, :],
                              AF.Copy, scale=adtT[:, h:h + 1]).then_inc(s.q_pb, 1)

            with sc.Fori(0, 4) as jz:
                for b in range(2):
                    s_c4.wait(s.q_ch)
                    sc.activation(dcy[:, :], dcy[:, :], AF.Exp, scale=-1.0).then_inc(s.q_ch, 1)
                    s_c7.wait(s.q_ch)
                    sc.activation(eacT[:, :], acumT[:, :], AF.Exp).then_inc(s.q_ch, 1)
                    s_c9.wait(s.q_ch)
                    sc.activation(easr[0:1, :], p_m[0:1, 128:256], AF.Exp).then_inc(s.q_ch, 1)
                    s_dm0.wait(s.q_dm)
                    mkpb(0)
                    mkpb(1)
                    for h in range(NH):
                        hs = (h % 2)
                        s_dm1.wait(s.q_dm)
                        if h < NH - 2:
                            mkpb(h + 2)
                        if h >= 2:
                            s_mh.wait(s.q_mh)
                        sc.activation(Lm[:, hs * 128:(hs + 1) * 128],
                                      p_d[:, hs * 512:hs * 512 + 128],
                                      AF.Exp).then_inc(s.q_l1, 1)
                        if h % 2 == 0:
                            sc.activation(pc16[:, :],
                                          C.prevT[:, h * 64:(h + 2) * 64],
                                          AF.Copy).then_inc(s.q_pc, 2)
                        s_ym.wait(s.q_ym)
                        if h >= 2:
                            s_t2.wait(s.q_t2)
                        sc.activation(t1b[:, hs * 64:(hs + 1) * 64],
                                      p_y[:, hs * 512 + 64:hs * 512 + 128],
                                      AF.Copy, scale=eacT[:, h:h + 1]).then_inc(s.q_t1, 1)
                    s_mh.bump(); s_mh.bump()
                    s_t2.bump(); s_t2.bump()
                    s_gt.wait(s.q_gt)
                    for hf in range(2):
                        fo = hf * (FI // 2)
                        sc.activation(sgb[:, :], gt[:, b * FI + fo:b * FI + fo + FI // 2],
                                      AF.Silu).then_inc(s.q_sg, 1)
                        s_go1.wait(s.q_go1)
                        for gg in range(4):
                            sc.activation(sgb[:, gg * 1024:(gg + 1) * 1024],
                                          gout[:, gg * 1024:(gg + 1) * 1024], AF.Square,
                                          accum_out=ssq[:, hf * 4 + gg:hf * 4 + gg + 1])
                        r0 = 8 + hf * 4
                        se.link(sc.activation(ssq[:, r0:r0 + 4], ssq[:, hf * 4:hf * 4 + 4],
                                              AF.Copy, scale=1.0 / 1024.0), s.q_ssl)
                        se.link(sc.activation(ssq[:, r0:r0 + 4], ssq[:, r0:r0 + 4],
                                              AF.Ln, bias=C.epsc[:, 0:1]), s.q_ssl)
                        sc.activation(ssq[:, r0:r0 + 4], ssq[:, r0:r0 + 4],
                                      AF.Exp, scale=-0.5).then_inc(s.q_rstd, 1)
                        s_og.wait(s.q_og)
                        s_gs.wait(s.q_gs)
                        sc.activation(og[:, :], gout[:, :], AF.Copy).then_inc(s.q_ogc, 1)

        @blk.tensor
        def _(tn):
          with contextlib.ExitStack() as es:
            t_zl = WR(tn, es, name + "_t_zl", 48, 48)
            t_c5 = WR(tn, es, name + "_t_c5", 4, M)
            t_c8 = WR(tn, es, name + "_t_c8", 7, M)
            t_c10 = WR(tn, es, name + "_t_c10", 9, M)
            t_c12 = WR(tn, es, name + "_t_c12", 11, M)
            t_pb = WR(tn, es, name + "_t_pb", 1, 1)
            t_l1 = WR(tn, es, name + "_t_l1", 0, 1)
            t_mh = WR(tn, es, name + "_t_mh", 17, 1)
            t_mhg = WR(tn, es, name + "_t_mhg", 0, 16)
            t_mhall = WR(tn, es, name + "_t_mhall", 16, 128)
            t_pc = WR(tn, es, name + "_t_pc", 1, 1)
            t_xc = WR(tn, es, name + "_t_xc", 1, 1)
            t_xd = WR(tn, es, name + "_t_xd", 1, 1)
            t_ev = WR(tn, es, name + "_t_ev", 1, 1)

            def pdmm(h):
                hs = (h % 2)
                tn.matmul(p_d[:, hs * 512:hs * 512 + 128],
                          Pb[:, hs * 128:(hs + 1) * 128], C.qtril[:, :],
                          start=True, stop=False)
                tn.matmul(p_d[:, hs * 512:hs * 512 + 128],
                          C.id16[:, :], C.negstrict[:, :],
                          start=False, stop=True).then_inc(s.q_dm, 1)

            with tn.Fori(0, 4) as jz:
                for b in range(2):
                    t_zl.wait(s.q_zl)
                    t_c5.wait(s.q_ch)
                    tn.transpose(p_pp[:, 0:128], dtz[:, b * 128:(b + 1) * 128], C.id32[:, :])
                    tn.transpose(p_pp[:, 128:256], dcy[:, :], C.id32[:, :])
                    tn.transpose(p_pp[:, 256:384], adt[:, :], C.id32[:, :])
                    ins = tn.transpose(p_pp[:, 384:512], acum[:, :], C.id32[:, :])
                    ins.then_inc(s.q_ch, 1)
                    t_c8.wait(s.q_ch)
                    t_mhall.wait(s.q_mh)
                    tn.matmul(p_m[0:1, 128:256], C.onescol[:, 0:1], adtT[:, :],
                              start=True, stop=True).then_inc(s.q_ch, 1)
                    t_c10.wait(s.q_ch)
                    tn.matmul(p_m[:, 256:384], C.ones32[0:1, 0:128], easr[0:1, :],
                              start=True, stop=True).then_inc(s.q_ch, 1)
                    t_c12.wait(s.q_ch)
                    for gg in range(NG):
                        ins = tn.transpose(p_bt[:, gg * 128:(gg + 1) * 128],
                                           pk[:, b * 2048 + gg * 128:b * 2048 + (gg + 1) * 128],
                                           C.id16[:, :])
                    ins.then_inc(s.q_ch, 1)
                    t_pb.wait(s.q_pb)
                    t_l1.wait(s.q_l1)
                    pdmm(0)
                    for h in range(NH):
                        g2 = h // 16
                        ys = (h % 2) * 512
                        if h % 16 == 0:
                            gg2 = (g2 % 2) * 512
                            t_mhg.wait(s.q_mh)
                            tn.matmul(p_m[:, gg2:gg2 + 128],
                                      pk[:, b * 2048 + g2 * 128:b * 2048 + (g2 + 1) * 128],
                                      pk[:, b * 2048 + 1024 + g2 * 128:b * 2048 + 1024 + (g2 + 1) * 128],
                                      start=True, stop=True).then_inc(s.q_mg, 1)
                        if h < NH - 1:
                            t_pb.wait(s.q_pb)
                            t_l1.wait(s.q_l1)
                            pdmm(h + 1)
                        t_mh.wait(s.q_mh)
                        t_ev.wait(s.q_ev)
                        t_xc.wait(s.q_xc)
                        tn.matmul(p_y[:, ys:ys + 64], Mh[:, (h % 2) * 128:(h % 2 + 1) * 128],
                                  xch[:, (h % 2) * 64:(h % 2 + 1) * 64], start=True, stop=True)
                        t_pc.wait(s.q_pc)
                        tn.matmul(p_y[:, ys + 64:ys + 128],
                                  pk[:, b * 2048 + 1024 + g2 * 128:b * 2048 + 1024 + (g2 + 1) * 128],
                                  pc16[:, (h % 2) * 64:(h % 2 + 1) * 64], start=True, stop=True)
                        t_xd.wait(s.q_xd)
                        ins = tn.matmul(p_y[:, ys + 128:ys + 192],
                                        btm[:, g2 * 128:(g2 + 1) * 128],
                                        xdb[:, (h % 2) * 64:(h % 2 + 1) * 64],
                                        start=True, stop=True)
                        ins.then_inc(s.q_ym, 1)

        @blk.vector
        def _(v):
          with contextlib.ExitStack() as es:
            v_zl = WR(v, es, name + "_v_zl", 48, 48)
            v_c5 = WR(v, es, name + "_v_c5", 5, M)
            v_c10 = WR(v, es, name + "_v_c10", 10, M)
            v_c13x = WR(v, es, name + "_v_c13x", 12, M)
            v_ymg = WR(v, es, name + "_v_ymg", 1, 1)
            v_l1 = WR(v, es, name + "_v_l1", 1, 1)
            v_mg = WR(v, es, name + "_v_mg", 1, 1)
            v_ym = WR(v, es, name + "_v_ym", 3, 1)
            v_t1 = WR(v, es, name + "_v_t1", 1, 1)
            v_sg = WR(v, es, name + "_v_sg", 1, 1)
            v_rstd = WR(v, es, name + "_v_rstd", 1, 1)
            v_ogc = WR(v, es, name + "_v_ogc", 0, 1)
            ve = WR(v, es, name + "_ve", 1, 1)

            def late(v, hp, b):
                hps = (hp % 2)
                v_t1.wait(s.q_t1)
                v.scalar_tensor_tensor(
                    t2b[:, hps * 64:(hps + 1) * 64],
                    X[:, b * FI + hp * 64:b * FI + (hp + 1) * 64],
                    C.dbc[:, hp:hp + 1], t1b[:, hps * 64:(hps + 1) * 64],
                    ALU.mult, ALU.add).then_inc(s.q_t2, 1)
                v.tensor_add(ych[:, hp * 64:(hp + 1) * 64],
                             t2b[:, hps * 64:(hps + 1) * 64],
                             p_y[:, hps * 512:hps * 512 + 64])
                v.scalar_tensor_tensor(
                    C.prevT[:, hp * 64:(hp + 1) * 64],
                    C.prevT[:, hp * 64:(hp + 1) * 64], easbc[:, hp:hp + 1],
                    p_y[:, hps * 512 + 128:hps * 512 + 192],
                    ALU.mult, ALU.add).then_inc(s.q_ev, 1)

            with v.Fori(0, 4) as jz:
                for b in range(2):
                    v_zl.wait(s.q_zl)
                    ve.link(v.tensor_scalar_mul(adt[:, :], dtz[:, b * 128:(b + 1) * 128],
                                                C.aneg[:, 0:1]), s.q_vs)
                    v.engine_nop().then_inc(s.q_ch, 1)
                    ve.link(v.tensor_tensor_scan(acum[:, :], adt[:, :], zbuf[:, :], 0.0,
                                                 ALU.add, ALU.add), s.q_vs)
                    v.engine_nop().then_inc(s.q_ch, 1)
                    ve.link(v.tensor_scalar(dcy[:, :], acum[:, :], acum[:, 127:128], None,
                                            ALU.subtract), s.q_vs)
                    v.engine_nop().then_inc(s.q_ch, 1)
                    v_c5.wait(s.q_ch)
                    ve.link(v.tensor_copy(dtT[:, :], p_pp[:, 0:128]), s.q_vs)
                    ve.link(v.tensor_copy(dcyT[:, :], p_pp[:, 128:256]), s.q_vs)
                    ve.link(v.tensor_copy(adtT[:, :], p_pp[:, 256:384]), s.q_vs)
                    ve.link(v.tensor_copy(acumT[:, :], p_pp[:, 384:512]), s.q_vs)
                    ve.link(v.tensor_mul(dd[:, :], dtT[:, :], dcyT[:, :]), s.q_vs)
                    v.engine_nop().then_inc(s.q_ch, 1)
                    v_c10.wait(s.q_ch)
                    ve.link(v.tensor_copy(easbc[:, :], p_m[:, 256:384]), s.q_vs)
                    v.engine_nop().then_inc(s.q_ch, 1)
                    v_c13x.wait(s.q_ch)
                    ve.link(v.tensor_copy(btm[:, :], p_bt[:, 0:1024]), s.q_vs)
                    v.engine_nop().then_inc(s.q_ch, 1)
                    for h in range(NH):
                        hs = (h % 2)
                        v_ymg.wait(s.q_ym)
                        v.tensor_scalar_mul(xch[:, hs * 64:(hs + 1) * 64],
                                            X[:, b * FI + h * 64:b * FI + (h + 1) * 64],
                                            dtT[:, h:h + 1]).then_inc(s.q_xc, 1)
                        v.tensor_scalar_mul(xdb[:, hs * 64:(hs + 1) * 64],
                                            X[:, b * FI + h * 64:b * FI + (h + 1) * 64],
                                            dd[:, h:h + 1]).then_inc(s.q_xd, 1)
                        if h % 16 == 0:
                            v_mg.wait(s.q_mg)
                        v_l1.wait(s.q_l1)
                        v.tensor_mul(Mh[:, hs * 128:(hs + 1) * 128],
                                     p_m[:, ((h // 16) % 2) * 512:((h // 16) % 2) * 512 + 128],
                                     Lm[:, hs * 128:(hs + 1) * 128]).then_inc(s.q_mh, 1)
                        if h >= 1:
                            v_ym.wait(s.q_ym)
                            late(v, h - 1, b)
                    v_ym.wait(s.q_ym)
                    late(v, NH - 1, b)
                    for hf in range(2):
                        fo = hf * (FI // 2)
                        v_sg.wait(s.q_sg)
                        v_ogc.wait(s.q_ogc)
                        v.tensor_mul(gout[:, :], sgb[:, :],
                                     ych[:, fo:fo + FI // 2]).then_inc(s.q_go1, 1)
                        v_rstd.wait(s.q_rstd)
                        for gg in range(4):
                            ins = v.tensor_scalar_mul(gout[:, gg * 1024:(gg + 1) * 1024],
                                                      gout[:, gg * 1024:(gg + 1) * 1024],
                                                      ssq[:, 8 + hf * 4 + gg:9 + hf * 4 + gg])
                        ins.then_inc(s.q_og, 1)
                    v.engine_nop().then_inc(s.q_zc, 1)

    if PHASES == 7 and DBG_SSD:
        sd = mk_sems(nc, C.st, name + "dbg", ["q"])
        with nc.Block(name + "_dbg") as blk:
            @blk.gpsimd
            def _(g):
                g.dma_start(out=C.outap[0:128, 0:4096], in_=ych[:, 0:4096]).then_inc(sd.q, 16)
                g.dma_start(out=C.outap[128:256, 0:4096], in_=ych[:, 4096:8192]).then_inc(sd.q, 16)
                g.dma_start(out=C.outap[256:384, 0:4096], in_=C.prevT[:, 0:4096]).then_inc(sd.q, 16)
                g.dma_start(out=C.outap[384:512, 0:4096], in_=C.prevT[:, 4096:8192]).then_inc(sd.q, 16)
                g.wait_ge(sd.q, 64)


def emit_outproj(nc, C, name="op"):
  s = mk_sems(nc, C.st, name, ["q_ol", "q_w", "q_m", "q_e", "q_as", "q_al",
                               "q_x1", "q_x2", "q_vs", "q_nw"])
  with contextlib.ExitStack() as _es:
    ofm = _es.enter_context(nc.sbuf_tensor([128, ITL * 1024], BF16))
    wob = _es.enter_context(nc.sbuf_tensor([128, 2 * 8192], BF16))
    ast = _es.enter_context(nc.sbuf_tensor([128, 2 * 1024], BF16))
    nwt = _es.enter_context(nc.sbuf_tensor([128, ITL], F32))
    ps = _es.enter_context(nc.psum_tensor([128, 2048], F32))

    with nc.Block(name + "_c") as blk:
        @blk.sync
        def _(sy):
            sy.sem_inc(s.q_m, 2)
            sy.sem_inc(s.q_e, 2)
            sy.sem_inc(s.q_as, 32)
            sy.dma_start(out=nwt[:, :], in_=C.nwt_in[:, :]).then_inc(s.q_nw, 16)
            for it in range(ITL):
                sy.dma_start_transpose(
                    ofm[:, it * 1024:(it + 1) * 1024],
                    C.sp_outg[0:1024, it * 128:(it + 1) * 128]).then_inc(s.q_ol, 16)
            sy.wait_ge(s.q_ol, ITL * 16)
        @blk.vector
        def _(v):
            v.wait_ge(s.q_nw, 16)
            for it in range(ITL):
                v.wait_ge(s.q_ol, (it + 1) * 16)
                v.tensor_scalar_mul(ofm[:, it * 1024:(it + 1) * 1024],
                                    ofm[:, it * 1024:(it + 1) * 1024],
                                    nwt[:, it:it + 1])

    with nc.Block(name) as blk:
        @blk.gpsimd
        def _(g):
            for k in range(DTL):
                g.wait_ge(s.q_m, k + 1)
                g.dma_start(out=wob[:, (k % 2) * 8192:(k % 2 + 1) * 8192],
                            in_=C.wout[k, :, :]).then_inc(s.q_w, 16)
        @blk.tensor
        def _(tn):
          with contextlib.ExitStack() as es:
            t_w = WR(tn, es, name + "_t_w", 16, 16)
            t_e = WR(tn, es, name + "_t_e", 1, 1)
            with tn.Fori(0, 16) as j:
                for b in range(2):
                    t_w.wait(s.q_w)
                    t_e.wait(s.q_e)
                    for tch in range(2):
                        for it in range(ITL):
                            ins = tn.matmul(
                                ps[:, b * 1024 + tch * 512:b * 1024 + (tch + 1) * 512],
                                wob[:, b * 8192 + it * 128:b * 8192 + (it + 1) * 128],
                                ofm[:, it * 1024 + tch * 512:it * 1024 + (tch + 1) * 512],
                                start=(it == 0), stop=(it == ITL - 1))
                    ins.then_inc(s.q_m, 1)
        @blk.vector
        def _(v):
          with contextlib.ExitStack() as es:
            v_m = WR(v, es, name + "_v_m", 3, 1)
            v_as = WR(v, es, name + "_v_as", 16, 16)
            with v.Fori(0, 16) as j:
                for b in range(2):
                    v_m.wait(s.q_m)
                    v_as.wait(s.q_as)
                    v.tensor_copy(ast[:, b * 1024:(b + 1) * 1024],
                                  ps[:, b * 1024:(b + 1) * 1024]).then_inc(s.q_e, 1)
        @blk.sync
        def _(sy):
          with contextlib.ExitStack() as es:
            y_e = WR(sy, es, name + "_y_e", 3, 1)
            with sy.Fori(0, 16) as j:
                for b in range(2):
                    y_e.wait(s.q_e)
                    sy.dma_start(out=C.sp_attn[ds(j * 256 + b * 128, 128), :],
                                 in_=ast[:, b * 1024:(b + 1) * 1024]).then_inc(s.q_as, 16)
            sy.wait_ge(s.q_as, 32 + 32 * 16)

  s2 = mk_sems(nc, C.st, name + "b", ["q_l", "q_d", "q_o", "q_vs"])
  with contextlib.ExitStack() as _es:
    atm = _es.enter_context(nc.sbuf_tensor([128, 2 * DM], BF16))
    x1t = _es.enter_context(nc.sbuf_tensor([128, 2 * DM], F32))
    x2t = _es.enter_context(nc.sbuf_tensor([128, 2 * DM], F32))
    with nc.Block(name + "b_c") as blk:
        @blk.sync
        def _(sy):
            sy.sem_inc(s2.q_d, 2)
            sy.sem_inc(s2.q_o, 32)
    with nc.Block(name + "b") as blk:
        @blk.sync
        def _(sy):
            for k in range(8):
                b = k % 2
                sy.wait_ge(s2.q_d, k + 1)
                sy.dma_start_transpose(
                    atm[:, b * DM:(b + 1) * DM],
                    C.sp_attn[0:DM, k * 128:(k + 1) * 128]).then_inc(s2.q_l, 16)
                sy.dma_start(out=x1t[:, b * DM:(b + 1) * DM],
                             in_=C.sp_x1[k * 128:(k + 1) * 128, :]).then_inc(s2.q_l, 16)
                sy.wait_ge(s2.q_d, k + 3)
                sy.dma_start(out=C.sp_x2[k * 128:(k + 1) * 128, :],
                             in_=x2t[:, b * DM:(b + 1) * DM]).then_inc(s2.q_o, 16)
            sy.wait_ge(s2.q_o, 32 + 8 * 16)
        @blk.vector
        def _(v):
          with contextlib.ExitStack() as es:
            v_l = WR(v, es, name + "_v_l", 32, 32)
            v_o = WR(v, es, name + "_v_o", 16, 16)
            with v.Fori(0, 4) as jz:
                for b in range(2):
                    v_l.wait(s2.q_l)
                    v_o.wait(s2.q_o)
                    v.tensor_add(x2t[:, b * DM:(b + 1) * DM], x1t[:, b * DM:(b + 1) * DM],
                                 atm[:, b * DM:(b + 1) * DM]).then_inc(s2.q_d, 1)


def _stage(inputs):
    g = {}
    g["wup"] = _pt(inputs["L0_ffn_up_w"])
    g["wdn"] = _pt(inputs["L0_ffn_down_w"])
    g["wf1"] = _pt(inputs["L0_ffn_fc1_w"])
    g["wex"] = _pt(inputs["L0_ffn_expert_w"])
    g["wf2"] = _pt(inputs["L0_ffn_fc2_w"])
    gw = np.asarray(inputs["L0_ffn_gate_w"], np.float32)
    g["wgt"] = np.ascontiguousarray(gw.reshape(DTL, 128, 8).transpose(1, 0, 2).reshape(128, DTL * 8))
    ip = np.asarray(inputs["L0_attn_in_proj_w"], np.float32)
    g["wip"] = _pt(np.ascontiguousarray(ip.T))
    op = np.asarray(inputs["L0_attn_out_proj_w"], np.float32)
    g["wout"] = _pt(np.ascontiguousarray(op.T))
    cw = np.asarray(inputs["L0_attn_conv_w"], np.float32).reshape(NCV * 128, 4)
    g["cw"] = np.ascontiguousarray(cw.reshape(NCV, 128, 4).transpose(1, 0, 2).reshape(128, NCV * 4))
    g["cwt"] = np.ascontiguousarray(cw.reshape(NCV, 128, 4))
    cb = np.asarray(inputs["L0_attn_conv_b"], np.float32)
    g["cb"] = np.ascontiguousarray(cb.reshape(NCV, 128).T)
    g["cbt"] = np.ascontiguousarray(cb.reshape(NCV, 128, 1))
    g["onescol"] = np.ones((128, 1), np.float32)
    g["dtb"] = np.asarray(inputs["L0_attn_dt_bias"], np.float32).reshape(128, 1)
    g["alog"] = np.asarray(inputs["L0_attn_A_log"], np.float32).reshape(128, 1)
    g["dbc"] = np.ascontiguousarray(np.broadcast_to(
        np.asarray(inputs["L0_attn_D"], np.float32)[None, :], (128, 128)))
    g["gbc"] = np.ascontiguousarray(np.broadcast_to(
        np.asarray(inputs["L0_ffn_ln_gamma"], np.float32)[None, :], (128, DM)))
    g["bbc"] = np.ascontiguousarray(np.broadcast_to(
        np.asarray(inputs["L0_ffn_ln_beta"], np.float32)[None, :], (128, DM)))
    g["nwt"] = np.ascontiguousarray(
        np.asarray(inputs["L0_attn_norm_w"], np.float32).reshape(ITL, 128).T)
    g["id16"] = np.eye(128, dtype=ml_dtypes.bfloat16)
    g["id32"] = np.eye(128, dtype=np.float32)
    g["ones32"] = np.ones((1, 128), np.float32)
    ar = np.arange(128)
    g["strictm"] = (ar[:, None] > ar[None, :]).astype(ml_dtypes.bfloat16)
    g["negstrict"] = (-30000.0 * (ar[:, None] > ar[None, :])).astype(ml_dtypes.bfloat16)
    g["qtril"] = (ar[:, None] <= ar[None, :]).astype(ml_dtypes.bfloat16)
    return g


def kernel(**inputs):
    xfull = np.asarray(inputs["L0_attn_x"], np.float32)
    g = _stage(inputs)
    nonce = np.array([[int(time.time_ns() & 0x7FFFFFF0) | 5]], np.int32)
    in_maps = []
    for c in range(8):
        b, h = c // 2, c % 2
        m = dict(g)
        m["x"] = np.ascontiguousarray(xfull[b, h * NT:(h + 1) * NT, :])
        m["pmask"] = np.full((128, 1), float(h), np.float32)
        m["nonce"] = nonce
        in_maps.append(m)
    nc = build()
    res = run_bass_kernel_spmd(nc, in_maps, core_ids=list(range(8)))
    outp = np.empty((4, 2048, DM), np.float32)
    for c in range(8):
        b, h = c // 2, c % 2
        outp[b, h * NT:(h + 1) * NT, :] = res.results[c]["out"]
    return outp


# version 09
# speedup vs baseline: 2.1769x; 1.1608x over previous
import time
import contextlib
import numpy as np
import ml_dtypes
import concourse.bass as bass
import concourse.mybir as mybir
from concourse.bass_utils import run_bass_kernel_spmd

F32 = mybir.dt.float32
BF16 = mybir.dt.bfloat16
I32 = mybir.dt.int32
AF = mybir.ActivationFunctionType
ALU = mybir.AluOpType
AX = mybir.AxisListType.X
ds = bass.ds

NT, DM, FI, LAT = 1024, 4096, 8192, 2048
DTL, ITL, LTL = DM // 128, FI // 128, LAT // 128
NE, NZ, NH, NG, HP, SN = 8, 8, 128, 8, 64, 128
EPS = 1e-5
NCV = 80
PAIRS = [(i, j) for i in range(NE) for j in range(i + 1, NE)]

PHASES = 9
SIM_SKIP_POLL = False
DBG_SSD = False


class _C:
    pass


def _pt(w):
    K, M = w.shape
    nk, nm = K // 128, M // 128
    return np.ascontiguousarray(
        np.asarray(w, np.float32).reshape(nk, 128, nm, 128).transpose(2, 1, 0, 3)
        .reshape(nm, 128, nk * 128).astype(ml_dtypes.bfloat16))



class _S:
    pass


def mk_sems(nc, st, prefix, names):
    S = _S()
    for nm in names:
        if st is None:
            setattr(S, nm, nc.alloc_semaphore(name=prefix + "_" + nm))
        else:
            setattr(S, nm, st.enter_context(nc.semaphore(prefix + "_" + nm)))
    return S



class WR:
    def __init__(self, eng, es, name, init, step):
        self.eng, self.step = eng, step
        self.reg = es.enter_context(eng.register(name))
        eng.reg_mov(self.reg, init)
    def wait(self, sem):
        self.eng.wait_ge(sem, self.reg)
        self.eng.reg_add(self.reg, self.reg, self.step)

    def link(self, ins, sem):
        ins.then_inc(sem, 1)
        self.eng.wait_ge(sem, self.reg)
        self.eng.reg_add(self.reg, self.reg, self.step)

    def bump(self):
        self.eng.reg_add(self.reg, self.reg, self.step)


_LN_ADV = {"q_x": 128, "q_sp": 160, "q_a": 8, "q_b": 8, "q_c": 8, "q_d": 10,
           "q_vs": 24, "q_ss": 8}


def emit_ln(nc, C, src, name):
  if not hasattr(C, "ln_sems"):
      C.ln_sems = mk_sems(nc, C.st, "lnS", list(_LN_ADV))
      C.ln_calls = 0
  S = C.ln_sems
  LB = {q: C.ln_calls * _LN_ADV[q] for q in _LN_ADV}
  C.ln_calls += 1
  with contextlib.ExitStack() as _es:
    C.xa = _es.enter_context(nc.sbuf_tensor([128, 2 * DM], F32))
    C.lnw = _es.enter_context(nc.sbuf_tensor([128, 2 * DM], BF16))
    C.sq = _es.enter_context(nc.sbuf_tensor([128, DM], F32))
    with nc.Block(name + "_c") as blk:
        @blk.sync
        def _(sy):
            sy.sem_inc(S.q_d, 2)
            sy.sem_inc(S.q_sp, 32)

    with nc.Block(name) as blk:
        @blk.sync
        def _(sy):
            for k in range(8):
                b = k % 2
                sy.wait_ge(S.q_d, LB["q_d"] + 1 + k)
                sy.dma_start(out=C.xa[:, b * DM:(b + 1) * DM],
                             in_=src[k * 128:(k + 1) * 128, :]).then_inc(S.q_x, 16)
                sy.wait_ge(S.q_d, LB["q_d"] + 3 + k)
                sy.dma_start(out=C.sp_lnt[k * 128:(k + 1) * 128, :],
                             in_=C.lnw[:, b * DM:(b + 1) * DM]).then_inc(S.q_sp, 16)
            sy.wait_ge(S.q_sp, LB["q_sp"] + 32 + 16 * 8)

        @blk.vector
        def _(v):
          with contextlib.ExitStack() as es:
            ve = WR(v, es, name + "_ve", 1 + LB["q_vs"], 1)
            v_x = WR(v, es, name + "_v_x", 16 + LB["q_x"], 16)
            v_a = WR(v, es, name + "_v_a", 1 + LB["q_a"], 1)
            v_c = WR(v, es, name + "_v_c", 1 + LB["q_c"], 1)
            v_sp = WR(v, es, name + "_v_sp", 16 + LB["q_sp"], 16)
            with v.Fori(0, 4) as t:
                for b in range(2):
                    xs = C.xa[:, b * DM:(b + 1) * DM]
                    r = C.red[:, b * 8:(b + 1) * 8]
                    v_x.wait(S.q_x)
                    ve.link(v.reduce_sum(r[:, 0:1], xs, axis=AX), S.q_vs)
                    v_a.wait(S.q_a)
                    ve.link(v.tensor_scalar_mul(r[:, 2:3], r[:, 0:1], 1.0 / DM), S.q_vs)
                    ve.link(v.tensor_mul(r[:, 3:4], r[:, 2:3], r[:, 2:3]), S.q_vs)
                    v.scalar_tensor_tensor(r[:, 4:5], r[:, 1:2], 1.0 / DM, r[:, 3:4],
                                           ALU.mult, ALU.subtract).then_inc(S.q_b, 1)
                    v_c.wait(S.q_c)
                    v_sp.wait(S.q_sp)
                    v.tensor_scalar(C.lnw[:, b * DM:(b + 1) * DM], xs, r[:, 2:3], r[:, 5:6],
                                    ALU.subtract, ALU.mult).then_inc(S.q_d, 1)

        @blk.scalar
        def _(sc):
          with contextlib.ExitStack() as es:
            se = WR(sc, es, name + "_se", 1 + LB["q_ss"], 1)
            s_x = WR(sc, es, name + "_s_x", 16 + LB["q_x"], 16)
            s_b = WR(sc, es, name + "_s_b", 1 + LB["q_b"], 1)
            with sc.Fori(0, 4) as t:
                for b in range(2):
                    xs = C.xa[:, b * DM:(b + 1) * DM]
                    r = C.red[:, b * 8:(b + 1) * 8]
                    s_x.wait(S.q_x)
                    sc.activation(C.sq[:, :], xs, AF.Square, accum_out=r[:, 1:2]).then_inc(S.q_a, 1)
                    s_b.wait(S.q_b)
                    se.link(sc.activation(r[:, 6:7], r[:, 4:5], AF.Ln, bias=C.epsc[:, 0:1]), S.q_ss)
                    sc.activation(r[:, 5:6], r[:, 6:7], AF.Exp, scale=-0.5).then_inc(S.q_c, 1)


_FFN_ADV = {"q_lnf": 1024, "q_upw": 2048, "q_upm": 130, "q_upe": 130, "q_ups": 2080,
            "q_f1w": 512, "q_f1m": 34, "q_f1e": 34, "q_f1s": 544,
            "q_exw": 512, "q_exm": 34, "q_exe": 34, "q_exs": 544,
            "q_dnw": 1024, "q_dnw2": 1024, "q_dnm": 66, "q_dne": 66, "q_dns": 1056,
            "q_lg": 2, "q_lge": 2, "q_lt": 2, "q_rwc": 2, "q_rwr": 6, "q_rwb": 2,
            "q_hl": 32, "q_ll": 32, "q_ml": 32, "q_ep": 256, "q_epo": 160,
            "q_a": 8, "q_b": 8, "q_c": 8, "q_d": 10, "q_gb": 32,
            "q_vs": 360, "q_vs2": 48, "q_ss2": 8}


def emit_ffn(nc, C, xin, xout, name, final_wait=False):
  if not hasattr(C, "ffn_sems"):
      C.ffn_sems = mk_sems(nc, C.st, "ffnS", list(_FFN_ADV))
      C.ffn_calls = 0
  s = C.ffn_sems
  FB = {q: C.ffn_calls * _FFN_ADV[q] for q in _FFN_ADV}
  C.ffn_calls += 1
  with contextlib.ExitStack() as _es:
    C.lnfm = _es.enter_context(nc.sbuf_tensor([128, DTL * 512], BF16))
    C.wbuf = _es.enter_context(nc.sbuf_tensor([128, 2 * 4096], BF16))
    C.wbig = _es.enter_context(nc.sbuf_tensor([128, 2 * 8192], BF16))
    C.w2buf = _es.enter_context(nc.sbuf_tensor([128, 2 * 2048], BF16))
    C.hbuf = _es.enter_context(nc.sbuf_tensor([128, ITL * 512], BF16))
    C.xlat = _es.enter_context(nc.sbuf_tensor([128, LTL * 512], BF16))
    C.moe = _es.enter_context(nc.sbuf_tensor([128, LTL * 512], BF16))
    C.upst = _es.enter_context(nc.sbuf_tensor([128, 2 * 512], BF16))
    C.f1st = _es.enter_context(nc.sbuf_tensor([128, 2 * 512], BF16))
    C.exst = _es.enter_context(nc.sbuf_tensor([128, 2 * 512], BF16))
    C.dnst = _es.enter_context(nc.sbuf_tensor([128, 2 * 512], BF16))
    C.scr = _es.enter_context(nc.sbuf_tensor([128, 512], F32))
    C.lgf = _es.enter_context(nc.sbuf_tensor([128, 512], F32))
    C.ltm = _es.enter_context(nc.sbuf_tensor([128, 32], F32))
    C.prs = _es.enter_context(nc.sbuf_tensor([128, 28], F32))
    C.rwc = _es.enter_context(nc.sbuf_tensor([128, 4], F32))
    C.rwrow = _es.enter_context(nc.sbuf_tensor([1, 512], F32))
    C.rwbc = _es.enter_context(nc.sbuf_tensor([128, 512], F32))
    C.ps = _es.enter_context(nc.psum_tensor([128, 2 * 512], F32))
    C.psg = _es.enter_context(nc.psum_tensor([128, 512], F32))
    C.pst = _es.enter_context(nc.psum_tensor([128, 512], F32))
    C.pst2 = _es.enter_context(nc.psum_tensor([128, 512], F32))
    with nc.Block(name + "_c") as blk:
        @blk.sync
        def _(sy):
            for q in (s.q_upm, s.q_f1m, s.q_exm, s.q_dnm):
                sy.sem_inc(q, 2)
            for q in (s.q_upe, s.q_f1e, s.q_exe, s.q_dne):
                sy.sem_inc(q, 2)
            for q in (s.q_ups, s.q_f1s, s.q_exs, s.q_dns):
                sy.sem_inc(q, 32)
        @blk.gpsimd
        def _(g):
            g.memset(C.lgf[:, :], 0.0)

    with nc.Block(name) as blk:
        @blk.sync
        def _(sy):
            for t in range(2):
                sy.wait_ge(s.q_dnm, FB["q_dnm"] + t * 32 + 2)
                for dt in range(DTL):
                    sy.dma_start_transpose(
                        C.lnfm[:, dt * 512:(dt + 1) * 512],
                        C.sp_lnt[t * 512:(t + 1) * 512, dt * 128:(dt + 1) * 128]).then_inc(s.q_lnf, 16)
                for k in range(64):
                    b = k % 2
                    sy.wait_ge(s.q_upe, FB["q_upe"] + t * 64 + k + 3)
                    sy.dma_start(out=C.sp_h[:, k * 512:(k + 1) * 512],
                                 in_=C.upst[:, b * 512:(b + 1) * 512]).then_inc(s.q_ups, 16)
                sy.wait_ge(s.q_ups, FB["q_ups"] + 32 + (t * 64 + 64) * 16)
                sy.dma_start(out=C.hbuf[:, :], in_=C.sp_h[:, :]).then_inc(s.q_hl, 16)
                for k in range(16):
                    b = k % 2
                    sy.wait_ge(s.q_f1e, FB["q_f1e"] + t * 16 + k + 3)
                    sy.dma_start(out=C.sp_lat[:, k * 512:(k + 1) * 512],
                                 in_=C.f1st[:, b * 512:(b + 1) * 512]).then_inc(s.q_f1s, 16)
                sy.wait_ge(s.q_f1s, FB["q_f1s"] + 32 + (t * 16 + 16) * 16)
                sy.dma_start(out=C.xlat[:, :], in_=C.sp_lat[:, :]).then_inc(s.q_ll, 16)
                for k in range(16):
                    b = k % 2
                    sy.wait_ge(s.q_exe, FB["q_exe"] + t * 16 + k + 3)
                    sy.dma_start(out=C.sp_moe[:, k * 512:(k + 1) * 512],
                                 in_=C.exst[:, b * 512:(b + 1) * 512]).then_inc(s.q_exs, 16)
                sy.wait_ge(s.q_exs, FB["q_exs"] + 32 + (t * 16 + 16) * 16)
                sy.dma_start(out=C.moe[:, :], in_=C.sp_moe[:, :]).then_inc(s.q_ml, 16)
                for k in range(32):
                    b = k % 2
                    sy.wait_ge(s.q_dne, FB["q_dne"] + t * 32 + k + 3)
                    sy.dma_start(out=C.sp_os[k * 128:(k + 1) * 128, t * 512:(t + 1) * 512],
                                 in_=C.dnst[:, b * 512:(b + 1) * 512]).then_inc(s.q_dns, 16)
            sy.wait_ge(s.q_dns, FB["q_dns"] + 32 + 16 * 64)

        @blk.gpsimd
        def _(g):
            for t in range(2):
                g.wait_ge(s.q_exm, FB["q_exm"] + 2 + 16 * t)
                for kk in range(64):
                    b = kk % 2
                    g.wait_ge(s.q_upm, FB["q_upm"] + 1 + 64 * t + kk)
                    g.dma_start(out=C.wbuf[:, b * 4096:(b + 1) * 4096],
                                in_=C.wup[kk, :, :]).then_inc(s.q_upw, 16)
                g.wait_ge(s.q_upm, FB["q_upm"] + 66 + 64 * t)
                for kk in range(16):
                    b = kk % 2
                    g.wait_ge(s.q_f1m, FB["q_f1m"] + 1 + 16 * t + kk)
                    g.dma_start(out=C.wbuf[:, b * 4096:(b + 1) * 4096],
                                in_=C.wf1[kk, :, :]).then_inc(s.q_f1w, 16)
                g.wait_ge(s.q_f1m, FB["q_f1m"] + 18 + 16 * t)
                for kk in range(16):
                    b = kk % 2
                    g.wait_ge(s.q_exm, FB["q_exm"] + 1 + 16 * t + kk)
                    g.dma_start(out=C.wbuf[:, b * 4096:b * 4096 + 2048],
                                in_=C.wex[kk, :, :]).then_inc(s.q_exw, 16)
                for kk in range(32):
                    b = kk % 2
                    g.wait_ge(s.q_dnm, FB["q_dnm"] + 1 + 32 * t + kk)
                    g.dma_start(out=C.wbig[:, b * 8192:(b + 1) * 8192],
                                in_=C.wdn[kk, :, :]).then_inc(s.q_dnw, 16)
                    g.dma_start(out=C.w2buf[:, b * 2048:(b + 1) * 2048],
                                in_=C.wf2[kk, :, :]).then_inc(s.q_dnw2, 16)

        @blk.tensor
        def _(tn):
          with contextlib.ExitStack() as es:
            t_lnf = WR(tn, es, name + "_t_lnf", 512 + FB["q_lnf"], 512)
            t_upw = WR(tn, es, name + "_t_upw", 16 + FB["q_upw"], 16)
            t_upe = WR(tn, es, name + "_t_upe", 1 + FB["q_upe"], 1)
            t_lge = WR(tn, es, name + "_t_lge", 1 + FB["q_lge"], 1)
            t_f1w = WR(tn, es, name + "_t_f1w", 16 + FB["q_f1w"], 16)
            t_f1e = WR(tn, es, name + "_t_f1e", 1 + FB["q_f1e"], 1)
            t_rwc = WR(tn, es, name + "_t_rwc", 1 + FB["q_rwc"], 1)
            t_rwr = WR(tn, es, name + "_t_rwr", 2 + FB["q_rwr"], 3)
            t_exw = WR(tn, es, name + "_t_exw", 16 + FB["q_exw"], 16)
            t_exe = WR(tn, es, name + "_t_exe", 1 + FB["q_exe"], 1)
            t_ll = WR(tn, es, name + "_t_ll", 16 + FB["q_ll"], 16)
            t_dnw = WR(tn, es, name + "_t_dnw", 16 + FB["q_dnw"], 16)
            t_dnw2 = WR(tn, es, name + "_t_dnw2", 16 + FB["q_dnw2"], 16)
            t_dne = WR(tn, es, name + "_t_dne", 1 + FB["q_dne"], 1)
            t_hl = WR(tn, es, name + "_t_hl", 16 + FB["q_hl"], 16)
            t_ml = WR(tn, es, name + "_t_ml", 16 + FB["q_ml"], 16)
            t_g1 = WR(tn, es, name + "_t_g1", 66 + FB["q_upe"], 64)
            t_g2 = WR(tn, es, name + "_t_g2", 18 + FB["q_f1e"], 16)
            t_g3 = WR(tn, es, name + "_t_g3", 18 + FB["q_exe"], 16)
            with tn.Fori(0, 2) as t:
                t_lnf.wait(s.q_lnf)
                for dt in range(DTL):
                    ins = tn.matmul(C.psg[0:8, 0:512], C.wg[:, dt * 8:(dt + 1) * 8],
                                    C.lnfm[:, dt * 512:(dt + 1) * 512],
                                    start=(dt == 0), stop=(dt == DTL - 1))
                ins.then_inc(s.q_lg, 1)
                with tn.Fori(0, 32) as j:
                    for b in range(2):
                        t_upw.wait(s.q_upw)
                        t_upe.wait(s.q_upe)
                        for dt in range(DTL):
                            ins = tn.matmul(C.ps[:, b * 512:(b + 1) * 512],
                                            C.wbuf[:, b * 4096 + dt * 128:b * 4096 + (dt + 1) * 128],
                                            C.lnfm[:, dt * 512:(dt + 1) * 512],
                                            start=(dt == 0), stop=(dt == DTL - 1))
                        ins.then_inc(s.q_upm, 1)
                t_lge.wait(s.q_lge)
                for sb in range(4):
                    ins = tn.transpose(C.pst[:, sb * 128:(sb + 1) * 128],
                                       C.lgf[:, sb * 128:(sb + 1) * 128], C.id32[:, :])
                ins.then_inc(s.q_lt, 1)
                t_g1.wait(s.q_upe)
                with tn.Fori(0, 8) as j:
                    for b in range(2):
                        t_f1w.wait(s.q_f1w)
                        t_f1e.wait(s.q_f1e)
                        for dt in range(DTL):
                            ins = tn.matmul(C.ps[:, b * 512:(b + 1) * 512],
                                            C.wbuf[:, b * 4096 + dt * 128:b * 4096 + (dt + 1) * 128],
                                            C.lnfm[:, dt * 512:(dt + 1) * 512],
                                            start=(dt == 0), stop=(dt == DTL - 1))
                        ins.then_inc(s.q_f1m, 1)
                t_rwc.wait(s.q_rwc)
                for sb in range(4):
                    ins = tn.matmul(C.pst2[0:1, sb * 128:(sb + 1) * 128], C.rwc[:, sb:sb + 1],
                                    C.id32[:, 0:128], start=True, stop=True)
                ins.then_inc(s.q_rwr, 1)
                t_rwr.wait(s.q_rwr)
                for sb in range(4):
                    ins = tn.matmul(C.psg[0:128, sb * 128:(sb + 1) * 128], C.ones32[0:1, 0:128],
                                    C.rwrow[0:1, sb * 128:(sb + 1) * 128], start=True, stop=True)
                ins.then_inc(s.q_rwr, 1)
                t_g2.wait(s.q_f1e)
                t_ll.wait(s.q_ll)
                with tn.Fori(0, 8) as j:
                    for b in range(2):
                        t_exw.wait(s.q_exw)
                        t_exe.wait(s.q_exe)
                        for lt in range(LTL):
                            ins = tn.matmul(C.ps[:, b * 512:(b + 1) * 512],
                                            C.wbuf[:, b * 4096 + lt * 128:b * 4096 + (lt + 1) * 128],
                                            C.xlat[:, lt * 512:(lt + 1) * 512],
                                            start=(lt == 0), stop=(lt == LTL - 1))
                        ins.then_inc(s.q_exm, 1)
                t_g3.wait(s.q_exe)
                t_hl.wait(s.q_hl)
                t_ml.wait(s.q_ml)
                with tn.Fori(0, 16) as j:
                    for b in range(2):
                        t_dnw.wait(s.q_dnw)
                        t_dnw2.wait(s.q_dnw2)
                        t_dne.wait(s.q_dne)
                        for it in range(ITL):
                            tn.matmul(C.ps[:, b * 512:(b + 1) * 512],
                                      C.wbig[:, b * 8192 + it * 128:b * 8192 + (it + 1) * 128],
                                      C.hbuf[:, it * 512:(it + 1) * 512],
                                      start=(it == 0), stop=False)
                        for lt in range(LTL):
                            ins = tn.matmul(C.ps[:, b * 512:(b + 1) * 512],
                                            C.w2buf[:, b * 2048 + lt * 128:b * 2048 + (lt + 1) * 128],
                                            C.moe[:, lt * 512:(lt + 1) * 512],
                                            start=False, stop=(lt == LTL - 1))
                        ins.then_inc(s.q_dnm, 1)

        @blk.vector
        def _(v):
            nvs = 0
            def lk(ins):
                nonlocal nvs
                ins.then_inc(s.q_vs, 1)
                nvs += 1
                v.wait_ge(s.q_vs, FB["q_vs"] + nvs)
            for t in range(2):
                v.wait_ge(s.q_lg, FB["q_lg"] + t + 1)
                v.tensor_copy(C.lgf[0:8, :], C.psg[0:8, 0:512]).then_inc(s.q_lge, 1)
                for k in range(64):
                    b = k % 2
                    kk = t * 64 + k
                    v.wait_ge(s.q_upm, FB["q_upm"] + 3 + kk)
                    v.wait_ge(s.q_ups, FB["q_ups"] + 16 * (kk + 1))
                    lk(v.tensor_relu(C.scr[:, 0:512], C.ps[:, b * 512:(b + 1) * 512]))
                    v.tensor_mul(C.upst[:, b * 512:(b + 1) * 512],
                                 C.scr[:, 0:512], C.scr[:, 0:512]).then_inc(s.q_upe, 1)
                v.wait_ge(s.q_lt, FB["q_lt"] + t + 1)
                for sb in range(4):
                    lk(v.tensor_copy(C.ltm[:, sb * 8:(sb + 1) * 8],
                                     C.pst[:, sb * 128:sb * 128 + 8]))
                for sb in range(4):
                    for ci, (i, j2) in enumerate(PAIRS):
                        lk(v.tensor_add(C.prs[:, ci:ci + 1], C.ltm[:, sb * 8 + i:sb * 8 + i + 1],
                                        C.ltm[:, sb * 8 + j2:sb * 8 + j2 + 1]))
                    ins = v.reduce_max(C.rwc[:, sb:sb + 1], C.prs[:, 0:28], axis=AX)
                ins.then_inc(s.q_rwc, 1)
                for k in range(16):
                    b = k % 2
                    kk = t * 16 + k
                    v.wait_ge(s.q_f1m, FB["q_f1m"] + 3 + kk)
                    v.wait_ge(s.q_f1s, FB["q_f1s"] + 16 * (kk + 1))
                    v.tensor_copy(C.f1st[:, b * 512:(b + 1) * 512],
                                  C.ps[:, b * 512:(b + 1) * 512]).then_inc(s.q_f1e, 1)
                v.wait_ge(s.q_rwr, FB["q_rwr"] + 1 + 3 * t)
                v.tensor_copy(C.rwrow[0:1, :], C.pst2[0:1, 0:512]).then_inc(s.q_rwr, 1)
                v.wait_ge(s.q_rwr, FB["q_rwr"] + 3 + 3 * t)
                v.tensor_copy(C.rwbc[:, :], C.psg[0:128, 0:512]).then_inc(s.q_rwb, 1)
                for k in range(16):
                    b = k % 2
                    kk = t * 16 + k
                    v.wait_ge(s.q_exm, FB["q_exm"] + 3 + kk)
                    v.wait_ge(s.q_exs, FB["q_exs"] + 16 * (kk + 1))
                    v.tensor_mul(C.exst[:, b * 512:(b + 1) * 512],
                                 C.ps[:, b * 512:(b + 1) * 512], C.rwbc[:, 0:512]).then_inc(s.q_exe, 1)
                for k in range(32):
                    b = k % 2
                    kk = t * 32 + k
                    v.wait_ge(s.q_dnm, FB["q_dnm"] + 3 + kk)
                    v.wait_ge(s.q_dns, FB["q_dns"] + 16 * (kk + 1))
                    v.tensor_copy(C.dnst[:, b * 512:(b + 1) * 512],
                                  C.ps[:, b * 512:(b + 1) * 512]).then_inc(s.q_dne, 1)


  with contextlib.ExitStack() as _es:
    C.otm = _es.enter_context(nc.sbuf_tensor([128, 2 * DM], BF16))
    C.xa = _es.enter_context(nc.sbuf_tensor([128, 2 * DM], F32))
    C.lnw = _es.enter_context(nc.sbuf_tensor([128, 2 * DM], F32))
    C.sq = _es.enter_context(nc.sbuf_tensor([128, DM], F32))
    C.sq2 = _es.enter_context(nc.sbuf_tensor([128, DM], F32))
    C.gbc = _es.enter_context(nc.sbuf_tensor([128, DM], F32))
    C.bbc = _es.enter_context(nc.sbuf_tensor([128, DM], F32))
    with nc.Block(name + "_eload") as blk:
        @blk.sync
        def _(sy):
            sy.dma_start(out=C.gbc[:, :], in_=C.gbc_in[:, :]).then_inc(s.q_gb, 16)
            sy.dma_start(out=C.bbc[:, :], in_=C.bbc_in[:, :]).then_inc(s.q_gb, 16)
            sy.wait_ge(s.q_gb, FB["q_gb"] + 32)
    with nc.Block(name + "_ec") as blk:
        @blk.sync
        def _(sy):
            sy.sem_inc(s.q_d, 2)
            sy.sem_inc(s.q_epo, 32)

    with nc.Block(name + "_e") as blk:
        @blk.sync
        def _(sy):
            for k in range(8):
                b = k % 2
                sy.wait_ge(s.q_d, FB["q_d"] + k + 1)
                sy.dma_start_transpose(
                    C.otm[:, b * DM:(b + 1) * DM],
                    C.sp_os[0:DM, k * 128:(k + 1) * 128]).then_inc(s.q_ep, 16)
                sy.dma_start(out=C.xa[:, b * DM:(b + 1) * DM],
                             in_=xin[k * 128:(k + 1) * 128, :]).then_inc(s.q_ep, 16)
                sy.wait_ge(s.q_d, FB["q_d"] + k + 3)
                sy.dma_start(out=xout[k * 128:(k + 1) * 128, :],
                             in_=C.lnw[:, b * DM:(b + 1) * DM]).then_inc(s.q_epo, 16)
            sy.wait_ge(s.q_epo, FB["q_epo"] + 32 + 16 * 8)

        @blk.vector
        def _(v):
          with contextlib.ExitStack() as es:
            ve = WR(v, es, name + "_ve2", 1 + FB["q_vs2"], 1)
            v_ep = WR(v, es, name + "_v_ep", 32 + FB["q_ep"], 32)
            v_a = WR(v, es, name + "_v_a2", 1 + FB["q_a"], 1)
            v_c = WR(v, es, name + "_v_c2", 1 + FB["q_c"], 1)
            v_eo = WR(v, es, name + "_v_eo", 16 + FB["q_epo"], 16)
            with v.Fori(0, 4) as tt:
                for b in range(2):
                    o = C.otm[:, b * DM:(b + 1) * DM]
                    r = C.red[:, b * 8:(b + 1) * 8]
                    v_ep.wait(s.q_ep)
                    ve.link(v.reduce_sum(r[:, 0:1], o, axis=AX), s.q_vs2)
                    v_a.wait(s.q_a)
                    ve.link(v.tensor_scalar_mul(r[:, 2:3], r[:, 0:1], 1.0 / DM), s.q_vs2)
                    ve.link(v.tensor_mul(r[:, 3:4], r[:, 2:3], r[:, 2:3]), s.q_vs2)
                    v.scalar_tensor_tensor(r[:, 4:5], r[:, 1:2], 1.0 / DM, r[:, 3:4],
                                           ALU.mult, ALU.subtract).then_inc(s.q_b, 1)
                    v_c.wait(s.q_c)
                    ve.link(v.tensor_scalar(C.sq[:, :], o, r[:, 2:3], r[:, 5:6],
                                            ALU.subtract, ALU.mult), s.q_vs2)
                    ve.link(v.tensor_mul(C.sq[:, :], C.sq[:, :], C.gbc[:, :]), s.q_vs2)
                    ve.link(v.tensor_add(C.sq[:, :], C.sq[:, :], C.bbc[:, :]), s.q_vs2)
                    v_eo.wait(s.q_epo)
                    v.scalar_tensor_tensor(C.lnw[:, b * DM:(b + 1) * DM], C.sq[:, :], 0.5,
                                           C.xa[:, b * DM:(b + 1) * DM],
                                           ALU.mult, ALU.add).then_inc(s.q_d, 1)

        @blk.scalar
        def _(sc):
          with contextlib.ExitStack() as es:
            se = WR(sc, es, name + "_se2", 1 + FB["q_ss2"], 1)
            s_ep = WR(sc, es, name + "_s_ep", 32 + FB["q_ep"], 32)
            s_b = WR(sc, es, name + "_s_b2", 1 + FB["q_b"], 1)
            with sc.Fori(0, 4) as tt:
                for b in range(2):
                    o = C.otm[:, b * DM:(b + 1) * DM]
                    r = C.red[:, b * 8:(b + 1) * 8]
                    s_ep.wait(s.q_ep)
                    sc.activation(C.sq2[:, :], o, AF.Square, accum_out=r[:, 1:2]).then_inc(s.q_a, 1)
                    s_b.wait(s.q_b)
                    se.link(sc.activation(r[:, 6:7], r[:, 4:5], AF.Ln, bias=C.epsc[:, 0:1]), s.q_ss2)
                    sc.activation(r[:, 5:6], r[:, 6:7], AF.Exp, scale=-0.5).then_inc(s.q_c, 1)

    if final_wait:
        with nc.Block(name + "_fw") as blk:
            @blk.sync
            def _(sy):
                sy.wait_ge(s.q_epo, FB["q_epo"] + 32 + 16 * 8)


def build():
    nc = bass.Bass("TRN2", target_bir_lowering=False, detect_race_conditions=False)
    C = _C()

    x_in = nc.dram_tensor("x", [NT, DM], F32, kind="ExternalInput").ap()
    out = nc.dram_tensor("out", [NT, DM], F32, kind="ExternalOutput").ap()
    C.wup = nc.dram_tensor("wup", [ITL, 128, DM], BF16, kind="ExternalInput").ap()
    C.wdn = nc.dram_tensor("wdn", [DTL, 128, FI], BF16, kind="ExternalInput").ap()
    C.wf1 = nc.dram_tensor("wf1", [LTL, 128, DM], BF16, kind="ExternalInput").ap()
    C.wex = nc.dram_tensor("wex", [LTL, 128, LAT], BF16, kind="ExternalInput").ap()
    C.wf2 = nc.dram_tensor("wf2", [DTL, 128, LAT], BF16, kind="ExternalInput").ap()
    wg_in = nc.dram_tensor("wgt", [128, DTL * 8], F32, kind="ExternalInput").ap()
    C.wip = nc.dram_tensor("wip", [145, 128, DM], BF16, kind="ExternalInput").ap()
    C.wout = nc.dram_tensor("wout", [DTL, 128, FI], BF16, kind="ExternalInput").ap()
    cw_in = nc.dram_tensor("cw", [128, NCV * 4], F32, kind="ExternalInput").ap()
    C.cwt = nc.dram_tensor("cwt", [NCV, 128, 4], F32, kind="ExternalInput").ap()
    C.cbt = nc.dram_tensor("cbt", [NCV, 128, 1], F32, kind="ExternalInput").ap()
    onescol_in = nc.dram_tensor("onescol", [128, 1], F32, kind="ExternalInput").ap()
    cb_in = nc.dram_tensor("cb", [128, NCV], F32, kind="ExternalInput").ap()
    dtb_in = nc.dram_tensor("dtb", [128, 1], F32, kind="ExternalInput").ap()
    alog_in = nc.dram_tensor("alog", [128, 1], F32, kind="ExternalInput").ap()
    dbc_in = nc.dram_tensor("dbc", [128, 128], F32, kind="ExternalInput").ap()
    C.gbc_in = nc.dram_tensor("gbc", [128, DM], F32, kind="ExternalInput").ap()
    C.bbc_in = nc.dram_tensor("bbc", [128, DM], F32, kind="ExternalInput").ap()
    C.nwt_in = nc.dram_tensor("nwt", [128, ITL], F32, kind="ExternalInput").ap()
    id16_in = nc.dram_tensor("id16", [128, 128], BF16, kind="ExternalInput").ap()
    id32_in = nc.dram_tensor("id32", [128, 128], F32, kind="ExternalInput").ap()
    ones32_in = nc.dram_tensor("ones32", [1, 128], F32, kind="ExternalInput").ap()
    strict_in = nc.dram_tensor("strictm", [128, 128], BF16, kind="ExternalInput").ap()
    negs_in = nc.dram_tensor("negstrict", [128, 128], BF16, kind="ExternalInput").ap()
    qtril_in = nc.dram_tensor("qtril", [128, 128], BF16, kind="ExternalInput").ap()
    mask_in = nc.dram_tensor("pmask", [128, 1], F32, kind="ExternalInput").ap()
    nonce_in = nc.dram_tensor("nonce", [1, 1], I32, kind="ExternalInput").ap()

    C.sp_lnt = nc.dram_tensor("sp_lnt", [NT, DM], BF16).ap()
    C.sp_x1 = nc.dram_tensor("sp_x1", [NT, DM], F32).ap()
    C.sp_x2 = nc.dram_tensor("sp_x2", [NT, DM], F32).ap()
    C.sp_h = nc.dram_tensor("sp_h", [128, ITL * 512], BF16).ap()
    C.sp_lat = nc.dram_tensor("sp_lat", [128, LTL * 512], BF16).ap()
    C.sp_moe = nc.dram_tensor("sp_moe", [128, LTL * 512], BF16).ap()
    C.sp_os = nc.dram_tensor("sp_os", [DM, NT], BF16).ap()
    C.sp_gate = nc.dram_tensor("sp_gate", [FI, NT], BF16).ap()
    C.sp_hsf = nc.dram_tensor("sp_hsf", [FI, NT], BF16).ap()
    C.sp_dtraw = nc.dram_tensor("sp_dtraw", [128, NT], F32).ap()
    C.sp_X = nc.dram_tensor("sp_X", [NZ, 128, FI], BF16).ap()
    C.sp_bc = nc.dram_tensor("sp_bc", [128, 16 * NT], F32).ap()
    C.sp_pack = nc.dram_tensor("sp_pack", [NZ, 128, 2048], BF16).ap()
    C.sp_dtv = nc.dram_tensor("sp_dtv", [128, NT], F32).ap()
    C.sp_pc3 = nc.dram_tensor("sp_pc3", [128, NCV * 3], F32).ap()
    C.sp_hout = nc.dram_tensor("sp_hout", [128, NCV * 3], F32).ap()
    C.sp_outg = nc.dram_tensor("sp_outg", [NT, FI], BF16).ap()
    C.sp_attn = nc.dram_tensor("sp_attn", [DM, NT], BF16).ap()
    C.shx = nc.dram_tensor("shx", [2 * 128, 8448], F32, kind="Internal", addr_space="Shared").ap()
    C.shf = nc.dram_tensor("shf", [1, 8], I32, kind="Internal", addr_space="Shared").ap()

    st = contextlib.ExitStack()
    C.st = st
    with st:
        C.q_pro = st.enter_context(nc.semaphore("q_pro"))
        C.q_prog = st.enter_context(nc.semaphore("q_prog"))

        C.wg = st.enter_context(nc.sbuf_tensor([128, DTL * 8], BF16))
        C.cw = st.enter_context(nc.sbuf_tensor([128, NCV * 4], F32))
        C.cb = st.enter_context(nc.sbuf_tensor([128, NCV], F32))
        C.cbn = st.enter_context(nc.sbuf_tensor([128, NCV], F32))
        C.dtb = st.enter_context(nc.sbuf_tensor([128, 1], F32))
        C.aneg = st.enter_context(nc.sbuf_tensor([128, 1], F32))
        C.dbc = st.enter_context(nc.sbuf_tensor([128, 128], F32))
        C.id16 = st.enter_context(nc.sbuf_tensor([128, 128], BF16))
        C.id32 = st.enter_context(nc.sbuf_tensor([128, 128], F32))
        C.ones32 = st.enter_context(nc.sbuf_tensor([1, 128], F32))
        C.onescol = st.enter_context(nc.sbuf_tensor([128, 1], F32))
        C.strict = st.enter_context(nc.sbuf_tensor([128, 128], BF16))
        C.negstrict = st.enter_context(nc.sbuf_tensor([128, 128], BF16))
        C.qtril = st.enter_context(nc.sbuf_tensor([128, 128], BF16))
        C.pmask = st.enter_context(nc.sbuf_tensor([128, 1], F32))
        C.noncesb = st.enter_context(nc.sbuf_tensor([1, 1], I32))
        C.flsb = st.enter_context(nc.sbuf_tensor([1, 1], I32))

        C.red = st.enter_context(nc.sbuf_tensor([128, 16], F32))
        C.epsc = st.enter_context(nc.sbuf_tensor([128, 1], F32))

        with nc.Block("prol") as blk:
            @blk.sync
            def _(sy):
                for dst, srcap in ((C.cw, cw_in), (C.cb, cb_in), (C.dtb, dtb_in),
                                   (C.aneg, alog_in), (C.dbc, dbc_in), (C.id16, id16_in),
                                   (C.id32, id32_in), (C.ones32, ones32_in), (C.onescol, onescol_in), (C.strict, strict_in),
                                   (C.qtril, qtril_in), (C.pmask, mask_in), (C.noncesb, nonce_in),
                                   (C.negstrict, negs_in)):
                    sy.dma_start(out=dst[:, :], in_=srcap[:, :]).then_inc(C.q_pro, 16)
                sy.wait_ge(C.q_pro, 14 * 16)
            @blk.gpsimd
            def _(g):
                g.memset(C.epsc[:, :], EPS)
                g.dma_start(out=C.wg[:, :], in_=wg_in[:, :]).then_inc(C.q_prog, 16)
                g.wait_ge(C.q_prog, 16)
                g.wait_ge(C.q_pro, 14 * 16)
                C.gnv = g.value_load(C.noncesb[0:1, 0:1])
                pid = g.partition_id()
                par = pid % 2
                oth = 1 - par
                C.e_par128 = par * 128
                C.e_par2 = par * 2
                C.e_par2p1 = par * 2 + 1
                C.e_oth128 = oth * 128
                C.e_oth2 = oth * 2
                C.e_oth2p1 = oth * 2 + 1
        with nc.Block("prol2") as blk:
            @blk.scalar
            def _(sc):
                sc.activation(C.aneg[:, :], C.aneg[:, :], AF.Exp)
            @blk.vector
            def _(v):
                v.tensor_scalar_mul(C.cbn[:, :], C.cb[:, :], -1.0)
        with nc.Block("prol3") as blk:
            @blk.vector
            def _(v):
                v.tensor_scalar_mul(C.aneg[:, :], C.aneg[:, :], -1.0)

        C.outap = out
        emit_ln(nc, C, x_in, "ln1")
        emit_ffn(nc, C, x_in, C.sp_x1 if PHASES > 1 else out, "ffn1",
                 final_wait=(PHASES == 1))

        def _recycle(fn, *args, **kw):
            snap = nc.snapshot_sems()
            old = C.st
            C.st = None
            fn(*args, **kw)
            C.st = old
            nc.clear_and_free_semaphores(nc.allocated_since(snap))
            nc.all_engine_barrier()

        if PHASES >= 2:
            emit_ln(nc, C, C.sp_x1, "ln2")
            _recycle(emit_inproj, nc, C)
        if PHASES >= 4:
            _recycle(emit_prep, nc, C)
        if PHASES >= 5:
            with contextlib.ExitStack() as sscope:
                C.prevT = sscope.enter_context(nc.sbuf_tensor([128, FI], F32))
                _recycle(emit_ssdA, nc, C)
                if PHASES >= 6:
                    _recycle(emit_exch2, nc, C)
                if PHASES >= 7:
                    _recycle(emit_ssdB, nc, C)
                if PHASES == 5 or PHASES == 6:
                    _dump_prev(nc, C, out)
        if PHASES >= 8:
            _recycle(emit_outproj, nc, C)
        if PHASES >= 9:
            emit_ln(nc, C, C.sp_x2, "ln3")
            emit_ffn(nc, C, C.sp_x2, out, "ffn2", final_wait=True)
        if PHASES < 9:
            _dump_dev(nc, C, out)

    return nc


def _dump_prev(nc, C, out):
    qq = C.st.enter_context(nc.semaphore("q_dprev"))
    with nc.Block("dprev") as blk:
        @blk.gpsimd
        def _(g):
            g.dma_start(out=C.sp_outg[0:128, 0:FI], in_=C.prevT[:, :]).then_inc(qq, 16)
            g.wait_ge(qq, 16)


def _dump_dev(nc, C, out):
    qq = C.st.enter_context(nc.semaphore("q_dump"))
    with nc.Block("dump") as blk:
        @blk.gpsimd
        def _(g):
            n = 0
            if PHASES in (2, 3):
                g.dma_start(out=out[0:1024, 0:1024], in_=C.sp_hsf[0:1024, :]).then_inc(qq, 16)
                g.dma_start(out=out[0:1024, 1024:2048], in_=C.sp_gate[0:1024, :]).then_inc(qq, 16)
                g.dma_start(out=out[0:128, 2048:3072], in_=C.sp_dtraw[:, :]).then_inc(qq, 16)
                g.dma_start(out=out[0:128, 3072:4096], in_=C.sp_bc[:, 0:1024]).then_inc(qq, 16)
                n = 4
                if PHASES == 3:
                    for z in range(NZ):
                        g.dma_start(out=out[128 + z * 64:128 + (z + 1) * 64, 2048:4096],
                                    in_=C.sp_X[z, 64:128, 0:2048]).then_inc(qq, 16)
                    n += 8
            elif PHASES == 4:
                g.dma_start(out=out[0:128, 0:2048], in_=C.sp_pack[0, :, :]).then_inc(qq, 16)
                g.dma_start(out=out[0:128, 2048:3072], in_=C.sp_dtv[:, :]).then_inc(qq, 16)
                g.dma_start(out=out[128:256, 0:4096], in_=C.sp_X[0, :, 0:4096]).then_inc(qq, 16)
                g.dma_start(out=out[256:384, 0:2048], in_=C.sp_pack[5, :, :]).then_inc(qq, 16)
                n = 4
            elif PHASES in (5, 6):
                g.dma_start(out=out[0:128, 0:4096], in_=C.sp_outg[0:128, 0:4096]).then_inc(qq, 16)
                g.dma_start(out=out[128:256, 0:4096], in_=C.sp_outg[0:128, 4096:8192]).then_inc(qq, 16)
                n = 2
            elif PHASES == 7:
                if DBG_SSD:
                    g.dma_start(out=out[512:1024, 0:4096],
                                in_=C.sp_outg[512:1024, 0:4096]).then_inc(qq, 16)
                else:
                    g.dma_start(out=out[0:1024, 0:4096], in_=C.sp_outg[:, 0:4096]).then_inc(qq, 16)
                n = 1
            elif PHASES == 8:
                g.dma_start(out=out[:, :], in_=C.sp_x2[:, :]).then_inc(qq, 16)
                n = 1
            g.wait_ge(qq, n * 16)




def emit_inproj(nc, C, name="ip"):
  s = mk_sems(nc, C.st, name, ["q_lnf", "q_w", "q_m", "q_e", "q_ge", "q_gs",
                               "q_cv", "q_cwl", "q_cbn", "q_cv2", "q_cv3", "q_hs",
                               "q_hss", "q_pcs", "q_bce", "q_bcs", "q_dte", "q_dts",
                               "q_vs"])
  with contextlib.ExitStack() as _es:
    lnip = _es.enter_context(nc.sbuf_tensor([128, DTL * 1024], BF16))
    wipb = _es.enter_context(nc.sbuf_tensor([128, 2 * 4096], BF16))
    esb = _es.enter_context(nc.sbuf_tensor([128, 2 * 1024], BF16))
    pre = _es.enter_context(nc.sbuf_tensor([128, 2 * 1024], F32))
    cv = _es.enter_context(nc.sbuf_tensor([128, 2 * 1024], F32))
    e1 = _es.enter_context(nc.sbuf_tensor([128, 1021], F32))
    r1 = _es.enter_context(nc.sbuf_tensor([128, 1021], F32))
    hsw = _es.enter_context(nc.sbuf_tensor([128, 2 * 1024], BF16))
    bce = _es.enter_context(nc.sbuf_tensor([128, 2 * 1024], F32))
    cwx = _es.enter_context(nc.sbuf_tensor([128, 2 * 4], F32))
    cbx = _es.enter_context(nc.sbuf_tensor([128, 2], F32))
    cbnx = _es.enter_context(nc.sbuf_tensor([128, 2], F32))
    ps = _es.enter_context(nc.psum_tensor([128, 2048], F32))

    with nc.Block(name + "_c") as blk:
        @blk.sync
        def _(sy):
            sy.sem_inc(s.q_m, 2)
            sy.sem_inc(s.q_e, 2)
            sy.sem_inc(s.q_ge, 2)
            sy.sem_inc(s.q_gs, 32)
            sy.sem_inc(s.q_bcs, 32)
            sy.sem_inc(s.q_pcs, 64)
            sy.sem_inc(s.q_cwl, 64)
            sy.sem_inc(s.q_hss, 32)
            sy.sem_inc(s.q_cv2, 2)
            for dt in range(DTL):
                sy.dma_start_transpose(
                    lnip[:, dt * 1024:(dt + 1) * 1024],
                    C.sp_lnt[0:1024, dt * 128:(dt + 1) * 128]).then_inc(s.q_lnf, 16)
            sy.wait_ge(s.q_lnf, DTL * 16)
        @blk.gpsimd
        def _(g):
            g.memset(hsw[:, 0:3], 0.0)
            g.memset(hsw[:, 1024:1027], 0.0)

    with nc.Block(name) as blk:
        @blk.gpsimd
        def _(g):
          with contextlib.ExitStack() as es:
            w_m = WR(g, es, name + "_w_m", 1, 1)
            for kk in range(145):
                b = kk % 2
                w_m.wait(s.q_m)
                g.dma_start(out=wipb[:, b * 4096:(b + 1) * 4096],
                            in_=C.wip[kk, :, :]).then_inc(s.q_w, 16)

        @blk.tensor
        def _(tn):
          with contextlib.ExitStack() as es:
            t_w = WR(tn, es, name + "_t_w", 16, 16)
            t_e = WR(tn, es, name + "_t_e", 1, 1)

            def mm_etile(b):
                t_w.wait(s.q_w)
                t_e.wait(s.q_e)
                for tch in range(2):
                    for dt in range(DTL):
                        ins = tn.matmul(
                            ps[:, b * 1024 + tch * 512:b * 1024 + (tch + 1) * 512],
                            wipb[:, b * 4096 + dt * 128:b * 4096 + (dt + 1) * 128],
                            lnip[:, dt * 1024 + tch * 512:dt * 1024 + (tch + 1) * 512],
                            start=(dt == 0), stop=(dt == DTL - 1))
                ins.then_inc(s.q_m, 1)

            with tn.Fori(0, 72) as j:
                for b in range(2):
                    mm_etile(b)
            mm_etile(0)

        @blk.vector
        def _(v):
          with contextlib.ExitStack() as es:
            v_m = WR(v, es, name + "_v_m", 3, 1)
            v_gs = WR(v, es, name + "_v_gs", 16, 16)
            v_pcs = WR(v, es, name + "_v_pcs", 32, 32)
            v_cwl = WR(v, es, name + "_v_cwl", 96, 32)
            v_cv3 = WR(v, es, name + "_v_cv3", 0, 1)
            v_hss = WR(v, es, name + "_v_hss", 16, 16)
            v_bcs = WR(v, es, name + "_v_bcs", 32, 32)
            ve = WR(v, es, name + "_ve", 1, 1)
            with v.Fori(0, 32) as j:
                for b in range(2):
                    v_m.wait(s.q_m)
                    v_gs.wait(s.q_gs)
                    v.tensor_copy(esb[:, b * 1024:(b + 1) * 1024],
                                  ps[:, b * 1024:(b + 1) * 1024]).then_inc(s.q_e, 1)
            with v.Fori(0, 32) as j:
                for b in range(2):
                    p0 = b * 1024
                    v_m.wait(s.q_m)
                    v_pcs.wait(s.q_pcs)
                    v.tensor_copy(pre[:, p0:p0 + 1024],
                                  ps[:, p0:p0 + 1024]).then_inc(s.q_e, 1)
                    v_cwl.wait(s.q_cwl)
                    v_cv3.wait(s.q_cv3)
                    ve.link(v.tensor_scalar_mul(cv[:, p0:p0 + 1021], pre[:, p0:p0 + 1021],
                                                cwx[:, 4 * b:4 * b + 1]), s.q_vs)
                    for tp in range(1, 4):
                        ve.link(v.scalar_tensor_tensor(cv[:, p0:p0 + 1021], pre[:, p0 + tp:p0 + tp + 1021],
                                                       cwx[:, 4 * b + tp:4 * b + tp + 1],
                                                       cv[:, p0:p0 + 1021], ALU.mult, ALU.add), s.q_vs)
                    v.engine_nop().then_inc(s.q_cv2, 1)
            with v.Fori(0, 8) as j:
                for b in range(2):
                    v_m.wait(s.q_m)
                    v_bcs.wait(s.q_bcs)
                    v.tensor_copy(bce[:, b * 1024:(b + 1) * 1024],
                                  ps[:, b * 1024:(b + 1) * 1024]).then_inc(s.q_e, 1)
            v_m.wait(s.q_m)
            v.wait_ge(s.q_bcs, 32 + 16 * 32)
            v.tensor_copy(bce[:, 0:1024], ps[:, 0:1024]).then_inc(s.q_e, 1)

        @blk.scalar
        def _(sc):
          with contextlib.ExitStack() as es:
            s_cv2 = WR(sc, es, name + "_s_cv2", 3, 1)
            s_cbn = WR(sc, es, name + "_s_cbn", 1, 1)
            for kk in range(64):
                b = kk % 2
                sc.wait_ge(s.q_e, 3 + kk)
                sc.dma_start(out=C.sp_gate[kk * 128:(kk + 1) * 128, :],
                             in_=esb[:, b * 1024:(b + 1) * 1024]).then_inc(s.q_gs, 16)
            with contextlib.ExitStack() as es2:
              s_hss = WR(sc, es2, name + "_s_hss", 16, 16)
              with sc.Fori(0, 32) as j:
                for b in range(2):
                    s_cv2.wait(s.q_cv2)
                    s_hss.wait(s.q_hss)
                    sc.activation(hsw[:, b * 1024 + 3:b * 1024 + 1024], cv[:, b * 1024:b * 1024 + 1021],
                                  AF.Silu, bias=cbx[:, b:b + 1]).then_inc(s.q_hs, 1)
                    sc.nop().then_inc(s.q_cv3, 1)
            for kk in range(16):
                b = kk % 2
                sc.wait_ge(s.q_e, 131 + kk)
                sc.dma_start(out=C.sp_bc[:, kk * 1024:(kk + 1) * 1024],
                             in_=bce[:, b * 1024:(b + 1) * 1024]).then_inc(s.q_bcs, 16)
                sc.dma_start(out=C.sp_hout[:, (64 + kk) * 3:(64 + kk) * 3 + 3],
                             in_=bce[:, b * 1024 + 1021:b * 1024 + 1024]).then_inc(s.q_bcs, 16)
            sc.wait_ge(s.q_e, 147)
            sc.dma_start(out=C.sp_dtraw[:, :], in_=bce[:, 0:1024]).then_inc(s.q_dts, 16)

        @blk.sync
        def _(sy):
            for kk in range(64):
                b = kk % 2
                sy.wait_ge(s.q_cv2, kk + 1)
                sy.dma_start(out=cwx[:, 4 * b:4 * (b + 1)],
                             in_=C.cwt[kk, :, :]).then_inc(s.q_cwl, 16)
                sy.dma_start(out=cbx[:, b:b + 1],
                             in_=C.cbt[kk, :, :]).then_inc(s.q_cwl, 16)
                sy.wait_ge(s.q_e, 67 + kk)
                sy.dma_start(out=C.sp_pc3[:, kk * 3:kk * 3 + 3],
                             in_=pre[:, b * 1024:b * 1024 + 3]).then_inc(s.q_pcs, 16)
                sy.dma_start(out=C.sp_hout[:, kk * 3:kk * 3 + 3],
                             in_=pre[:, b * 1024 + 1021:b * 1024 + 1024]).then_inc(s.q_pcs, 16)
                sy.wait_ge(s.q_hs, kk + 1)
                sy.dma_start(out=C.sp_hsf[kk * 128:(kk + 1) * 128, :],
                             in_=hsw[:, b * 1024:(b + 1) * 1024]).then_inc(s.q_hss, 16)
            sy.wait_ge(s.q_dts, 16)
            sy.wait_ge(s.q_hss, 32 + 64 * 16)
            sy.wait_ge(s.q_gs, 32 + 64 * 16)
            sy.wait_ge(s.q_bcs, 32 + 16 * 32)
            sy.wait_ge(s.q_pcs, 64 + 128 * 16)
            sy.wait_ge(s.q_cwl, 64 + 128 * 16)



def emit_xtrans(nc, C, name="xt"):
  s = mk_sems(nc, C.st, name, ["q_l", "q_tr", "q_te", "q_sp"])
  with contextlib.ExitStack() as _es:
    ht = _es.enter_context(nc.sbuf_tensor([128, 2 * 1024], BF16))
    trev = _es.enter_context(nc.sbuf_tensor([128, 2 * 1024], BF16))
    ptr = _es.enter_context(nc.psum_tensor([128, 2048], BF16))
    with nc.Block(name + "_c") as blk:
        @blk.sync
        def _(sy):
            sy.sem_inc(s.q_tr, 2)
            sy.sem_inc(s.q_sp, 2)
            sy.sem_inc(s.q_te, 2)
    with nc.Block(name) as blk:
        @blk.sync
        def _(sy):
            for kk in range(64):
                b = kk % 2
                sy.wait_ge(s.q_tr, kk + 1)
                sy.dma_start(out=ht[:, b * 1024:(b + 1) * 1024],
                             in_=C.sp_hsf[kk * 128:(kk + 1) * 128, :]).then_inc(s.q_l, 16)
                sy.wait_ge(s.q_te, kk + 3)
                for z in range(NZ):
                    sy.dma_start(out=C.sp_X[z, :, kk * 128:(kk + 1) * 128],
                                 in_=trev[:, b * 1024 + z * 128:b * 1024 + (z + 1) * 128]
                                 ).then_inc(s.q_sp, 16)
            sy.wait_ge(s.q_sp, 2 + 128 * 64)
        @blk.tensor
        def _(tn):
          with contextlib.ExitStack() as es:
            t_l = WR(tn, es, name + "_t_l", 16, 16)
            t_te = WR(tn, es, name + "_t_te", 1, 1)
            with tn.Fori(0, 32) as j:
                for b in range(2):
                    t_l.wait(s.q_l)
                    t_te.wait(s.q_te)
                    for z in range(NZ):
                        ins = tn.transpose(ptr[:, b * 1024 + z * 128:b * 1024 + (z + 1) * 128],
                                           ht[:, b * 1024 + z * 128:b * 1024 + (z + 1) * 128],
                                           C.id16[:, :])
                    ins.then_inc(s.q_tr, 1)
        @blk.vector
        def _(v):
          with contextlib.ExitStack() as es:
            v_tr = WR(v, es, name + "_v_tr", 3, 1)
            v_sp = WR(v, es, name + "_v_sp", 2, 128)
            with v.Fori(0, 32) as j:
                for b in range(2):
                    v_tr.wait(s.q_tr)
                    v_sp.wait(s.q_sp)
                    v.tensor_copy(trev[:, b * 1024:(b + 1) * 1024],
                                  ptr[:, b * 1024:(b + 1) * 1024]).then_inc(s.q_te, 1)


def emit_prep(nc, C, name="pp"):
  s = mk_sems(nc, C.st, name, ["q_x", "q_fl", "q_po", "q_hr", "q_cv", "q_ss",
                               "q_pk", "q_ps", "q_dt", "q_dts", "q_p3a", "q_p3b",
                               "q_p3c", "q_tr", "q_te", "q_p3x", "q_vs", "q_ssl"])
  with contextlib.ExitStack() as _es:
    hout = _es.enter_context(nc.sbuf_tensor([128, 240], F32))
    hrecv = _es.enter_context(nc.sbuf_tensor([128, 240], F32))
    bcb = _es.enter_context(nc.sbuf_tensor([128, 16 * 1024], F32))
    ext = _es.enter_context(nc.sbuf_tensor([128, 1027], F32))
    cvb = _es.enter_context(nc.sbuf_tensor([128, 2 * 1024], F32))
    eb = _es.enter_context(nc.sbuf_tensor([128, 1024], F32))
    rb = _es.enter_context(nc.sbuf_tensor([128, 1024], F32))
    bcs = _es.enter_context(nc.sbuf_tensor([128, 16 * 1024], BF16))
    dtr = _es.enter_context(nc.sbuf_tensor([128, 1024], F32))
    dtv = _es.enter_context(nc.sbuf_tensor([128, 1024], F32))
    u1 = _es.enter_context(nc.sbuf_tensor([128, 1024], F32))
    u2 = _es.enter_context(nc.sbuf_tensor([128, 1024], F32))
    pc3 = _es.enter_context(nc.sbuf_tensor([128, 192], F32))
    c3 = _es.enter_context(nc.sbuf_tensor([128, 6], F32))
    w3 = _es.enter_context(nc.sbuf_tensor([128, 6], F32))
    pad = _es.enter_context(nc.sbuf_tensor([128, 128], BF16))
    p3s = _es.enter_context(nc.sbuf_tensor([4, 2 * 128], BF16))
    ptr = _es.enter_context(nc.psum_tensor([128, 2048], BF16))

    with nc.Block(name + "_x") as blk:
        @blk.sync
        def _(sy):
            sy.dma_start(out=hout[:, :], in_=C.sp_hout[:, :]).then_inc(s.q_x, 16)
            sy.dma_start(out=pc3[:, :], in_=C.sp_pc3[:, 0:192]).then_inc(s.q_x, 16)
            sy.dma_start(out=bcb[:, :], in_=C.sp_bc[:, :]).then_inc(s.q_x, 16)
            sy.dma_start(out=dtr[:, :], in_=C.sp_dtraw[:, :]).then_inc(s.q_x, 16)
            sy.wait_ge(s.q_x, 64)
        @blk.gpsimd
        def _(g):
            g.memset(pad[:, :], 0.0)
            g.wait_ge(s.q_x, 16)
            with (g.register(name + "_nd") as nd, g.register(name + "_rexp") as rexp):
                g.dma_start(out=C.shx[bass.ds(C.e_par128, 128), 0:240],
                            in_=hout[:, :]).then_inc(s.q_fl, 16)
                g.wait_ge(s.q_fl, 16)
                g.dma_start(out=C.shf[0:1, bass.ds(C.e_par2, 1)],
                            in_=C.noncesb[0:1, 0:1]).then_inc(s.q_fl, 16)
                g.wait_ge(s.q_fl, 32)
                g.reg_mov(nd, 1)
                g.reg_mov(rexp, 32)
                if not SIM_SKIP_POLL:
                    with g.While(nd):
                        g.dma_start(out=C.flsb[0:1, 0:1],
                                    in_=C.shf[0:1, bass.ds(C.e_oth2, 1)]).then_inc(s.q_fl, 16)
                        g.reg_add(rexp, rexp, 16)
                        g.wait_ge(s.q_fl, rexp)
                        fv = g.value_load(C.flsb[0:1, 0:1])
                        g.reg_mov(nd, fv - C.gnv)
                else:
                    g.dma_start(out=C.shx[bass.ds(C.e_oth128, 128), 0:240],
                                in_=hout[:, :]).then_inc(s.q_fl, 16)
                    g.wait_ge(s.q_fl, 48)
                g.dma_start(out=hrecv[:, :],
                            in_=C.shx[bass.ds(C.e_oth128, 128), 0:240]).then_inc(s.q_po, 16)
                g.wait_ge(s.q_po, 16)

    with nc.Block(name + "_m") as blk:
        @blk.vector
        def _(v):
          with contextlib.ExitStack() as es:
            ve = WR(v, es, name + "_ve", 1, 1)
            ve.link(v.tensor_scalar_mul(hrecv[:, :], hrecv[:, :], C.pmask[:, 0:1]), s.q_vs)
            v.engine_nop().then_inc(s.q_hr, 1)
            for ct in range(16):
                cg = 64 + ct
                if ct >= 2:
                    v.wait_ge(s.q_pk, ct - 1)
                ve.link(v.tensor_copy(ext[:, 0:3], hrecv[:, cg * 3:cg * 3 + 3]), s.q_vs)
                ve.link(v.tensor_copy(ext[:, 3:1027], bcb[:, ct * 1024:(ct + 1) * 1024]), s.q_vs)
                ve.link(v.tensor_scalar_mul(cvb[:, (ct % 2) * 1024:(ct % 2) * 1024 + 1024],
                                            ext[:, 0:1024],
                                            C.cw[:, cg * 4:cg * 4 + 1]), s.q_vs)
                for tp in range(1, 4):
                    ve.link(v.scalar_tensor_tensor(cvb[:, (ct % 2) * 1024:(ct % 2) * 1024 + 1024],
                                                   ext[:, tp:tp + 1024],
                                                   C.cw[:, cg * 4 + tp:cg * 4 + tp + 1],
                                                   cvb[:, (ct % 2) * 1024:(ct % 2) * 1024 + 1024],
                                                   ALU.mult, ALU.add), s.q_vs)
                v.engine_nop().then_inc(s.q_cv, 1)
            ve.link(v.tensor_scalar_add(u1[:, :], dtr[:, :], C.dtb[:, 0:1]), s.q_vs)
            v.engine_nop().then_inc(s.q_dt, 1)
            v.wait_ge(s.q_dt, 2)
            ve.link(v.tensor_scalar_max(dtv[:, :], u1[:, :], 0.0), s.q_vs)
            ve.link(v.tensor_add(dtv[:, :], dtv[:, :], u2[:, :]), s.q_vs)
            ve.link(v.tensor_scalar_min(dtv[:, :], dtv[:, :], 100.0), s.q_vs)
            v.engine_nop().then_inc(s.q_dt, 1)
            for ct in range(64):
                if ct >= 2:
                    v.wait_ge(s.q_p3c, ct - 1)
                ve.link(v.tensor_copy(c3[:, 0:3], hrecv[:, ct * 3:ct * 3 + 3]), s.q_vs)
                ve.link(v.tensor_copy(c3[:, 3:6], pc3[:, ct * 3:ct * 3 + 3]), s.q_vs)
                hw = (ct % 2) * 3
                ve.link(v.tensor_scalar_mul(w3[:, hw:hw + 3], c3[:, 0:3],
                                            C.cw[:, ct * 4:ct * 4 + 1]), s.q_vs)
                for tp in range(1, 4):
                    ve.link(v.scalar_tensor_tensor(w3[:, hw:hw + 3], c3[:, tp:tp + 3],
                                                   C.cw[:, ct * 4 + tp:ct * 4 + tp + 1],
                                                   w3[:, hw:hw + 3], ALU.mult, ALU.add), s.q_vs)
                v.engine_nop().then_inc(s.q_p3a, 1)

        @blk.scalar
        def _(sc):
          with contextlib.ExitStack() as es:
            se = WR(sc, es, name + "_se", 1, 1)
            for ct in range(16):
                sc.wait_ge(s.q_cv, ct + 1)
                sc.activation(bcs[:, ct * 1024:(ct + 1) * 1024],
                              cvb[:, (ct % 2) * 1024:(ct % 2) * 1024 + 1024],
                              AF.Silu, bias=C.cb[:, 64 + ct:64 + ct + 1]).then_inc(s.q_pk, 1)
            sc.wait_ge(s.q_dt, 1)
            se.link(sc.activation(u2[:, :], u1[:, :], AF.Abs), s.q_ssl)
            se.link(sc.activation(u2[:, :], u2[:, :], AF.Exp, scale=-1.0), s.q_ssl)
            sc.activation(u2[:, :], u2[:, :], AF.Ln, bias=1.0).then_inc(s.q_dt, 1)
            for ct in range(64):
                sc.wait_ge(s.q_p3a, ct + 1)
                if ct >= 2:
                    sc.wait_ge(s.q_p3x, (ct - 1) * 16)
                hw = (ct % 2) * 3
                sc.activation(pad[:, hw:hw + 3], w3[:, hw:hw + 3],
                              AF.Silu, bias=C.cb[:, ct:ct + 1]).then_inc(s.q_p3c, 1)

        @blk.sync
        def _(sy):
            for ct in range(16):
                sy.wait_ge(s.q_pk, ct + 1)
                base = ct * 128 if ct < 8 else 1024 + (ct - 8) * 128
                for z in range(NZ):
                    sy.dma_start(out=C.sp_pack[z, :, base:base + 128],
                                 in_=bcs[:, ct * 1024 + z * 128:ct * 1024 + (z + 1) * 128]
                                 ).then_inc(s.q_ps, 16)
            sy.wait_ge(s.q_dt, 3)
            sy.dma_start(out=C.sp_dtv[:, :], in_=dtv[:, :]).then_inc(s.q_dts, 16)
            sy.wait_ge(s.q_p3x, 64 * 16)
            sy.wait_ge(s.q_ps, 16 * 16 * NZ)
            sy.wait_ge(s.q_dts, 16)

        @blk.gpsimd
        def _(g):
            for ct in range(64):
                g.wait_ge(s.q_p3c, ct + 1)
                g.dma_start(out=C.sp_hsf[ct * 128:(ct + 1) * 128, 0:3],
                            in_=pad[:, (ct % 2) * 3:(ct % 2) * 3 + 3]).then_inc(s.q_p3x, 16)
            g.wait_ge(s.q_p3x, 64 * 16)


def emit_ssdA(nc, C, name="sa"):
  s = mk_sems(nc, C.st, name, ["q_zl", "q_zc", "q_ch", "q_xd", "q_stm", "q_stf",
                               "q_so", "q_vs"])
  M = 12
  with contextlib.ExitStack() as _es:
    X = _es.enter_context(nc.sbuf_tensor([128, 2 * FI], BF16))
    pk = _es.enter_context(nc.sbuf_tensor([128, 2 * 2048], BF16))
    dtz = _es.enter_context(nc.sbuf_tensor([128, 2 * 128], F32))
    adt = _es.enter_context(nc.sbuf_tensor([128, 128], F32))
    acum = _es.enter_context(nc.sbuf_tensor([128, 128], F32))
    dcy = _es.enter_context(nc.sbuf_tensor([128, 128], F32))
    dtT = _es.enter_context(nc.sbuf_tensor([128, 128], F32))
    dcyT = _es.enter_context(nc.sbuf_tensor([128, 128], F32))
    adtT = _es.enter_context(nc.sbuf_tensor([128, 128], F32))
    dd = _es.enter_context(nc.sbuf_tensor([128, 128], F32))
    easr = _es.enter_context(nc.sbuf_tensor([1, 128], F32))
    easbc = _es.enter_context(nc.sbuf_tensor([128, 128], F32))
    btm = _es.enter_context(nc.sbuf_tensor([128, 1024], BF16))
    xd = _es.enter_context(nc.sbuf_tensor([128, 2 * 512], BF16))
    zbuf = _es.enter_context(nc.sbuf_tensor([128, 128], F32))
    p_st = _es.enter_context(nc.psum_tensor([128, 1024], F32))
    p_pp = _es.enter_context(nc.psum_tensor([128, 1024], F32))
    p_bt = _es.enter_context(nc.psum_tensor([128, 1024], BF16))

    with nc.Block(name + "_c") as blk:
        @blk.gpsimd
        def _(g):
            g.memset(C.prevT[:, :], 0.0)
            g.memset(zbuf[:, :], 0.0)
        @blk.sync
        def _(sy):
            sy.sem_inc(s.q_zc, 2)
            sy.sem_inc(s.q_stf, 2)

    with nc.Block(name) as blk:
        @blk.sync
        def _(sy):
            for z in range(NZ):
                b = z % 2
                sy.wait_ge(s.q_zc, z + 1)
                sy.dma_start_transpose(
                    X[:, b * FI:(b + 1) * FI],
                    C.sp_hsf[0:FI, z * 128:(z + 1) * 128]).then_inc(s.q_zl, 16)
                sy.dma_start(out=pk[:, b * 2048:(b + 1) * 2048],
                             in_=C.sp_pack[z, :, :]).then_inc(s.q_zl, 16)
                sy.dma_start(out=dtz[:, b * 128:(b + 1) * 128],
                             in_=C.sp_dtv[:, z * 128:(z + 1) * 128]).then_inc(s.q_zl, 16)
            sy.wait_ge(s.q_zc, 10)

        @blk.vector
        def _(v):
          with contextlib.ExitStack() as es:
            v_zl = WR(v, es, name + "_v_zl", 48, 48)
            v_c5 = WR(v, es, name + "_v_c5", 5, M)
            v_c9 = WR(v, es, name + "_v_c9", 9, M)
            v_c11 = WR(v, es, name + "_v_c11", 11, M)
            v_stm = WR(v, es, name + "_v_stm", 1, 1)
            ve = WR(v, es, name + "_ve", 1, 1)
            with v.Fori(0, 4) as jz:
                for b in range(2):
                    v_zl.wait(s.q_zl)
                    ve.link(v.tensor_scalar_mul(adt[:, :], dtz[:, b * 128:(b + 1) * 128],
                                                C.aneg[:, 0:1]), s.q_vs)
                    v.engine_nop().then_inc(s.q_ch, 1)
                    ve.link(v.tensor_tensor_scan(acum[:, :], adt[:, :], zbuf[:, :], 0.0,
                                                 ALU.add, ALU.add), s.q_vs)
                    v.engine_nop().then_inc(s.q_ch, 1)
                    ve.link(v.tensor_scalar(dcy[:, :], acum[:, :], acum[:, 127:128], None,
                                            ALU.subtract), s.q_vs)
                    v.engine_nop().then_inc(s.q_ch, 1)
                    v_c5.wait(s.q_ch)
                    ve.link(v.tensor_copy(dtT[:, :], p_pp[:, 0:128]), s.q_vs)
                    ve.link(v.tensor_copy(dcyT[:, :], p_pp[:, 128:256]), s.q_vs)
                    ve.link(v.tensor_copy(adtT[:, :], p_pp[:, 256:384]), s.q_vs)
                    ve.link(v.tensor_mul(dd[:, :], dtT[:, :], dcyT[:, :]), s.q_vs)
                    v.engine_nop().then_inc(s.q_ch, 1)
                    v_c9.wait(s.q_ch)
                    ve.link(v.tensor_copy(easbc[:, :], p_pp[:, 640:768]), s.q_vs)
                    v.engine_nop().then_inc(s.q_ch, 1)
                    v_c11.wait(s.q_ch)
                    ve.link(v.tensor_copy(btm[:, :], p_bt[:, 0:1024]), s.q_vs)
                    v.engine_nop().then_inc(s.q_ch, 1)
                    for grp in range(16):
                        gslot = grp % 2
                        for hh in range(8):
                            h = grp * 8 + hh
                            ins = v.tensor_scalar_mul(
                                xd[:, gslot * 512 + hh * 64:gslot * 512 + (hh + 1) * 64],
                                X[:, b * FI + h * 64:b * FI + (h + 1) * 64], dd[:, h:h + 1])
                        ins.then_inc(s.q_xd, 1)
                        v_stm.wait(s.q_stm)
                        for hh in range(8):
                            h = grp * 8 + hh
                            ins = v.scalar_tensor_tensor(
                                C.prevT[:, h * 64:(h + 1) * 64],
                                C.prevT[:, h * 64:(h + 1) * 64], easbc[:, h:h + 1],
                                p_st[:, gslot * 512 + hh * 64:gslot * 512 + (hh + 1) * 64],
                                ALU.mult, ALU.add)
                        ins.then_inc(s.q_stf, 1)
                    v.engine_nop().then_inc(s.q_zc, 1)

        @blk.scalar
        def _(sc):
          with contextlib.ExitStack() as es:
            s_c4 = WR(sc, es, name + "_s_c4", 3, M)
            s_c8 = WR(sc, es, name + "_s_c8", 7, M)
            with sc.Fori(0, 4) as jz:
                for b in range(2):
                    s_c4.wait(s.q_ch)
                    sc.activation(dcy[:, :], dcy[:, :], AF.Exp, scale=-1.0).then_inc(s.q_ch, 1)
                    s_c8.wait(s.q_ch)
                    sc.activation(easr[0:1, :], p_pp[0:1, 512:640], AF.Exp).then_inc(s.q_ch, 1)

        @blk.tensor
        def _(tn):
          with contextlib.ExitStack() as es:
            t_zl = WR(tn, es, name + "_t_zl", 48, 48)
            t_c5 = WR(tn, es, name + "_t_c5", 4, M)
            t_c7 = WR(tn, es, name + "_t_c7", 6, M)
            t_c9 = WR(tn, es, name + "_t_c9", 8, M)
            t_c11 = WR(tn, es, name + "_t_c11", 10, M)
            t_xd = WR(tn, es, name + "_t_xd", 1, 1)
            t_stf = WR(tn, es, name + "_t_stf", 1, 1)
            with tn.Fori(0, 4) as jz:
                for b in range(2):
                    t_zl.wait(s.q_zl)
                    t_c5.wait(s.q_ch)
                    tn.transpose(p_pp[:, 0:128], dtz[:, b * 128:(b + 1) * 128], C.id32[:, :])
                    tn.transpose(p_pp[:, 128:256], dcy[:, :], C.id32[:, :])
                    ins = tn.transpose(p_pp[:, 256:384], adt[:, :], C.id32[:, :])
                    ins.then_inc(s.q_ch, 1)
                    t_c7.wait(s.q_ch)
                    tn.matmul(p_pp[0:1, 512:640], C.onescol[:, 0:1], adtT[:, :],
                              start=True, stop=True).then_inc(s.q_ch, 1)
                    t_c9.wait(s.q_ch)
                    tn.matmul(p_pp[:, 640:768], C.ones32[0:1, 0:128], easr[0:1, :],
                              start=True, stop=True).then_inc(s.q_ch, 1)
                    t_c11.wait(s.q_ch)
                    for gg in range(NG):
                        ins = tn.transpose(p_bt[:, gg * 128:(gg + 1) * 128],
                                           pk[:, b * 2048 + gg * 128:b * 2048 + (gg + 1) * 128],
                                           C.id16[:, :])
                    ins.then_inc(s.q_ch, 1)
                    for grp in range(16):
                        gslot = grp % 2
                        t_xd.wait(s.q_xd)
                        t_stf.wait(s.q_stf)
                        for hh in range(8):
                            h = grp * 8 + hh
                            ins = tn.matmul(
                                p_st[:, gslot * 512 + hh * 64:gslot * 512 + (hh + 1) * 64],
                                btm[:, (h // 16) * 128:(h // 16 + 1) * 128],
                                xd[:, gslot * 512 + hh * 64:gslot * 512 + (hh + 1) * 64],
                                start=True, stop=True)
                        ins.then_inc(s.q_stm, 1)

    with nc.Block(name + "_so") as blk:
        @blk.gpsimd
        def _(g):
            g.dma_start(out=C.shx[bass.ds(C.e_par128, 128), 256:8448],
                        in_=C.prevT[:, :]).then_inc(s.q_so, 16)
            g.wait_ge(s.q_so, 16)
            g.dma_start(out=C.shf[0:1, bass.ds(C.e_par2p1, 1)],
                        in_=C.noncesb[0:1, 0:1]).then_inc(s.q_so, 16)
            g.wait_ge(s.q_so, 32)


def emit_exch2(nc, C, name="x2"):
  s = mk_sems(nc, C.st, name, ["q_fl", "q_po"])
  with contextlib.ExitStack() as _es:
    sin = _es.enter_context(nc.sbuf_tensor([128, FI], F32))
    with nc.Block(name) as blk:
        @blk.gpsimd
        def _(g):
            with (g.register(name + "_nd") as nd, g.register(name + "_rexp") as rexp):
                g.reg_mov(nd, 1)
                g.reg_mov(rexp, 16)
                if not SIM_SKIP_POLL:
                    with g.While(nd):
                        g.dma_start(out=C.flsb[0:1, 0:1],
                                    in_=C.shf[0:1, bass.ds(C.e_oth2p1, 1)]).then_inc(s.q_fl, 16)
                        g.wait_ge(s.q_fl, rexp)
                        g.reg_add(rexp, rexp, 16)
                        fv = g.value_load(C.flsb[0:1, 0:1])
                        g.reg_mov(nd, fv - C.gnv)
                else:
                    g.dma_start(out=C.shx[bass.ds(C.e_oth128, 128), 256:8448],
                                in_=C.prevT[:, :]).then_inc(s.q_fl, 16)
                    g.wait_ge(s.q_fl, 16)
                g.dma_start(out=sin[:, :],
                            in_=C.shx[bass.ds(C.e_oth128, 128), 256:8448]).then_inc(s.q_po, 16)
                g.wait_ge(s.q_po, 16)
    with nc.Block(name + "_m") as blk:
        @blk.vector
        def _(v):
            v.tensor_scalar_mul(C.prevT[:, :], sin[:, :], C.pmask[:, 0:1])


def emit_ssdB(nc, C, name="sb"):
  s = mk_sems(nc, C.st, name, ["q_zl", "q_gt", "q_zc", "q_ch", "q_pb", "q_dm",
                               "q_l1", "q_mg", "q_mh", "q_xc", "q_xd",
                               "q_pc", "q_ym", "q_t1", "q_t2", "q_ev", "q_sg",
                               "q_go1", "q_rstd", "q_og", "q_ogc", "q_gs", "q_vs",
                               "q_ssl"])
  M = 13
  with contextlib.ExitStack() as _es:
    X = _es.enter_context(nc.sbuf_tensor([128, 2 * FI], BF16))
    gt = _es.enter_context(nc.sbuf_tensor([128, 2 * FI], BF16))
    pk = _es.enter_context(nc.sbuf_tensor([128, 2 * 2048], BF16))
    dtz = _es.enter_context(nc.sbuf_tensor([128, 2 * 128], F32))
    adt = _es.enter_context(nc.sbuf_tensor([128, 128], F32))
    acum = _es.enter_context(nc.sbuf_tensor([128, 128], F32))
    dcy = _es.enter_context(nc.sbuf_tensor([128, 128], F32))
    dtT = _es.enter_context(nc.sbuf_tensor([128, 128], F32))
    dcyT = _es.enter_context(nc.sbuf_tensor([128, 128], F32))
    adtT = _es.enter_context(nc.sbuf_tensor([128, 128], F32))
    acumT = _es.enter_context(nc.sbuf_tensor([128, 128], F32))
    eacT = _es.enter_context(nc.sbuf_tensor([128, 128], F32))
    dd = _es.enter_context(nc.sbuf_tensor([128, 128], F32))
    easr = _es.enter_context(nc.sbuf_tensor([1, 128], F32))
    easbc = _es.enter_context(nc.sbuf_tensor([128, 128], F32))
    btm = _es.enter_context(nc.sbuf_tensor([128, 1024], BF16))
    zbuf = _es.enter_context(nc.sbuf_tensor([128, 128], F32))
    Pb = _es.enter_context(nc.sbuf_tensor([128, 2 * 128], BF16))
    Lm = _es.enter_context(nc.sbuf_tensor([128, 2 * 128], F32))
    Mh = _es.enter_context(nc.sbuf_tensor([128, 2 * 128], BF16))
    xch = _es.enter_context(nc.sbuf_tensor([128, 2 * 64], BF16))
    xdb = _es.enter_context(nc.sbuf_tensor([128, 2 * 64], BF16))
    pc16 = _es.enter_context(nc.sbuf_tensor([128, 2 * 64], BF16))
    t1b = _es.enter_context(nc.sbuf_tensor([128, 2 * 64], F32))
    t2b = _es.enter_context(nc.sbuf_tensor([128, 2 * 64], F32))
    ych = _es.enter_context(nc.sbuf_tensor([128, FI], F32))
    sgb = _es.enter_context(nc.sbuf_tensor([128, FI // 2], F32))
    gout = _es.enter_context(nc.sbuf_tensor([128, FI // 2], F32))
    og = _es.enter_context(nc.sbuf_tensor([128, FI // 2], BF16))
    ssq = _es.enter_context(nc.sbuf_tensor([128, 16], F32))
    p_d = _es.enter_context(nc.psum_tensor([128, 1024], F32))
    p_y = _es.enter_context(nc.psum_tensor([128, 1024], F32))
    p_m = _es.enter_context(nc.psum_tensor([128, 1024], F32))
    p_pp = _es.enter_context(nc.psum_tensor([128, 512], F32))
    p_bt = _es.enter_context(nc.psum_tensor([128, 1024], BF16))

    with nc.Block(name + "_c") as blk:
        @blk.gpsimd
        def _(g):
            g.memset(zbuf[:, :], 0.0)
        @blk.sync
        def _(sy):
            sy.sem_inc(s.q_zc, 2)
            sy.sem_inc(s.q_gs, 16)
            sy.sem_inc(s.q_ev, 2)
            sy.sem_inc(s.q_ym, 2)
            sy.sem_inc(s.q_mh, 16)

    with nc.Block(name) as blk:
        @blk.sync
        def _(sy):
            def spills(c):
                for hf in range(2):
                    sy.wait_ge(s.q_ogc, 2 * c + hf + 1)
                    sy.dma_start(out=C.sp_outg[c * 128:(c + 1) * 128,
                                               hf * (FI // 2):(hf + 1) * (FI // 2)],
                                 in_=og[:, :]).then_inc(s.q_gs, 16)
            for z in range(NZ):
                b = z % 2
                if z >= 1:
                    spills(z - 1)
                sy.wait_ge(s.q_zc, z + 1)
                sy.dma_start_transpose(
                    X[:, b * FI:(b + 1) * FI],
                    C.sp_hsf[0:FI, z * 128:(z + 1) * 128]).then_inc(s.q_zl, 16)
                sy.dma_start(out=pk[:, b * 2048:(b + 1) * 2048],
                             in_=C.sp_pack[z, :, :]).then_inc(s.q_zl, 16)
                sy.dma_start(out=dtz[:, b * 128:(b + 1) * 128],
                             in_=C.sp_dtv[:, z * 128:(z + 1) * 128]).then_inc(s.q_zl, 16)
                sy.dma_start_transpose(
                    gt[:, b * FI:(b + 1) * FI],
                    C.sp_gate[0:FI, z * 128:(z + 1) * 128]).then_inc(s.q_gt, 16)
            spills(NZ - 1)
            sy.wait_ge(s.q_gs, 16 + 32 * NZ)

        @blk.scalar
        def _(sc):
          with contextlib.ExitStack() as es:
            s_c4 = WR(sc, es, name + "_s_c4", 3, M)
            s_c7 = WR(sc, es, name + "_s_c7", 6, M)
            s_c9 = WR(sc, es, name + "_s_c9", 8, M)
            s_dm0 = WR(sc, es, name + "_s_dm0", 0, 128)
            s_dm1 = WR(sc, es, name + "_s_dm1", 1, 1)
            s_mh = WR(sc, es, name + "_s_mh", 17, 1)
            s_ym = WR(sc, es, name + "_s_ym", 3, 1)
            s_t2 = WR(sc, es, name + "_s_t2", 1, 1)
            s_gt = WR(sc, es, name + "_s_gt", 16, 16)
            s_go1 = WR(sc, es, name + "_s_go1", 1, 1)
            s_og = WR(sc, es, name + "_s_og", 1, 1)
            s_gs = WR(sc, es, name + "_s_gs", 16, 16)
            se = WR(sc, es, name + "_se", 1, 1)

            def mkpb(h):
                sc.activation(Pb[:, (h % 2) * 128:(h % 2) * 128 + 128], C.strict[:, :],
                              AF.Copy, scale=adtT[:, h:h + 1]).then_inc(s.q_pb, 1)

            with sc.Fori(0, 4) as jz:
                for b in range(2):
                    s_c4.wait(s.q_ch)
                    sc.activation(dcy[:, :], dcy[:, :], AF.Exp, scale=-1.0).then_inc(s.q_ch, 1)
                    s_c7.wait(s.q_ch)
                    sc.activation(eacT[:, :], acumT[:, :], AF.Exp).then_inc(s.q_ch, 1)
                    s_c9.wait(s.q_ch)
                    sc.activation(easr[0:1, :], p_m[0:1, 128:256], AF.Exp).then_inc(s.q_ch, 1)
                    s_dm0.wait(s.q_dm)
                    mkpb(0)
                    mkpb(1)
                    for h in range(NH):
                        hs = (h % 2)
                        s_dm1.wait(s.q_dm)
                        if h < NH - 2:
                            mkpb(h + 2)
                        if h >= 2:
                            s_mh.wait(s.q_mh)
                        sc.activation(Lm[:, hs * 128:(hs + 1) * 128],
                                      p_d[:, hs * 512:hs * 512 + 128],
                                      AF.Exp).then_inc(s.q_l1, 1)
                        if h % 2 == 0:
                            sc.activation(pc16[:, :],
                                          C.prevT[:, h * 64:(h + 2) * 64],
                                          AF.Copy).then_inc(s.q_pc, 2)
                        s_ym.wait(s.q_ym)
                        if h >= 2:
                            s_t2.wait(s.q_t2)
                        sc.activation(t1b[:, hs * 64:(hs + 1) * 64],
                                      p_y[:, hs * 512 + 64:hs * 512 + 128],
                                      AF.Copy, scale=eacT[:, h:h + 1]).then_inc(s.q_t1, 1)
                    s_mh.bump(); s_mh.bump()
                    s_t2.bump(); s_t2.bump()
                    s_gt.wait(s.q_gt)
                    for hf in range(2):
                        fo = hf * (FI // 2)
                        sc.activation(sgb[:, :], gt[:, b * FI + fo:b * FI + fo + FI // 2],
                                      AF.Silu).then_inc(s.q_sg, 1)
                        s_go1.wait(s.q_go1)
                        for gg in range(4):
                            sc.activation(sgb[:, gg * 1024:(gg + 1) * 1024],
                                          gout[:, gg * 1024:(gg + 1) * 1024], AF.Square,
                                          accum_out=ssq[:, hf * 4 + gg:hf * 4 + gg + 1])
                        r0 = 8 + hf * 4
                        se.link(sc.activation(ssq[:, r0:r0 + 4], ssq[:, hf * 4:hf * 4 + 4],
                                              AF.Copy, scale=1.0 / 1024.0), s.q_ssl)
                        se.link(sc.activation(ssq[:, r0:r0 + 4], ssq[:, r0:r0 + 4],
                                              AF.Ln, bias=C.epsc[:, 0:1]), s.q_ssl)
                        sc.activation(ssq[:, r0:r0 + 4], ssq[:, r0:r0 + 4],
                                      AF.Exp, scale=-0.5).then_inc(s.q_rstd, 1)
                        s_og.wait(s.q_og)
                        s_gs.wait(s.q_gs)
                        sc.activation(og[:, :], gout[:, :], AF.Copy).then_inc(s.q_ogc, 1)

        @blk.tensor
        def _(tn):
          with contextlib.ExitStack() as es:
            t_zl = WR(tn, es, name + "_t_zl", 48, 48)
            t_c5 = WR(tn, es, name + "_t_c5", 4, M)
            t_c8 = WR(tn, es, name + "_t_c8", 7, M)
            t_c10 = WR(tn, es, name + "_t_c10", 9, M)
            t_c12 = WR(tn, es, name + "_t_c12", 11, M)
            t_pb = WR(tn, es, name + "_t_pb", 1, 1)
            t_l1 = WR(tn, es, name + "_t_l1", 0, 1)
            t_mh = WR(tn, es, name + "_t_mh", 17, 1)
            t_mhg = WR(tn, es, name + "_t_mhg", 0, 16)
            t_mhall = WR(tn, es, name + "_t_mhall", 16, 128)
            t_pc = WR(tn, es, name + "_t_pc", 1, 1)
            t_xc = WR(tn, es, name + "_t_xc", 1, 1)
            t_xd = WR(tn, es, name + "_t_xd", 1, 1)
            t_ev = WR(tn, es, name + "_t_ev", 1, 1)

            def pdmm(h):
                hs = (h % 2)
                tn.matmul(p_d[:, hs * 512:hs * 512 + 128],
                          Pb[:, hs * 128:(hs + 1) * 128], C.qtril[:, :],
                          start=True, stop=False)
                tn.matmul(p_d[:, hs * 512:hs * 512 + 128],
                          C.id16[:, :], C.negstrict[:, :],
                          start=False, stop=True).then_inc(s.q_dm, 1)

            with tn.Fori(0, 4) as jz:
                for b in range(2):
                    t_zl.wait(s.q_zl)
                    t_c5.wait(s.q_ch)
                    tn.transpose(p_pp[:, 0:128], dtz[:, b * 128:(b + 1) * 128], C.id32[:, :])
                    tn.transpose(p_pp[:, 128:256], dcy[:, :], C.id32[:, :])
                    tn.transpose(p_pp[:, 256:384], adt[:, :], C.id32[:, :])
                    ins = tn.transpose(p_pp[:, 384:512], acum[:, :], C.id32[:, :])
                    ins.then_inc(s.q_ch, 1)
                    t_c8.wait(s.q_ch)
                    t_mhall.wait(s.q_mh)
                    tn.matmul(p_m[0:1, 128:256], C.onescol[:, 0:1], adtT[:, :],
                              start=True, stop=True).then_inc(s.q_ch, 1)
                    t_c10.wait(s.q_ch)
                    tn.matmul(p_m[:, 256:384], C.ones32[0:1, 0:128], easr[0:1, :],
                              start=True, stop=True).then_inc(s.q_ch, 1)
                    t_c12.wait(s.q_ch)
                    for gg in range(NG):
                        ins = tn.transpose(p_bt[:, gg * 128:(gg + 1) * 128],
                                           pk[:, b * 2048 + gg * 128:b * 2048 + (gg + 1) * 128],
                                           C.id16[:, :])
                    ins.then_inc(s.q_ch, 1)
                    t_pb.wait(s.q_pb)
                    t_l1.wait(s.q_l1)
                    pdmm(0)
                    for h in range(NH):
                        g2 = h // 16
                        ys = (h % 2) * 512
                        if h % 16 == 0:
                            gg2 = (g2 % 2) * 512
                            t_mhg.wait(s.q_mh)
                            tn.matmul(p_m[:, gg2:gg2 + 128],
                                      pk[:, b * 2048 + g2 * 128:b * 2048 + (g2 + 1) * 128],
                                      pk[:, b * 2048 + 1024 + g2 * 128:b * 2048 + 1024 + (g2 + 1) * 128],
                                      start=True, stop=True).then_inc(s.q_mg, 1)
                        if h < NH - 1:
                            t_pb.wait(s.q_pb)
                            t_l1.wait(s.q_l1)
                            pdmm(h + 1)
                        t_mh.wait(s.q_mh)
                        t_ev.wait(s.q_ev)
                        t_xc.wait(s.q_xc)
                        tn.matmul(p_y[:, ys:ys + 64], Mh[:, (h % 2) * 128:(h % 2 + 1) * 128],
                                  xch[:, (h % 2) * 64:(h % 2 + 1) * 64], start=True, stop=True)
                        t_pc.wait(s.q_pc)
                        tn.matmul(p_y[:, ys + 64:ys + 128],
                                  pk[:, b * 2048 + 1024 + g2 * 128:b * 2048 + 1024 + (g2 + 1) * 128],
                                  pc16[:, (h % 2) * 64:(h % 2 + 1) * 64], start=True, stop=True)
                        t_xd.wait(s.q_xd)
                        ins = tn.matmul(p_y[:, ys + 128:ys + 192],
                                        btm[:, g2 * 128:(g2 + 1) * 128],
                                        xdb[:, (h % 2) * 64:(h % 2 + 1) * 64],
                                        start=True, stop=True)
                        ins.then_inc(s.q_ym, 1)

        @blk.vector
        def _(v):
          with contextlib.ExitStack() as es:
            v_zl = WR(v, es, name + "_v_zl", 48, 48)
            v_c5 = WR(v, es, name + "_v_c5", 5, M)
            v_c10 = WR(v, es, name + "_v_c10", 10, M)
            v_c13x = WR(v, es, name + "_v_c13x", 12, M)
            v_ymg = WR(v, es, name + "_v_ymg", 1, 1)
            v_l1 = WR(v, es, name + "_v_l1", 1, 1)
            v_mg = WR(v, es, name + "_v_mg", 1, 1)
            v_ym = WR(v, es, name + "_v_ym", 3, 1)
            v_t1 = WR(v, es, name + "_v_t1", 1, 1)
            v_sg = WR(v, es, name + "_v_sg", 1, 1)
            v_rstd = WR(v, es, name + "_v_rstd", 1, 1)
            v_ogc = WR(v, es, name + "_v_ogc", 0, 1)
            ve = WR(v, es, name + "_ve", 1, 1)

            def late(v, hp, b):
                hps = (hp % 2)
                v_t1.wait(s.q_t1)
                v.scalar_tensor_tensor(
                    t2b[:, hps * 64:(hps + 1) * 64],
                    X[:, b * FI + hp * 64:b * FI + (hp + 1) * 64],
                    C.dbc[:, hp:hp + 1], t1b[:, hps * 64:(hps + 1) * 64],
                    ALU.mult, ALU.add).then_inc(s.q_t2, 1)
                v.tensor_add(ych[:, hp * 64:(hp + 1) * 64],
                             t2b[:, hps * 64:(hps + 1) * 64],
                             p_y[:, hps * 512:hps * 512 + 64])
                v.scalar_tensor_tensor(
                    C.prevT[:, hp * 64:(hp + 1) * 64],
                    C.prevT[:, hp * 64:(hp + 1) * 64], easbc[:, hp:hp + 1],
                    p_y[:, hps * 512 + 128:hps * 512 + 192],
                    ALU.mult, ALU.add).then_inc(s.q_ev, 1)

            with v.Fori(0, 4) as jz:
                for b in range(2):
                    v_zl.wait(s.q_zl)
                    ve.link(v.tensor_scalar_mul(adt[:, :], dtz[:, b * 128:(b + 1) * 128],
                                                C.aneg[:, 0:1]), s.q_vs)
                    v.engine_nop().then_inc(s.q_ch, 1)
                    ve.link(v.tensor_tensor_scan(acum[:, :], adt[:, :], zbuf[:, :], 0.0,
                                                 ALU.add, ALU.add), s.q_vs)
                    v.engine_nop().then_inc(s.q_ch, 1)
                    ve.link(v.tensor_scalar(dcy[:, :], acum[:, :], acum[:, 127:128], None,
                                            ALU.subtract), s.q_vs)
                    v.engine_nop().then_inc(s.q_ch, 1)
                    v_c5.wait(s.q_ch)
                    ve.link(v.tensor_copy(dtT[:, :], p_pp[:, 0:128]), s.q_vs)
                    ve.link(v.tensor_copy(dcyT[:, :], p_pp[:, 128:256]), s.q_vs)
                    ve.link(v.tensor_copy(adtT[:, :], p_pp[:, 256:384]), s.q_vs)
                    ve.link(v.tensor_copy(acumT[:, :], p_pp[:, 384:512]), s.q_vs)
                    ve.link(v.tensor_mul(dd[:, :], dtT[:, :], dcyT[:, :]), s.q_vs)
                    v.engine_nop().then_inc(s.q_ch, 1)
                    v_c10.wait(s.q_ch)
                    ve.link(v.tensor_copy(easbc[:, :], p_m[:, 256:384]), s.q_vs)
                    v.engine_nop().then_inc(s.q_ch, 1)
                    v_c13x.wait(s.q_ch)
                    ve.link(v.tensor_copy(btm[:, :], p_bt[:, 0:1024]), s.q_vs)
                    v.engine_nop().then_inc(s.q_ch, 1)
                    for h in range(NH):
                        hs = (h % 2)
                        v_ymg.wait(s.q_ym)
                        v.tensor_scalar_mul(xch[:, hs * 64:(hs + 1) * 64],
                                            X[:, b * FI + h * 64:b * FI + (h + 1) * 64],
                                            dtT[:, h:h + 1]).then_inc(s.q_xc, 1)
                        v.tensor_scalar_mul(xdb[:, hs * 64:(hs + 1) * 64],
                                            X[:, b * FI + h * 64:b * FI + (h + 1) * 64],
                                            dd[:, h:h + 1]).then_inc(s.q_xd, 1)
                        if h % 16 == 0:
                            v_mg.wait(s.q_mg)
                        v_l1.wait(s.q_l1)
                        v.tensor_mul(Mh[:, hs * 128:(hs + 1) * 128],
                                     p_m[:, ((h // 16) % 2) * 512:((h // 16) % 2) * 512 + 128],
                                     Lm[:, hs * 128:(hs + 1) * 128]).then_inc(s.q_mh, 1)
                        if h >= 1:
                            v_ym.wait(s.q_ym)
                            late(v, h - 1, b)
                    v_ym.wait(s.q_ym)
                    late(v, NH - 1, b)
                    for hf in range(2):
                        fo = hf * (FI // 2)
                        v_sg.wait(s.q_sg)
                        v_ogc.wait(s.q_ogc)
                        v.tensor_mul(gout[:, :], sgb[:, :],
                                     ych[:, fo:fo + FI // 2]).then_inc(s.q_go1, 1)
                        v_rstd.wait(s.q_rstd)
                        for gg in range(4):
                            ins = v.tensor_scalar_mul(gout[:, gg * 1024:(gg + 1) * 1024],
                                                      gout[:, gg * 1024:(gg + 1) * 1024],
                                                      ssq[:, 8 + hf * 4 + gg:9 + hf * 4 + gg])
                        ins.then_inc(s.q_og, 1)
                    v.engine_nop().then_inc(s.q_zc, 1)

    if PHASES == 7 and DBG_SSD:
        sd = mk_sems(nc, C.st, name + "dbg", ["q"])
        with nc.Block(name + "_dbg") as blk:
            @blk.gpsimd
            def _(g):
                g.dma_start(out=C.outap[0:128, 0:4096], in_=ych[:, 0:4096]).then_inc(sd.q, 16)
                g.dma_start(out=C.outap[128:256, 0:4096], in_=ych[:, 4096:8192]).then_inc(sd.q, 16)
                g.dma_start(out=C.outap[256:384, 0:4096], in_=C.prevT[:, 0:4096]).then_inc(sd.q, 16)
                g.dma_start(out=C.outap[384:512, 0:4096], in_=C.prevT[:, 4096:8192]).then_inc(sd.q, 16)
                g.wait_ge(sd.q, 64)


def emit_outproj(nc, C, name="op"):
  s = mk_sems(nc, C.st, name, ["q_ol", "q_w", "q_m", "q_e", "q_as", "q_al",
                               "q_x1", "q_x2", "q_vs", "q_nw"])
  with contextlib.ExitStack() as _es:
    ofm = _es.enter_context(nc.sbuf_tensor([128, ITL * 1024], BF16))
    wob = _es.enter_context(nc.sbuf_tensor([128, 2 * 8192], BF16))
    ast = _es.enter_context(nc.sbuf_tensor([128, 2 * 1024], BF16))
    nwt = _es.enter_context(nc.sbuf_tensor([128, ITL], F32))
    ps = _es.enter_context(nc.psum_tensor([128, 2048], F32))

    with nc.Block(name + "_c") as blk:
        @blk.sync
        def _(sy):
            sy.sem_inc(s.q_m, 2)
            sy.sem_inc(s.q_e, 2)
            sy.sem_inc(s.q_as, 32)
            sy.dma_start(out=nwt[:, :], in_=C.nwt_in[:, :]).then_inc(s.q_nw, 16)
            for it in range(ITL):
                sy.dma_start_transpose(
                    ofm[:, it * 1024:(it + 1) * 1024],
                    C.sp_outg[0:1024, it * 128:(it + 1) * 128]).then_inc(s.q_ol, 16)
            sy.wait_ge(s.q_ol, ITL * 16)
        @blk.vector
        def _(v):
            v.wait_ge(s.q_nw, 16)
            for it in range(ITL):
                v.wait_ge(s.q_ol, (it + 1) * 16)
                v.tensor_scalar_mul(ofm[:, it * 1024:(it + 1) * 1024],
                                    ofm[:, it * 1024:(it + 1) * 1024],
                                    nwt[:, it:it + 1])

    with nc.Block(name) as blk:
        @blk.gpsimd
        def _(g):
            for k in range(DTL):
                g.wait_ge(s.q_m, k + 1)
                g.dma_start(out=wob[:, (k % 2) * 8192:(k % 2 + 1) * 8192],
                            in_=C.wout[k, :, :]).then_inc(s.q_w, 16)
        @blk.tensor
        def _(tn):
          with contextlib.ExitStack() as es:
            t_w = WR(tn, es, name + "_t_w", 16, 16)
            t_e = WR(tn, es, name + "_t_e", 1, 1)
            with tn.Fori(0, 16) as j:
                for b in range(2):
                    t_w.wait(s.q_w)
                    t_e.wait(s.q_e)
                    for tch in range(2):
                        for it in range(ITL):
                            ins = tn.matmul(
                                ps[:, b * 1024 + tch * 512:b * 1024 + (tch + 1) * 512],
                                wob[:, b * 8192 + it * 128:b * 8192 + (it + 1) * 128],
                                ofm[:, it * 1024 + tch * 512:it * 1024 + (tch + 1) * 512],
                                start=(it == 0), stop=(it == ITL - 1))
                    ins.then_inc(s.q_m, 1)
        @blk.vector
        def _(v):
          with contextlib.ExitStack() as es:
            v_m = WR(v, es, name + "_v_m", 3, 1)
            v_as = WR(v, es, name + "_v_as", 16, 16)
            with v.Fori(0, 16) as j:
                for b in range(2):
                    v_m.wait(s.q_m)
                    v_as.wait(s.q_as)
                    v.tensor_copy(ast[:, b * 1024:(b + 1) * 1024],
                                  ps[:, b * 1024:(b + 1) * 1024]).then_inc(s.q_e, 1)
        @blk.sync
        def _(sy):
          with contextlib.ExitStack() as es:
            y_e = WR(sy, es, name + "_y_e", 3, 1)
            with sy.Fori(0, 16) as j:
                for b in range(2):
                    y_e.wait(s.q_e)
                    sy.dma_start(out=C.sp_attn[ds(j * 256 + b * 128, 128), :],
                                 in_=ast[:, b * 1024:(b + 1) * 1024]).then_inc(s.q_as, 16)
            sy.wait_ge(s.q_as, 32 + 32 * 16)

  s2 = mk_sems(nc, C.st, name + "b", ["q_l", "q_d", "q_o", "q_vs"])
  with contextlib.ExitStack() as _es:
    atm = _es.enter_context(nc.sbuf_tensor([128, 2 * DM], BF16))
    x1t = _es.enter_context(nc.sbuf_tensor([128, 2 * DM], F32))
    x2t = _es.enter_context(nc.sbuf_tensor([128, 2 * DM], F32))
    with nc.Block(name + "b_c") as blk:
        @blk.sync
        def _(sy):
            sy.sem_inc(s2.q_d, 2)
            sy.sem_inc(s2.q_o, 32)
    with nc.Block(name + "b") as blk:
        @blk.sync
        def _(sy):
            for k in range(8):
                b = k % 2
                sy.wait_ge(s2.q_d, k + 1)
                sy.dma_start_transpose(
                    atm[:, b * DM:(b + 1) * DM],
                    C.sp_attn[0:DM, k * 128:(k + 1) * 128]).then_inc(s2.q_l, 16)
                sy.dma_start(out=x1t[:, b * DM:(b + 1) * DM],
                             in_=C.sp_x1[k * 128:(k + 1) * 128, :]).then_inc(s2.q_l, 16)
                sy.wait_ge(s2.q_d, k + 3)
                sy.dma_start(out=C.sp_x2[k * 128:(k + 1) * 128, :],
                             in_=x2t[:, b * DM:(b + 1) * DM]).then_inc(s2.q_o, 16)
            sy.wait_ge(s2.q_o, 32 + 8 * 16)
        @blk.vector
        def _(v):
          with contextlib.ExitStack() as es:
            v_l = WR(v, es, name + "_v_l", 32, 32)
            v_o = WR(v, es, name + "_v_o", 16, 16)
            with v.Fori(0, 4) as jz:
                for b in range(2):
                    v_l.wait(s2.q_l)
                    v_o.wait(s2.q_o)
                    v.tensor_add(x2t[:, b * DM:(b + 1) * DM], x1t[:, b * DM:(b + 1) * DM],
                                 atm[:, b * DM:(b + 1) * DM]).then_inc(s2.q_d, 1)


def _stage(inputs):
    g = {}
    g["wup"] = _pt(inputs["L0_ffn_up_w"])
    g["wdn"] = _pt(inputs["L0_ffn_down_w"])
    g["wf1"] = _pt(inputs["L0_ffn_fc1_w"])
    g["wex"] = _pt(inputs["L0_ffn_expert_w"])
    g["wf2"] = _pt(inputs["L0_ffn_fc2_w"])
    gw = np.asarray(inputs["L0_ffn_gate_w"], np.float32)
    g["wgt"] = np.ascontiguousarray(gw.reshape(DTL, 128, 8).transpose(1, 0, 2).reshape(128, DTL * 8))
    ip = np.asarray(inputs["L0_attn_in_proj_w"], np.float32)
    g["wip"] = _pt(np.ascontiguousarray(ip.T))
    op = np.asarray(inputs["L0_attn_out_proj_w"], np.float32)
    g["wout"] = _pt(np.ascontiguousarray(op.T))
    cw = np.asarray(inputs["L0_attn_conv_w"], np.float32).reshape(NCV * 128, 4)
    g["cw"] = np.ascontiguousarray(cw.reshape(NCV, 128, 4).transpose(1, 0, 2).reshape(128, NCV * 4))
    g["cwt"] = np.ascontiguousarray(cw.reshape(NCV, 128, 4))
    cb = np.asarray(inputs["L0_attn_conv_b"], np.float32)
    g["cb"] = np.ascontiguousarray(cb.reshape(NCV, 128).T)
    g["cbt"] = np.ascontiguousarray(cb.reshape(NCV, 128, 1))
    g["onescol"] = np.ones((128, 1), np.float32)
    g["dtb"] = np.asarray(inputs["L0_attn_dt_bias"], np.float32).reshape(128, 1)
    g["alog"] = np.asarray(inputs["L0_attn_A_log"], np.float32).reshape(128, 1)
    g["dbc"] = np.ascontiguousarray(np.broadcast_to(
        np.asarray(inputs["L0_attn_D"], np.float32)[None, :], (128, 128)))
    g["gbc"] = np.ascontiguousarray(np.broadcast_to(
        np.asarray(inputs["L0_ffn_ln_gamma"], np.float32)[None, :], (128, DM)))
    g["bbc"] = np.ascontiguousarray(np.broadcast_to(
        np.asarray(inputs["L0_ffn_ln_beta"], np.float32)[None, :], (128, DM)))
    g["nwt"] = np.ascontiguousarray(
        np.asarray(inputs["L0_attn_norm_w"], np.float32).reshape(ITL, 128).T)
    g["id16"] = np.eye(128, dtype=ml_dtypes.bfloat16)
    g["id32"] = np.eye(128, dtype=np.float32)
    g["ones32"] = np.ones((1, 128), np.float32)
    ar = np.arange(128)
    g["strictm"] = (ar[:, None] > ar[None, :]).astype(ml_dtypes.bfloat16)
    g["negstrict"] = (-30000.0 * (ar[:, None] > ar[None, :])).astype(ml_dtypes.bfloat16)
    g["qtril"] = (ar[:, None] <= ar[None, :]).astype(ml_dtypes.bfloat16)
    return g


def kernel(**inputs):
    xfull = np.asarray(inputs["L0_attn_x"], np.float32)
    g = _stage(inputs)
    nonce = np.array([[int(time.time_ns() & 0x7FFFFFF0) | 5]], np.int32)
    in_maps = []
    for c in range(8):
        b, h = c // 2, c % 2
        m = dict(g)
        m["x"] = np.ascontiguousarray(xfull[b, h * NT:(h + 1) * NT, :])
        m["pmask"] = np.full((128, 1), float(h), np.float32)
        m["nonce"] = nonce
        in_maps.append(m)
    nc = build()
    res = run_bass_kernel_spmd(nc, in_maps, core_ids=list(range(8)))
    outp = np.empty((4, 2048, DM), np.float32)
    for c in range(8):
        b, h = c // 2, c % 2
        outp[b, h * NT:(h + 1) * NT, :] = res.results[c]["out"]
    return outp


# version 10 (final)
# speedup vs baseline: 2.2278x; 1.0234x over previous
import time
import contextlib
import numpy as np
import ml_dtypes
import concourse.bass as bass
import concourse.mybir as mybir
from concourse.bass_utils import run_bass_kernel_spmd

F32 = mybir.dt.float32
BF16 = mybir.dt.bfloat16
I32 = mybir.dt.int32
AF = mybir.ActivationFunctionType
ALU = mybir.AluOpType
AX = mybir.AxisListType.X
ds = bass.ds

NT, DM, FI, LAT = 1024, 4096, 8192, 2048
DTL, ITL, LTL = DM // 128, FI // 128, LAT // 128
NE, NZ, NH, NG, HP, SN = 8, 8, 128, 8, 64, 128
EPS = 1e-5
NCV = 80
PAIRS = [(i, j) for i in range(NE) for j in range(i + 1, NE)]

PHASES = 9
SIM_SKIP_POLL = False
DBG_SSD = False


class _C:
    pass


def _pt(w):
    K, M = w.shape
    nk, nm = K // 128, M // 128
    return np.ascontiguousarray(
        np.asarray(w, np.float32).reshape(nk, 128, nm, 128).transpose(2, 1, 0, 3)
        .reshape(nm, 128, nk * 128).astype(ml_dtypes.bfloat16))



class _S:
    pass


def mk_sems(nc, st, prefix, names):
    S = _S()
    for nm in names:
        if st is None:
            setattr(S, nm, nc.alloc_semaphore(name=prefix + "_" + nm))
        else:
            setattr(S, nm, st.enter_context(nc.semaphore(prefix + "_" + nm)))
    return S



class WR:
    def __init__(self, eng, es, name, init, step):
        self.eng, self.step = eng, step
        self.reg = es.enter_context(eng.register(name))
        eng.reg_mov(self.reg, init)
    def wait(self, sem):
        self.eng.wait_ge(sem, self.reg)
        self.eng.reg_add(self.reg, self.reg, self.step)

    def link(self, ins, sem):
        ins.then_inc(sem, 1)
        self.eng.wait_ge(sem, self.reg)
        self.eng.reg_add(self.reg, self.reg, self.step)

    def bump(self):
        self.eng.reg_add(self.reg, self.reg, self.step)


_LN_ADV = {"q_x": 128, "q_sp": 160, "q_a": 8, "q_b": 8, "q_c": 8, "q_d": 10,
           "q_vs": 24, "q_ss": 8}


def emit_ln(nc, C, src, name):
  if not hasattr(C, "ln_sems"):
      C.ln_sems = mk_sems(nc, C.st, "lnS", list(_LN_ADV))
      C.ln_calls = 0
  S = C.ln_sems
  LB = {q: C.ln_calls * _LN_ADV[q] for q in _LN_ADV}
  C.ln_calls += 1
  with contextlib.ExitStack() as _es:
    C.xa = _es.enter_context(nc.sbuf_tensor([128, 2 * DM], F32))
    C.lnw = _es.enter_context(nc.sbuf_tensor([128, 2 * DM], BF16))
    C.sq = _es.enter_context(nc.sbuf_tensor([128, DM], F32))
    with nc.Block(name + "_c") as blk:
        @blk.sync
        def _(sy):
            sy.sem_inc(S.q_d, 2)
            sy.sem_inc(S.q_sp, 32)

    with nc.Block(name) as blk:
        @blk.sync
        def _(sy):
            for k in range(8):
                b = k % 2
                sy.wait_ge(S.q_d, LB["q_d"] + 1 + k)
                sy.dma_start(out=C.xa[:, b * DM:(b + 1) * DM],
                             in_=src[k * 128:(k + 1) * 128, :]).then_inc(S.q_x, 16)
                sy.wait_ge(S.q_d, LB["q_d"] + 3 + k)
                sy.dma_start(out=C.sp_lnt[k * 128:(k + 1) * 128, :],
                             in_=C.lnw[:, b * DM:(b + 1) * DM]).then_inc(S.q_sp, 16)
            sy.wait_ge(S.q_sp, LB["q_sp"] + 32 + 16 * 8)

        @blk.vector
        def _(v):
          with contextlib.ExitStack() as es:
            ve = WR(v, es, name + "_ve", 1 + LB["q_vs"], 1)
            v_x = WR(v, es, name + "_v_x", 16 + LB["q_x"], 16)
            v_a = WR(v, es, name + "_v_a", 1 + LB["q_a"], 1)
            v_c = WR(v, es, name + "_v_c", 1 + LB["q_c"], 1)
            v_sp = WR(v, es, name + "_v_sp", 16 + LB["q_sp"], 16)
            with v.Fori(0, 4) as t:
                for b in range(2):
                    xs = C.xa[:, b * DM:(b + 1) * DM]
                    r = C.red[:, b * 8:(b + 1) * 8]
                    v_x.wait(S.q_x)
                    ve.link(v.reduce_sum(r[:, 0:1], xs, axis=AX), S.q_vs)
                    v_a.wait(S.q_a)
                    ve.link(v.tensor_scalar_mul(r[:, 2:3], r[:, 0:1], 1.0 / DM), S.q_vs)
                    ve.link(v.tensor_mul(r[:, 3:4], r[:, 2:3], r[:, 2:3]), S.q_vs)
                    v.scalar_tensor_tensor(r[:, 4:5], r[:, 1:2], 1.0 / DM, r[:, 3:4],
                                           ALU.mult, ALU.subtract).then_inc(S.q_b, 1)
                    v_c.wait(S.q_c)
                    v_sp.wait(S.q_sp)
                    v.tensor_scalar(C.lnw[:, b * DM:(b + 1) * DM], xs, r[:, 2:3], r[:, 5:6],
                                    ALU.subtract, ALU.mult).then_inc(S.q_d, 1)

        @blk.scalar
        def _(sc):
          with contextlib.ExitStack() as es:
            se = WR(sc, es, name + "_se", 1 + LB["q_ss"], 1)
            s_x = WR(sc, es, name + "_s_x", 16 + LB["q_x"], 16)
            s_b = WR(sc, es, name + "_s_b", 1 + LB["q_b"], 1)
            with sc.Fori(0, 4) as t:
                for b in range(2):
                    xs = C.xa[:, b * DM:(b + 1) * DM]
                    r = C.red[:, b * 8:(b + 1) * 8]
                    s_x.wait(S.q_x)
                    sc.activation(C.sq[:, :], xs, AF.Square, accum_out=r[:, 1:2]).then_inc(S.q_a, 1)
                    s_b.wait(S.q_b)
                    se.link(sc.activation(r[:, 6:7], r[:, 4:5], AF.Ln, bias=C.epsc[:, 0:1]), S.q_ss)
                    sc.activation(r[:, 5:6], r[:, 6:7], AF.Exp, scale=-0.5).then_inc(S.q_c, 1)


_FFN_ADV = {"q_lnf": 1024, "q_upw": 2048, "q_upm": 130, "q_upe": 130, "q_ups": 2080,
            "q_f1w": 512, "q_f1m": 34, "q_f1e": 34, "q_f1s": 544,
            "q_exw": 512, "q_exm": 34, "q_exe": 34, "q_exs": 544,
            "q_dnw": 1024, "q_dnw2": 1024, "q_dnm": 66, "q_dne": 66, "q_dns": 1056,
            "q_lg": 2, "q_lge": 2, "q_lt": 2, "q_rwc": 2, "q_rwr": 6, "q_rwb": 2,
            "q_hl": 32, "q_ll": 32, "q_ml": 32, "q_ep": 256, "q_epo": 160,
            "q_a": 8, "q_b": 8, "q_c": 8, "q_d": 10, "q_gb": 32,
            "q_vs": 360, "q_vs2": 48, "q_ss2": 8}


def emit_ffn(nc, C, xin, xout, name, final_wait=False):
  if not hasattr(C, "ffn_sems"):
      C.ffn_sems = mk_sems(nc, C.st, "ffnS", list(_FFN_ADV))
      C.ffn_calls = 0
  s = C.ffn_sems
  FB = {q: C.ffn_calls * _FFN_ADV[q] for q in _FFN_ADV}
  C.ffn_calls += 1
  with contextlib.ExitStack() as _es:
    C.lnfm = _es.enter_context(nc.sbuf_tensor([128, DTL * 512], BF16))
    C.wbuf = _es.enter_context(nc.sbuf_tensor([128, 2 * 4096], BF16))
    C.wbig = _es.enter_context(nc.sbuf_tensor([128, 2 * 8192], BF16))
    C.w2buf = _es.enter_context(nc.sbuf_tensor([128, 2 * 2048], BF16))
    C.hbuf = _es.enter_context(nc.sbuf_tensor([128, ITL * 512], BF16))
    C.xlat = _es.enter_context(nc.sbuf_tensor([128, LTL * 512], BF16))
    C.moe = _es.enter_context(nc.sbuf_tensor([128, LTL * 512], BF16))
    C.upst = _es.enter_context(nc.sbuf_tensor([128, 2 * 512], BF16))
    C.f1st = _es.enter_context(nc.sbuf_tensor([128, 2 * 512], BF16))
    C.exst = _es.enter_context(nc.sbuf_tensor([128, 2 * 512], BF16))
    C.dnst = _es.enter_context(nc.sbuf_tensor([128, 2 * 512], BF16))
    C.scr = _es.enter_context(nc.sbuf_tensor([128, 512], F32))
    C.lgf = _es.enter_context(nc.sbuf_tensor([128, 512], F32))
    C.ltm = _es.enter_context(nc.sbuf_tensor([128, 32], F32))
    C.prs = _es.enter_context(nc.sbuf_tensor([128, 28], F32))
    C.rwc = _es.enter_context(nc.sbuf_tensor([128, 4], F32))
    C.rwrow = _es.enter_context(nc.sbuf_tensor([1, 512], F32))
    C.rwbc = _es.enter_context(nc.sbuf_tensor([128, 512], F32))
    C.ps = _es.enter_context(nc.psum_tensor([128, 2 * 512], F32))
    C.psg = _es.enter_context(nc.psum_tensor([128, 512], F32))
    C.pst = _es.enter_context(nc.psum_tensor([128, 512], F32))
    C.pst2 = _es.enter_context(nc.psum_tensor([128, 512], F32))
    with nc.Block(name + "_c") as blk:
        @blk.sync
        def _(sy):
            for q in (s.q_upm, s.q_f1m, s.q_exm, s.q_dnm):
                sy.sem_inc(q, 2)
            for q in (s.q_upe, s.q_f1e, s.q_exe, s.q_dne):
                sy.sem_inc(q, 2)
            for q in (s.q_ups, s.q_f1s, s.q_exs, s.q_dns):
                sy.sem_inc(q, 32)
        @blk.gpsimd
        def _(g):
            g.memset(C.lgf[:, :], 0.0)

    with nc.Block(name) as blk:
        @blk.sync
        def _(sy):
            for t in range(2):
                sy.wait_ge(s.q_dnm, FB["q_dnm"] + t * 32 + 2)
                for dt in range(DTL):
                    sy.dma_start_transpose(
                        C.lnfm[:, dt * 512:(dt + 1) * 512],
                        C.sp_lnt[t * 512:(t + 1) * 512, dt * 128:(dt + 1) * 128]).then_inc(s.q_lnf, 16)
                for k in range(64):
                    b = k % 2
                    sy.wait_ge(s.q_upe, FB["q_upe"] + t * 64 + k + 3)
                    sy.dma_start(out=C.sp_h[:, k * 512:(k + 1) * 512],
                                 in_=C.upst[:, b * 512:(b + 1) * 512]).then_inc(s.q_ups, 16)
                sy.wait_ge(s.q_ups, FB["q_ups"] + 32 + (t * 64 + 64) * 16)
                sy.dma_start(out=C.hbuf[:, :], in_=C.sp_h[:, :]).then_inc(s.q_hl, 16)
                for k in range(16):
                    b = k % 2
                    sy.wait_ge(s.q_f1e, FB["q_f1e"] + t * 16 + k + 3)
                    sy.dma_start(out=C.sp_lat[:, k * 512:(k + 1) * 512],
                                 in_=C.f1st[:, b * 512:(b + 1) * 512]).then_inc(s.q_f1s, 16)
                sy.wait_ge(s.q_f1s, FB["q_f1s"] + 32 + (t * 16 + 16) * 16)
                sy.dma_start(out=C.xlat[:, :], in_=C.sp_lat[:, :]).then_inc(s.q_ll, 16)
                for k in range(16):
                    b = k % 2
                    sy.wait_ge(s.q_exe, FB["q_exe"] + t * 16 + k + 3)
                    sy.dma_start(out=C.sp_moe[:, k * 512:(k + 1) * 512],
                                 in_=C.exst[:, b * 512:(b + 1) * 512]).then_inc(s.q_exs, 16)
                sy.wait_ge(s.q_exs, FB["q_exs"] + 32 + (t * 16 + 16) * 16)
                sy.dma_start(out=C.moe[:, :], in_=C.sp_moe[:, :]).then_inc(s.q_ml, 16)
                for k in range(32):
                    b = k % 2
                    sy.wait_ge(s.q_dne, FB["q_dne"] + t * 32 + k + 3)
                    sy.dma_start(out=C.sp_os[k * 128:(k + 1) * 128, t * 512:(t + 1) * 512],
                                 in_=C.dnst[:, b * 512:(b + 1) * 512]).then_inc(s.q_dns, 16)
            sy.wait_ge(s.q_dns, FB["q_dns"] + 32 + 16 * 64)

        @blk.gpsimd
        def _(g):
            for t in range(2):
                g.wait_ge(s.q_exm, FB["q_exm"] + 2 + 16 * t)
                for kk in range(64):
                    b = kk % 2
                    g.wait_ge(s.q_upm, FB["q_upm"] + 1 + 64 * t + kk)
                    g.dma_start(out=C.wbuf[:, b * 4096:(b + 1) * 4096],
                                in_=C.wup[kk, :, :]).then_inc(s.q_upw, 16)
                g.wait_ge(s.q_upm, FB["q_upm"] + 66 + 64 * t)
                for kk in range(16):
                    b = kk % 2
                    g.wait_ge(s.q_f1m, FB["q_f1m"] + 1 + 16 * t + kk)
                    g.dma_start(out=C.wbuf[:, b * 4096:(b + 1) * 4096],
                                in_=C.wf1[kk, :, :]).then_inc(s.q_f1w, 16)
                g.wait_ge(s.q_f1m, FB["q_f1m"] + 18 + 16 * t)
                for kk in range(16):
                    b = kk % 2
                    g.wait_ge(s.q_exm, FB["q_exm"] + 1 + 16 * t + kk)
                    g.dma_start(out=C.wbuf[:, b * 4096:b * 4096 + 2048],
                                in_=C.wex[kk, :, :]).then_inc(s.q_exw, 16)
                for kk in range(32):
                    b = kk % 2
                    g.wait_ge(s.q_dnm, FB["q_dnm"] + 1 + 32 * t + kk)
                    g.dma_start(out=C.wbig[:, b * 8192:(b + 1) * 8192],
                                in_=C.wdn[kk, :, :]).then_inc(s.q_dnw, 16)
                    g.dma_start(out=C.w2buf[:, b * 2048:(b + 1) * 2048],
                                in_=C.wf2[kk, :, :]).then_inc(s.q_dnw2, 16)

        @blk.tensor
        def _(tn):
          with contextlib.ExitStack() as es:
            t_lnf = WR(tn, es, name + "_t_lnf", 512 + FB["q_lnf"], 512)
            t_upw = WR(tn, es, name + "_t_upw", 16 + FB["q_upw"], 16)
            t_upe = WR(tn, es, name + "_t_upe", 1 + FB["q_upe"], 1)
            t_lge = WR(tn, es, name + "_t_lge", 1 + FB["q_lge"], 1)
            t_f1w = WR(tn, es, name + "_t_f1w", 16 + FB["q_f1w"], 16)
            t_f1e = WR(tn, es, name + "_t_f1e", 1 + FB["q_f1e"], 1)
            t_rwc = WR(tn, es, name + "_t_rwc", 1 + FB["q_rwc"], 1)
            t_rwr = WR(tn, es, name + "_t_rwr", 2 + FB["q_rwr"], 3)
            t_exw = WR(tn, es, name + "_t_exw", 16 + FB["q_exw"], 16)
            t_exe = WR(tn, es, name + "_t_exe", 1 + FB["q_exe"], 1)
            t_ll = WR(tn, es, name + "_t_ll", 16 + FB["q_ll"], 16)
            t_dnw = WR(tn, es, name + "_t_dnw", 16 + FB["q_dnw"], 16)
            t_dnw2 = WR(tn, es, name + "_t_dnw2", 16 + FB["q_dnw2"], 16)
            t_dne = WR(tn, es, name + "_t_dne", 1 + FB["q_dne"], 1)
            t_hl = WR(tn, es, name + "_t_hl", 16 + FB["q_hl"], 16)
            t_ml = WR(tn, es, name + "_t_ml", 16 + FB["q_ml"], 16)
            t_g1 = WR(tn, es, name + "_t_g1", 66 + FB["q_upe"], 64)
            t_g2 = WR(tn, es, name + "_t_g2", 18 + FB["q_f1e"], 16)
            t_g3 = WR(tn, es, name + "_t_g3", 18 + FB["q_exe"], 16)
            with tn.Fori(0, 2) as t:
                t_lnf.wait(s.q_lnf)
                for dt in range(DTL):
                    ins = tn.matmul(C.psg[0:8, 0:512], C.wg[:, dt * 8:(dt + 1) * 8],
                                    C.lnfm[:, dt * 512:(dt + 1) * 512],
                                    start=(dt == 0), stop=(dt == DTL - 1))
                ins.then_inc(s.q_lg, 1)
                with tn.Fori(0, 32) as j:
                    for b in range(2):
                        t_upw.wait(s.q_upw)
                        t_upe.wait(s.q_upe)
                        for dt in range(DTL):
                            ins = tn.matmul(C.ps[:, b * 512:(b + 1) * 512],
                                            C.wbuf[:, b * 4096 + dt * 128:b * 4096 + (dt + 1) * 128],
                                            C.lnfm[:, dt * 512:(dt + 1) * 512],
                                            start=(dt == 0), stop=(dt == DTL - 1))
                        ins.then_inc(s.q_upm, 1)
                t_lge.wait(s.q_lge)
                for sb in range(4):
                    ins = tn.transpose(C.pst[:, sb * 128:(sb + 1) * 128],
                                       C.lgf[:, sb * 128:(sb + 1) * 128], C.id32[:, :])
                ins.then_inc(s.q_lt, 1)
                t_g1.wait(s.q_upe)
                with tn.Fori(0, 8) as j:
                    for b in range(2):
                        t_f1w.wait(s.q_f1w)
                        t_f1e.wait(s.q_f1e)
                        for dt in range(DTL):
                            ins = tn.matmul(C.ps[:, b * 512:(b + 1) * 512],
                                            C.wbuf[:, b * 4096 + dt * 128:b * 4096 + (dt + 1) * 128],
                                            C.lnfm[:, dt * 512:(dt + 1) * 512],
                                            start=(dt == 0), stop=(dt == DTL - 1))
                        ins.then_inc(s.q_f1m, 1)
                t_rwc.wait(s.q_rwc)
                for sb in range(4):
                    ins = tn.matmul(C.pst2[0:1, sb * 128:(sb + 1) * 128], C.rwc[:, sb:sb + 1],
                                    C.id32[:, 0:128], start=True, stop=True)
                ins.then_inc(s.q_rwr, 1)
                t_rwr.wait(s.q_rwr)
                for sb in range(4):
                    ins = tn.matmul(C.psg[0:128, sb * 128:(sb + 1) * 128], C.ones32[0:1, 0:128],
                                    C.rwrow[0:1, sb * 128:(sb + 1) * 128], start=True, stop=True)
                ins.then_inc(s.q_rwr, 1)
                t_g2.wait(s.q_f1e)
                t_ll.wait(s.q_ll)
                with tn.Fori(0, 8) as j:
                    for b in range(2):
                        t_exw.wait(s.q_exw)
                        t_exe.wait(s.q_exe)
                        for lt in range(LTL):
                            ins = tn.matmul(C.ps[:, b * 512:(b + 1) * 512],
                                            C.wbuf[:, b * 4096 + lt * 128:b * 4096 + (lt + 1) * 128],
                                            C.xlat[:, lt * 512:(lt + 1) * 512],
                                            start=(lt == 0), stop=(lt == LTL - 1))
                        ins.then_inc(s.q_exm, 1)
                t_g3.wait(s.q_exe)
                t_hl.wait(s.q_hl)
                t_ml.wait(s.q_ml)
                with tn.Fori(0, 16) as j:
                    for b in range(2):
                        t_dnw.wait(s.q_dnw)
                        t_dnw2.wait(s.q_dnw2)
                        t_dne.wait(s.q_dne)
                        for it in range(ITL):
                            tn.matmul(C.ps[:, b * 512:(b + 1) * 512],
                                      C.wbig[:, b * 8192 + it * 128:b * 8192 + (it + 1) * 128],
                                      C.hbuf[:, it * 512:(it + 1) * 512],
                                      start=(it == 0), stop=False)
                        for lt in range(LTL):
                            ins = tn.matmul(C.ps[:, b * 512:(b + 1) * 512],
                                            C.w2buf[:, b * 2048 + lt * 128:b * 2048 + (lt + 1) * 128],
                                            C.moe[:, lt * 512:(lt + 1) * 512],
                                            start=False, stop=(lt == LTL - 1))
                        ins.then_inc(s.q_dnm, 1)

        @blk.vector
        def _(v):
            nvs = 0
            def lk(ins):
                nonlocal nvs
                ins.then_inc(s.q_vs, 1)
                nvs += 1
                v.wait_ge(s.q_vs, FB["q_vs"] + nvs)
            for t in range(2):
                v.wait_ge(s.q_lg, FB["q_lg"] + t + 1)
                v.tensor_copy(C.lgf[0:8, :], C.psg[0:8, 0:512]).then_inc(s.q_lge, 1)
                for k in range(64):
                    b = k % 2
                    kk = t * 64 + k
                    v.wait_ge(s.q_upm, FB["q_upm"] + 3 + kk)
                    v.wait_ge(s.q_ups, FB["q_ups"] + 16 * (kk + 1))
                    lk(v.tensor_relu(C.scr[:, 0:512], C.ps[:, b * 512:(b + 1) * 512]))
                    v.tensor_mul(C.upst[:, b * 512:(b + 1) * 512],
                                 C.scr[:, 0:512], C.scr[:, 0:512]).then_inc(s.q_upe, 1)
                v.wait_ge(s.q_lt, FB["q_lt"] + t + 1)
                for sb in range(4):
                    lk(v.tensor_copy(C.ltm[:, sb * 8:(sb + 1) * 8],
                                     C.pst[:, sb * 128:sb * 128 + 8]))
                for sb in range(4):
                    for ci, (i, j2) in enumerate(PAIRS):
                        lk(v.tensor_add(C.prs[:, ci:ci + 1], C.ltm[:, sb * 8 + i:sb * 8 + i + 1],
                                        C.ltm[:, sb * 8 + j2:sb * 8 + j2 + 1]))
                    ins = v.reduce_max(C.rwc[:, sb:sb + 1], C.prs[:, 0:28], axis=AX)
                ins.then_inc(s.q_rwc, 1)
                for k in range(16):
                    b = k % 2
                    kk = t * 16 + k
                    v.wait_ge(s.q_f1m, FB["q_f1m"] + 3 + kk)
                    v.wait_ge(s.q_f1s, FB["q_f1s"] + 16 * (kk + 1))
                    v.tensor_copy(C.f1st[:, b * 512:(b + 1) * 512],
                                  C.ps[:, b * 512:(b + 1) * 512]).then_inc(s.q_f1e, 1)
                v.wait_ge(s.q_rwr, FB["q_rwr"] + 1 + 3 * t)
                v.tensor_copy(C.rwrow[0:1, :], C.pst2[0:1, 0:512]).then_inc(s.q_rwr, 1)
                v.wait_ge(s.q_rwr, FB["q_rwr"] + 3 + 3 * t)
                v.tensor_copy(C.rwbc[:, :], C.psg[0:128, 0:512]).then_inc(s.q_rwb, 1)
                for k in range(16):
                    b = k % 2
                    kk = t * 16 + k
                    v.wait_ge(s.q_exm, FB["q_exm"] + 3 + kk)
                    v.wait_ge(s.q_exs, FB["q_exs"] + 16 * (kk + 1))
                    v.tensor_mul(C.exst[:, b * 512:(b + 1) * 512],
                                 C.ps[:, b * 512:(b + 1) * 512], C.rwbc[:, 0:512]).then_inc(s.q_exe, 1)
                for k in range(32):
                    b = k % 2
                    kk = t * 32 + k
                    v.wait_ge(s.q_dnm, FB["q_dnm"] + 3 + kk)
                    v.wait_ge(s.q_dns, FB["q_dns"] + 16 * (kk + 1))
                    v.tensor_copy(C.dnst[:, b * 512:(b + 1) * 512],
                                  C.ps[:, b * 512:(b + 1) * 512]).then_inc(s.q_dne, 1)


  with contextlib.ExitStack() as _es:
    C.otm = _es.enter_context(nc.sbuf_tensor([128, 2 * DM], BF16))
    C.xa = _es.enter_context(nc.sbuf_tensor([128, 2 * DM], F32))
    C.lnw = _es.enter_context(nc.sbuf_tensor([128, 2 * DM], F32))
    C.sq = _es.enter_context(nc.sbuf_tensor([128, DM], F32))
    C.sq2 = _es.enter_context(nc.sbuf_tensor([128, DM], F32))
    C.gbc = _es.enter_context(nc.sbuf_tensor([128, DM], F32))
    C.bbc = _es.enter_context(nc.sbuf_tensor([128, DM], F32))
    with nc.Block(name + "_eload") as blk:
        @blk.sync
        def _(sy):
            sy.dma_start(out=C.gbc[:, :], in_=C.gbc_in[:, :]).then_inc(s.q_gb, 16)
            sy.dma_start(out=C.bbc[:, :], in_=C.bbc_in[:, :]).then_inc(s.q_gb, 16)
            sy.wait_ge(s.q_gb, FB["q_gb"] + 32)
    with nc.Block(name + "_ec") as blk:
        @blk.sync
        def _(sy):
            sy.sem_inc(s.q_d, 2)
            sy.sem_inc(s.q_epo, 32)

    with nc.Block(name + "_e") as blk:
        @blk.sync
        def _(sy):
            for k in range(8):
                b = k % 2
                sy.wait_ge(s.q_d, FB["q_d"] + k + 1)
                sy.dma_start_transpose(
                    C.otm[:, b * DM:(b + 1) * DM],
                    C.sp_os[0:DM, k * 128:(k + 1) * 128]).then_inc(s.q_ep, 16)
                sy.dma_start(out=C.xa[:, b * DM:(b + 1) * DM],
                             in_=xin[k * 128:(k + 1) * 128, :]).then_inc(s.q_ep, 16)
                sy.wait_ge(s.q_d, FB["q_d"] + k + 3)
                sy.dma_start(out=xout[k * 128:(k + 1) * 128, :],
                             in_=C.lnw[:, b * DM:(b + 1) * DM]).then_inc(s.q_epo, 16)
            sy.wait_ge(s.q_epo, FB["q_epo"] + 32 + 16 * 8)

        @blk.vector
        def _(v):
          with contextlib.ExitStack() as es:
            ve = WR(v, es, name + "_ve2", 1 + FB["q_vs2"], 1)
            v_ep = WR(v, es, name + "_v_ep", 32 + FB["q_ep"], 32)
            v_a = WR(v, es, name + "_v_a2", 1 + FB["q_a"], 1)
            v_c = WR(v, es, name + "_v_c2", 1 + FB["q_c"], 1)
            v_eo = WR(v, es, name + "_v_eo", 16 + FB["q_epo"], 16)
            with v.Fori(0, 4) as tt:
                for b in range(2):
                    o = C.otm[:, b * DM:(b + 1) * DM]
                    r = C.red[:, b * 8:(b + 1) * 8]
                    v_ep.wait(s.q_ep)
                    ve.link(v.reduce_sum(r[:, 0:1], o, axis=AX), s.q_vs2)
                    v_a.wait(s.q_a)
                    ve.link(v.tensor_scalar_mul(r[:, 2:3], r[:, 0:1], 1.0 / DM), s.q_vs2)
                    ve.link(v.tensor_mul(r[:, 3:4], r[:, 2:3], r[:, 2:3]), s.q_vs2)
                    v.scalar_tensor_tensor(r[:, 4:5], r[:, 1:2], 1.0 / DM, r[:, 3:4],
                                           ALU.mult, ALU.subtract).then_inc(s.q_b, 1)
                    v_c.wait(s.q_c)
                    ve.link(v.tensor_scalar(C.sq[:, :], o, r[:, 2:3], r[:, 5:6],
                                            ALU.subtract, ALU.mult), s.q_vs2)
                    ve.link(v.tensor_mul(C.sq[:, :], C.sq[:, :], C.gbc[:, :]), s.q_vs2)
                    ve.link(v.tensor_add(C.sq[:, :], C.sq[:, :], C.bbc[:, :]), s.q_vs2)
                    v_eo.wait(s.q_epo)
                    v.scalar_tensor_tensor(C.lnw[:, b * DM:(b + 1) * DM], C.sq[:, :], 0.5,
                                           C.xa[:, b * DM:(b + 1) * DM],
                                           ALU.mult, ALU.add).then_inc(s.q_d, 1)

        @blk.scalar
        def _(sc):
          with contextlib.ExitStack() as es:
            se = WR(sc, es, name + "_se2", 1 + FB["q_ss2"], 1)
            s_ep = WR(sc, es, name + "_s_ep", 32 + FB["q_ep"], 32)
            s_b = WR(sc, es, name + "_s_b2", 1 + FB["q_b"], 1)
            with sc.Fori(0, 4) as tt:
                for b in range(2):
                    o = C.otm[:, b * DM:(b + 1) * DM]
                    r = C.red[:, b * 8:(b + 1) * 8]
                    s_ep.wait(s.q_ep)
                    sc.activation(C.sq2[:, :], o, AF.Square, accum_out=r[:, 1:2]).then_inc(s.q_a, 1)
                    s_b.wait(s.q_b)
                    se.link(sc.activation(r[:, 6:7], r[:, 4:5], AF.Ln, bias=C.epsc[:, 0:1]), s.q_ss2)
                    sc.activation(r[:, 5:6], r[:, 6:7], AF.Exp, scale=-0.5).then_inc(s.q_c, 1)

    if final_wait:
        with nc.Block(name + "_fw") as blk:
            @blk.sync
            def _(sy):
                sy.wait_ge(s.q_epo, FB["q_epo"] + 32 + 16 * 8)


def build():
    nc = bass.Bass("TRN2", target_bir_lowering=False, detect_race_conditions=False)
    C = _C()

    x_in = nc.dram_tensor("x", [NT, DM], F32, kind="ExternalInput").ap()
    out = nc.dram_tensor("out", [NT, DM], F32, kind="ExternalOutput").ap()
    C.wup = nc.dram_tensor("wup", [ITL, 128, DM], BF16, kind="ExternalInput").ap()
    C.wdn = nc.dram_tensor("wdn", [DTL, 128, FI], BF16, kind="ExternalInput").ap()
    C.wf1 = nc.dram_tensor("wf1", [LTL, 128, DM], BF16, kind="ExternalInput").ap()
    C.wex = nc.dram_tensor("wex", [LTL, 128, LAT], BF16, kind="ExternalInput").ap()
    C.wf2 = nc.dram_tensor("wf2", [DTL, 128, LAT], BF16, kind="ExternalInput").ap()
    wg_in = nc.dram_tensor("wgt", [128, DTL * 8], F32, kind="ExternalInput").ap()
    C.wip = nc.dram_tensor("wip", [145, 128, DM], BF16, kind="ExternalInput").ap()
    C.wout = nc.dram_tensor("wout", [DTL, 128, FI], BF16, kind="ExternalInput").ap()
    cw_in = nc.dram_tensor("cw", [128, NCV * 4], F32, kind="ExternalInput").ap()
    C.cwt = nc.dram_tensor("cwt", [NCV, 128, 4], F32, kind="ExternalInput").ap()
    C.cbt = nc.dram_tensor("cbt", [NCV, 128, 1], F32, kind="ExternalInput").ap()
    onescol_in = nc.dram_tensor("onescol", [128, 1], F32, kind="ExternalInput").ap()
    cb_in = nc.dram_tensor("cb", [128, NCV], F32, kind="ExternalInput").ap()
    dtb_in = nc.dram_tensor("dtb", [128, 1], F32, kind="ExternalInput").ap()
    alog_in = nc.dram_tensor("alog", [128, 1], F32, kind="ExternalInput").ap()
    dbc_in = nc.dram_tensor("dbc", [128, 128], F32, kind="ExternalInput").ap()
    C.gbc_in = nc.dram_tensor("gbc", [128, DM], F32, kind="ExternalInput").ap()
    C.bbc_in = nc.dram_tensor("bbc", [128, DM], F32, kind="ExternalInput").ap()
    C.nwt_in = nc.dram_tensor("nwt", [128, ITL], F32, kind="ExternalInput").ap()
    id16_in = nc.dram_tensor("id16", [128, 128], BF16, kind="ExternalInput").ap()
    id32_in = nc.dram_tensor("id32", [128, 128], F32, kind="ExternalInput").ap()
    ones32_in = nc.dram_tensor("ones32", [1, 128], F32, kind="ExternalInput").ap()
    strict_in = nc.dram_tensor("strictm", [128, 128], BF16, kind="ExternalInput").ap()
    negs_in = nc.dram_tensor("negstrict", [128, 128], BF16, kind="ExternalInput").ap()
    qtril_in = nc.dram_tensor("qtril", [128, 128], BF16, kind="ExternalInput").ap()
    mask_in = nc.dram_tensor("pmask", [128, 1], F32, kind="ExternalInput").ap()
    nonce_in = nc.dram_tensor("nonce", [1, 1], I32, kind="ExternalInput").ap()

    C.sp_lnt = nc.dram_tensor("sp_lnt", [NT, DM], BF16).ap()
    C.sp_x1 = nc.dram_tensor("sp_x1", [NT, DM], F32).ap()
    C.sp_x2 = nc.dram_tensor("sp_x2", [NT, DM], F32).ap()
    C.sp_h = nc.dram_tensor("sp_h", [128, ITL * 512], BF16).ap()
    C.sp_lat = nc.dram_tensor("sp_lat", [128, LTL * 512], BF16).ap()
    C.sp_moe = nc.dram_tensor("sp_moe", [128, LTL * 512], BF16).ap()
    C.sp_os = nc.dram_tensor("sp_os", [DM, NT], BF16).ap()
    C.sp_gate = nc.dram_tensor("sp_gate", [FI, NT], BF16).ap()
    C.sp_hsf = nc.dram_tensor("sp_hsf", [FI, NT], BF16).ap()
    C.sp_dtraw = nc.dram_tensor("sp_dtraw", [128, NT], F32).ap()
    C.sp_X = nc.dram_tensor("sp_X", [NZ, 128, FI], BF16).ap()
    C.sp_bc = nc.dram_tensor("sp_bc", [128, 16 * NT], F32).ap()
    C.sp_pack = nc.dram_tensor("sp_pack", [NZ, 128, 2048], BF16).ap()
    C.sp_dtv = nc.dram_tensor("sp_dtv", [128, NT], F32).ap()
    C.sp_pc3 = nc.dram_tensor("sp_pc3", [128, NCV * 3], F32).ap()
    C.sp_hout = nc.dram_tensor("sp_hout", [128, NCV * 3], F32).ap()
    C.sp_outg = nc.dram_tensor("sp_outg", [NT, FI], BF16).ap()
    C.sp_attn = nc.dram_tensor("sp_attn", [DM, NT], BF16).ap()
    C.shx = nc.dram_tensor("shx", [2 * 128, 8448], F32, kind="Internal", addr_space="Shared").ap()
    C.shf = nc.dram_tensor("shf", [1, 8], I32, kind="Internal", addr_space="Shared").ap()

    st = contextlib.ExitStack()
    C.st = st
    with st:
        C.q_pro = st.enter_context(nc.semaphore("q_pro"))
        C.q_prog = st.enter_context(nc.semaphore("q_prog"))

        C.wg = st.enter_context(nc.sbuf_tensor([128, DTL * 8], BF16))
        C.cw = st.enter_context(nc.sbuf_tensor([128, NCV * 4], F32))
        C.cb = st.enter_context(nc.sbuf_tensor([128, NCV], F32))
        C.cbn = st.enter_context(nc.sbuf_tensor([128, NCV], F32))
        C.dtb = st.enter_context(nc.sbuf_tensor([128, 1], F32))
        C.aneg = st.enter_context(nc.sbuf_tensor([128, 1], F32))
        C.dbc = st.enter_context(nc.sbuf_tensor([128, 128], F32))
        C.id16 = st.enter_context(nc.sbuf_tensor([128, 128], BF16))
        C.id32 = st.enter_context(nc.sbuf_tensor([128, 128], F32))
        C.ones32 = st.enter_context(nc.sbuf_tensor([1, 128], F32))
        C.onescol = st.enter_context(nc.sbuf_tensor([128, 1], F32))
        C.strict = st.enter_context(nc.sbuf_tensor([128, 128], BF16))
        C.negstrict = st.enter_context(nc.sbuf_tensor([128, 128], BF16))
        C.qtril = st.enter_context(nc.sbuf_tensor([128, 128], BF16))
        C.pmask = st.enter_context(nc.sbuf_tensor([128, 1], F32))
        C.noncesb = st.enter_context(nc.sbuf_tensor([1, 1], I32))
        C.flsb = st.enter_context(nc.sbuf_tensor([1, 1], I32))

        C.red = st.enter_context(nc.sbuf_tensor([128, 16], F32))
        C.epsc = st.enter_context(nc.sbuf_tensor([128, 1], F32))

        with nc.Block("prol") as blk:
            @blk.sync
            def _(sy):
                for dst, srcap in ((C.cw, cw_in), (C.cb, cb_in), (C.dtb, dtb_in),
                                   (C.aneg, alog_in), (C.dbc, dbc_in), (C.id16, id16_in),
                                   (C.id32, id32_in), (C.ones32, ones32_in), (C.onescol, onescol_in), (C.strict, strict_in),
                                   (C.qtril, qtril_in), (C.pmask, mask_in), (C.noncesb, nonce_in),
                                   (C.negstrict, negs_in)):
                    sy.dma_start(out=dst[:, :], in_=srcap[:, :]).then_inc(C.q_pro, 16)
                sy.wait_ge(C.q_pro, 14 * 16)
            @blk.gpsimd
            def _(g):
                g.memset(C.epsc[:, :], EPS)
                g.dma_start(out=C.wg[:, :], in_=wg_in[:, :]).then_inc(C.q_prog, 16)
                g.wait_ge(C.q_prog, 16)
                g.wait_ge(C.q_pro, 14 * 16)
                C.gnv = g.value_load(C.noncesb[0:1, 0:1])
                pid = g.partition_id()
                par = pid % 2
                oth = 1 - par
                C.e_par128 = par * 128
                C.e_par2 = par * 2
                C.e_par2p1 = par * 2 + 1
                C.e_oth128 = oth * 128
                C.e_oth2 = oth * 2
                C.e_oth2p1 = oth * 2 + 1
        with nc.Block("prol2") as blk:
            @blk.scalar
            def _(sc):
                sc.activation(C.aneg[:, :], C.aneg[:, :], AF.Exp)
            @blk.vector
            def _(v):
                v.tensor_scalar_mul(C.cbn[:, :], C.cb[:, :], -1.0)
        with nc.Block("prol3") as blk:
            @blk.vector
            def _(v):
                v.tensor_scalar_mul(C.aneg[:, :], C.aneg[:, :], -1.0)

        C.outap = out
        emit_ln(nc, C, x_in, "ln1")
        emit_ffn(nc, C, x_in, C.sp_x1 if PHASES > 1 else out, "ffn1",
                 final_wait=(PHASES == 1))

        def _recycle(fn, *args, **kw):
            snap = nc.snapshot_sems()
            old = C.st
            C.st = None
            fn(*args, **kw)
            C.st = old
            nc.clear_and_free_semaphores(nc.allocated_since(snap))
            nc.all_engine_barrier()

        if PHASES >= 2:
            emit_ln(nc, C, C.sp_x1, "ln2")
            _recycle(emit_inproj, nc, C)
        if PHASES >= 4:
            _recycle(emit_prep, nc, C)
        if PHASES >= 5:
            with contextlib.ExitStack() as sscope:
                C.prevT = sscope.enter_context(nc.sbuf_tensor([128, FI], F32))
                _recycle(emit_ssdA, nc, C)
                if PHASES >= 6:
                    _recycle(emit_exch2, nc, C)
                if PHASES >= 7:
                    _recycle(emit_ssdB, nc, C)
                if PHASES == 5 or PHASES == 6:
                    _dump_prev(nc, C, out)
        if PHASES >= 8:
            _recycle(emit_outproj, nc, C)
        if PHASES >= 9:
            emit_ln(nc, C, C.sp_x2, "ln3")
            emit_ffn(nc, C, C.sp_x2, out, "ffn2", final_wait=True)
        if PHASES < 9:
            _dump_dev(nc, C, out)

    return nc


def _dump_prev(nc, C, out):
    qq = C.st.enter_context(nc.semaphore("q_dprev"))
    with nc.Block("dprev") as blk:
        @blk.gpsimd
        def _(g):
            g.dma_start(out=C.sp_outg[0:128, 0:FI], in_=C.prevT[:, :]).then_inc(qq, 16)
            g.wait_ge(qq, 16)


def _dump_dev(nc, C, out):
    qq = C.st.enter_context(nc.semaphore("q_dump"))
    with nc.Block("dump") as blk:
        @blk.gpsimd
        def _(g):
            n = 0
            if PHASES in (2, 3):
                g.dma_start(out=out[0:1024, 0:1024], in_=C.sp_hsf[0:1024, :]).then_inc(qq, 16)
                g.dma_start(out=out[0:1024, 1024:2048], in_=C.sp_gate[0:1024, :]).then_inc(qq, 16)
                g.dma_start(out=out[0:128, 2048:3072], in_=C.sp_dtraw[:, :]).then_inc(qq, 16)
                g.dma_start(out=out[0:128, 3072:4096], in_=C.sp_bc[:, 0:1024]).then_inc(qq, 16)
                n = 4
                if PHASES == 3:
                    for z in range(NZ):
                        g.dma_start(out=out[128 + z * 64:128 + (z + 1) * 64, 2048:4096],
                                    in_=C.sp_X[z, 64:128, 0:2048]).then_inc(qq, 16)
                    n += 8
            elif PHASES == 4:
                g.dma_start(out=out[0:128, 0:2048], in_=C.sp_pack[0, :, :]).then_inc(qq, 16)
                g.dma_start(out=out[0:128, 2048:3072], in_=C.sp_dtv[:, :]).then_inc(qq, 16)
                g.dma_start(out=out[128:256, 0:4096], in_=C.sp_X[0, :, 0:4096]).then_inc(qq, 16)
                g.dma_start(out=out[256:384, 0:2048], in_=C.sp_pack[5, :, :]).then_inc(qq, 16)
                n = 4
            elif PHASES in (5, 6):
                g.dma_start(out=out[0:128, 0:4096], in_=C.sp_outg[0:128, 0:4096]).then_inc(qq, 16)
                g.dma_start(out=out[128:256, 0:4096], in_=C.sp_outg[0:128, 4096:8192]).then_inc(qq, 16)
                n = 2
            elif PHASES == 7:
                if DBG_SSD:
                    g.dma_start(out=out[512:1024, 0:4096],
                                in_=C.sp_outg[512:1024, 0:4096]).then_inc(qq, 16)
                else:
                    g.dma_start(out=out[0:1024, 0:4096], in_=C.sp_outg[:, 0:4096]).then_inc(qq, 16)
                n = 1
            elif PHASES == 8:
                g.dma_start(out=out[:, :], in_=C.sp_x2[:, :]).then_inc(qq, 16)
                n = 1
            g.wait_ge(qq, n * 16)




def emit_inproj(nc, C, name="ip"):
  s = mk_sems(nc, C.st, name, ["q_lnf", "q_w", "q_m", "q_e", "q_ge", "q_gs",
                               "q_cv", "q_cwl", "q_cbn", "q_cv2", "q_cv3", "q_hs",
                               "q_hss", "q_pcs", "q_bce", "q_bcs", "q_dte", "q_dts",
                               "q_vs"])
  with contextlib.ExitStack() as _es:
    lnip = _es.enter_context(nc.sbuf_tensor([128, DTL * 1024], BF16))
    wipb = _es.enter_context(nc.sbuf_tensor([128, 2 * 4096], BF16))
    esb = _es.enter_context(nc.sbuf_tensor([128, 2 * 1024], BF16))
    pre = _es.enter_context(nc.sbuf_tensor([128, 2 * 1024], F32))
    cv = _es.enter_context(nc.sbuf_tensor([128, 2 * 1024], F32))
    e1 = _es.enter_context(nc.sbuf_tensor([128, 1021], F32))
    r1 = _es.enter_context(nc.sbuf_tensor([128, 1021], F32))
    hsw = _es.enter_context(nc.sbuf_tensor([128, 2 * 1024], BF16))
    bce = _es.enter_context(nc.sbuf_tensor([128, 2 * 1024], F32))
    cwx = _es.enter_context(nc.sbuf_tensor([128, 2 * 4], F32))
    cbx = _es.enter_context(nc.sbuf_tensor([128, 2], F32))
    cbnx = _es.enter_context(nc.sbuf_tensor([128, 2], F32))
    ps = _es.enter_context(nc.psum_tensor([128, 2048], F32))

    with nc.Block(name + "_c") as blk:
        @blk.sync
        def _(sy):
            sy.sem_inc(s.q_m, 2)
            sy.sem_inc(s.q_e, 2)
            sy.sem_inc(s.q_ge, 2)
            sy.sem_inc(s.q_gs, 32)
            sy.sem_inc(s.q_bcs, 32)
            sy.sem_inc(s.q_pcs, 64)
            sy.sem_inc(s.q_cwl, 64)
            sy.sem_inc(s.q_hss, 32)
            sy.sem_inc(s.q_cv2, 2)
            for dt in range(DTL):
                sy.dma_start_transpose(
                    lnip[:, dt * 1024:(dt + 1) * 1024],
                    C.sp_lnt[0:1024, dt * 128:(dt + 1) * 128]).then_inc(s.q_lnf, 16)
            sy.wait_ge(s.q_lnf, DTL * 16)
        @blk.gpsimd
        def _(g):
            g.memset(hsw[:, 0:3], 0.0)
            g.memset(hsw[:, 1024:1027], 0.0)

    with nc.Block(name) as blk:
        @blk.gpsimd
        def _(g):
          with contextlib.ExitStack() as es:
            w_m = WR(g, es, name + "_w_m", 1, 1)
            for kk in range(145):
                b = kk % 2
                w_m.wait(s.q_m)
                g.dma_start(out=wipb[:, b * 4096:(b + 1) * 4096],
                            in_=C.wip[kk, :, :]).then_inc(s.q_w, 16)

        @blk.tensor
        def _(tn):
          with contextlib.ExitStack() as es:
            t_w = WR(tn, es, name + "_t_w", 16, 16)
            t_e = WR(tn, es, name + "_t_e", 1, 1)

            def mm_etile(b):
                t_w.wait(s.q_w)
                t_e.wait(s.q_e)
                for tch in range(2):
                    for dt in range(DTL):
                        ins = tn.matmul(
                            ps[:, b * 1024 + tch * 512:b * 1024 + (tch + 1) * 512],
                            wipb[:, b * 4096 + dt * 128:b * 4096 + (dt + 1) * 128],
                            lnip[:, dt * 1024 + tch * 512:dt * 1024 + (tch + 1) * 512],
                            start=(dt == 0), stop=(dt == DTL - 1))
                ins.then_inc(s.q_m, 1)

            with tn.Fori(0, 72) as j:
                for b in range(2):
                    mm_etile(b)
            mm_etile(0)

        @blk.vector
        def _(v):
          with contextlib.ExitStack() as es:
            v_m = WR(v, es, name + "_v_m", 3, 1)
            v_gs = WR(v, es, name + "_v_gs", 16, 16)
            v_pcs = WR(v, es, name + "_v_pcs", 32, 32)
            v_cwl = WR(v, es, name + "_v_cwl", 96, 32)
            v_cv3 = WR(v, es, name + "_v_cv3", 0, 1)
            v_hss = WR(v, es, name + "_v_hss", 16, 16)
            v_bcs = WR(v, es, name + "_v_bcs", 32, 32)
            ve = WR(v, es, name + "_ve", 1, 1)
            with v.Fori(0, 32) as j:
                for b in range(2):
                    v_m.wait(s.q_m)
                    v_gs.wait(s.q_gs)
                    v.tensor_copy(esb[:, b * 1024:(b + 1) * 1024],
                                  ps[:, b * 1024:(b + 1) * 1024]).then_inc(s.q_e, 1)
            with v.Fori(0, 32) as j:
                for b in range(2):
                    p0 = b * 1024
                    v_m.wait(s.q_m)
                    v_pcs.wait(s.q_pcs)
                    v.tensor_copy(pre[:, p0:p0 + 1024],
                                  ps[:, p0:p0 + 1024]).then_inc(s.q_e, 1)
                    v_cwl.wait(s.q_cwl)
                    v_cv3.wait(s.q_cv3)
                    ve.link(v.tensor_scalar_mul(cv[:, p0:p0 + 1021], pre[:, p0:p0 + 1021],
                                                cwx[:, 4 * b:4 * b + 1]), s.q_vs)
                    for tp in range(1, 4):
                        ve.link(v.scalar_tensor_tensor(cv[:, p0:p0 + 1021], pre[:, p0 + tp:p0 + tp + 1021],
                                                       cwx[:, 4 * b + tp:4 * b + tp + 1],
                                                       cv[:, p0:p0 + 1021], ALU.mult, ALU.add), s.q_vs)
                    v.engine_nop().then_inc(s.q_cv2, 1)
            with v.Fori(0, 8) as j:
                for b in range(2):
                    v_m.wait(s.q_m)
                    v_bcs.wait(s.q_bcs)
                    v.tensor_copy(bce[:, b * 1024:(b + 1) * 1024],
                                  ps[:, b * 1024:(b + 1) * 1024]).then_inc(s.q_e, 1)
            v_m.wait(s.q_m)
            v.wait_ge(s.q_bcs, 32 + 16 * 32)
            v.tensor_copy(bce[:, 0:1024], ps[:, 0:1024]).then_inc(s.q_e, 1)

        @blk.scalar
        def _(sc):
          with contextlib.ExitStack() as es:
            s_cv2 = WR(sc, es, name + "_s_cv2", 3, 1)
            s_cbn = WR(sc, es, name + "_s_cbn", 1, 1)
            for kk in range(64):
                b = kk % 2
                sc.wait_ge(s.q_e, 3 + kk)
                sc.dma_start(out=C.sp_gate[kk * 128:(kk + 1) * 128, :],
                             in_=esb[:, b * 1024:(b + 1) * 1024]).then_inc(s.q_gs, 16)
            with contextlib.ExitStack() as es2:
              s_hss = WR(sc, es2, name + "_s_hss", 16, 16)
              with sc.Fori(0, 32) as j:
                for b in range(2):
                    s_cv2.wait(s.q_cv2)
                    s_hss.wait(s.q_hss)
                    sc.activation(hsw[:, b * 1024 + 3:b * 1024 + 1024], cv[:, b * 1024:b * 1024 + 1021],
                                  AF.Silu, bias=cbx[:, b:b + 1]).then_inc(s.q_hs, 1)
                    sc.nop().then_inc(s.q_cv3, 1)
            for kk in range(16):
                b = kk % 2
                sc.wait_ge(s.q_e, 131 + kk)
                sc.dma_start(out=C.sp_bc[:, kk * 1024:(kk + 1) * 1024],
                             in_=bce[:, b * 1024:(b + 1) * 1024]).then_inc(s.q_bcs, 16)
                sc.dma_start(out=C.sp_hout[:, (64 + kk) * 3:(64 + kk) * 3 + 3],
                             in_=bce[:, b * 1024 + 1021:b * 1024 + 1024]).then_inc(s.q_bcs, 16)
            sc.wait_ge(s.q_e, 147)
            sc.dma_start(out=C.sp_dtraw[:, :], in_=bce[:, 0:1024]).then_inc(s.q_dts, 16)

        @blk.sync
        def _(sy):
            for kk in range(64):
                b = kk % 2
                sy.wait_ge(s.q_cv2, kk + 1)
                sy.dma_start(out=cwx[:, 4 * b:4 * (b + 1)],
                             in_=C.cwt[kk, :, :]).then_inc(s.q_cwl, 16)
                sy.dma_start(out=cbx[:, b:b + 1],
                             in_=C.cbt[kk, :, :]).then_inc(s.q_cwl, 16)
                sy.wait_ge(s.q_e, 67 + kk)
                sy.dma_start(out=C.sp_pc3[:, kk * 3:kk * 3 + 3],
                             in_=pre[:, b * 1024:b * 1024 + 3]).then_inc(s.q_pcs, 16)
                sy.dma_start(out=C.sp_hout[:, kk * 3:kk * 3 + 3],
                             in_=pre[:, b * 1024 + 1021:b * 1024 + 1024]).then_inc(s.q_pcs, 16)
                sy.wait_ge(s.q_hs, kk + 1)
                sy.dma_start(out=C.sp_hsf[kk * 128:(kk + 1) * 128, :],
                             in_=hsw[:, b * 1024:(b + 1) * 1024]).then_inc(s.q_hss, 16)
            sy.wait_ge(s.q_dts, 16)
            sy.wait_ge(s.q_hss, 32 + 64 * 16)
            sy.wait_ge(s.q_gs, 32 + 64 * 16)
            sy.wait_ge(s.q_bcs, 32 + 16 * 32)
            sy.wait_ge(s.q_pcs, 64 + 128 * 16)
            sy.wait_ge(s.q_cwl, 64 + 128 * 16)



def emit_xtrans(nc, C, name="xt"):
  s = mk_sems(nc, C.st, name, ["q_l", "q_tr", "q_te", "q_sp"])
  with contextlib.ExitStack() as _es:
    ht = _es.enter_context(nc.sbuf_tensor([128, 2 * 1024], BF16))
    trev = _es.enter_context(nc.sbuf_tensor([128, 2 * 1024], BF16))
    ptr = _es.enter_context(nc.psum_tensor([128, 2048], BF16))
    with nc.Block(name + "_c") as blk:
        @blk.sync
        def _(sy):
            sy.sem_inc(s.q_tr, 2)
            sy.sem_inc(s.q_sp, 2)
            sy.sem_inc(s.q_te, 2)
    with nc.Block(name) as blk:
        @blk.sync
        def _(sy):
            for kk in range(64):
                b = kk % 2
                sy.wait_ge(s.q_tr, kk + 1)
                sy.dma_start(out=ht[:, b * 1024:(b + 1) * 1024],
                             in_=C.sp_hsf[kk * 128:(kk + 1) * 128, :]).then_inc(s.q_l, 16)
                sy.wait_ge(s.q_te, kk + 3)
                for z in range(NZ):
                    sy.dma_start(out=C.sp_X[z, :, kk * 128:(kk + 1) * 128],
                                 in_=trev[:, b * 1024 + z * 128:b * 1024 + (z + 1) * 128]
                                 ).then_inc(s.q_sp, 16)
            sy.wait_ge(s.q_sp, 2 + 128 * 64)
        @blk.tensor
        def _(tn):
          with contextlib.ExitStack() as es:
            t_l = WR(tn, es, name + "_t_l", 16, 16)
            t_te = WR(tn, es, name + "_t_te", 1, 1)
            with tn.Fori(0, 32) as j:
                for b in range(2):
                    t_l.wait(s.q_l)
                    t_te.wait(s.q_te)
                    for z in range(NZ):
                        ins = tn.transpose(ptr[:, b * 1024 + z * 128:b * 1024 + (z + 1) * 128],
                                           ht[:, b * 1024 + z * 128:b * 1024 + (z + 1) * 128],
                                           C.id16[:, :])
                    ins.then_inc(s.q_tr, 1)
        @blk.vector
        def _(v):
          with contextlib.ExitStack() as es:
            v_tr = WR(v, es, name + "_v_tr", 3, 1)
            v_sp = WR(v, es, name + "_v_sp", 2, 128)
            with v.Fori(0, 32) as j:
                for b in range(2):
                    v_tr.wait(s.q_tr)
                    v_sp.wait(s.q_sp)
                    v.tensor_copy(trev[:, b * 1024:(b + 1) * 1024],
                                  ptr[:, b * 1024:(b + 1) * 1024]).then_inc(s.q_te, 1)


def emit_prep(nc, C, name="pp"):
  s = mk_sems(nc, C.st, name, ["q_x", "q_fl", "q_po", "q_hr", "q_cv", "q_ss",
                               "q_pk", "q_ps", "q_dt", "q_dts", "q_p3a", "q_p3b",
                               "q_p3c", "q_tr", "q_te", "q_p3x", "q_vs", "q_ssl"])
  with contextlib.ExitStack() as _es:
    hout = _es.enter_context(nc.sbuf_tensor([128, 240], F32))
    hrecv = _es.enter_context(nc.sbuf_tensor([128, 240], F32))
    bcb = _es.enter_context(nc.sbuf_tensor([128, 16 * 1024], F32))
    ext = _es.enter_context(nc.sbuf_tensor([128, 1027], F32))
    cvb = _es.enter_context(nc.sbuf_tensor([128, 2 * 1024], F32))
    eb = _es.enter_context(nc.sbuf_tensor([128, 1024], F32))
    rb = _es.enter_context(nc.sbuf_tensor([128, 1024], F32))
    bcs = _es.enter_context(nc.sbuf_tensor([128, 16 * 1024], BF16))
    dtr = _es.enter_context(nc.sbuf_tensor([128, 1024], F32))
    dtv = _es.enter_context(nc.sbuf_tensor([128, 1024], F32))
    u1 = _es.enter_context(nc.sbuf_tensor([128, 1024], F32))
    u2 = _es.enter_context(nc.sbuf_tensor([128, 1024], F32))
    pc3 = _es.enter_context(nc.sbuf_tensor([128, 192], F32))
    c3 = _es.enter_context(nc.sbuf_tensor([128, 6], F32))
    w3 = _es.enter_context(nc.sbuf_tensor([128, 6], F32))
    pad = _es.enter_context(nc.sbuf_tensor([128, 128], BF16))
    p3s = _es.enter_context(nc.sbuf_tensor([4, 2 * 128], BF16))
    ptr = _es.enter_context(nc.psum_tensor([128, 2048], BF16))

    with nc.Block(name + "_x") as blk:
        @blk.sync
        def _(sy):
            sy.dma_start(out=hout[:, :], in_=C.sp_hout[:, :]).then_inc(s.q_x, 16)
            sy.dma_start(out=pc3[:, :], in_=C.sp_pc3[:, 0:192]).then_inc(s.q_x, 16)
            sy.dma_start(out=bcb[:, :], in_=C.sp_bc[:, :]).then_inc(s.q_x, 16)
            sy.dma_start(out=dtr[:, :], in_=C.sp_dtraw[:, :]).then_inc(s.q_x, 16)
            sy.wait_ge(s.q_x, 64)
        @blk.gpsimd
        def _(g):
            g.memset(pad[:, :], 0.0)
            g.wait_ge(s.q_x, 16)
            with (g.register(name + "_nd") as nd, g.register(name + "_rexp") as rexp):
                g.dma_start(out=C.shx[bass.ds(C.e_par128, 128), 0:240],
                            in_=hout[:, :]).then_inc(s.q_fl, 16)
                g.wait_ge(s.q_fl, 16)
                g.dma_start(out=C.shf[0:1, bass.ds(C.e_par2, 1)],
                            in_=C.noncesb[0:1, 0:1]).then_inc(s.q_fl, 16)
                g.wait_ge(s.q_fl, 32)
                g.reg_mov(nd, 1)
                g.reg_mov(rexp, 32)
                if not SIM_SKIP_POLL:
                    with g.While(nd):
                        g.dma_start(out=C.flsb[0:1, 0:1],
                                    in_=C.shf[0:1, bass.ds(C.e_oth2, 1)]).then_inc(s.q_fl, 16)
                        g.reg_add(rexp, rexp, 16)
                        g.wait_ge(s.q_fl, rexp)
                        fv = g.value_load(C.flsb[0:1, 0:1])
                        g.reg_mov(nd, fv - C.gnv)
                else:
                    g.dma_start(out=C.shx[bass.ds(C.e_oth128, 128), 0:240],
                                in_=hout[:, :]).then_inc(s.q_fl, 16)
                    g.wait_ge(s.q_fl, 48)
                g.dma_start(out=hrecv[:, :],
                            in_=C.shx[bass.ds(C.e_oth128, 128), 0:240]).then_inc(s.q_po, 16)
                g.wait_ge(s.q_po, 16)

    with nc.Block(name + "_m") as blk:
        @blk.vector
        def _(v):
          with contextlib.ExitStack() as es:
            ve = WR(v, es, name + "_ve", 1, 1)
            ve.link(v.tensor_scalar_mul(hrecv[:, :], hrecv[:, :], C.pmask[:, 0:1]), s.q_vs)
            v.engine_nop().then_inc(s.q_hr, 1)
            for ct in range(16):
                cg = 64 + ct
                if ct >= 2:
                    v.wait_ge(s.q_pk, ct - 1)
                ve.link(v.tensor_copy(ext[:, 0:3], hrecv[:, cg * 3:cg * 3 + 3]), s.q_vs)
                ve.link(v.tensor_copy(ext[:, 3:1027], bcb[:, ct * 1024:(ct + 1) * 1024]), s.q_vs)
                ve.link(v.tensor_scalar_mul(cvb[:, (ct % 2) * 1024:(ct % 2) * 1024 + 1024],
                                            ext[:, 0:1024],
                                            C.cw[:, cg * 4:cg * 4 + 1]), s.q_vs)
                for tp in range(1, 4):
                    ve.link(v.scalar_tensor_tensor(cvb[:, (ct % 2) * 1024:(ct % 2) * 1024 + 1024],
                                                   ext[:, tp:tp + 1024],
                                                   C.cw[:, cg * 4 + tp:cg * 4 + tp + 1],
                                                   cvb[:, (ct % 2) * 1024:(ct % 2) * 1024 + 1024],
                                                   ALU.mult, ALU.add), s.q_vs)
                v.engine_nop().then_inc(s.q_cv, 1)
            ve.link(v.tensor_scalar_add(u1[:, :], dtr[:, :], C.dtb[:, 0:1]), s.q_vs)
            v.engine_nop().then_inc(s.q_dt, 1)
            v.wait_ge(s.q_dt, 2)
            ve.link(v.tensor_scalar_max(dtv[:, :], u1[:, :], 0.0), s.q_vs)
            ve.link(v.tensor_add(dtv[:, :], dtv[:, :], u2[:, :]), s.q_vs)
            ve.link(v.tensor_scalar_min(dtv[:, :], dtv[:, :], 100.0), s.q_vs)
            v.engine_nop().then_inc(s.q_dt, 1)
            for ct in range(64):
                if ct >= 2:
                    v.wait_ge(s.q_p3c, ct - 1)
                ve.link(v.tensor_copy(c3[:, 0:3], hrecv[:, ct * 3:ct * 3 + 3]), s.q_vs)
                ve.link(v.tensor_copy(c3[:, 3:6], pc3[:, ct * 3:ct * 3 + 3]), s.q_vs)
                hw = (ct % 2) * 3
                ve.link(v.tensor_scalar_mul(w3[:, hw:hw + 3], c3[:, 0:3],
                                            C.cw[:, ct * 4:ct * 4 + 1]), s.q_vs)
                for tp in range(1, 4):
                    ve.link(v.scalar_tensor_tensor(w3[:, hw:hw + 3], c3[:, tp:tp + 3],
                                                   C.cw[:, ct * 4 + tp:ct * 4 + tp + 1],
                                                   w3[:, hw:hw + 3], ALU.mult, ALU.add), s.q_vs)
                v.engine_nop().then_inc(s.q_p3a, 1)

        @blk.scalar
        def _(sc):
          with contextlib.ExitStack() as es:
            se = WR(sc, es, name + "_se", 1, 1)
            for ct in range(16):
                sc.wait_ge(s.q_cv, ct + 1)
                sc.activation(bcs[:, ct * 1024:(ct + 1) * 1024],
                              cvb[:, (ct % 2) * 1024:(ct % 2) * 1024 + 1024],
                              AF.Silu, bias=C.cb[:, 64 + ct:64 + ct + 1]).then_inc(s.q_pk, 1)
            sc.wait_ge(s.q_dt, 1)
            se.link(sc.activation(u2[:, :], u1[:, :], AF.Abs), s.q_ssl)
            se.link(sc.activation(u2[:, :], u2[:, :], AF.Exp, scale=-1.0), s.q_ssl)
            sc.activation(u2[:, :], u2[:, :], AF.Ln, bias=1.0).then_inc(s.q_dt, 1)
            for ct in range(64):
                sc.wait_ge(s.q_p3a, ct + 1)
                if ct >= 2:
                    sc.wait_ge(s.q_p3x, (ct - 1) * 16)
                hw = (ct % 2) * 3
                sc.activation(pad[:, hw:hw + 3], w3[:, hw:hw + 3],
                              AF.Silu, bias=C.cb[:, ct:ct + 1]).then_inc(s.q_p3c, 1)

        @blk.sync
        def _(sy):
            for ct in range(16):
                sy.wait_ge(s.q_pk, ct + 1)
                base = ct * 128 if ct < 8 else 1024 + (ct - 8) * 128
                for z in range(NZ):
                    sy.dma_start(out=C.sp_pack[z, :, base:base + 128],
                                 in_=bcs[:, ct * 1024 + z * 128:ct * 1024 + (z + 1) * 128]
                                 ).then_inc(s.q_ps, 16)
            sy.wait_ge(s.q_dt, 3)
            sy.dma_start(out=C.sp_dtv[:, :], in_=dtv[:, :]).then_inc(s.q_dts, 16)
            sy.wait_ge(s.q_p3x, 64 * 16)
            sy.wait_ge(s.q_ps, 16 * 16 * NZ)
            sy.wait_ge(s.q_dts, 16)

        @blk.gpsimd
        def _(g):
            for ct in range(64):
                g.wait_ge(s.q_p3c, ct + 1)
                g.dma_start(out=C.sp_hsf[ct * 128:(ct + 1) * 128, 0:3],
                            in_=pad[:, (ct % 2) * 3:(ct % 2) * 3 + 3]).then_inc(s.q_p3x, 16)
            g.wait_ge(s.q_p3x, 64 * 16)


def emit_ssdA(nc, C, name="sa"):
  s = mk_sems(nc, C.st, name, ["q_zl", "q_zc", "q_ch", "q_xd", "q_stm", "q_stf",
                               "q_so", "q_vs"])
  M = 12
  with contextlib.ExitStack() as _es:
    X = _es.enter_context(nc.sbuf_tensor([128, 2 * FI], BF16))
    pk = _es.enter_context(nc.sbuf_tensor([128, 2 * 2048], BF16))
    dtz = _es.enter_context(nc.sbuf_tensor([128, 2 * 128], F32))
    adt = _es.enter_context(nc.sbuf_tensor([128, 128], F32))
    acum = _es.enter_context(nc.sbuf_tensor([128, 128], F32))
    dcy = _es.enter_context(nc.sbuf_tensor([128, 128], F32))
    dtT = _es.enter_context(nc.sbuf_tensor([128, 128], F32))
    dcyT = _es.enter_context(nc.sbuf_tensor([128, 128], F32))
    adtT = _es.enter_context(nc.sbuf_tensor([128, 128], F32))
    dd = _es.enter_context(nc.sbuf_tensor([128, 128], F32))
    easr = _es.enter_context(nc.sbuf_tensor([1, 128], F32))
    easbc = _es.enter_context(nc.sbuf_tensor([128, 128], F32))
    btm = _es.enter_context(nc.sbuf_tensor([128, 1024], BF16))
    xd = _es.enter_context(nc.sbuf_tensor([128, 2 * 512], BF16))
    zbuf = _es.enter_context(nc.sbuf_tensor([128, 128], F32))
    p_st = _es.enter_context(nc.psum_tensor([128, 1024], F32))
    p_pp = _es.enter_context(nc.psum_tensor([128, 1024], F32))
    p_bt = _es.enter_context(nc.psum_tensor([128, 1024], BF16))

    with nc.Block(name + "_c") as blk:
        @blk.gpsimd
        def _(g):
            g.memset(C.prevT[:, :], 0.0)
            g.memset(zbuf[:, :], 0.0)
        @blk.sync
        def _(sy):
            sy.sem_inc(s.q_zc, 2)
            sy.sem_inc(s.q_stf, 2)

    with nc.Block(name) as blk:
        @blk.sync
        def _(sy):
            for z in range(NZ):
                b = z % 2
                sy.wait_ge(s.q_zc, z + 1)
                sy.dma_start_transpose(
                    X[:, b * FI:(b + 1) * FI],
                    C.sp_hsf[0:FI, z * 128:(z + 1) * 128]).then_inc(s.q_zl, 16)
                sy.dma_start(out=pk[:, b * 2048:(b + 1) * 2048],
                             in_=C.sp_pack[z, :, :]).then_inc(s.q_zl, 16)
                sy.dma_start(out=dtz[:, b * 128:(b + 1) * 128],
                             in_=C.sp_dtv[:, z * 128:(z + 1) * 128]).then_inc(s.q_zl, 16)
            sy.wait_ge(s.q_zc, 10)

        @blk.vector
        def _(v):
          with contextlib.ExitStack() as es:
            v_zl = WR(v, es, name + "_v_zl", 48, 48)
            v_c5 = WR(v, es, name + "_v_c5", 5, M)
            v_c9 = WR(v, es, name + "_v_c9", 9, M)
            v_c11 = WR(v, es, name + "_v_c11", 11, M)
            v_stm = WR(v, es, name + "_v_stm", 1, 1)
            ve = WR(v, es, name + "_ve", 1, 1)
            with v.Fori(0, 4) as jz:
                for b in range(2):
                    v_zl.wait(s.q_zl)
                    ve.link(v.tensor_scalar_mul(adt[:, :], dtz[:, b * 128:(b + 1) * 128],
                                                C.aneg[:, 0:1]), s.q_vs)
                    v.engine_nop().then_inc(s.q_ch, 1)
                    ve.link(v.tensor_tensor_scan(acum[:, :], adt[:, :], zbuf[:, :], 0.0,
                                                 ALU.add, ALU.add), s.q_vs)
                    v.engine_nop().then_inc(s.q_ch, 1)
                    ve.link(v.tensor_scalar(dcy[:, :], acum[:, :], acum[:, 127:128], None,
                                            ALU.subtract), s.q_vs)
                    v.engine_nop().then_inc(s.q_ch, 1)
                    v_c5.wait(s.q_ch)
                    ve.link(v.tensor_copy(dtT[:, :], p_pp[:, 0:128]), s.q_vs)
                    ve.link(v.tensor_copy(dcyT[:, :], p_pp[:, 128:256]), s.q_vs)
                    ve.link(v.tensor_copy(adtT[:, :], p_pp[:, 256:384]), s.q_vs)
                    ve.link(v.tensor_mul(dd[:, :], dtT[:, :], dcyT[:, :]), s.q_vs)
                    v.engine_nop().then_inc(s.q_ch, 1)
                    v_c9.wait(s.q_ch)
                    ve.link(v.tensor_copy(easbc[:, :], p_pp[:, 640:768]), s.q_vs)
                    v.engine_nop().then_inc(s.q_ch, 1)
                    v_c11.wait(s.q_ch)
                    ve.link(v.tensor_copy(btm[:, :], p_bt[:, 0:1024]), s.q_vs)
                    v.engine_nop().then_inc(s.q_ch, 1)
                    for grp in range(16):
                        gslot = grp % 2
                        for hh in range(8):
                            h = grp * 8 + hh
                            ins = v.tensor_scalar_mul(
                                xd[:, gslot * 512 + hh * 64:gslot * 512 + (hh + 1) * 64],
                                X[:, b * FI + h * 64:b * FI + (h + 1) * 64], dd[:, h:h + 1])
                        ins.then_inc(s.q_xd, 1)
                        v_stm.wait(s.q_stm)
                        for hh in range(8):
                            h = grp * 8 + hh
                            ins = v.scalar_tensor_tensor(
                                C.prevT[:, h * 64:(h + 1) * 64],
                                C.prevT[:, h * 64:(h + 1) * 64], easbc[:, h:h + 1],
                                p_st[:, gslot * 512 + hh * 64:gslot * 512 + (hh + 1) * 64],
                                ALU.mult, ALU.add)
                        ins.then_inc(s.q_stf, 1)
                    v.engine_nop().then_inc(s.q_zc, 1)

        @blk.scalar
        def _(sc):
          with contextlib.ExitStack() as es:
            s_c4 = WR(sc, es, name + "_s_c4", 3, M)
            s_c8 = WR(sc, es, name + "_s_c8", 7, M)
            with sc.Fori(0, 4) as jz:
                for b in range(2):
                    s_c4.wait(s.q_ch)
                    sc.activation(dcy[:, :], dcy[:, :], AF.Exp, scale=-1.0).then_inc(s.q_ch, 1)
                    s_c8.wait(s.q_ch)
                    sc.activation(easr[0:1, :], p_pp[0:1, 512:640], AF.Exp).then_inc(s.q_ch, 1)

        @blk.tensor
        def _(tn):
          with contextlib.ExitStack() as es:
            t_zl = WR(tn, es, name + "_t_zl", 48, 48)
            t_c5 = WR(tn, es, name + "_t_c5", 4, M)
            t_c7 = WR(tn, es, name + "_t_c7", 6, M)
            t_c9 = WR(tn, es, name + "_t_c9", 8, M)
            t_c11 = WR(tn, es, name + "_t_c11", 10, M)
            t_xd = WR(tn, es, name + "_t_xd", 1, 1)
            t_stf = WR(tn, es, name + "_t_stf", 1, 1)
            with tn.Fori(0, 4) as jz:
                for b in range(2):
                    t_zl.wait(s.q_zl)
                    t_c5.wait(s.q_ch)
                    tn.transpose(p_pp[:, 0:128], dtz[:, b * 128:(b + 1) * 128], C.id32[:, :])
                    tn.transpose(p_pp[:, 128:256], dcy[:, :], C.id32[:, :])
                    ins = tn.transpose(p_pp[:, 256:384], adt[:, :], C.id32[:, :])
                    ins.then_inc(s.q_ch, 1)
                    t_c7.wait(s.q_ch)
                    tn.matmul(p_pp[0:1, 512:640], C.onescol[:, 0:1], adtT[:, :],
                              start=True, stop=True).then_inc(s.q_ch, 1)
                    t_c9.wait(s.q_ch)
                    tn.matmul(p_pp[:, 640:768], C.ones32[0:1, 0:128], easr[0:1, :],
                              start=True, stop=True).then_inc(s.q_ch, 1)
                    t_c11.wait(s.q_ch)
                    for gg in range(NG):
                        ins = tn.transpose(p_bt[:, gg * 128:(gg + 1) * 128],
                                           pk[:, b * 2048 + gg * 128:b * 2048 + (gg + 1) * 128],
                                           C.id16[:, :])
                    ins.then_inc(s.q_ch, 1)
                    for grp in range(16):
                        gslot = grp % 2
                        t_xd.wait(s.q_xd)
                        t_stf.wait(s.q_stf)
                        for hh in range(8):
                            h = grp * 8 + hh
                            ins = tn.matmul(
                                p_st[:, gslot * 512 + hh * 64:gslot * 512 + (hh + 1) * 64],
                                btm[:, (h // 16) * 128:(h // 16 + 1) * 128],
                                xd[:, gslot * 512 + hh * 64:gslot * 512 + (hh + 1) * 64],
                                start=True, stop=True)
                        ins.then_inc(s.q_stm, 1)

    with nc.Block(name + "_so") as blk:
        @blk.gpsimd
        def _(g):
            g.dma_start(out=C.shx[bass.ds(C.e_par128, 128), 256:8448],
                        in_=C.prevT[:, :]).then_inc(s.q_so, 16)
            g.wait_ge(s.q_so, 16)
            g.dma_start(out=C.shf[0:1, bass.ds(C.e_par2p1, 1)],
                        in_=C.noncesb[0:1, 0:1]).then_inc(s.q_so, 16)
            g.wait_ge(s.q_so, 32)


def emit_exch2(nc, C, name="x2"):
  s = mk_sems(nc, C.st, name, ["q_fl", "q_po"])
  with contextlib.ExitStack() as _es:
    sin = _es.enter_context(nc.sbuf_tensor([128, FI], F32))
    with nc.Block(name) as blk:
        @blk.gpsimd
        def _(g):
            with (g.register(name + "_nd") as nd, g.register(name + "_rexp") as rexp):
                g.reg_mov(nd, 1)
                g.reg_mov(rexp, 16)
                if not SIM_SKIP_POLL:
                    with g.While(nd):
                        g.dma_start(out=C.flsb[0:1, 0:1],
                                    in_=C.shf[0:1, bass.ds(C.e_oth2p1, 1)]).then_inc(s.q_fl, 16)
                        g.wait_ge(s.q_fl, rexp)
                        g.reg_add(rexp, rexp, 16)
                        fv = g.value_load(C.flsb[0:1, 0:1])
                        g.reg_mov(nd, fv - C.gnv)
                else:
                    g.dma_start(out=C.shx[bass.ds(C.e_oth128, 128), 256:8448],
                                in_=C.prevT[:, :]).then_inc(s.q_fl, 16)
                    g.wait_ge(s.q_fl, 16)
                g.dma_start(out=sin[:, :],
                            in_=C.shx[bass.ds(C.e_oth128, 128), 256:8448]).then_inc(s.q_po, 16)
                g.wait_ge(s.q_po, 16)
    with nc.Block(name + "_m") as blk:
        @blk.vector
        def _(v):
            v.tensor_scalar_mul(C.prevT[:, :], sin[:, :], C.pmask[:, 0:1])


def emit_ssdB(nc, C, name="sb"):
  s = mk_sems(nc, C.st, name, ["q_zl", "q_gt", "q_zc", "q_ch", "q_pb", "q_dm",
                               "q_l1", "q_mg", "q_mh", "q_xc", "q_xd",
                               "q_pc", "q_ym", "q_t1", "q_t2", "q_ev", "q_sg",
                               "q_go1", "q_rstd", "q_og", "q_ogc", "q_gs", "q_vs",
                               "q_ssl"])
  M = 13
  with contextlib.ExitStack() as _es:
    X = _es.enter_context(nc.sbuf_tensor([128, 2 * FI], BF16))
    gt = _es.enter_context(nc.sbuf_tensor([128, 2 * FI], BF16))
    pk = _es.enter_context(nc.sbuf_tensor([128, 2 * 2048], BF16))
    dtz = _es.enter_context(nc.sbuf_tensor([128, 2 * 128], F32))
    adt = _es.enter_context(nc.sbuf_tensor([128, 128], F32))
    acum = _es.enter_context(nc.sbuf_tensor([128, 128], F32))
    dcy = _es.enter_context(nc.sbuf_tensor([128, 128], F32))
    dtT = _es.enter_context(nc.sbuf_tensor([128, 128], F32))
    dcyT = _es.enter_context(nc.sbuf_tensor([128, 128], F32))
    adtT = _es.enter_context(nc.sbuf_tensor([128, 128], F32))
    acumT = _es.enter_context(nc.sbuf_tensor([128, 128], F32))
    eacT = _es.enter_context(nc.sbuf_tensor([128, 128], F32))
    dd = _es.enter_context(nc.sbuf_tensor([128, 128], F32))
    easr = _es.enter_context(nc.sbuf_tensor([1, 128], F32))
    easbc = _es.enter_context(nc.sbuf_tensor([128, 128], F32))
    btm = _es.enter_context(nc.sbuf_tensor([128, 1024], BF16))
    zbuf = _es.enter_context(nc.sbuf_tensor([128, 128], F32))
    Pb = _es.enter_context(nc.sbuf_tensor([128, 2 * 128], BF16))
    Lm = _es.enter_context(nc.sbuf_tensor([128, 2 * 128], F32))
    Mh = _es.enter_context(nc.sbuf_tensor([128, 2 * 128], BF16))
    xch = _es.enter_context(nc.sbuf_tensor([128, 2 * 64], BF16))
    xdb = _es.enter_context(nc.sbuf_tensor([128, 2 * 64], BF16))
    pc16 = _es.enter_context(nc.sbuf_tensor([128, 2 * 64], BF16))
    t1b = _es.enter_context(nc.sbuf_tensor([128, 2 * 64], F32))
    t2b = _es.enter_context(nc.sbuf_tensor([128, 2 * 64], F32))
    ych = _es.enter_context(nc.sbuf_tensor([128, FI], F32))
    sgb = _es.enter_context(nc.sbuf_tensor([128, FI // 2], F32))
    gout = _es.enter_context(nc.sbuf_tensor([128, FI // 2], F32))
    og = _es.enter_context(nc.sbuf_tensor([128, FI // 2], BF16))
    ssq = _es.enter_context(nc.sbuf_tensor([128, 16], F32))
    p_d = _es.enter_context(nc.psum_tensor([128, 1024], F32))
    p_y = _es.enter_context(nc.psum_tensor([128, 1024], F32))
    p_m = _es.enter_context(nc.psum_tensor([128, 1024], F32))
    p_pp = _es.enter_context(nc.psum_tensor([128, 512], F32))
    p_bt = _es.enter_context(nc.psum_tensor([128, 1024], BF16))

    with nc.Block(name + "_c") as blk:
        @blk.gpsimd
        def _(g):
            g.memset(zbuf[:, :], 0.0)
        @blk.sync
        def _(sy):
            sy.sem_inc(s.q_zc, 2)
            sy.sem_inc(s.q_gs, 16)
            sy.sem_inc(s.q_ev, 2)
            sy.sem_inc(s.q_ym, 2)
            sy.sem_inc(s.q_mh, 16)

    with nc.Block(name) as blk:
        @blk.sync
        def _(sy):
            def spills(c):
                for hf in range(2):
                    sy.wait_ge(s.q_ogc, 2 * c + hf + 1)
                    sy.dma_start(out=C.sp_outg[c * 128:(c + 1) * 128,
                                               hf * (FI // 2):(hf + 1) * (FI // 2)],
                                 in_=og[:, :]).then_inc(s.q_gs, 16)
            for z in range(NZ):
                b = z % 2
                if z >= 1:
                    spills(z - 1)
                sy.wait_ge(s.q_zc, z + 1)
                sy.dma_start_transpose(
                    X[:, b * FI:(b + 1) * FI],
                    C.sp_hsf[0:FI, z * 128:(z + 1) * 128]).then_inc(s.q_zl, 16)
                sy.dma_start(out=pk[:, b * 2048:(b + 1) * 2048],
                             in_=C.sp_pack[z, :, :]).then_inc(s.q_zl, 16)
                sy.dma_start(out=dtz[:, b * 128:(b + 1) * 128],
                             in_=C.sp_dtv[:, z * 128:(z + 1) * 128]).then_inc(s.q_zl, 16)
                sy.dma_start_transpose(
                    gt[:, b * FI:(b + 1) * FI],
                    C.sp_gate[0:FI, z * 128:(z + 1) * 128]).then_inc(s.q_gt, 16)
            spills(NZ - 1)
            sy.wait_ge(s.q_gs, 16 + 32 * NZ)

        @blk.scalar
        def _(sc):
          with contextlib.ExitStack() as es:
            s_c4 = WR(sc, es, name + "_s_c4", 3, M)
            s_c7 = WR(sc, es, name + "_s_c7", 6, M)
            s_c9 = WR(sc, es, name + "_s_c9", 8, M)
            s_dm0 = WR(sc, es, name + "_s_dm0", 0, 128)
            s_dm1 = WR(sc, es, name + "_s_dm1", 1, 1)
            s_mh = WR(sc, es, name + "_s_mh", 17, 1)
            s_ym = WR(sc, es, name + "_s_ym", 3, 1)
            s_t2 = WR(sc, es, name + "_s_t2", 1, 1)
            s_gt = WR(sc, es, name + "_s_gt", 16, 16)
            s_go1 = WR(sc, es, name + "_s_go1", 1, 1)
            s_og = WR(sc, es, name + "_s_og", 1, 1)
            s_gs = WR(sc, es, name + "_s_gs", 16, 16)
            se = WR(sc, es, name + "_se", 1, 1)

            def mkpb(h):
                sc.activation(Pb[:, (h % 2) * 128:(h % 2) * 128 + 128], C.strict[:, :],
                              AF.Copy, scale=adtT[:, h:h + 1]).then_inc(s.q_pb, 1)

            with sc.Fori(0, 4) as jz:
                for b in range(2):
                    s_c4.wait(s.q_ch)
                    sc.activation(dcy[:, :], dcy[:, :], AF.Exp, scale=-1.0).then_inc(s.q_ch, 1)
                    s_c7.wait(s.q_ch)
                    sc.activation(eacT[:, :], acumT[:, :], AF.Exp).then_inc(s.q_ch, 1)
                    s_c9.wait(s.q_ch)
                    sc.activation(easr[0:1, :], p_m[0:1, 128:256], AF.Exp).then_inc(s.q_ch, 1)
                    s_dm0.wait(s.q_dm)
                    mkpb(0)
                    mkpb(1)
                    for h in range(NH):
                        hs = (h % 2)
                        s_dm1.wait(s.q_dm)
                        if h < NH - 2:
                            mkpb(h + 2)
                        if h >= 2:
                            s_mh.wait(s.q_mh)
                        sc.activation(Lm[:, hs * 128:(hs + 1) * 128],
                                      p_d[:, hs * 512:hs * 512 + 128],
                                      AF.Exp).then_inc(s.q_l1, 1)
                        if h % 2 == 0:
                            sc.activation(pc16[:, :],
                                          C.prevT[:, h * 64:(h + 2) * 64],
                                          AF.Copy).then_inc(s.q_pc, 2)
                        s_ym.wait(s.q_ym)
                        if h >= 2:
                            s_t2.wait(s.q_t2)
                        sc.activation(t1b[:, hs * 64:(hs + 1) * 64],
                                      p_y[:, hs * 512 + 64:hs * 512 + 128],
                                      AF.Copy, scale=eacT[:, h:h + 1]).then_inc(s.q_t1, 1)
                    s_mh.bump(); s_mh.bump()
                    s_t2.bump(); s_t2.bump()
                    s_gt.wait(s.q_gt)
                    for hf in range(2):
                        fo = hf * (FI // 2)
                        sc.activation(sgb[:, :], gt[:, b * FI + fo:b * FI + fo + FI // 2],
                                      AF.Silu).then_inc(s.q_sg, 1)
                        s_go1.wait(s.q_go1)
                        for gg in range(4):
                            sc.activation(sgb[:, gg * 1024:(gg + 1) * 1024],
                                          gout[:, gg * 1024:(gg + 1) * 1024], AF.Square,
                                          accum_out=ssq[:, hf * 4 + gg:hf * 4 + gg + 1])
                        r0 = 8 + hf * 4
                        se.link(sc.activation(ssq[:, r0:r0 + 4], ssq[:, hf * 4:hf * 4 + 4],
                                              AF.Copy, scale=1.0 / 1024.0), s.q_ssl)
                        se.link(sc.activation(ssq[:, r0:r0 + 4], ssq[:, r0:r0 + 4],
                                              AF.Ln, bias=C.epsc[:, 0:1]), s.q_ssl)
                        sc.activation(ssq[:, r0:r0 + 4], ssq[:, r0:r0 + 4],
                                      AF.Exp, scale=-0.5).then_inc(s.q_rstd, 1)
                        s_og.wait(s.q_og)
                        s_gs.wait(s.q_gs)
                        sc.activation(og[:, :], gout[:, :], AF.Copy).then_inc(s.q_ogc, 1)

        @blk.tensor
        def _(tn):
          with contextlib.ExitStack() as es:
            t_zl = WR(tn, es, name + "_t_zl", 48, 48)
            t_c5 = WR(tn, es, name + "_t_c5", 4, M)
            t_c8 = WR(tn, es, name + "_t_c8", 7, M)
            t_c10 = WR(tn, es, name + "_t_c10", 9, M)
            t_c12 = WR(tn, es, name + "_t_c12", 11, M)
            t_pb = WR(tn, es, name + "_t_pb", 1, 1)
            t_l1 = WR(tn, es, name + "_t_l1", 0, 1)
            t_mh = WR(tn, es, name + "_t_mh", 17, 1)
            t_mhg = WR(tn, es, name + "_t_mhg", 0, 16)
            t_mhall = WR(tn, es, name + "_t_mhall", 16, 128)
            t_pc = WR(tn, es, name + "_t_pc", 1, 1)
            t_xd = WR(tn, es, name + "_t_xd", 1, 1)
            t_ev = WR(tn, es, name + "_t_ev", 1, 1)

            def pdmm(h):
                hs = (h % 2)
                tn.matmul(p_d[:, hs * 512:hs * 512 + 128],
                          Pb[:, hs * 128:(hs + 1) * 128], C.qtril[:, :],
                          start=True, stop=False)
                tn.matmul(p_d[:, hs * 512:hs * 512 + 128],
                          C.id16[:, :], C.negstrict[:, :],
                          start=False, stop=True).then_inc(s.q_dm, 1)

            with tn.Fori(0, 4) as jz:
                for b in range(2):
                    t_zl.wait(s.q_zl)
                    t_c5.wait(s.q_ch)
                    tn.transpose(p_pp[:, 0:128], dtz[:, b * 128:(b + 1) * 128], C.id32[:, :])
                    tn.transpose(p_pp[:, 128:256], dcy[:, :], C.id32[:, :])
                    tn.transpose(p_pp[:, 256:384], adt[:, :], C.id32[:, :])
                    ins = tn.transpose(p_pp[:, 384:512], acum[:, :], C.id32[:, :])
                    ins.then_inc(s.q_ch, 1)
                    t_c8.wait(s.q_ch)
                    t_mhall.wait(s.q_mh)
                    tn.matmul(p_m[0:1, 128:256], C.onescol[:, 0:1], adtT[:, :],
                              start=True, stop=True).then_inc(s.q_ch, 1)
                    t_c10.wait(s.q_ch)
                    tn.matmul(p_m[:, 256:384], C.ones32[0:1, 0:128], easr[0:1, :],
                              start=True, stop=True).then_inc(s.q_ch, 1)
                    t_c12.wait(s.q_ch)
                    for gg in range(NG):
                        ins = tn.transpose(p_bt[:, gg * 128:(gg + 1) * 128],
                                           pk[:, b * 2048 + gg * 128:b * 2048 + (gg + 1) * 128],
                                           C.id16[:, :])
                    ins.then_inc(s.q_ch, 1)
                    t_pb.wait(s.q_pb)
                    t_l1.wait(s.q_l1)
                    pdmm(0)
                    for h in range(NH):
                        g2 = h // 16
                        ys = (h % 2) * 512
                        if h % 16 == 0:
                            gg2 = (g2 % 2) * 512
                            t_mhg.wait(s.q_mh)
                            tn.matmul(p_m[:, gg2:gg2 + 128],
                                      pk[:, b * 2048 + g2 * 128:b * 2048 + (g2 + 1) * 128],
                                      pk[:, b * 2048 + 1024 + g2 * 128:b * 2048 + 1024 + (g2 + 1) * 128],
                                      start=True, stop=True).then_inc(s.q_mg, 1)
                        if h < NH - 1:
                            t_pb.wait(s.q_pb)
                            t_l1.wait(s.q_l1)
                            pdmm(h + 1)
                        t_mh.wait(s.q_mh)
                        t_ev.wait(s.q_ev)
                        t_xd.wait(s.q_xd)
                        tn.matmul(p_y[:, ys:ys + 64], Mh[:, (h % 2) * 128:(h % 2 + 1) * 128],
                                  xch[:, (h % 2) * 64:(h % 2 + 1) * 64], start=True, stop=True)
                        t_pc.wait(s.q_pc)
                        tn.matmul(p_y[:, ys + 64:ys + 128],
                                  pk[:, b * 2048 + 1024 + g2 * 128:b * 2048 + 1024 + (g2 + 1) * 128],
                                  pc16[:, (h % 2) * 64:(h % 2 + 1) * 64], start=True, stop=True)
                        ins = tn.matmul(p_y[:, ys + 128:ys + 192],
                                        btm[:, g2 * 128:(g2 + 1) * 128],
                                        xdb[:, (h % 2) * 64:(h % 2 + 1) * 64],
                                        start=True, stop=True)
                        ins.then_inc(s.q_ym, 1)

        @blk.vector
        def _(v):
          with contextlib.ExitStack() as es:
            v_zl = WR(v, es, name + "_v_zl", 48, 48)
            v_c5 = WR(v, es, name + "_v_c5", 5, M)
            v_c10 = WR(v, es, name + "_v_c10", 10, M)
            v_c13x = WR(v, es, name + "_v_c13x", 12, M)
            v_ymg = WR(v, es, name + "_v_ymg", 1, 1)
            v_l1 = WR(v, es, name + "_v_l1", 1, 1)
            v_mg = WR(v, es, name + "_v_mg", 1, 1)
            v_ym = WR(v, es, name + "_v_ym", 3, 1)
            v_t1 = WR(v, es, name + "_v_t1", 1, 1)
            v_sg = WR(v, es, name + "_v_sg", 1, 1)
            v_rstd = WR(v, es, name + "_v_rstd", 1, 1)
            v_ogc = WR(v, es, name + "_v_ogc", 0, 1)
            ve = WR(v, es, name + "_ve", 1, 1)

            def late(v, hp, b):
                hps = (hp % 2)
                v_t1.wait(s.q_t1)
                v.scalar_tensor_tensor(
                    t2b[:, hps * 64:(hps + 1) * 64],
                    X[:, b * FI + hp * 64:b * FI + (hp + 1) * 64],
                    C.dbc[:, hp:hp + 1], t1b[:, hps * 64:(hps + 1) * 64],
                    ALU.mult, ALU.add).then_inc(s.q_t2, 1)
                v.tensor_add(ych[:, hp * 64:(hp + 1) * 64],
                             t2b[:, hps * 64:(hps + 1) * 64],
                             p_y[:, hps * 512:hps * 512 + 64])
                v.scalar_tensor_tensor(
                    C.prevT[:, hp * 64:(hp + 1) * 64],
                    C.prevT[:, hp * 64:(hp + 1) * 64], easbc[:, hp:hp + 1],
                    p_y[:, hps * 512 + 128:hps * 512 + 192],
                    ALU.mult, ALU.add).then_inc(s.q_ev, 1)

            with v.Fori(0, 4) as jz:
                for b in range(2):
                    v_zl.wait(s.q_zl)
                    ve.link(v.tensor_scalar_mul(adt[:, :], dtz[:, b * 128:(b + 1) * 128],
                                                C.aneg[:, 0:1]), s.q_vs)
                    v.engine_nop().then_inc(s.q_ch, 1)
                    ve.link(v.tensor_tensor_scan(acum[:, :], adt[:, :], zbuf[:, :], 0.0,
                                                 ALU.add, ALU.add), s.q_vs)
                    v.engine_nop().then_inc(s.q_ch, 1)
                    ve.link(v.tensor_scalar(dcy[:, :], acum[:, :], acum[:, 127:128], None,
                                            ALU.subtract), s.q_vs)
                    v.engine_nop().then_inc(s.q_ch, 1)
                    v_c5.wait(s.q_ch)
                    ve.link(v.tensor_copy(dtT[:, :], p_pp[:, 0:128]), s.q_vs)
                    ve.link(v.tensor_copy(dcyT[:, :], p_pp[:, 128:256]), s.q_vs)
                    ve.link(v.tensor_copy(adtT[:, :], p_pp[:, 256:384]), s.q_vs)
                    ve.link(v.tensor_copy(acumT[:, :], p_pp[:, 384:512]), s.q_vs)
                    ve.link(v.tensor_mul(dd[:, :], dtT[:, :], dcyT[:, :]), s.q_vs)
                    v.engine_nop().then_inc(s.q_ch, 1)
                    v_c10.wait(s.q_ch)
                    ve.link(v.tensor_copy(easbc[:, :], p_m[:, 256:384]), s.q_vs)
                    v.engine_nop().then_inc(s.q_ch, 1)
                    v_c13x.wait(s.q_ch)
                    ve.link(v.tensor_copy(btm[:, :], p_bt[:, 0:1024]), s.q_vs)
                    v.engine_nop().then_inc(s.q_ch, 1)
                    def xload(k):
                        ks = (k % 2)
                        v_ymg.wait(s.q_ym)
                        v.tensor_scalar_mul(xch[:, ks * 64:(ks + 1) * 64],
                                            X[:, b * FI + k * 64:b * FI + (k + 1) * 64],
                                            dtT[:, k:k + 1])
                        v.tensor_scalar_mul(xdb[:, ks * 64:(ks + 1) * 64],
                                            X[:, b * FI + k * 64:b * FI + (k + 1) * 64],
                                            dd[:, k:k + 1]).then_inc(s.q_xd, 1)
                    xload(0)
                    for h in range(NH):
                        hs = (h % 2)
                        if h + 1 < NH:
                            xload(h + 1)
                        if h % 16 == 0:
                            v_mg.wait(s.q_mg)
                        v_l1.wait(s.q_l1)
                        v.tensor_mul(Mh[:, hs * 128:(hs + 1) * 128],
                                     p_m[:, ((h // 16) % 2) * 512:((h // 16) % 2) * 512 + 128],
                                     Lm[:, hs * 128:(hs + 1) * 128]).then_inc(s.q_mh, 1)
                        if h >= 1:
                            v_ym.wait(s.q_ym)
                            late(v, h - 1, b)
                    v_ym.wait(s.q_ym)
                    late(v, NH - 1, b)
                    for hf in range(2):
                        fo = hf * (FI // 2)
                        v_sg.wait(s.q_sg)
                        v_ogc.wait(s.q_ogc)
                        v.tensor_mul(gout[:, :], sgb[:, :],
                                     ych[:, fo:fo + FI // 2]).then_inc(s.q_go1, 1)
                        v_rstd.wait(s.q_rstd)
                        for gg in range(4):
                            ins = v.tensor_scalar_mul(gout[:, gg * 1024:(gg + 1) * 1024],
                                                      gout[:, gg * 1024:(gg + 1) * 1024],
                                                      ssq[:, 8 + hf * 4 + gg:9 + hf * 4 + gg])
                        ins.then_inc(s.q_og, 1)
                    v.engine_nop().then_inc(s.q_zc, 1)

    if PHASES == 7 and DBG_SSD:
        sd = mk_sems(nc, C.st, name + "dbg", ["q"])
        with nc.Block(name + "_dbg") as blk:
            @blk.gpsimd
            def _(g):
                g.dma_start(out=C.outap[0:128, 0:4096], in_=ych[:, 0:4096]).then_inc(sd.q, 16)
                g.dma_start(out=C.outap[128:256, 0:4096], in_=ych[:, 4096:8192]).then_inc(sd.q, 16)
                g.dma_start(out=C.outap[256:384, 0:4096], in_=C.prevT[:, 0:4096]).then_inc(sd.q, 16)
                g.dma_start(out=C.outap[384:512, 0:4096], in_=C.prevT[:, 4096:8192]).then_inc(sd.q, 16)
                g.wait_ge(sd.q, 64)


def emit_outproj(nc, C, name="op"):
  s = mk_sems(nc, C.st, name, ["q_ol", "q_w", "q_m", "q_e", "q_as", "q_al",
                               "q_x1", "q_x2", "q_vs", "q_nw"])
  with contextlib.ExitStack() as _es:
    ofm = _es.enter_context(nc.sbuf_tensor([128, ITL * 1024], BF16))
    wob = _es.enter_context(nc.sbuf_tensor([128, 2 * 8192], BF16))
    ast = _es.enter_context(nc.sbuf_tensor([128, 2 * 1024], BF16))
    nwt = _es.enter_context(nc.sbuf_tensor([128, ITL], F32))
    ps = _es.enter_context(nc.psum_tensor([128, 2048], F32))

    with nc.Block(name + "_c") as blk:
        @blk.sync
        def _(sy):
            sy.sem_inc(s.q_m, 2)
            sy.sem_inc(s.q_e, 2)
            sy.sem_inc(s.q_as, 32)
            sy.dma_start(out=nwt[:, :], in_=C.nwt_in[:, :]).then_inc(s.q_nw, 16)
            for it in range(ITL):
                sy.dma_start_transpose(
                    ofm[:, it * 1024:(it + 1) * 1024],
                    C.sp_outg[0:1024, it * 128:(it + 1) * 128]).then_inc(s.q_ol, 16)
            sy.wait_ge(s.q_ol, ITL * 16)
        @blk.vector
        def _(v):
            v.wait_ge(s.q_nw, 16)
            for it in range(ITL):
                v.wait_ge(s.q_ol, (it + 1) * 16)
                v.tensor_scalar_mul(ofm[:, it * 1024:(it + 1) * 1024],
                                    ofm[:, it * 1024:(it + 1) * 1024],
                                    nwt[:, it:it + 1])

    with nc.Block(name) as blk:
        @blk.gpsimd
        def _(g):
            for k in range(DTL):
                g.wait_ge(s.q_m, k + 1)
                g.dma_start(out=wob[:, (k % 2) * 8192:(k % 2 + 1) * 8192],
                            in_=C.wout[k, :, :]).then_inc(s.q_w, 16)
        @blk.tensor
        def _(tn):
          with contextlib.ExitStack() as es:
            t_w = WR(tn, es, name + "_t_w", 16, 16)
            t_e = WR(tn, es, name + "_t_e", 1, 1)
            with tn.Fori(0, 16) as j:
                for b in range(2):
                    t_w.wait(s.q_w)
                    t_e.wait(s.q_e)
                    for tch in range(2):
                        for it in range(ITL):
                            ins = tn.matmul(
                                ps[:, b * 1024 + tch * 512:b * 1024 + (tch + 1) * 512],
                                wob[:, b * 8192 + it * 128:b * 8192 + (it + 1) * 128],
                                ofm[:, it * 1024 + tch * 512:it * 1024 + (tch + 1) * 512],
                                start=(it == 0), stop=(it == ITL - 1))
                    ins.then_inc(s.q_m, 1)
        @blk.vector
        def _(v):
          with contextlib.ExitStack() as es:
            v_m = WR(v, es, name + "_v_m", 3, 1)
            v_as = WR(v, es, name + "_v_as", 16, 16)
            with v.Fori(0, 16) as j:
                for b in range(2):
                    v_m.wait(s.q_m)
                    v_as.wait(s.q_as)
                    v.tensor_copy(ast[:, b * 1024:(b + 1) * 1024],
                                  ps[:, b * 1024:(b + 1) * 1024]).then_inc(s.q_e, 1)
        @blk.sync
        def _(sy):
          with contextlib.ExitStack() as es:
            y_e = WR(sy, es, name + "_y_e", 3, 1)
            with sy.Fori(0, 16) as j:
                for b in range(2):
                    y_e.wait(s.q_e)
                    sy.dma_start(out=C.sp_attn[ds(j * 256 + b * 128, 128), :],
                                 in_=ast[:, b * 1024:(b + 1) * 1024]).then_inc(s.q_as, 16)
            sy.wait_ge(s.q_as, 32 + 32 * 16)

  s2 = mk_sems(nc, C.st, name + "b", ["q_l", "q_d", "q_o", "q_vs"])
  with contextlib.ExitStack() as _es:
    atm = _es.enter_context(nc.sbuf_tensor([128, 2 * DM], BF16))
    x1t = _es.enter_context(nc.sbuf_tensor([128, 2 * DM], F32))
    x2t = _es.enter_context(nc.sbuf_tensor([128, 2 * DM], F32))
    with nc.Block(name + "b_c") as blk:
        @blk.sync
        def _(sy):
            sy.sem_inc(s2.q_d, 2)
            sy.sem_inc(s2.q_o, 32)
    with nc.Block(name + "b") as blk:
        @blk.sync
        def _(sy):
            for k in range(8):
                b = k % 2
                sy.wait_ge(s2.q_d, k + 1)
                sy.dma_start_transpose(
                    atm[:, b * DM:(b + 1) * DM],
                    C.sp_attn[0:DM, k * 128:(k + 1) * 128]).then_inc(s2.q_l, 16)
                sy.dma_start(out=x1t[:, b * DM:(b + 1) * DM],
                             in_=C.sp_x1[k * 128:(k + 1) * 128, :]).then_inc(s2.q_l, 16)
                sy.wait_ge(s2.q_d, k + 3)
                sy.dma_start(out=C.sp_x2[k * 128:(k + 1) * 128, :],
                             in_=x2t[:, b * DM:(b + 1) * DM]).then_inc(s2.q_o, 16)
            sy.wait_ge(s2.q_o, 32 + 8 * 16)
        @blk.vector
        def _(v):
          with contextlib.ExitStack() as es:
            v_l = WR(v, es, name + "_v_l", 32, 32)
            v_o = WR(v, es, name + "_v_o", 16, 16)
            with v.Fori(0, 4) as jz:
                for b in range(2):
                    v_l.wait(s2.q_l)
                    v_o.wait(s2.q_o)
                    v.tensor_add(x2t[:, b * DM:(b + 1) * DM], x1t[:, b * DM:(b + 1) * DM],
                                 atm[:, b * DM:(b + 1) * DM]).then_inc(s2.q_d, 1)


def _stage(inputs):
    g = {}
    g["wup"] = _pt(inputs["L0_ffn_up_w"])
    g["wdn"] = _pt(inputs["L0_ffn_down_w"])
    g["wf1"] = _pt(inputs["L0_ffn_fc1_w"])
    g["wex"] = _pt(inputs["L0_ffn_expert_w"])
    g["wf2"] = _pt(inputs["L0_ffn_fc2_w"])
    gw = np.asarray(inputs["L0_ffn_gate_w"], np.float32)
    g["wgt"] = np.ascontiguousarray(gw.reshape(DTL, 128, 8).transpose(1, 0, 2).reshape(128, DTL * 8))
    ip = np.asarray(inputs["L0_attn_in_proj_w"], np.float32)
    g["wip"] = _pt(np.ascontiguousarray(ip.T))
    op = np.asarray(inputs["L0_attn_out_proj_w"], np.float32)
    g["wout"] = _pt(np.ascontiguousarray(op.T))
    cw = np.asarray(inputs["L0_attn_conv_w"], np.float32).reshape(NCV * 128, 4)
    g["cw"] = np.ascontiguousarray(cw.reshape(NCV, 128, 4).transpose(1, 0, 2).reshape(128, NCV * 4))
    g["cwt"] = np.ascontiguousarray(cw.reshape(NCV, 128, 4))
    cb = np.asarray(inputs["L0_attn_conv_b"], np.float32)
    g["cb"] = np.ascontiguousarray(cb.reshape(NCV, 128).T)
    g["cbt"] = np.ascontiguousarray(cb.reshape(NCV, 128, 1))
    g["onescol"] = np.ones((128, 1), np.float32)
    g["dtb"] = np.asarray(inputs["L0_attn_dt_bias"], np.float32).reshape(128, 1)
    g["alog"] = np.asarray(inputs["L0_attn_A_log"], np.float32).reshape(128, 1)
    g["dbc"] = np.ascontiguousarray(np.broadcast_to(
        np.asarray(inputs["L0_attn_D"], np.float32)[None, :], (128, 128)))
    g["gbc"] = np.ascontiguousarray(np.broadcast_to(
        np.asarray(inputs["L0_ffn_ln_gamma"], np.float32)[None, :], (128, DM)))
    g["bbc"] = np.ascontiguousarray(np.broadcast_to(
        np.asarray(inputs["L0_ffn_ln_beta"], np.float32)[None, :], (128, DM)))
    g["nwt"] = np.ascontiguousarray(
        np.asarray(inputs["L0_attn_norm_w"], np.float32).reshape(ITL, 128).T)
    g["id16"] = np.eye(128, dtype=ml_dtypes.bfloat16)
    g["id32"] = np.eye(128, dtype=np.float32)
    g["ones32"] = np.ones((1, 128), np.float32)
    ar = np.arange(128)
    g["strictm"] = (ar[:, None] > ar[None, :]).astype(ml_dtypes.bfloat16)
    g["negstrict"] = (-30000.0 * (ar[:, None] > ar[None, :])).astype(ml_dtypes.bfloat16)
    g["qtril"] = (ar[:, None] <= ar[None, :]).astype(ml_dtypes.bfloat16)
    return g


def kernel(**inputs):
    xfull = np.asarray(inputs["L0_attn_x"], np.float32)
    g = _stage(inputs)
    nonce = np.array([[int(time.time_ns() & 0x7FFFFFF0) | 5]], np.int32)
    in_maps = []
    for c in range(8):
        b, h = c // 2, c % 2
        m = dict(g)
        m["x"] = np.ascontiguousarray(xfull[b, h * NT:(h + 1) * NT, :])
        m["pmask"] = np.full((128, 1), float(h), np.float32)
        m["nonce"] = nonce
        in_maps.append(m)
    nc = build()
    res = run_bass_kernel_spmd(nc, in_maps, core_ids=list(range(8)))
    outp = np.empty((4, 2048, DM), np.float32)
    for c in range(8):
        b, h = c // 2, c % 2
        outp[b, h * NT:(h + 1) * NT, :] = res.results[c]["out"]
    return outp
